# version 12
# baseline (speedup 1.0000x reference)
import contextlib
import math
import os

import numpy as np

import concourse.bass as bass
import concourse.mybir as mybir
from concourse.bass_utils import run_bass_kernel_spmd

F32 = mybir.dt.float32
BF16 = mybir.dt.bfloat16
I32 = mybir.dt.int32
AF = mybir.ActivationFunctionType
ALU = mybir.AluOpType
AX = mybir.AxisListType

D = 1024
NH = 16
FF = 4096
NE = 8
S = 16384
NCORES = 8
TC = S // NCORES
NCH = D // 128
TCH = TC // 128
HALF = TC // 2
NSUB = 4
TSUB = TC // NSUB
NFS = 4
FSW = FF // NFS
EPS_IN = 1e-12
EPS_OUT = 1e-5
LN8 = math.log(8.0)

RO_S0, RO_S1, RO_G1, RO_G2 = 0, D, 2 * D, 3 * D
S_BQ, S_G1, S_BE1, S_G2, S_BE2 = range(5)


class Eng:
    def __init__(self, nc, stack, name):
        self.e = getattr(nc, name)
        self.name = name
        self.sem = stack.enter_context(nc.semaphore("sf_" + name))
        self.n = 0
        self.seen = {}

    def tok(self, ins):
        self.n += 1
        ins.then_inc(self.sem, 1)
        return (self.sem, self.n)

    def wait(self, *toks):
        for t in toks:
            if t is None:
                continue
            sem, val = t
            k = id(sem)
            if val > self.seen.get(k, 0):
                self.e.wait_ge(sem, val)
                self.seen[k] = val


class DmaGrp:
    def __init__(self, nc, stack, name):
        self.sem = stack.enter_context(nc.semaphore(name))
        self.count = 0

    def start(self, eng, out, in_, **kw):
        ins = eng.e.dma_start(out, in_, **kw)
        self.count += 16
        ins.then_inc(self.sem, 16)
        return ins

    def tok(self):
        return (self.sem, self.count)


class _StopBuild(Exception):
    pass


def build(debug=(), stop_after=None):
    nc = bass.Bass("TRN2", target_bir_lowering=False)

    x_h = nc.dram_tensor("x", [TC, D], F32, kind="ExternalInput").ap()
    Wq_h = nc.dram_tensor("Wq", [D, D], F32, kind="ExternalInput").ap()
    bq_h = nc.dram_tensor("bq", [D], F32, kind="ExternalInput").ap()
    Wk_h = nc.dram_tensor("Wk", [D, D], F32, kind="ExternalInput").ap()
    bk_h = nc.dram_tensor("bk", [D], F32, kind="ExternalInput").ap()
    Wv_h = nc.dram_tensor("Wv", [D, D], F32, kind="ExternalInput").ap()
    bv_h = nc.dram_tensor("bv", [D], F32, kind="ExternalInput").ap()
    Wo_h = nc.dram_tensor("Wo", [D, D], F32, kind="ExternalInput").ap()
    bo_h = nc.dram_tensor("bo", [D], F32, kind="ExternalInput").ap()
    g1_h = nc.dram_tensor("g1", [D], F32, kind="ExternalInput").ap()
    be1_h = nc.dram_tensor("be1", [D], F32, kind="ExternalInput").ap()
    W1_h = nc.dram_tensor("W1", [D, FF], F32, kind="ExternalInput").ap()
    b1_h = nc.dram_tensor("b1", [FF], F32, kind="ExternalInput").ap()
    W2_h = nc.dram_tensor("W2", [FF, D], F32, kind="ExternalInput").ap()
    b2_h = nc.dram_tensor("b2", [D], F32, kind="ExternalInput").ap()
    g2_h = nc.dram_tensor("g2", [D], F32, kind="ExternalInput").ap()
    be2_h = nc.dram_tensor("be2", [D], F32, kind="ExternalInput").ap()
    Wr_h = nc.dram_tensor("Wr", [D, NE], F32, kind="ExternalInput").ap()
    W1m_h = nc.dram_tensor("W1m", [NE, D, FF], F32, kind="ExternalInput").ap()
    b1m_h = nc.dram_tensor("b1m", [NE, FF], F32, kind="ExternalInput").ap()
    W2m_h = nc.dram_tensor("W2m", [NE, FF, D], F32, kind="ExternalInput").ap()
    b2m_h = nc.dram_tensor("b2m", [NE, D], F32, kind="ExternalInput").ap()
    out_h = nc.dram_tensor("out", [TC, D], F32, kind="ExternalOutput").ap()
    xfm_d = nc.dram_tensor("xfm_d", [NCH, 128, TC], F32)
    W1sel = [nc.dram_tensor("W1sel%d" % e, [D, FF], F32) for e in range(2)]
    W2sel = [nc.dram_tensor("W2sel%d" % e, [FF, D], F32) for e in range(2)]
    cc1i = nc.dram_tensor("cc1i", [128, NH * 64], F32)
    cc1o = nc.dram_tensor("cc1o", [128, NH * 64], F32)
    cc2i = nc.dram_tensor("cc2i", [1, NE], F32)
    cc2o = nc.dram_tensor("cc2o", [1, NE], F32)

    dbg_specs = {
        "d_xfm": ([NCH, 128, TC], F32), "d_eq": ([128, NCH * TC], BF16),
        "d_kv": ([128, NH * 64], F32), "d_ctx": ([128, NCH * TC], BF16),
        "d_attn": ([128, NCH * TSUB], F32), "d_x2b": ([128, NCH * TC], BF16),
        "d_lg": ([1, 64], F32), "d_rq": ([16, TC], F32),
        "d_z1": ([128, NCH * TSUB], F32), "d_kn": ([128, 2048], BF16),
    }
    dbg = {}
    for name in debug:
        shp, dt = dbg_specs[name]
        dbg[name] = nc.dram_tensor(name, shp, dt, kind="ExternalOutput").ap()

    st = contextlib.ExitStack()

    RS = st.enter_context(nc.sbuf_tensor("RS", [128, NCH * TSUB], F32))
    RT = st.enter_context(nc.sbuf_tensor("RT", [128, NCH * TSUB], F32))
    XS = st.enter_context(nc.sbuf_tensor("XS", [128, NCH * TSUB], F32))
    X2S = st.enter_context(nc.sbuf_tensor("X2S", [128, NCH * TSUB], F32))
    B1 = st.enter_context(nc.sbuf_tensor("B1", [128, NCH * TC], BF16))
    B2 = st.enter_context(nc.sbuf_tensor("B2", [128, NCH * TC], BF16))
    WFS = st.enter_context(nc.sbuf_tensor("WFS", [128, NCH * TC], BF16))
    SBF = st.enter_context(nc.sbuf_tensor("SBF", [128, 4096], BF16))
    SSB = st.enter_context(nc.sbuf_tensor("SSB", [128, 1024], BF16))
    KNV = st.enter_context(nc.sbuf_tensor("KNV", [128, 2048], BF16))
    LNR = st.enter_context(nc.sbuf_tensor("LNR", [1, 5 * TSUB], F32))
    IDN = st.enter_context(nc.sbuf_tensor("IDN", [128, 128], F32))
    KVB = st.enter_context(nc.sbuf_tensor("KVB", [128, NH * 64], BF16))
    BIA = st.enter_context(nc.sbuf_tensor("BIA", [128, 84], F32))
    B1F = st.enter_context(nc.sbuf_tensor("B1F", [128, 96], F32))
    ROWS = st.enter_context(nc.sbuf_tensor("ROWS", [1, 4 * D], F32))
    RTR = st.enter_context(nc.sbuf_tensor("RTR", [1, 64], F32))
    RTI = st.enter_context(nc.sbuf_tensor("RTI", [1, 2], I32))
    POOL = st.enter_context(nc.sbuf_tensor("POOL", [128, 32], F32))
    WEB = st.enter_context(nc.sbuf_tensor("WEB", [128, 2], F32))
    BRX = st.enter_context(nc.sbuf_tensor("BRX", [32, 128], F32))
    IND16 = st.enter_context(nc.sbuf_tensor("IND16", [128, 128], BF16))
    ONESC = st.enter_context(nc.sbuf_tensor("ONESC", [128, 1], BF16))
    ONESCF = st.enter_context(nc.sbuf_tensor("ONESCF", [128, 1], F32))
    ONESR = st.enter_context(nc.sbuf_tensor("ONESR", [1, 512], F32))
    WRFM = st.enter_context(nc.sbuf_tensor("WRFM", [128, NCH * NE], F32))

    PA = st.enter_context(nc.psum_tensor("PA", [128, 1024], F32))
    PB = st.enter_context(nc.psum_tensor("PB", [128, 1024], F32))
    PC = st.enter_context(nc.psum_tensor("PC", [128, 1024], F32))
    PD = st.enter_context(nc.psum_tensor("PD", [128, 1024], F32))

    ind16_np = np.zeros((128, 128), dtype=np.float32)
    ind16T_np = np.zeros((16, 8 * 128), dtype=np.float32)
    for c in range(8):
        for p in range(128):
            ind16_np[p, c * 16 + 2 * c + (p // 64)] = 1.0
            ind16T_np[2 * c + (p // 64), c * 128 + p] = 1.0
    identC = nc.inline_tensor(np.eye(128, dtype=np.float32), name="identC").ap()
    ind16C = nc.inline_tensor(ind16_np, name="ind16C").ap()
    ind16TC = nc.inline_tensor(ind16T_np, name="ind16TC").ap()
    ones128cC = nc.inline_tensor(np.ones((128, 1), np.float32), name="ones128cC").ap()
    ones1rC = nc.inline_tensor(np.ones((1, 512), np.float32), name="ones1rC").ap()
    iota8C = nc.inline_tensor(np.arange(8, dtype=np.float32).reshape(1, 8), name="iota8C").ap()

    blk = st.enter_context(nc.Block())
    T = Eng(nc, st, "tensor")
    V = Eng(nc, st, "vector")
    C = Eng(nc, st, "scalar")
    G = Eng(nc, st, "gpsimd")
    Y = Eng(nc, st, "sync")

    g_const = DmaGrp(nc, st, "g_const")
    g_cstG = DmaGrp(nc, st, "g_cstG")
    g_x = DmaGrp(nc, st, "g_x")
    g_xfm = [DmaGrp(nc, st, "g_xfmA"), DmaGrp(nc, st, "g_xfmB")]
    g_xs = DmaGrp(nc, st, "g_xs")
    g_wq = DmaGrp(nc, st, "g_wq")
    g_wkv = DmaGrp(nc, st, "g_wkv")
    g_bias = DmaGrp(nc, st, "g_bias")
    g_rows2 = DmaGrp(nc, st, "g_rows2")
    g_kv1 = DmaGrp(nc, st, "g_kv1")
    g_wfsA = DmaGrp(nc, st, "g_wfsA")
    g_wfsB = DmaGrp(nc, st, "g_wfsB")
    g_wroA = DmaGrp(nc, st, "g_wroA")
    g_wroB = DmaGrp(nc, st, "g_wroB")
    g_lg = DmaGrp(nc, st, "g_lg")
    g_moeb = DmaGrp(nc, st, "g_moeb")
    g_wsel = [DmaGrp(nc, st, "g_wsel%d" % i) for i in range(8)]
    g_out = DmaGrp(nc, st, "g_out")
    g_dbg = DmaGrp(nc, st, "g_dbg")
    cc_sem = st.enter_context(nc.semaphore("cc_sem"))
    reg0 = st.enter_context(nc.gpsimd.register("eidx0"))
    reg1 = st.enter_context(nc.gpsimd.register("eidx1"))
    yreg0 = st.enter_context(nc.sync.register("yidx0"))
    yreg1 = st.enter_context(nc.sync.register("yidx1"))

    stopped = [False]

    def sec(eng, fn):
        if stopped[0]:
            return
        getattr(blk, eng.name)(fn)

    def CHK(name):
        if stop_after == name:
            stopped[0] = True

    from concourse.hw_specs import get_activation_tables
    _tabs = list(get_activation_tables(nc.m.arch).items())
    SET_LNEXP = next(i for i, (n, fs_) in enumerate(_tabs)
                     if AF.Exp in fs_ and AF.Ln in fs_)
    SET_GELU = next(i for i, (n, fs_) in enumerate(_tabs) if AF.Gelu in fs_)

    def load_act_set(cc, idx):
        cc.add_instruction(mybir.InstLoadActFuncSet(
            name=nc.get_next_instruction_name(), act_func_set_id=idx,
            ins=[], outs=[]))

    def b1v(c, lo, n):
        return B1[:, c * TC + lo: c * TC + lo + n]

    def b2v(c, lo, n):
        return B2[:, c * TC + lo: c * TC + lo + n]

    def rsv(c):
        return RS[:, c * TSUB:(c + 1) * TSUB]

    def rtv(c):
        return RT[:, c * TSUB:(c + 1) * TSUB]

    def xsv(c):
        return XS[:, c * TSUB:(c + 1) * TSUB]

    def x2sv(c):
        return X2S[:, c * TSUB:(c + 1) * TSUB]

    rq_rows = RT[0:16, 2048:4096]
    IND16T = SBF[0:16, 0:2048].bitcast(F32)
    wro_slots = [KVB[:, 0:1024], SSB[:, 0:1024]]
    IOTA8 = RTR[0:1, 40:48]


    def c_pre(cc):
        load_act_set(cc, SET_GELU)
    sec(C, c_pre)

    tok_cbias = {}
    def v_cb(v):
        v.memset(BIA[:, 40:41], -LN8)
        v.memset(BIA[:, 41:42], EPS_IN)
        tok_cbias[0] = V.tok(v.memset(BIA[:, 42:43], EPS_OUT))
    sec(V, v_cb)

    def p0_y(y):
        g_const.start(Y, IDN[:], identC[:])
        g_const.start(Y, ONESCF[:], ones128cC[:])
        g_const.start(Y, ONESR[:], ones1rC[:])
        g_const.start(Y, IOTA8, iota8C[:])
        for c in range(NCH):
            g_const.start(Y, WRFM[:, c * NE:(c + 1) * NE], Wr_h[c * 128:(c + 1) * 128, :])
        for off, src in [(RO_S0, bk_h), (RO_S1, bv_h), (RO_G1, g1_h), (RO_G2, g2_h)]:
            g_const.start(Y, ROWS[0:1, off:off + D], src.rearrange("d -> () d"))
    sec(Y, p0_y)
    tok_constY = g_const.tok()

    def p0_g(g):
        g_cstG.start(G, IND16[:], ind16C[:])
        g_cstG.start(G, ONESC[:], ones128cC[:])
        for hh in range(2):
            g_wq.start(G, WFS[:, hh * 4096:(hh + 1) * 4096].rearrange("p (k c) -> p k c", k=4),
                       Wq_h.rearrange("(k p) c -> p k c", p=128)[:, hh * 4:(hh + 1) * 4, :])
    sec(G, p0_g)

    tok_tr = {}
    tok_ev = {}
    tok_xbf = {}
    for t in range(TCH):
        xv = RS[:, (t % 2) * 1024:(t % 2) * 1024 + 1024]
        pw = PA if t % 2 == 0 else PB
        xslot = XS[:, (t % 2) * 2048:(t % 2) * 2048 + 1024]

        def y_load(y, t=t, xv=xv):
            if t >= 2:
                Y.wait(tok_tr[t - 2])
            g_x.start(Y, xv, x_h[t * 128:(t + 1) * 128, :])
        sec(Y, y_load)

        def t_tr(tt, t=t, xv=xv, pw=pw):
            T.wait(tok_constY)
            T.wait((g_x.sem, (t + 1) * 16))
            if t >= 2:
                T.wait(tok_ev[t - 2])
            for c in range(NCH):
                ins = tt.matmul(pw[:, c * 128:(c + 1) * 128],
                                xv[:, c * 128:(c + 1) * 128], IDN[:],
                                start=True, stop=True)
            tok_tr[t] = T.tok(ins)
        sec(T, t_tr)

        def c_ev(cc, t=t, pw=pw, xslot=xslot):
            C.wait(tok_tr[t])
            if t >= 2:
                C.wait((g_xfm[t % 2].sem, (t // 2) * 16))
            t1 = C.tok(cc.copy(xslot, pw[:]))
            C.wait(t1)
            ins = cc.copy(
                B1[:].rearrange("p (c n) -> p c n", c=NCH)[:, :, t * 128:(t + 1) * 128],
                xslot.rearrange("p (c n) -> p c n", c=NCH))
            tok_ev[t] = t1
            tok_xbf[t] = C.tok(ins)
        sec(C, c_ev)

        def y_xfm(y, t=t, xslot=xslot):
            Y.wait(tok_xbf[t])
            g_xfm[t % 2].start(Y, xfm_d[:, :, t * 128:(t + 1) * 128].rearrange("c p n -> p c n"),
                               xslot.rearrange("p (c n) -> p c n", c=NCH))
        sec(Y, y_xfm)

    if "d_xfm" in dbg:
        def y_d(y):
            Y.wait(g_xfm[0].tok())
            Y.wait(g_xfm[1].tok())
            g_dbg.start(Y, dbg["d_xfm"][:], xfm_d.ap())
        sec(Y, y_d)

    CHK("ingest")

    tok_biaT = {}
    tok_biaV = {}
    specs = [(bq_h, S_BQ), (g1_h, S_G1), (be1_h, S_BE1), (g2_h, S_G2), (be2_h, S_BE2)]
    for i, (src, slotc) in enumerate(specs):
        def g_ld(g, src=src, i=i):
            if i >= 1:
                G.wait(tok_biaT[i - 1])
            g_bias.start(G, BRX[0:8, :], src.rearrange("(a b) -> a b", b=128))
        sec(G, g_ld)

        def t_tp(tt, i=i):
            T.wait(tok_constY)
            T.wait((g_bias.sem, (i + 1) * 16))
            if i >= 2:
                T.wait(tok_biaV[i - 2])
            ins = tt.matmul(PC[:, (i % 2) * 512:(i % 2) * 512 + 8], BRX[0:8, :],
                            IDN[0:8, 0:8], start=True, stop=True)
            tok_biaT[i] = T.tok(ins)
        sec(T, t_tp)

        def v_ev(v, i=i, slotc=slotc):
            V.wait(tok_biaT[i])
            t0 = V.tok(v.tensor_copy(BIA[:, slotc * 8:(slotc + 1) * 8],
                                     PC[:, (i % 2) * 512:(i % 2) * 512 + 8]))
            if i == 0:
                V.wait(t0)
                t0 = V.tok(v.tensor_scalar(BIA[:, 44:52], BIA[:, 0:8], -1.0,
                                           None, ALU.mult))
            tok_biaV[i] = t0
        sec(V, v_ev)

    def g_ldb1(g):
        G.wait(tok_biaT[4])
        g_bias.start(G, BRX[:, :], b1_h.rearrange("(a b) -> a b", b=128))
    sec(G, g_ldb1)

    def t_tpb1(tt):
        T.wait(tok_constY)
        T.wait((g_bias.sem, 6 * 16))
        T.wait(tok_biaV[4])
        ins = tt.matmul(PC[:, 0:32], BRX[0:32, :], IDN[0:32, 0:32], start=True, stop=True)
        tok_biaT[5] = T.tok(ins)
    sec(T, t_tpb1)

    def v_evb1(v):
        V.wait(tok_biaT[5])
        tok_biaV[5] = V.tok(v.tensor_copy(B1F[:, 0:32], PC[:, 0:32]))
    sec(V, v_evb1)

    tok_rep = {}
    for j in range(2):
        def t_rep(tt, j=j):
            T.wait(g_const.tok())
            T.wait(tok_biaV[5] if j == 0 else tok_rep[0])
            for h in range(2):
                ins = tt.matmul(PD[:, h * 512:(h + 1) * 512], ONESR[0:1, 0:128],
                                ROWS[0:1, j * D + h * 512: j * D + (h + 1) * 512],
                                start=True, stop=True)
            tok_rep[(j, "t")] = T.tok(ins)
        sec(T, t_rep)

        def v_rep(v, j=j):
            V.wait(tok_rep[(j, "t")])
            tok_rep[j] = V.tok(v.tensor_copy(RT[:, j * 1024:(j + 1) * 1024], PD[:]))
        sec(V, v_rep)
    tok_bias_done = tok_rep.get(1)

    def y_rows2(y):
        Y.wait(tok_rep[(1, "t")])
        g_rows2.start(Y, ROWS[0:1, RO_S0:RO_S0 + D], bo_h.rearrange("d -> () d"))
        g_rows2.start(Y, ROWS[0:1, RO_S1:RO_S1 + D], b2_h.rearrange("d -> () d"))
    sec(Y, y_rows2)

    CHK("bias")

    tok_qmm = {}
    tok_elur = {}
    tok_eluy = {}
    tok_eq = {}
    tok_qsq = {}
    tok_qss = {}
    tok_rqv = {}
    tok_rq = {}
    def emit_qmm(qi):
        half, c = qi // NCH, qi % NCH
        lo = half * HALF
        pq = (PA, PB, PD)[qi % 3]

        def t_qmm(tt, qi=qi, c=c, lo=lo, pq=pq):
            T.wait(g_wq.tok())
            T.wait(g_x.tok())
            T.wait(tok_xbf[TCH - 1])
            if qi == 2:
                T.wait(tok_bias_done)
            if qi >= 3:
                T.wait(tok_elur[qi - 3])
            for k in range(NCH):
                for nn in range(2):
                    ins = tt.matmul(pq[:, nn * 512:(nn + 1) * 512],
                                    WFS[:, k * 1024 + c * 128: k * 1024 + (c + 1) * 128],
                                    b1v(k, lo + nn * 512, 512),
                                    start=(k == 0), stop=(k == NCH - 1))
            tok_qmm[qi] = T.tok(ins)
        sec(T, t_qmm)

    def emit_qrest(qi):
        half, c = qi // NCH, qi % NCH
        lo = half * HALF
        pq = (PA, PB, PD)[qi % 3]

        def c_elu(cc, qi=qi, c=c, pq=pq):
            C.wait(tok_qmm[qi])
            C.wait(tok_bias_done)
            if qi >= 1:
                C.wait(tok_eq[qi - 1])
            t_, r_, d_, y_ = RS[:, 0:1024], RS[:, 1024:2048], RS[:, 2048:3072], RS[:, 3072:4096]
            t1 = C.tok(cc.activation(t_, pq[:], AF.Relu, scale=-1.0,
                                     bias=BIA[:, 44 + c:45 + c]))
            C.wait(t1)
            t2 = C.tok(cc.activation(t_, t_, AF.Tanh, scale=-0.5))
            tok_elur[qi] = C.tok(cc.activation(r_, pq[:], AF.Relu,
                                               bias=BIA[:, S_BQ * 8 + c:S_BQ * 8 + c + 1]))
            C.wait(t2)
            t3 = C.tok(cc.activation(d_, t_, AF.Copy, scale=-1.0, bias=1.0))
            C.wait(t3)
            tok_eluy[qi] = C.tok(cc.activation(y_, d_, AF.Copy, scale=-0.4854,
                                               bias=1.4571))
        sec(C, c_elu)

        def v_eq(v, qi=qi, c=c, lo=lo):
            V.wait(tok_eluy[qi])
            def stepi(ins):
                tk = V.tok(ins)
                V.wait(tk)
                return tk
            t_, r_, d_, y_ = RS[:, 0:1024], RS[:, 1024:2048], RS[:, 2048:3072], RS[:, 3072:4096]
            stepi(v.tensor_mul(d_, d_, y_))
            stepi(v.tensor_scalar(d_, d_, -1.0, 2.0, ALU.mult, ALU.add))
            stepi(v.tensor_mul(y_, y_, d_))
            stepi(v.scalar_tensor_tensor(d_, t_, 2.0, y_, ALU.mult, ALU.mult))
            ins = v.scalar_tensor_tensor(b2v(c, lo, HALF), d_, 1.0, r_,
                                         ALU.mult, ALU.add)
            tok_eq[qi] = V.tok(ins)
        sec(V, v_eq)

        def c_sq(cc, qi=qi, c=c, lo=lo):
            C.wait(tok_eq[qi])
            if qi >= 2:
                C.wait(tok_qss[qi - 2])
            sv = SBF[:, (qi % 2) * 1024:(qi % 2) * 1024 + 1024]
            tok_qsq[qi] = C.tok(cc.activation(sv, b2v(c, lo, HALF), AF.Square))
        sec(C, c_sq)

        def t_ss(tt, qi=qi, half=half, c=c):
            T.wait(tok_qsq[qi])
            T.wait(g_cstG.tok())
            if c == 0 and half == 1:
                T.wait(tok_rqv[0])
            sv = SBF[:, (qi % 2) * 1024:(qi % 2) * 1024 + 1024]
            for nn in range(2):
                ins = tt.matmul(PC[0:16, nn * 512:(nn + 1) * 512],
                                IND16[:, c * 16:(c + 1) * 16],
                                sv[:, nn * 512:(nn + 1) * 512],
                                start=(c == 0), stop=(c == NCH - 1))
            tok_qss[qi] = T.tok(ins)
        sec(T, t_ss)

        if c == NCH - 1:
            def v_rqev(v, half=half, lo=lo, qi=qi):
                V.wait(tok_qss[qi])
                tok_rqv[half] = V.tok(v.tensor_copy(rq_rows[:, lo:lo + HALF],
                                                    PC[0:16, 0:1024]))
            sec(V, v_rqev)

            def v_rq(v, half=half, lo=lo):
                V.wait(tok_rqv[half])
                def stepi(ins):
                    tk = V.tok(ins)
                    V.wait(tk)
                    return tk
                r = rq_rows[:, lo:lo + HALF]
                scrA = KNV[0:16, 0:2048].bitcast(F32)
                scrB = SBF[0:16, 2048:4096].bitcast(F32)
                stepi(v.tensor_scalar(scrB.bitcast(I32), r.bitcast(I32), 1, None,
                                      ALU.logical_shift_right))
                stepi(v.tensor_scalar(scrB.bitcast(I32), scrB.bitcast(I32), -1,
                                      0x5F3759DF, ALU.mult, ALU.add))
                stepi(v.tensor_scalar(scrA, r, -0.5, None, ALU.mult))
                for _ in range(3):
                    stepi(v.tensor_mul(r, scrB, scrB))
                    stepi(v.tensor_mul(r, r, scrA))
                    stepi(v.tensor_scalar(r, r, 1.5, None, ALU.add))
                    stepi(v.tensor_mul(scrB, scrB, r))
                tok_rq[half] = stepi(v.tensor_scalar(r, scrB, 0.125, None, ALU.mult))
            sec(V, v_rq)


    for qi in range(2 * NCH):
        emit_qmm(qi)
        if qi >= 1:
            emit_qrest(qi - 1)
    emit_qrest(2 * NCH - 1)
    if "d_eq" in dbg:
        def y_d2(y):
            Y.wait(tok_eq[2 * NCH - 1])
            g_dbg.start(Y, dbg["d_eq"][:], B2[:])
        sec(Y, y_d2)
    if "d_rq" in dbg:
        def y_d2b(y):
            Y.wait(tok_rq[1])
            g_dbg.start(Y, dbg["d_rq"][:], rq_rows[:, :])
        sec(Y, y_d2b)

    CHK("p1q")

    def g_wkvload(g):
        G.wait(tok_qmm[2 * NCH - 1])
        for hh in range(2):
            g_wkv.start(G, WFS[:, 8192 + hh * 4096: 8192 + (hh + 1) * 4096].rearrange("p (k c) -> p k c", k=4),
                        Wk_h.rearrange("(k p) c -> p k c", p=128)[:, hh * 4:(hh + 1) * 4, :])
        for hh in range(2):
            g_wkv.start(G, WFS[:, hh * 4096:(hh + 1) * 4096].rearrange("p (k c) -> p k c", k=4),
                        Wv_h.rearrange("(k p) c -> p k c", p=128)[:, hh * 4:(hh + 1) * 4, :])
    sec(G, g_wkvload)

    tok_kmm = {}
    tok_kb = {}
    tok_kex = {}
    tok_keluy = {}
    tok_kel = {}
    tok_ksq = {}
    tok_red = {}
    tok_krs = {}
    tok_kn = {}
    tok_vev = {}
    tok_kvmm = {}
    tok_kvadd = {}
    KRD = POOL[:, 0:32]

    def emit_tkv(t):
        def t_kv(tt, t=t):
            T.wait(g_wkv.tok())
            if t == 0:
                T.wait(tok_rqv[1])
            if t >= 1:
                T.wait(tok_kb[t - 1])
                T.wait(tok_vev[t - 1])
            for k in range(NCH):
                xst = b1v(k, t * 128, 128)
                for nn in range(2):
                    tt.matmul(PA[:, nn * 512:(nn + 1) * 512], xst,
                              WFS[:, 8192 + k * 1024 + nn * 512: 8192 + k * 1024 + (nn + 1) * 512],
                              start=(k == 0), stop=(k == NCH - 1))
                for nn in range(2):
                    ins = tt.matmul(PB[:, nn * 512:(nn + 1) * 512], xst,
                                    WFS[:, k * 1024 + nn * 512: k * 1024 + (nn + 1) * 512],
                                    start=(k == 0), stop=(k == NCH - 1))
            tok_kmm[t] = T.tok(ins)
        sec(T, t_kv)

    KVSB = X2S[:, 0:1024]

    def emit_kvacc(t):
        def t_kvacc(tt, t=t):
            T.wait(tok_kn[t])
            T.wait(tok_vev[t])
            if t >= 2:
                T.wait(tok_kvadd[t - 2])
            for p in range(8):
                ins = tt.matmul(PC[:, p * 128:(p + 1) * 128],
                                KNV[:, p * 128:(p + 1) * 128],
                                KNV[:, 1024 + p * 128:1024 + (p + 1) * 128],
                                start=True, stop=True)
            tok_kvmm[t] = T.tok(ins)
        sec(T, t_kvacc)

        def v_kvadd(v, t=t):
            V.wait(tok_kvmm[t])
            if t == 0:
                ins = v.tensor_copy(KVSB, PC[:])
            else:
                ins = v.tensor_add(KVSB, KVSB, PC[:])
            tok_kvadd[t] = V.tok(ins)
        sec(V, v_kvadd)

    for t in range(TCH):
        emit_tkv(t)
        if t >= 1:
            emit_kvacc(t - 1)

        def v_kb(v, t=t):
            V.wait(tok_kmm[t])
            V.wait(tok_bias_done)
            if t >= 1:
                V.wait(tok_kex[t - 1])
            ins = v.scalar_tensor_tensor(RS[:, 0:1024], PA[:], 1.0,
                                         RT[:, 0:1024], ALU.mult, ALU.add)
            tok_kb[t] = V.tok(ins)
        sec(V, v_kb)

        def v_vev(v, t=t):
            if t >= 1:
                V.wait(tok_kvmm[t - 1])
            ins = v.scalar_tensor_tensor(KNV[:, 1024:2048], PB[:], 1.0,
                                         RT[:, 1024:2048], ALU.mult, ALU.add)
            tok_vev[t] = V.tok(ins)
        sec(V, v_vev)

        def c_kelu(cc, t=t):
            C.wait(tok_kb[t])
            t_, r_ = RS[:, 1024:2048], RS[:, 2048:3072]
            d_, y_ = RS[:, 3072:4096], RS[:, 0:1024]
            t1 = C.tok(cc.activation(t_, RS[:, 0:1024], AF.Relu, scale=-1.0))
            C.wait(t1)
            t2 = C.tok(cc.activation(t_, t_, AF.Tanh, scale=-0.5))
            tok_kex[t] = C.tok(cc.activation(r_, RS[:, 0:1024], AF.Relu))
            C.wait(t2)
            C.wait(tok_kex[t])
            t3 = C.tok(cc.activation(d_, t_, AF.Copy, scale=-1.0, bias=1.0))
            C.wait(t3)
            tok_keluy[t] = C.tok(cc.activation(y_, d_, AF.Copy, scale=-0.4854,
                                               bias=1.4571))
        sec(C, c_kelu)

        def v_ek(v, t=t):
            V.wait(tok_keluy[t])
            if t >= 1:
                V.wait(tok_kvmm[t - 1])
            def stepi(ins):
                tk = V.tok(ins)
                V.wait(tk)
                return tk
            t_, r_ = RS[:, 1024:2048], RS[:, 2048:3072]
            d_, y_ = RS[:, 3072:4096], RS[:, 0:1024]
            stepi(v.tensor_mul(d_, d_, y_))
            stepi(v.tensor_scalar(d_, d_, -1.0, 2.0, ALU.mult, ALU.add))
            stepi(v.tensor_mul(y_, y_, d_))
            stepi(v.scalar_tensor_tensor(d_, t_, 2.0, y_, ALU.mult, ALU.mult))
            ins = v.scalar_tensor_tensor(KNV[:, 0:1024], d_, 1.0, r_,
                                         ALU.mult, ALU.add)
            tok_kel[t] = V.tok(ins)
        sec(V, v_ek)

        def c_ksq(cc, t=t):
            C.wait(tok_kel[t])
            if t >= 1:
                C.wait(tok_red[t - 1])
            tok_ksq[t] = C.tok(cc.activation(SBF[:, 2048:3072], KNV[:, 0:1024], AF.Square))
        sec(C, c_ksq)

        def v_red(v, t=t):
            V.wait(tok_ksq[t])
            ins = v.reduce_sum(KRD[:, 0:16],
                               SBF[:, 2048:3072].rearrange("p (h d) -> p h d", h=16),
                               axis=AX.X)
            tok_red[t] = V.tok(ins)
        sec(V, v_red)

        def v_krs(v, t=t):
            V.wait(tok_red[t])
            def stepi(ins):
                tk = V.tok(ins)
                V.wait(tk)
                return tk
            r = KRD[:, 16:32]
            sA = BIA[:, 52:68]
            sB = BIA[:, 68:84]
            stepi(v.tensor_scalar(r.bitcast(I32), KRD[:, 0:16].bitcast(I32), 1,
                                  None, ALU.logical_shift_right))
            stepi(v.tensor_scalar(r.bitcast(I32), r.bitcast(I32), -1, 0x5F3759DF,
                                  ALU.mult, ALU.add))
            stepi(v.tensor_scalar(sA, KRD[:, 0:16], -0.5, None, ALU.mult))
            last = None
            for _ in range(2):
                stepi(v.tensor_mul(sB, r, r))
                stepi(v.tensor_mul(sB, sB, sA))
                stepi(v.tensor_scalar(sB, sB, 1.5, None, ALU.add))
                last = stepi(v.tensor_mul(r, r, sB))
            tok_krs[t] = last
        sec(V, v_krs)

        def v_kn(v, t=t):
            V.wait(tok_krs[t])
            ins = v.tensor_mul(KNV[:, 0:1024].rearrange("p (h d) -> p h d", h=16),
                               KNV[:, 0:1024].rearrange("p (h d) -> p h d", h=16),
                               KRD[:, 16:32].rearrange("p h -> p h ()").broadcast_to([128, 16, 64]))
            tok_kn[t] = V.tok(ins)
        sec(V, v_kn)
    emit_kvacc(TCH - 1)

    if "d_kn" in dbg:
        def y_dkn(y):
            Y.wait(tok_kn[TCH - 1])
            Y.wait(tok_vev[TCH - 1])
            g_dbg.start(Y, dbg["d_kn"][:], KNV[:])
        sec(Y, y_dkn)

    CHK("p1kv")

    tok_kvcp = {}
    def v_kvz(v):
        V.wait(tok_kvadd[TCH - 1])
        V.wait(tok_kex[TCH - 1])
        t1 = V.tok(v.memset(RT[:, 0:1024], 0.0))
        V.wait(t1)
        last = None
        for p in range(8):
            for hh in range(2):
                sl = slice(hh * 64, (hh + 1) * 64)
                last = v.tensor_copy(RT[sl, p * 128 + hh * 64: p * 128 + (hh + 1) * 64],
                                     KVSB[sl, p * 128 + hh * 64: p * 128 + (hh + 1) * 64])
        tok_kvcp[0] = V.tok(last)
    sec(V, v_kvz)

    def g_ar1(g):
        G.wait(tok_kvcp[0])
        g_kv1.start(G, cc1i.ap(), RT[:, 0:1024])
        g.wait_ge(g_kv1.sem, g_kv1.count)
        g.collective_compute("AllReduce", ALU.add,
                             replica_groups=[list(range(NCORES))],
                             ins=[cc1i.ap()], outs=[cc1o.ap()]).then_inc(cc_sem, 1)
        g.wait_ge(cc_sem, 1)
        g_kv1.start(G, RT[:, 0:1024], cc1o.ap())
    sec(G, g_ar1)

    tok_kvbf = {}
    def v_kvbf(v):
        V.wait(g_kv1.tok())
        tok_kvbf[0] = V.tok(v.tensor_copy(KVB[:], RT[:, 0:1024]))
    sec(V, v_kvbf)

    if "d_kv" in dbg:
        def y_d3(y):
            Y.wait(tok_kvbf[0])
            g_dbg.start(Y, dbg["d_kv"][:], RT[:, 0:1024])
        sec(Y, y_d3)

    CHK("ar1")

    def g_ind16t(g):
        G.wait(tok_red[TCH - 1])
        G.wait(tok_qss[2 * NCH - 1])
        g_cstG.start(G, IND16T[:], ind16TC[:])
    sec(G, g_ind16t)
    tok_ind16t = g_cstG.tok()

    tok_ctxv = {}
    prev_ctx = {0: None}
    for c in range(NCH):
        for half in range(2):
            lo = half * HALF

            def t_ctx(tt, c=c, half=half, lo=lo):
                T.wait(tok_rq[1])
                T.wait(tok_kvbf[0])
                T.wait(tok_ind16t)
                if prev_ctx[0] is not None:
                    T.wait(prev_ctx[0])
                for nn in range(2):
                    tt.matmul(PD[:, nn * 512:(nn + 1) * 512],
                              IND16T[:, c * 128:(c + 1) * 128],
                              rq_rows[:, lo + nn * 512: lo + (nn + 1) * 512],
                              start=True, stop=True)
                for nn in range(2):
                    ins = tt.matmul(PA[:, nn * 512:(nn + 1) * 512],
                                    KVB[:, c * 128:(c + 1) * 128],
                                    b2v(c, lo + nn * 512, 512),
                                    start=True, stop=True)
                tok_ctxv[(c, half, "t")] = T.tok(ins)
            sec(T, t_ctx)

            def v_ctx(v, c=c, half=half, lo=lo):
                V.wait(tok_ctxv[(c, half, "t")])
                t1 = V.tok(v.tensor_copy(SSB[:], PD[:]))
                V.wait(t1)
                ins = v.tensor_mul(b1v(c, lo, HALF), PA[:], SSB[:])
                tok_ctxv[(c, half)] = V.tok(ins)
                prev_ctx[0] = tok_ctxv[(c, half)]
            sec(V, v_ctx)
    tok_ctx_done = prev_ctx[0]

    if "d_ctx" in dbg:
        def y_d4(y):
            Y.wait(tok_ctx_done)
            g_dbg.start(Y, dbg["d_ctx"][:], B1[:])
        sec(Y, y_d4)

    CHK("ctx")

    state = {
        "gelu_loaded": False,
        "lnlast": None,
        "statfree": tok_ctx_done,
        "accfree": None,
        "wslot": {},
        "hpw": {0: None, 1: None},
        "hread": {},
        "woslot": {0: None, 1: None},
    }

    def ln_block(src_region, dst_region, g_slot, be_slot, g_row_off, eps_col,
                 pre_tok, out_cast=None, pool_sub=None):
        if stopped[0]:
            return {NCH - 1: None, 0: None}, {}, {}
        tok_st = {}
        tok_zc = {}
        zslot_war = {0: None, 1: None}
        for c in range(NCH):
            def c_stats(cc, c=c):
                C.wait(pre_tok(c))
                if zslot_war[c % 2] is not None:
                    C.wait(zslot_war[c % 2])
                sl = (c % 2) * 1024
                cc.copy(SBF[:, sl:sl + 512], src_region(c))
                ins = cc.activation(SBF[:, sl + 512:sl + 1024], src_region(c), AF.Square)
                tok_zc[c] = C.tok(ins)
            sec(C, c_stats)

            def t_stats(tt, c=c):
                T.wait(tok_zc[c])
                T.wait(g_cstG.tok())
                if c == 0 and state["statfree"] is not None:
                    T.wait(state["statfree"])
                sl = (c % 2) * 1024
                tt.matmul(PC[0:1, 0:512], ONESC[:], SBF[:, sl:sl + 512],
                          start=(c == 0), stop=(c == NCH - 1))
                ins = tt.matmul(PC[0:1, 512:1024], ONESC[:], SBF[:, sl + 512:sl + 1024],
                                start=(c == 0), stop=(c == NCH - 1))
                tok_st[c] = T.tok(ins)
            sec(T, t_stats)
            zslot_war[c % 2] = tok_st[c]

        tok_var = {}
        def v_rows(v):
            V.wait(tok_st[NCH - 1])
            v.tensor_scalar(LNR[0:1, 0:TSUB], PC[0:1, 0:512], 1.0 / D, None, ALU.mult)
            t2 = V.tok(v.tensor_scalar(LNR[0:1, TSUB:2 * TSUB], PC[0:1, 512:1024], 1.0 / D, None, ALU.mult))
            V.wait(t2)
            t3 = V.tok(v.tensor_mul(LNR[0:1, 2 * TSUB:3 * TSUB], LNR[0:1, 0:TSUB], LNR[0:1, 0:TSUB]))
            V.wait(t3)
            tok_var[0] = V.tok(v.tensor_sub(LNR[0:1, TSUB:2 * TSUB], LNR[0:1, TSUB:2 * TSUB], LNR[0:1, 2 * TSUB:3 * TSUB]))
        sec(V, v_rows)
        state["statfree"] = tok_var[0]

        tok_rstd = {}
        r1 = LNR[0:1, TSUB:2 * TSUB]
        r2 = LNR[0:1, 2 * TSUB:3 * TSUB]
        r3 = LNR[0:1, 3 * TSUB:4 * TSUB]
        r4 = LNR[0:1, 4 * TSUB:5 * TSUB]
        eps_val = EPS_IN if eps_col == 41 else EPS_OUT

        def v_rstd(v):
            def stepi(ins):
                tk = V.tok(ins)
                V.wait(tk)
                return tk
            V.wait(tok_var[0])
            stepi(v.tensor_scalar(r1, r1, float(eps_val), None, ALU.add))
            stepi(v.tensor_scalar(r2, r1, -0.5, None, ALU.mult))
            stepi(v.tensor_scalar(r3.bitcast(I32), r1.bitcast(I32), 1, None,
                                  ALU.logical_shift_right))
            stepi(v.tensor_scalar(r3.bitcast(I32), r3.bitcast(I32), -1, 0x5F3759DF,
                                  ALU.mult, ALU.add))
            for _ in range(2):
                stepi(v.tensor_mul(r4, r3, r3))
                stepi(v.tensor_mul(r4, r4, r2))
                stepi(v.tensor_scalar(r4, r4, 1.5, None, ALU.add))
                stepi(v.tensor_mul(r3, r3, r4))
            tok_rstd[0] = V.tok(v.tensor_copy(r1, r3))
        sec(V, v_rstd)

        tok_mr = {}
        def v_mr(v):
            V.wait(tok_rstd[0])
            tok_mr[0] = V.tok(v.tensor_mul(LNR[0:1, 0:TSUB], LNR[0:1, 0:TSUB], LNR[0:1, TSUB:2 * TSUB]))
        sec(V, v_mr)

        tok_sb = {}
        def t_sb(tt):
            T.wait(tok_mr[0])
            if state["lnlast"] is not None:
                T.wait(state["lnlast"])
            tok_sb[0] = T.tok(tt.matmul(PD[:, 0:512], ONESR[0:1, 0:128], LNR[0:1, TSUB:2 * TSUB],
                                        start=True, stop=True))
        sec(T, t_sb)

        cast_toks = {}
        pool_toks = {}
        tok_mg = {}
        tok_app = {}
        mgb = [PD[:, 512:1024], PB[:, 0:512]]
        for c in range(NCH):
            def t_mg(tt, c=c):
                if c >= 2:
                    T.wait(tok_app[c - 2])
                if g_row_off is not None:
                    lhs = ROWS[0:1, g_row_off + c * 128: g_row_off + (c + 1) * 128]
                else:
                    lhs = ONESR[0:1, 0:128]
                tok_mg[c] = T.tok(tt.matmul(mgb[c % 2], lhs, LNR[0:1, 0:TSUB],
                                            start=True, stop=True))
            sec(T, t_mg)

            def v_app(v, c=c):
                V.wait(tok_sb[0])
                V.wait(tok_mg[c])
                if g_slot is not None:
                    t1 = V.tok(v.scalar_tensor_tensor(
                        src_region(c), src_region(c),
                        BIA[:, g_slot * 8 + c:g_slot * 8 + c + 1],
                        PD[:, 0:512], ALU.mult, ALU.mult))
                else:
                    t1 = V.tok(v.scalar_tensor_tensor(
                        src_region(c), src_region(c), 1.0,
                        PD[:, 0:512], ALU.mult, ALU.mult))
                V.wait(t1)
                if be_slot is not None:
                    ins = v.scalar_tensor_tensor(
                        dst_region(c), src_region(c),
                        BIA[:, be_slot * 8 + c:be_slot * 8 + c + 1],
                        mgb[c % 2], ALU.add, ALU.subtract)
                else:
                    ins = v.scalar_tensor_tensor(
                        dst_region(c), src_region(c), 0.0,
                        mgb[c % 2], ALU.add, ALU.subtract)
                tok_app[c] = V.tok(ins)
            sec(V, v_app)

            if out_cast is not None:
                def c_cast(cc, c=c):
                    C.wait(tok_app[c])
                    cast_toks[c] = C.tok(cc.copy(out_cast(c), dst_region(c)))
                sec(C, c_cast)
            if pool_sub is not None:
                def v_pool(v, c=c):
                    V.wait(tok_app[c])
                    ins = v.reduce_sum(POOL[:, c * 4 + pool_sub:c * 4 + pool_sub + 1],
                                       dst_region(c), axis=AX.X)
                    pool_toks[c] = V.tok(ins)
                sec(V, v_pool)
        state["lnlast"] = tok_app[NCH - 1]
        return tok_app, cast_toks, pool_toks

    def ff_block(in_wait_toks, in_cols, b1col0, seed_region, dst_region,
                 scale, scale_wait, w1_src_fn, w2_src_fn, b2_row_off, wslot_key,
                 slot_extra=None, dst_wait=None, fs_wait=None):
        ev_toks = {}
        if stopped[0]:
            return {(NFS - 1, 0): None, (NFS - 1, 1): None}
        for fs in range(NFS):
            slot = fs % 2
            w1v = (WFS if slot == 0 else B2)[:, 0:8 * FSW]
            w2v = (WFS if slot == 0 else B2)[:, 8 * FSW:16 * FSW]
            grp = g_wfsA if slot == 0 else g_wfsB

            def g_wload(g, fs=fs, slot=slot, grp=grp, w1v=w1v, w2v=w2v):
                if state["wslot"].get((wslot_key, slot)) is not None:
                    G.wait(state["wslot"][(wslot_key, slot)])
                if slot_extra:
                    for tk in slot_extra.get(slot, ()):
                        G.wait(tk)
                if fs_wait is not None:
                    G.wait(fs_wait(fs))
                w1_src = w1_src_fn()
                w2_src = w2_src_fn()
                for hh in range(2):
                    grp.start(G, w1v.rearrange("p (k c) -> p k c", k=8)[:, hh * 4:(hh + 1) * 4, :],
                              w1_src.rearrange("(k p) c -> p k c", p=128)[:, hh * 4:(hh + 1) * 4,
                                                                          fs * FSW:(fs + 1) * FSW])
                for hh in range(2):
                    grp.start(G, w2v.rearrange("p (f c) -> p f c", f=8)[:, hh * 4:(hh + 1) * 4, :],
                              w2_src.rearrange("(f p) c -> p f c", p=128)[:, fs * 8 + hh * 4: fs * 8 + (hh + 1) * 4, :])
            sec(G, g_wload)
            ld_tok = (grp.sem, grp.count)

            h_toks = {}
            ffstage = int(os.environ.get("KERN_FFSTAGE", "9"))
            for f in range(8 if ffstage >= 1 else 0):
                fg = fs * 8 + f

                def t_w1(tt, f=f, ld_tok=ld_tok, w1v=w1v):
                    T.wait(ld_tok)
                    if state["hpw"][f % 2] is not None:
                        T.wait(state["hpw"][f % 2])
                    ph = PA[:, (f % 2) * 512:(f % 2) * 512 + 512]
                    for k in range(NCH):
                        if in_wait_toks:
                            T.wait(in_wait_toks[k])
                        ins = tt.matmul(ph, w1v[:, k * FSW + f * 128: k * FSW + (f + 1) * 128],
                                        in_cols(k), start=(k == 0), stop=(k == NCH - 1))
                    h_toks[(f, "mm")] = T.tok(ins)
                sec(T, t_w1)

                if ffstage < 2:
                    h_toks[f] = h_toks[(f, "mm")]
                    continue

                def c_gelu(cc, f=f, fg=fg):
                    C.wait(h_toks[(f, "mm")])
                    if state["hread"].get(f) is not None:
                        C.wait(state["hread"][f])
                    ph = PA[:, (f % 2) * 512:(f % 2) * 512 + 512]
                    ins = cc.activation(SBF[:, f * 512:(f + 1) * 512], ph, AF.Gelu,
                                        bias=B1F[:, b1col0 + fg:b1col0 + fg + 1])
                    h_toks[f] = C.tok(ins)
                    state["hpw"][f % 2] = h_toks[f]
                sec(C, c_gelu)

            accv = [PC[:, 0:512], PC[:, 512:1024], PD[:, 0:512], PD[:, 512:1024]]
            tok_w2 = {}
            for oh in range(2 if ffstage >= 3 else 0):
                for f in range(8):
                    def t_w2(tt, f=f, oh=oh, fs=fs, w2v=w2v):
                        T.wait(h_toks[f])
                        if f == 0 and oh == 0:
                            if state["accfree"] is not None:
                                T.wait(state["accfree"])
                            if state["statfree"] is not None:
                                T.wait(state["statfree"])
                            if state["lnlast"] is not None:
                                T.wait(state["lnlast"])
                        if f == 0 and oh == 1:
                            T.wait(ev_toks[(fs, 0)])
                        last_f = (f == 7)
                        fold = (b2_row_off is not None and fs == 0
                                and not os.environ.get("KERN_NOFOLD"))
                        for ot in range(4):
                            og = oh * 4 + ot
                            ins = tt.matmul(accv[ot],
                                            w2v[:, f * 1024 + og * 128: f * 1024 + (og + 1) * 128],
                                            SBF[:, f * 512:(f + 1) * 512],
                                            start=(f == 0), stop=(last_f and not fold))
                        if last_f and fold:
                            for ot in range(4):
                                og = oh * 4 + ot
                                ins = tt.matmul(
                                    accv[ot],
                                    ROWS[0:1, b2_row_off + og * 128: b2_row_off + (og + 1) * 128],
                                    ONESR[0:1, 0:512], start=False, stop=True)
                        tok_w2[(oh, f)] = T.tok(ins)
                        state["hread"][f] = tok_w2[(oh, f)]
                    sec(T, t_w2)

                if ffstage < 4:
                    ev_toks[(fs, oh)] = tok_w2[(oh, 7)]
                    continue

                def v_acc(v, oh=oh, fs=fs):
                    V.wait(tok_w2[(oh, 7)])
                    if scale_wait is not None:
                        V.wait(scale_wait)
                    if dst_wait is not None:
                        V.wait(dst_wait)
                    last = None
                    for ot in range(4):
                        oc = oh * 4 + ot
                        if fs == 0 and (seed_region is not None):
                            last = v.scalar_tensor_tensor(dst_region(oc), accv[ot], scale,
                                                          seed_region(oc), ALU.mult, ALU.add)
                        elif os.environ.get("KERN_ACC_COPY"):
                            last = v.tensor_copy(dst_region(oc), accv[ot])
                        elif os.environ.get("KERN_ACC_SCRATCH"):
                            last = v.scalar_tensor_tensor(X2S[:, oc * TSUB:(oc + 1) * TSUB],
                                                          accv[ot], scale,
                                                          dst_region(oc), ALU.mult, ALU.add)
                        else:
                            last = v.scalar_tensor_tensor(dst_region(oc), accv[ot], scale,
                                                          dst_region(oc), ALU.mult, ALU.add)
                    ev_toks[(fs, oh)] = V.tok(last)
                sec(V, v_acc)
            state["accfree"] = ev_toks.get((fs, 1))
            state["wslot"][(wslot_key, slot)] = (tok_w2.get((1, 7)) or h_toks.get(7)
                                                 or h_toks.get((7, "mm")) or ld_tok)
        if not ev_toks:
            ev_toks = {(NFS - 1, 0): None, (NFS - 1, 1): None}
        for fs2 in range(NFS):
            for oh2 in range(2):
                ev_toks.setdefault((fs2, oh2), None)
        return ev_toks

    tok_x2cast = {}
    tok_pool_all = {}
    tok_zo_prev = {0: None}
    tok_xs = {}
    for sub in range(NSUB):
        slo = sub * TSUB

        def y_xs(y, sub=sub, slo=slo):
            Y.wait(g_xfm[0].tok())
            Y.wait(g_xfm[1].tok())
            Y.wait(tok_xbf[TCH - 1])
            if tok_zo_prev[0] is not None:
                Y.wait(tok_zo_prev[0])
            g_xs.start(Y, XS[:].rearrange("p (c n) -> p c n", c=NCH),
                       xfm_d[:, :, slo:slo + TSUB].rearrange("c p n -> p c n"))
            tok_xs[sub] = g_xs.tok()
        sec(Y, y_xs)
        CHK("xs%d" % sub)

        tok_wo = {}
        for e in range(NCH):
            slot = e % 2
            grp = g_wroA if slot == 0 else g_wroB

            def g_wo(g, e=e, slot=slot, grp=grp):
                if state["woslot"][slot] is not None:
                    G.wait(state["woslot"][slot])
                else:
                    G.wait(tok_ctx_done)
                grp.start(G, wro_slots[slot], Wo_h[e * 128:(e + 1) * 128, :])
            sec(G, g_wo)
            ld_tok = (grp.sem, grp.count)

            def t_wo(tt, e=e, slot=slot, ld_tok=ld_tok, slo=slo):
                T.wait(ld_tok)
                if e == 0:
                    if state["accfree"] is not None:
                        T.wait(state["accfree"])
                    if state["statfree"] is not None:
                        T.wait(state["statfree"])
                    if state["lnlast"] is not None:
                        T.wait(state["lnlast"])
                    T.wait(tok_ctx_done)
                for o in range(NCH):
                    pbank = [PA, PB, PC, PD][o // 2][:, (o % 2) * 512:(o % 2) * 512 + 512]
                    ins = tt.matmul(pbank, wro_slots[slot][:, o * 128:(o + 1) * 128],
                                    b1v(e, slo, TSUB), start=(e == 0), stop=False)
                tok_wo[e] = T.tok(ins)
                state["woslot"][slot] = tok_wo[e]
            sec(T, t_wo)

        def t_wof(tt, sub=sub):
            T.wait(g_rows2.tok())
            for o in range(NCH):
                pbank = [PA, PB, PC, PD][o // 2][:, (o % 2) * 512:(o % 2) * 512 + 512]
                ins = tt.matmul(pbank, ROWS[0:1, RO_S0 + o * 128: RO_S0 + (o + 1) * 128],
                                ONESR[0:1, 0:512], start=False, stop=True)
            tok_wo["fold"] = T.tok(ins)
        sec(T, t_wof)
        CHK("wo%d" % sub)

        tok_z1 = {}
        def v_z1(v, sub=sub, slo=slo):
            V.wait(tok_wo["fold"])
            V.wait(tok_xs[sub])
            for o in range(NCH):
                pbank = [PA, PB, PC, PD][o // 2][:, (o % 2) * 512:(o % 2) * 512 + 512]
                ins = v.scalar_tensor_tensor(rsv(o), pbank, 1.0,
                                             xsv(o), ALU.mult, ALU.add)
                tok_z1[o] = V.tok(ins)
        sec(V, v_z1)
        CHK("z1%d" % sub)
        state["accfree"] = tok_z1.get(NCH - 1)
        state["statfree"] = tok_z1.get(NCH - 1)

        if "d_z1" in dbg and sub == 0:
            def y_dz1(y):
                Y.wait(tok_z1[NCH - 1])
                g_dbg.start(Y, dbg["d_z1"][:], RS[:])
            sec(Y, y_dz1)

        app1, cast1, _ = ln_block(
            rsv, rsv, S_G1, S_BE1, RO_G1, 41,
            pre_tok=lambda c: tok_z1.get(c),
            out_cast=lambda c, slo=slo: b1v(c, slo, TSUB))
        CHK("ln1%d" % sub)

        if "d_attn" in dbg and sub == 0:
            def y_dat(y):
                Y.wait(app1[NCH - 1])
                g_dbg.start(Y, dbg["d_attn"][:], RS[:])
            sec(Y, y_dat)

        ff1 = ff_block(cast1, lambda k, slo=slo: b1v(k, slo, TSUB), 0,
                       rsv, rtv, 1.0, None, (lambda: W1_h), (lambda: W2_h), RO_S1, "ff",
                       slot_extra=({0: [tok_kmm.get(TCH - 1)],
                                    1: [tok_ctx_done]} if sub == 0 else None))
        CHK("ff%d" % sub)

        app2, _, _ = ln_block(
            rtv, rtv, S_G2, S_BE2, RO_G2, 41,
            pre_tok=lambda c: ff1.get((NFS - 1, 1)))
        CHK("ln2%d" % sub)

        tok_zo = {}
        def v_zo(v, sub=sub, slo=slo):
            for c in range(NCH):
                V.wait(app2[c])
                ins = v.tensor_add(rtv(c), rtv(c), xsv(c))
                tok_zo[c] = V.tok(ins)
        sec(V, v_zo)
        tok_zo_prev[0] = tok_zo.get(NCH - 1)

        app3, cast3, pools = ln_block(
            rtv, rtv, None, None, None, 42,
            pre_tok=lambda c: tok_zo[c],
            out_cast=lambda c, slo=slo: b1v(c, slo, TSUB),
            pool_sub=sub)
        for c in range(NCH):
            tok_x2cast[(sub, c)] = cast3.get(c)
            tok_pool_all[(sub, c)] = pools.get(c)
        CHK("p2s%d" % sub)

    if "d_x2b" in dbg:
        def y_dx2(y):
            for c in range(NCH):
                Y.wait(tok_x2cast[(NSUB - 1, c)])
            g_dbg.start(Y, dbg["d_x2b"][:], B1[:])
        sec(Y, y_dx2)

    CHK("p2")

    tok_lgT = {}
    def t_lg(tt):
        for sub in range(NSUB):
            for c in range(NCH):
                T.wait(tok_pool_all[(sub, c)])
        T.wait(state["statfree"])
        n = 0
        for c in range(NCH):
            for sub in range(NSUB):
                ins = tt.matmul(PB[0:1, 0:8], POOL[:, c * 4 + sub:c * 4 + sub + 1],
                                WRFM[:, c * NE:(c + 1) * NE],
                                start=(n == 0), stop=(n == 31))
                n += 1
        tok_lgT[0] = T.tok(ins)
    sec(T, t_lg)

    tok_rtr = {}
    def v_lg(v):
        V.wait(tok_lgT[0])
        tok_rtr[0] = V.tok(v.tensor_scalar(RTR[0:1, 0:8], PB[0:1, 0:8],
                                           1.0 / S, None, ALU.mult))
    sec(V, v_lg)

    def g_ar2(g):
        G.wait(tok_rtr[0])
        g_lg.start(G, cc2i.ap(), RTR[0:1, 0:8])
        g.wait_ge(g_lg.sem, g_lg.count)
        g.collective_compute("AllReduce", ALU.add,
                             replica_groups=[list(range(NCORES))],
                             ins=[cc2i.ap()], outs=[cc2o.ap()]).then_inc(cc_sem, 1)
        g.wait_ge(cc_sem, 2)
        g_lg.start(G, RTR[0:1, 0:8], cc2o.ap())
    sec(G, g_ar2)

    tok_rt2 = {}
    def v_rt(v):
        V.wait(g_lg.tok())
        V.wait(g_const.tok())
        def step(ins):
            tk = V.tok(ins)
            V.wait(tk)
            return tk
        step(v.reduce_max(RTR[0:1, 32:33], RTR[0:1, 0:8], axis=AX.X))
        step(v.tensor_tensor(RTR[0:1, 8:16], RTR[0:1, 0:8],
                             RTR[0:1, 32:33].broadcast_to([1, 8]), ALU.is_ge))
        step(v.tensor_scalar(RTR[0:1, 16:24], RTR[0:1, 8:16], -1e30, None, ALU.mult))
        step(v.tensor_add(RTR[0:1, 16:24], RTR[0:1, 16:24], RTR[0:1, 0:8]))
        step(v.reduce_max(RTR[0:1, 33:34], RTR[0:1, 16:24], axis=AX.X))
        step(v.tensor_tensor(RTR[0:1, 16:24], RTR[0:1, 16:24],
                             RTR[0:1, 33:34].broadcast_to([1, 8]), ALU.is_ge))
        step(v.tensor_mul(RTR[0:1, 24:32], RTR[0:1, 8:16], IOTA8))
        step(v.reduce_sum(RTR[0:1, 38:39], RTR[0:1, 24:32], axis=AX.X))
        step(v.tensor_mul(RTR[0:1, 24:32], RTR[0:1, 16:24], IOTA8))
        step(v.reduce_sum(RTR[0:1, 39:40], RTR[0:1, 24:32], axis=AX.X))
        ins = v.tensor_sub(RTR[0:1, 34:35], RTR[0:1, 33:34], RTR[0:1, 32:33])
        tok_rt2[0] = V.tok(ins)
    sec(V, v_rt)

    def v_rtexp(v):
        V.wait(tok_rt2[0])
        u = RTR[0:1, 40:41]
        p = RTR[0:1, 35:36]
        def stepi(ins):
            tk = V.tok(ins)
            V.wait(tk)
            return tk
        stepi(v.tensor_scalar(u, RTR[0:1, 34:35], -24.0, None, ALU.max))
        stepi(v.tensor_scalar(u, u, 1.0 / 64.0, None, ALU.mult))
        stepi(v.tensor_scalar(p, u, 0.25, 1.0, ALU.mult, ALU.add))
        stepi(v.tensor_mul(p, p, u))
        stepi(v.tensor_scalar(p, p, 1.0 / 3.0, 1.0, ALU.mult, ALU.add))
        stepi(v.tensor_mul(p, p, u))
        stepi(v.tensor_scalar(p, p, 0.5, 1.0, ALU.mult, ALU.add))
        stepi(v.tensor_mul(p, p, u))
        stepi(v.tensor_scalar(p, p, 1.0, 1.0, ALU.mult, ALU.add))
        for _ in range(5):
            stepi(v.tensor_mul(p, p, p))
        tok_rt2[1] = V.tok(v.tensor_mul(p, p, p))
    sec(V, v_rtexp)

    def v_rt2(v):
        V.wait(tok_rt2[1])
        def step(ins):
            tk = V.tok(ins)
            V.wait(tk)
            return tk
        step(v.tensor_scalar(RTR[0:1, 36:37], RTR[0:1, 35:36], 1.0, None, ALU.add))
        step(v.reciprocal(RTR[0:1, 36:37], RTR[0:1, 36:37]))
        ins = v.tensor_scalar(RTR[0:1, 37:38], RTR[0:1, 36:37], -1.0, 1.0,
                              ALU.mult, ALU.add)
        tok_rt2[2] = V.tok(ins)
    sec(V, v_rt2)

    if "d_lg" in dbg:
        def y_dlg(y):
            Y.wait(tok_rt2[2])
            g_dbg.start(Y, dbg["d_lg"][:], RTR[0:1, :])
        sec(Y, y_dlg)

    tok_web = {}
    def t_web(tt):
        T.wait(tok_rt2[2])
        tt.matmul(PB[:, 0:1], ONESR[0:1, 0:128], RTR[0:1, 36:37], start=True, stop=True)
        ins = tt.matmul(PB[:, 512:513], ONESR[0:1, 0:128], RTR[0:1, 37:38],
                        start=True, stop=True)
        tok_web[0] = T.tok(ins)
    sec(T, t_web)

    def v_web(v):
        V.wait(tok_web[0])
        v.tensor_copy(WEB[:, 0:1], PB[:, 0:1])
        tok_web[1] = V.tok(v.tensor_copy(WEB[:, 1:2], PB[:, 512:513]))
    sec(V, v_web)

    tok_rti = {}
    def c_rti(cc):
        C.wait(tok_rt2[2])
        tok_rti[0] = C.tok(cc.copy(RTI[0:1, 0:2], RTR[0:1, 38:40]))
    sec(C, c_rti)

    CHK("router")

    iv = [None, None]
    def g_snap(g):
        G.wait(tok_rti[0])
        g.reg_load(reg0, RTI[0:1, 0:1])
        g.reg_load(reg1, RTI[0:1, 1:2])
        iv[0] = g.snap(reg0, min_val=0, max_val=NE - 1)
        iv[1] = g.snap(reg1, min_val=0, max_val=NE - 1)
    sec(G, g_snap)

    def y_wsel(y):
        Y.wait(tok_rti[0])
        y.reg_load(yreg0, RTI[0:1, 0:1])
        y.reg_load(yreg1, RTI[0:1, 1:2])
        jv0 = y.snap(yreg0, min_val=0, max_val=NE - 1)
        jv1 = y.snap(yreg1, min_val=0, max_val=NE - 1)
        for es, jv in ((0, jv0), (1, jv1)):
            for fs in range(NFS):
                grp = g_wsel[es * NFS + fs]
                grp.start(Y, W1sel[es].ap()[:, fs * FSW:(fs + 1) * FSW],
                          W1m_h[jv][:, fs * FSW:(fs + 1) * FSW])
                grp.start(Y, W2sel[es].ap()[fs * FSW:(fs + 1) * FSW, :],
                          W2m_h[jv][fs * FSW:(fs + 1) * FSW, :])
    sec(Y, y_wsel)

    tok_moeb = {}
    for es in range(2):
        def g_mb(g, es=es):
            if es == 0:
                G.wait(state["woslot"][0])
                G.wait(state["woslot"][1])
                G.wait(state["wslot"][("ff", 0)])
                G.wait(state["wslot"][("ff", 1)])
            else:
                G.wait(tok_moeb[(0, "v")])
            roff = RO_S0 if es == 0 else RO_S1
            g_moeb.start(G, ROWS[0:1, roff:roff + D],
                         b2m_h[iv[es]].rearrange("d -> () d"))
            g_moeb.start(G, BRX[:, :], b1m_h[iv[es]].rearrange("(a b) -> a b", b=128))
        sec(G, g_mb)
        ld = (g_moeb.sem, (es + 1) * 32)

        def t_mb(tt, es=es, ld=ld):
            T.wait(ld)
            T.wait(tok_web[1])
            if state["statfree"] is not None:
                T.wait(state["statfree"])
            ins = tt.matmul(PB[:, 0:32] if es == 0 else PB[:, 512:544],
                            BRX[0:32, :], IDN[0:32, 0:32], start=True, stop=True)
            tok_moeb[(es, "t")] = T.tok(ins)
        sec(T, t_mb)

        def v_mb(v, es=es):
            V.wait(tok_moeb[(es, "t")])
            ins = v.tensor_copy(B1F[:, 32 + es * 32:64 + es * 32],
                                PB[:, 0:32] if es == 0 else PB[:, 512:544])
            tok_moeb[(es, "v")] = V.tok(ins)
        sec(V, v_mb)

    CHK("moeb")

    tok_otr = {}
    tok_oev = {}
    tok_upc = {}
    for sub in range(NSUB):
        slo = sub * TSUB

        def c_upc(cc, sub=sub, slo=slo):
            for c in range(NCH):
                C.wait(tok_x2cast[(sub, c)])
            if (sub - 1, "lnfree") in tok_upc:
                C.wait(tok_upc[(sub - 1, "lnfree")])
            ins = cc.copy(X2S[:].rearrange("p (c n) -> p c n", c=NCH),
                          B1[:].rearrange("p (c n) -> p c n", c=NCH)[:, :, slo:slo + TSUB])
            tok_upc[sub] = C.tok(ins)
        sec(C, c_upc)
        upc = tok_upc.get(sub)

        moe_last = None
        for es in range(2):
            in_toks = {c: tok_x2cast.get((sub, c)) for c in range(NCH)}
            ev = ff_block(in_toks, lambda k, slo=slo: b1v(k, slo, TSUB),
                          32 + es * 32,
                          None, x2sv,
                          WEB[:, es:es + 1], tok_web.get(1),
                          (lambda es=es: W1sel[es].ap()), (lambda es=es: W2sel[es].ap()),
                          (RO_S0 if es == 0 else RO_S1), "ff",
                          slot_extra={0: [tok_moeb.get((es, "v"))],
                                      1: [tok_moeb.get((es, "v"))]},
                          dst_wait=upc,
                          fs_wait=(lambda fs, es=es: (g_wsel[es * NFS + fs].sem, 32)))
            moe_last = ev

        appF, _, _ = ln_block(
            x2sv, rsv, None, None, None, 42,
            pre_tok=lambda c, ev=moe_last: ev[(NFS - 1, 1)])
        tok_upc[(sub, "lnfree")] = appF.get(NCH - 1)

        for tcq in range(TSUB // 128):
            tglob = sub * 4 + tcq
            pw = PA if tcq % 2 == 0 else PB

            def t_otr(tt, sub=sub, tcq=tcq, pw=pw, appF=appF):
                for c in range(NCH):
                    T.wait(appF[c])
                if (sub, tcq - 2) in tok_oev:
                    T.wait(tok_oev[(sub, tcq - 2)])
                if tcq < 2 and (sub - 1, tcq + 2) in tok_oev:
                    T.wait(tok_oev[(sub - 1, tcq + 2)])
                for c in range(NCH):
                    ins = tt.matmul(pw[:, c * 128:(c + 1) * 128],
                                    RS[:, c * TSUB + tcq * 128: c * TSUB + (tcq + 1) * 128],
                                    IDN[:], start=True, stop=True)
                tok_otr[(sub, tcq)] = T.tok(ins)
            sec(T, t_otr)

            def c_oev(cc, sub=sub, tcq=tcq, pw=pw):
                C.wait(tok_otr[(sub, tcq)])
                gidx = sub * 4 + tcq
                C.wait((g_out.sem, max(0, gidx - 3) * 16))
                ins = cc.copy(RT[:, tcq * 1024:(tcq + 1) * 1024], pw[:])
                tok_oev[(sub, tcq)] = C.tok(ins)
            sec(C, c_oev)

            def y_ost(y, sub=sub, tcq=tcq, tglob=tglob):
                Y.wait(tok_oev[(sub, tcq)])
                g_out.start(Y, out_h[tglob * 128:(tglob + 1) * 128, :],
                            RT[:, tcq * 1024:(tcq + 1) * 1024])
            sec(Y, y_ost)
        state["hpw"][0] = tok_oev[(sub, 3)]
        state["hpw"][1] = tok_oev[(sub, 3)]
        CHK("p3s%d" % sub)

    def y_fin(y):
        Y.wait(g_out.tok())
        if g_dbg.count:
            Y.wait(g_dbg.tok())
    sec(Y, y_fin)

    st.close()
    return nc


_NC_CACHE = {}


def kernel(**inputs):
    if "main" not in _NC_CACHE:
        _NC_CACHE["main"] = build()
    nc = _NC_CACHE["main"]
    x = np.ascontiguousarray(np.asarray(inputs["L0_attn_x"], dtype=np.float32)[0])
    base = {
        "Wq": inputs["L0_attn_Wq"], "bq": inputs["L0_attn_bq"],
        "Wk": inputs["L0_attn_Wk"], "bk": inputs["L0_attn_bk"],
        "Wv": inputs["L0_attn_Wv"], "bv": inputs["L0_attn_bv"],
        "Wo": inputs["L0_attn_Wo"], "bo": inputs["L0_attn_bo"],
        "g1": inputs["L0_attn_g1"], "be1": inputs["L0_attn_be1"],
        "W1": inputs["L0_attn_W1"], "b1": inputs["L0_attn_b1"],
        "W2": inputs["L0_attn_W2"], "b2": inputs["L0_attn_b2"],
        "g2": inputs["L0_attn_g2"], "be2": inputs["L0_attn_be2"],
        "Wr": inputs["L0_ffn_Wr"], "W1m": inputs["L0_ffn_W1"],
        "b1m": inputs["L0_ffn_b1"], "W2m": inputs["L0_ffn_W2"],
        "b2m": inputs["L0_ffn_b2"],
    }
    base = {k: np.ascontiguousarray(np.asarray(v, dtype=np.float32)) for k, v in base.items()}
    in_maps = []
    for i in range(NCORES):
        m = dict(base)
        m["x"] = np.ascontiguousarray(x[i * TC:(i + 1) * TC, :])
        in_maps.append(m)
    res = run_bass_kernel_spmd(nc, in_maps, core_ids=list(range(NCORES)))
    out = np.concatenate([np.asarray(r["out"]) for r in res.results], axis=0)
    return out.reshape(1, S, D)


# version 13
# speedup vs baseline: 1.0029x; 1.0029x over previous
import contextlib
import math
import os

import numpy as np

import concourse.bass as bass
import concourse.mybir as mybir
from concourse.bass_utils import run_bass_kernel_spmd

F32 = mybir.dt.float32
BF16 = mybir.dt.bfloat16
I32 = mybir.dt.int32
AF = mybir.ActivationFunctionType
ALU = mybir.AluOpType
AX = mybir.AxisListType

D = 1024
NH = 16
FF = 4096
NE = 8
S = 16384
NCORES = 8
TC = S // NCORES
NCH = D // 128
TCH = TC // 128
HALF = TC // 2
NSUB = 4
TSUB = TC // NSUB
NFS = 4
FSW = FF // NFS
EPS_IN = 1e-12
EPS_OUT = 1e-5
LN8 = math.log(8.0)

RO_S0, RO_S1, RO_G1, RO_G2 = 0, D, 2 * D, 3 * D
S_BQ, S_G1, S_BE1, S_G2, S_BE2 = range(5)


class Eng:
    def __init__(self, nc, stack, name):
        self.e = getattr(nc, name)
        self.name = name
        self.sem = stack.enter_context(nc.semaphore("sf_" + name))
        self.n = 0
        self.seen = {}

    def tok(self, ins):
        self.n += 1
        ins.then_inc(self.sem, 1)
        return (self.sem, self.n)

    def wait(self, *toks):
        for t in toks:
            if t is None:
                continue
            sem, val = t
            k = id(sem)
            if val > self.seen.get(k, 0):
                self.e.wait_ge(sem, val)
                self.seen[k] = val


class DmaGrp:
    def __init__(self, nc, stack, name):
        self.sem = stack.enter_context(nc.semaphore(name))
        self.count = 0

    def start(self, eng, out, in_, **kw):
        ins = eng.e.dma_start(out, in_, **kw)
        self.count += 16
        ins.then_inc(self.sem, 16)
        return ins

    def tok(self):
        return (self.sem, self.count)


class _StopBuild(Exception):
    pass


def build(debug=(), stop_after=None):
    nc = bass.Bass("TRN2", target_bir_lowering=False)

    x_h = nc.dram_tensor("x", [TC, D], F32, kind="ExternalInput").ap()
    Wq_h = nc.dram_tensor("Wq", [D, D], F32, kind="ExternalInput").ap()
    bq_h = nc.dram_tensor("bq", [D], F32, kind="ExternalInput").ap()
    Wk_h = nc.dram_tensor("Wk", [D, D], F32, kind="ExternalInput").ap()
    bk_h = nc.dram_tensor("bk", [D], F32, kind="ExternalInput").ap()
    Wv_h = nc.dram_tensor("Wv", [D, D], F32, kind="ExternalInput").ap()
    bv_h = nc.dram_tensor("bv", [D], F32, kind="ExternalInput").ap()
    Wo_h = nc.dram_tensor("Wo", [D, D], F32, kind="ExternalInput").ap()
    bo_h = nc.dram_tensor("bo", [D], F32, kind="ExternalInput").ap()
    g1_h = nc.dram_tensor("g1", [D], F32, kind="ExternalInput").ap()
    be1_h = nc.dram_tensor("be1", [D], F32, kind="ExternalInput").ap()
    W1_h = nc.dram_tensor("W1", [D, FF], F32, kind="ExternalInput").ap()
    b1_h = nc.dram_tensor("b1", [FF], F32, kind="ExternalInput").ap()
    W2_h = nc.dram_tensor("W2", [FF, D], F32, kind="ExternalInput").ap()
    b2_h = nc.dram_tensor("b2", [D], F32, kind="ExternalInput").ap()
    g2_h = nc.dram_tensor("g2", [D], F32, kind="ExternalInput").ap()
    be2_h = nc.dram_tensor("be2", [D], F32, kind="ExternalInput").ap()
    Wr_h = nc.dram_tensor("Wr", [D, NE], F32, kind="ExternalInput").ap()
    W1m_h = nc.dram_tensor("W1m", [NE, D, FF], F32, kind="ExternalInput").ap()
    b1m_h = nc.dram_tensor("b1m", [NE, FF], F32, kind="ExternalInput").ap()
    W2m_h = nc.dram_tensor("W2m", [NE, FF, D], F32, kind="ExternalInput").ap()
    b2m_h = nc.dram_tensor("b2m", [NE, D], F32, kind="ExternalInput").ap()
    out_h = nc.dram_tensor("out", [TC, D], F32, kind="ExternalOutput").ap()
    xfm_d = nc.dram_tensor("xfm_d", [NCH, 128, TC], F32)
    W1sel = [nc.dram_tensor("W1sel%d" % e, [D, FF], F32) for e in range(2)]
    W2sel = [nc.dram_tensor("W2sel%d" % e, [FF, D], F32) for e in range(2)]
    cc1i = nc.dram_tensor("cc1i", [128, NH * 64], F32)
    cc1o = nc.dram_tensor("cc1o", [128, NH * 64], F32)
    cc2i = nc.dram_tensor("cc2i", [1, NE], F32)
    cc2o = nc.dram_tensor("cc2o", [1, NE], F32)

    dbg_specs = {
        "d_xfm": ([NCH, 128, TC], F32), "d_eq": ([128, NCH * TC], BF16),
        "d_kv": ([128, NH * 64], F32), "d_ctx": ([128, NCH * TC], BF16),
        "d_attn": ([128, NCH * TSUB], F32), "d_x2b": ([128, NCH * TC], BF16),
        "d_lg": ([1, 64], F32), "d_rq": ([16, TC], F32),
        "d_z1": ([128, NCH * TSUB], F32), "d_kn": ([128, 2048], BF16),
    }
    dbg = {}
    for name in debug:
        shp, dt = dbg_specs[name]
        dbg[name] = nc.dram_tensor(name, shp, dt, kind="ExternalOutput").ap()

    st = contextlib.ExitStack()

    RS = st.enter_context(nc.sbuf_tensor("RS", [128, NCH * TSUB], F32))
    RT = st.enter_context(nc.sbuf_tensor("RT", [128, NCH * TSUB], F32))
    XS = st.enter_context(nc.sbuf_tensor("XS", [128, NCH * TSUB], F32))
    X2S = st.enter_context(nc.sbuf_tensor("X2S", [128, NCH * TSUB], F32))
    B1 = st.enter_context(nc.sbuf_tensor("B1", [128, NCH * TC], BF16))
    B2 = st.enter_context(nc.sbuf_tensor("B2", [128, NCH * TC], BF16))
    WFS = st.enter_context(nc.sbuf_tensor("WFS", [128, NCH * TC], BF16))
    SBF = st.enter_context(nc.sbuf_tensor("SBF", [128, 4096], BF16))
    SSB = st.enter_context(nc.sbuf_tensor("SSB", [128, 1024], BF16))
    KNV = st.enter_context(nc.sbuf_tensor("KNV", [128, 2048], BF16))
    LNR = st.enter_context(nc.sbuf_tensor("LNR", [1, 5 * TSUB], F32))
    IDN = st.enter_context(nc.sbuf_tensor("IDN", [128, 128], F32))
    KVB = st.enter_context(nc.sbuf_tensor("KVB", [128, NH * 64], BF16))
    BIA = st.enter_context(nc.sbuf_tensor("BIA", [128, 84], F32))
    B1F = st.enter_context(nc.sbuf_tensor("B1F", [128, 96], F32))
    ROWS = st.enter_context(nc.sbuf_tensor("ROWS", [1, 4 * D], F32))
    RTR = st.enter_context(nc.sbuf_tensor("RTR", [1, 64], F32))
    RTI = st.enter_context(nc.sbuf_tensor("RTI", [1, 2], I32))
    POOL = st.enter_context(nc.sbuf_tensor("POOL", [128, 32], F32))
    WEB = st.enter_context(nc.sbuf_tensor("WEB", [128, 2], F32))
    BRX = st.enter_context(nc.sbuf_tensor("BRX", [32, 128], F32))
    IND16 = st.enter_context(nc.sbuf_tensor("IND16", [128, 128], BF16))
    ONESC = st.enter_context(nc.sbuf_tensor("ONESC", [128, 1], BF16))
    ONESCF = st.enter_context(nc.sbuf_tensor("ONESCF", [128, 1], F32))
    ONESR = st.enter_context(nc.sbuf_tensor("ONESR", [1, 512], F32))
    WRFM = st.enter_context(nc.sbuf_tensor("WRFM", [128, NCH * NE], F32))

    PA = st.enter_context(nc.psum_tensor("PA", [128, 1024], F32))
    PB = st.enter_context(nc.psum_tensor("PB", [128, 1024], F32))
    PC = st.enter_context(nc.psum_tensor("PC", [128, 1024], F32))
    PD = st.enter_context(nc.psum_tensor("PD", [128, 1024], F32))

    ind16_np = np.zeros((128, 128), dtype=np.float32)
    ind16T_np = np.zeros((16, 8 * 128), dtype=np.float32)
    for c in range(8):
        for p in range(128):
            ind16_np[p, c * 16 + 2 * c + (p // 64)] = 1.0
            ind16T_np[2 * c + (p // 64), c * 128 + p] = 1.0
    identC = nc.inline_tensor(np.eye(128, dtype=np.float32), name="identC").ap()
    ind16C = nc.inline_tensor(ind16_np, name="ind16C").ap()
    ind16TC = nc.inline_tensor(ind16T_np, name="ind16TC").ap()
    ones128cC = nc.inline_tensor(np.ones((128, 1), np.float32), name="ones128cC").ap()
    ones1rC = nc.inline_tensor(np.ones((1, 512), np.float32), name="ones1rC").ap()
    iota8C = nc.inline_tensor(np.arange(8, dtype=np.float32).reshape(1, 8), name="iota8C").ap()

    blk = st.enter_context(nc.Block())
    T = Eng(nc, st, "tensor")
    V = Eng(nc, st, "vector")
    C = Eng(nc, st, "scalar")
    G = Eng(nc, st, "gpsimd")
    Y = Eng(nc, st, "sync")

    g_const = DmaGrp(nc, st, "g_const")
    g_cstG = DmaGrp(nc, st, "g_cstG")
    g_x = DmaGrp(nc, st, "g_x")
    g_xfm = [DmaGrp(nc, st, "g_xfmA"), DmaGrp(nc, st, "g_xfmB")]
    g_xs = DmaGrp(nc, st, "g_xs")
    g_wq = DmaGrp(nc, st, "g_wq")
    g_wkv = DmaGrp(nc, st, "g_wkv")
    g_bias = DmaGrp(nc, st, "g_bias")
    g_rows2 = DmaGrp(nc, st, "g_rows2")
    g_kv1 = DmaGrp(nc, st, "g_kv1")
    g_wfsA = DmaGrp(nc, st, "g_wfsA")
    g_wfsB = DmaGrp(nc, st, "g_wfsB")
    g_wroA = DmaGrp(nc, st, "g_wroA")
    g_wroB = DmaGrp(nc, st, "g_wroB")
    g_lg = DmaGrp(nc, st, "g_lg")
    g_moeb = DmaGrp(nc, st, "g_moeb")
    g_wsel = [DmaGrp(nc, st, "g_wsel%d" % i) for i in range(8)]
    g_out = [DmaGrp(nc, st, "g_out%d" % i) for i in range(4)]
    g_dbg = DmaGrp(nc, st, "g_dbg")
    cc_sem = st.enter_context(nc.semaphore("cc_sem"))
    reg0 = st.enter_context(nc.gpsimd.register("eidx0"))
    reg1 = st.enter_context(nc.gpsimd.register("eidx1"))
    yreg0 = st.enter_context(nc.sync.register("yidx0"))
    yreg1 = st.enter_context(nc.sync.register("yidx1"))

    stopped = [False]

    def sec(eng, fn):
        if stopped[0]:
            return
        getattr(blk, eng.name)(fn)

    def CHK(name):
        if stop_after == name:
            stopped[0] = True

    from concourse.hw_specs import get_activation_tables
    _tabs = list(get_activation_tables(nc.m.arch).items())
    SET_LNEXP = next(i for i, (n, fs_) in enumerate(_tabs)
                     if AF.Exp in fs_ and AF.Ln in fs_)
    SET_GELU = next(i for i, (n, fs_) in enumerate(_tabs) if AF.Gelu in fs_)

    def load_act_set(cc, idx):
        cc.add_instruction(mybir.InstLoadActFuncSet(
            name=nc.get_next_instruction_name(), act_func_set_id=idx,
            ins=[], outs=[]))

    def b1v(c, lo, n):
        return B1[:, c * TC + lo: c * TC + lo + n]

    def b2v(c, lo, n):
        return B2[:, c * TC + lo: c * TC + lo + n]

    def rsv(c):
        return RS[:, c * TSUB:(c + 1) * TSUB]

    def rtv(c):
        return RT[:, c * TSUB:(c + 1) * TSUB]

    def xsv(c):
        return XS[:, c * TSUB:(c + 1) * TSUB]

    def x2sv(c):
        return X2S[:, c * TSUB:(c + 1) * TSUB]

    rq_rows = RT[0:16, 2048:4096]
    IND16T = SBF[0:16, 0:2048].bitcast(F32)
    wro_slots = [KVB[:, 0:1024], SSB[:, 0:1024]]
    IOTA8 = RTR[0:1, 40:48]


    def c_pre(cc):
        load_act_set(cc, SET_GELU)
    sec(C, c_pre)

    tok_cbias = {}
    def v_cb(v):
        v.memset(BIA[:, 40:41], -LN8)
        v.memset(BIA[:, 41:42], EPS_IN)
        tok_cbias[0] = V.tok(v.memset(BIA[:, 42:43], EPS_OUT))
    sec(V, v_cb)

    def p0_y(y):
        g_const.start(Y, IDN[:], identC[:])
        g_const.start(Y, ONESCF[:], ones128cC[:])
        g_const.start(Y, ONESR[:], ones1rC[:])
        g_const.start(Y, IOTA8, iota8C[:])
        for c in range(NCH):
            g_const.start(Y, WRFM[:, c * NE:(c + 1) * NE], Wr_h[c * 128:(c + 1) * 128, :])
        for off, src in [(RO_S0, bk_h), (RO_S1, bv_h), (RO_G1, g1_h), (RO_G2, g2_h)]:
            g_const.start(Y, ROWS[0:1, off:off + D], src.rearrange("d -> () d"))
    sec(Y, p0_y)
    tok_constY = g_const.tok()

    def p0_g(g):
        g_cstG.start(G, IND16[:], ind16C[:])
        g_cstG.start(G, ONESC[:], ones128cC[:])
        for hh in range(2):
            g_wq.start(G, WFS[:, hh * 4096:(hh + 1) * 4096].rearrange("p (k c) -> p k c", k=4),
                       Wq_h.rearrange("(k p) c -> p k c", p=128)[:, hh * 4:(hh + 1) * 4, :])
    sec(G, p0_g)

    tok_tr = {}
    tok_ev = {}
    tok_xbf = {}
    for t in range(TCH):
        xv = RS[:, (t % 2) * 1024:(t % 2) * 1024 + 1024]
        pw = PA if t % 2 == 0 else PB
        xslot = XS[:, (t % 2) * 2048:(t % 2) * 2048 + 1024]

        def y_load(y, t=t, xv=xv):
            if t >= 2:
                Y.wait(tok_tr[t - 2])
            g_x.start(Y, xv, x_h[t * 128:(t + 1) * 128, :])
        sec(Y, y_load)

        def t_tr(tt, t=t, xv=xv, pw=pw):
            T.wait(tok_constY)
            T.wait((g_x.sem, (t + 1) * 16))
            if t >= 2:
                T.wait(tok_ev[t - 2])
            for c in range(NCH):
                ins = tt.matmul(pw[:, c * 128:(c + 1) * 128],
                                xv[:, c * 128:(c + 1) * 128], IDN[:],
                                start=True, stop=True)
            tok_tr[t] = T.tok(ins)
        sec(T, t_tr)

        def c_ev(cc, t=t, pw=pw, xslot=xslot):
            C.wait(tok_tr[t])
            if t >= 2:
                C.wait((g_xfm[t % 2].sem, (t // 2) * 16))
            t1 = C.tok(cc.copy(xslot, pw[:]))
            C.wait(t1)
            ins = cc.copy(
                B1[:].rearrange("p (c n) -> p c n", c=NCH)[:, :, t * 128:(t + 1) * 128],
                xslot.rearrange("p (c n) -> p c n", c=NCH))
            tok_ev[t] = t1
            tok_xbf[t] = C.tok(ins)
        sec(C, c_ev)

        def y_xfm(y, t=t, xslot=xslot):
            Y.wait(tok_xbf[t])
            g_xfm[t % 2].start(Y, xfm_d[:, :, t * 128:(t + 1) * 128].rearrange("c p n -> p c n"),
                               xslot.rearrange("p (c n) -> p c n", c=NCH))
        sec(Y, y_xfm)

    if "d_xfm" in dbg:
        def y_d(y):
            Y.wait(g_xfm[0].tok())
            Y.wait(g_xfm[1].tok())
            g_dbg.start(Y, dbg["d_xfm"][:], xfm_d.ap())
        sec(Y, y_d)

    CHK("ingest")

    tok_biaT = {}
    tok_biaV = {}
    specs = [(bq_h, S_BQ), (g1_h, S_G1), (be1_h, S_BE1), (g2_h, S_G2), (be2_h, S_BE2)]
    for i, (src, slotc) in enumerate(specs):
        def g_ld(g, src=src, i=i):
            if i >= 1:
                G.wait(tok_biaT[i - 1])
            g_bias.start(G, BRX[0:8, :], src.rearrange("(a b) -> a b", b=128))
        sec(G, g_ld)

        def t_tp(tt, i=i):
            T.wait(tok_constY)
            T.wait((g_bias.sem, (i + 1) * 16))
            if i >= 2:
                T.wait(tok_biaV[i - 2])
            ins = tt.matmul(PC[:, (i % 2) * 512:(i % 2) * 512 + 8], BRX[0:8, :],
                            IDN[0:8, 0:8], start=True, stop=True)
            tok_biaT[i] = T.tok(ins)
        sec(T, t_tp)

        def v_ev(v, i=i, slotc=slotc):
            V.wait(tok_biaT[i])
            t0 = V.tok(v.tensor_copy(BIA[:, slotc * 8:(slotc + 1) * 8],
                                     PC[:, (i % 2) * 512:(i % 2) * 512 + 8]))
            if i == 0:
                V.wait(t0)
                t0 = V.tok(v.tensor_scalar(BIA[:, 44:52], BIA[:, 0:8], -1.0,
                                           None, ALU.mult))
            tok_biaV[i] = t0
        sec(V, v_ev)

    def g_ldb1(g):
        G.wait(tok_biaT[4])
        g_bias.start(G, BRX[:, :], b1_h.rearrange("(a b) -> a b", b=128))
    sec(G, g_ldb1)

    def t_tpb1(tt):
        T.wait(tok_constY)
        T.wait((g_bias.sem, 6 * 16))
        T.wait(tok_biaV[4])
        ins = tt.matmul(PC[:, 0:32], BRX[0:32, :], IDN[0:32, 0:32], start=True, stop=True)
        tok_biaT[5] = T.tok(ins)
    sec(T, t_tpb1)

    def v_evb1(v):
        V.wait(tok_biaT[5])
        tok_biaV[5] = V.tok(v.tensor_copy(B1F[:, 0:32], PC[:, 0:32]))
    sec(V, v_evb1)

    tok_rep = {}
    for j in range(2):
        def t_rep(tt, j=j):
            T.wait(g_const.tok())
            T.wait(tok_biaV[5] if j == 0 else tok_rep[0])
            for h in range(2):
                ins = tt.matmul(PD[:, h * 512:(h + 1) * 512], ONESR[0:1, 0:128],
                                ROWS[0:1, j * D + h * 512: j * D + (h + 1) * 512],
                                start=True, stop=True)
            tok_rep[(j, "t")] = T.tok(ins)
        sec(T, t_rep)

        def v_rep(v, j=j):
            V.wait(tok_rep[(j, "t")])
            tok_rep[j] = V.tok(v.tensor_copy(RT[:, j * 1024:(j + 1) * 1024], PD[:]))
        sec(V, v_rep)
    tok_bias_done = tok_rep.get(1)

    def y_rows2(y):
        Y.wait(tok_rep[(1, "t")])
        g_rows2.start(Y, ROWS[0:1, RO_S0:RO_S0 + D], bo_h.rearrange("d -> () d"))
        g_rows2.start(Y, ROWS[0:1, RO_S1:RO_S1 + D], b2_h.rearrange("d -> () d"))
    sec(Y, y_rows2)

    CHK("bias")

    tok_qmm = {}
    tok_elur = {}
    tok_eluy = {}
    tok_eq = {}
    tok_qsq = {}
    tok_qss = {}
    tok_rqv = {}
    tok_rq = {}
    def emit_qmm(qi):
        half, c = qi // NCH, qi % NCH
        lo = half * HALF
        pq = (PA, PB, PD)[qi % 3]

        def t_qmm(tt, qi=qi, c=c, lo=lo, pq=pq):
            T.wait(g_wq.tok())
            T.wait(g_x.tok())
            T.wait(tok_xbf[TCH - 1])
            if qi == 2:
                T.wait(tok_bias_done)
            if qi >= 3:
                T.wait(tok_elur[qi - 3])
            for k in range(NCH):
                for nn in range(2):
                    ins = tt.matmul(pq[:, nn * 512:(nn + 1) * 512],
                                    WFS[:, k * 1024 + c * 128: k * 1024 + (c + 1) * 128],
                                    b1v(k, lo + nn * 512, 512),
                                    start=(k == 0), stop=(k == NCH - 1))
            tok_qmm[qi] = T.tok(ins)
        sec(T, t_qmm)

    def emit_qrest(qi):
        half, c = qi // NCH, qi % NCH
        lo = half * HALF
        pq = (PA, PB, PD)[qi % 3]

        def c_elu(cc, qi=qi, c=c, pq=pq):
            C.wait(tok_qmm[qi])
            C.wait(tok_bias_done)
            if qi >= 1:
                C.wait(tok_eq[qi - 1])
            t_, r_, d_, y_ = RS[:, 0:1024], RS[:, 1024:2048], RS[:, 2048:3072], RS[:, 3072:4096]
            t1 = C.tok(cc.activation(t_, pq[:], AF.Relu, scale=-1.0,
                                     bias=BIA[:, 44 + c:45 + c]))
            C.wait(t1)
            t2 = C.tok(cc.activation(t_, t_, AF.Tanh, scale=-0.5))
            tok_elur[qi] = C.tok(cc.activation(r_, pq[:], AF.Relu,
                                               bias=BIA[:, S_BQ * 8 + c:S_BQ * 8 + c + 1]))
            C.wait(t2)
            t3 = C.tok(cc.activation(d_, t_, AF.Copy, scale=-1.0, bias=1.0))
            C.wait(t3)
            tok_eluy[qi] = C.tok(cc.activation(y_, d_, AF.Copy, scale=-0.4854,
                                               bias=1.4571))
        sec(C, c_elu)

        def v_eq(v, qi=qi, c=c, lo=lo):
            V.wait(tok_eluy[qi])
            def stepi(ins):
                tk = V.tok(ins)
                V.wait(tk)
                return tk
            t_, r_, d_, y_ = RS[:, 0:1024], RS[:, 1024:2048], RS[:, 2048:3072], RS[:, 3072:4096]
            stepi(v.tensor_mul(d_, d_, y_))
            stepi(v.tensor_scalar(d_, d_, -1.0, 2.0, ALU.mult, ALU.add))
            stepi(v.tensor_mul(y_, y_, d_))
            stepi(v.scalar_tensor_tensor(d_, t_, 2.0, y_, ALU.mult, ALU.mult))
            ins = v.scalar_tensor_tensor(b2v(c, lo, HALF), d_, 1.0, r_,
                                         ALU.mult, ALU.add)
            tok_eq[qi] = V.tok(ins)
        sec(V, v_eq)

        def c_sq(cc, qi=qi, c=c, lo=lo):
            C.wait(tok_eq[qi])
            if qi >= 2:
                C.wait(tok_qss[qi - 2])
            sv = SBF[:, (qi % 2) * 1024:(qi % 2) * 1024 + 1024]
            tok_qsq[qi] = C.tok(cc.activation(sv, b2v(c, lo, HALF), AF.Square))
        sec(C, c_sq)

        def t_ss(tt, qi=qi, half=half, c=c):
            T.wait(tok_qsq[qi])
            T.wait(g_cstG.tok())
            if c == 0 and half == 1:
                T.wait(tok_rqv[0])
            sv = SBF[:, (qi % 2) * 1024:(qi % 2) * 1024 + 1024]
            for nn in range(2):
                ins = tt.matmul(PC[0:16, nn * 512:(nn + 1) * 512],
                                IND16[:, c * 16:(c + 1) * 16],
                                sv[:, nn * 512:(nn + 1) * 512],
                                start=(c == 0), stop=(c == NCH - 1))
            tok_qss[qi] = T.tok(ins)
        sec(T, t_ss)

        if c == NCH - 1:
            def v_rqev(v, half=half, lo=lo, qi=qi):
                V.wait(tok_qss[qi])
                tok_rqv[half] = V.tok(v.tensor_copy(rq_rows[:, lo:lo + HALF],
                                                    PC[0:16, 0:1024]))
            sec(V, v_rqev)

            def v_rq(v, half=half, lo=lo):
                V.wait(tok_rqv[half])
                def stepi(ins):
                    tk = V.tok(ins)
                    V.wait(tk)
                    return tk
                r = rq_rows[:, lo:lo + HALF]
                scrA = KNV[0:16, 0:2048].bitcast(F32)
                scrB = SBF[0:16, 2048:4096].bitcast(F32)
                stepi(v.tensor_scalar(scrB.bitcast(I32), r.bitcast(I32), 1, None,
                                      ALU.logical_shift_right))
                stepi(v.tensor_scalar(scrB.bitcast(I32), scrB.bitcast(I32), -1,
                                      0x5F3759DF, ALU.mult, ALU.add))
                stepi(v.tensor_scalar(scrA, r, -0.5, None, ALU.mult))
                for _ in range(3):
                    stepi(v.tensor_mul(r, scrB, scrB))
                    stepi(v.tensor_mul(r, r, scrA))
                    stepi(v.tensor_scalar(r, r, 1.5, None, ALU.add))
                    stepi(v.tensor_mul(scrB, scrB, r))
                tok_rq[half] = stepi(v.tensor_scalar(r, scrB, 0.125, None, ALU.mult))
            sec(V, v_rq)


    for qi in range(2 * NCH):
        emit_qmm(qi)
        if qi >= 1:
            emit_qrest(qi - 1)
    emit_qrest(2 * NCH - 1)
    if "d_eq" in dbg:
        def y_d2(y):
            Y.wait(tok_eq[2 * NCH - 1])
            g_dbg.start(Y, dbg["d_eq"][:], B2[:])
        sec(Y, y_d2)
    if "d_rq" in dbg:
        def y_d2b(y):
            Y.wait(tok_rq[1])
            g_dbg.start(Y, dbg["d_rq"][:], rq_rows[:, :])
        sec(Y, y_d2b)

    CHK("p1q")

    def g_wkvload(g):
        G.wait(tok_qmm[2 * NCH - 1])
        for hh in range(2):
            g_wkv.start(G, WFS[:, 8192 + hh * 4096: 8192 + (hh + 1) * 4096].rearrange("p (k c) -> p k c", k=4),
                        Wk_h.rearrange("(k p) c -> p k c", p=128)[:, hh * 4:(hh + 1) * 4, :])
        for hh in range(2):
            g_wkv.start(G, WFS[:, hh * 4096:(hh + 1) * 4096].rearrange("p (k c) -> p k c", k=4),
                        Wv_h.rearrange("(k p) c -> p k c", p=128)[:, hh * 4:(hh + 1) * 4, :])
    sec(G, g_wkvload)

    tok_kmm = {}
    tok_kb = {}
    tok_kex = {}
    tok_keluy = {}
    tok_kel = {}
    tok_ksq = {}
    tok_red = {}
    tok_krs = {}
    tok_kn = {}
    tok_vev = {}
    tok_kvmm = {}
    tok_kvadd = {}
    KRD = POOL[:, 0:32]

    def emit_tkv(t):
        def t_kv(tt, t=t):
            T.wait(g_wkv.tok())
            if t == 0:
                T.wait(tok_rqv[1])
            if t >= 1:
                T.wait(tok_kb[t - 1])
                T.wait(tok_vev[t - 1])
            for k in range(NCH):
                xst = b1v(k, t * 128, 128)
                for nn in range(2):
                    tt.matmul(PA[:, nn * 512:(nn + 1) * 512], xst,
                              WFS[:, 8192 + k * 1024 + nn * 512: 8192 + k * 1024 + (nn + 1) * 512],
                              start=(k == 0), stop=(k == NCH - 1))
                for nn in range(2):
                    ins = tt.matmul(PB[:, nn * 512:(nn + 1) * 512], xst,
                                    WFS[:, k * 1024 + nn * 512: k * 1024 + (nn + 1) * 512],
                                    start=(k == 0), stop=(k == NCH - 1))
            tok_kmm[t] = T.tok(ins)
        sec(T, t_kv)

    KVSB = X2S[:, 0:1024]

    def emit_kvacc(t):
        def t_kvacc(tt, t=t):
            T.wait(tok_kn[t])
            T.wait(tok_vev[t])
            if t >= 2:
                T.wait(tok_kvadd[t - 2])
            for p in range(8):
                ins = tt.matmul(PC[:, p * 128:(p + 1) * 128],
                                KNV[:, p * 128:(p + 1) * 128],
                                KNV[:, 1024 + p * 128:1024 + (p + 1) * 128],
                                start=True, stop=True)
            tok_kvmm[t] = T.tok(ins)
        sec(T, t_kvacc)

        def v_kvadd(v, t=t):
            V.wait(tok_kvmm[t])
            if t == 0:
                ins = v.tensor_copy(KVSB, PC[:])
            else:
                ins = v.tensor_add(KVSB, KVSB, PC[:])
            tok_kvadd[t] = V.tok(ins)
        sec(V, v_kvadd)

    for t in range(TCH):
        emit_tkv(t)
        if t >= 1:
            emit_kvacc(t - 1)

        def v_kb(v, t=t):
            V.wait(tok_kmm[t])
            V.wait(tok_bias_done)
            if t >= 1:
                V.wait(tok_kex[t - 1])
            ins = v.scalar_tensor_tensor(RS[:, 0:1024], PA[:], 1.0,
                                         RT[:, 0:1024], ALU.mult, ALU.add)
            tok_kb[t] = V.tok(ins)
        sec(V, v_kb)

        def v_vev(v, t=t):
            if t >= 1:
                V.wait(tok_kvmm[t - 1])
            ins = v.scalar_tensor_tensor(KNV[:, 1024:2048], PB[:], 1.0,
                                         RT[:, 1024:2048], ALU.mult, ALU.add)
            tok_vev[t] = V.tok(ins)
        sec(V, v_vev)

        def c_kelu(cc, t=t):
            C.wait(tok_kb[t])
            t_, r_ = RS[:, 1024:2048], RS[:, 2048:3072]
            d_, y_ = RS[:, 3072:4096], RS[:, 0:1024]
            t1 = C.tok(cc.activation(t_, RS[:, 0:1024], AF.Relu, scale=-1.0))
            C.wait(t1)
            t2 = C.tok(cc.activation(t_, t_, AF.Tanh, scale=-0.5))
            tok_kex[t] = C.tok(cc.activation(r_, RS[:, 0:1024], AF.Relu))
            C.wait(t2)
            C.wait(tok_kex[t])
            t3 = C.tok(cc.activation(d_, t_, AF.Copy, scale=-1.0, bias=1.0))
            C.wait(t3)
            tok_keluy[t] = C.tok(cc.activation(y_, d_, AF.Copy, scale=-0.4854,
                                               bias=1.4571))
        sec(C, c_kelu)

        def v_ek(v, t=t):
            V.wait(tok_keluy[t])
            if t >= 1:
                V.wait(tok_kvmm[t - 1])
            def stepi(ins):
                tk = V.tok(ins)
                V.wait(tk)
                return tk
            t_, r_ = RS[:, 1024:2048], RS[:, 2048:3072]
            d_, y_ = RS[:, 3072:4096], RS[:, 0:1024]
            stepi(v.tensor_mul(d_, d_, y_))
            stepi(v.tensor_scalar(d_, d_, -1.0, 2.0, ALU.mult, ALU.add))
            stepi(v.tensor_mul(y_, y_, d_))
            stepi(v.scalar_tensor_tensor(d_, t_, 2.0, y_, ALU.mult, ALU.mult))
            ins = v.scalar_tensor_tensor(KNV[:, 0:1024], d_, 1.0, r_,
                                         ALU.mult, ALU.add)
            tok_kel[t] = V.tok(ins)
        sec(V, v_ek)

        def c_ksq(cc, t=t):
            C.wait(tok_kel[t])
            if t >= 1:
                C.wait(tok_red[t - 1])
            tok_ksq[t] = C.tok(cc.activation(SBF[:, 2048:3072], KNV[:, 0:1024], AF.Square))
        sec(C, c_ksq)

        def v_red(v, t=t):
            V.wait(tok_ksq[t])
            ins = v.reduce_sum(KRD[:, 0:16],
                               SBF[:, 2048:3072].rearrange("p (h d) -> p h d", h=16),
                               axis=AX.X)
            tok_red[t] = V.tok(ins)
        sec(V, v_red)

        def v_krs(v, t=t):
            V.wait(tok_red[t])
            def stepi(ins):
                tk = V.tok(ins)
                V.wait(tk)
                return tk
            r = KRD[:, 16:32]
            sA = BIA[:, 52:68]
            sB = BIA[:, 68:84]
            stepi(v.tensor_scalar(r.bitcast(I32), KRD[:, 0:16].bitcast(I32), 1,
                                  None, ALU.logical_shift_right))
            stepi(v.tensor_scalar(r.bitcast(I32), r.bitcast(I32), -1, 0x5F3759DF,
                                  ALU.mult, ALU.add))
            stepi(v.tensor_scalar(sA, KRD[:, 0:16], -0.5, None, ALU.mult))
            last = None
            for _ in range(2):
                stepi(v.tensor_mul(sB, r, r))
                stepi(v.tensor_mul(sB, sB, sA))
                stepi(v.tensor_scalar(sB, sB, 1.5, None, ALU.add))
                last = stepi(v.tensor_mul(r, r, sB))
            tok_krs[t] = last
        sec(V, v_krs)

        def v_kn(v, t=t):
            V.wait(tok_krs[t])
            ins = v.tensor_mul(KNV[:, 0:1024].rearrange("p (h d) -> p h d", h=16),
                               KNV[:, 0:1024].rearrange("p (h d) -> p h d", h=16),
                               KRD[:, 16:32].rearrange("p h -> p h ()").broadcast_to([128, 16, 64]))
            tok_kn[t] = V.tok(ins)
        sec(V, v_kn)
    emit_kvacc(TCH - 1)

    if "d_kn" in dbg:
        def y_dkn(y):
            Y.wait(tok_kn[TCH - 1])
            Y.wait(tok_vev[TCH - 1])
            g_dbg.start(Y, dbg["d_kn"][:], KNV[:])
        sec(Y, y_dkn)

    CHK("p1kv")

    tok_kvcp = {}
    def v_kvz(v):
        V.wait(tok_kvadd[TCH - 1])
        V.wait(tok_kex[TCH - 1])
        t1 = V.tok(v.memset(RT[:, 0:1024], 0.0))
        V.wait(t1)
        last = None
        for p in range(8):
            for hh in range(2):
                sl = slice(hh * 64, (hh + 1) * 64)
                last = v.tensor_copy(RT[sl, p * 128 + hh * 64: p * 128 + (hh + 1) * 64],
                                     KVSB[sl, p * 128 + hh * 64: p * 128 + (hh + 1) * 64])
        tok_kvcp[0] = V.tok(last)
    sec(V, v_kvz)

    def g_ar1(g):
        G.wait(tok_kvcp[0])
        g_kv1.start(G, cc1i.ap(), RT[:, 0:1024])
        g.wait_ge(g_kv1.sem, g_kv1.count)
        g.collective_compute("AllReduce", ALU.add,
                             replica_groups=[list(range(NCORES))],
                             ins=[cc1i.ap()], outs=[cc1o.ap()]).then_inc(cc_sem, 1)
        g.wait_ge(cc_sem, 1)
        g_kv1.start(G, RT[:, 0:1024], cc1o.ap())
    sec(G, g_ar1)

    tok_kvbf = {}
    def v_kvbf(v):
        V.wait(g_kv1.tok())
        tok_kvbf[0] = V.tok(v.tensor_copy(KVB[:], RT[:, 0:1024]))
    sec(V, v_kvbf)

    if "d_kv" in dbg:
        def y_d3(y):
            Y.wait(tok_kvbf[0])
            g_dbg.start(Y, dbg["d_kv"][:], RT[:, 0:1024])
        sec(Y, y_d3)

    CHK("ar1")

    def g_ind16t(g):
        G.wait(tok_red[TCH - 1])
        G.wait(tok_qss[2 * NCH - 1])
        g_cstG.start(G, IND16T[:], ind16TC[:])
    sec(G, g_ind16t)
    tok_ind16t = g_cstG.tok()

    tok_ctxv = {}
    prev_ctx = {0: None}
    for c in range(NCH):
        for half in range(2):
            lo = half * HALF

            def t_ctx(tt, c=c, half=half, lo=lo):
                T.wait(tok_rq[1])
                T.wait(tok_kvbf[0])
                T.wait(tok_ind16t)
                if prev_ctx[0] is not None:
                    T.wait(prev_ctx[0])
                for nn in range(2):
                    tt.matmul(PD[:, nn * 512:(nn + 1) * 512],
                              IND16T[:, c * 128:(c + 1) * 128],
                              rq_rows[:, lo + nn * 512: lo + (nn + 1) * 512],
                              start=True, stop=True)
                for nn in range(2):
                    ins = tt.matmul(PA[:, nn * 512:(nn + 1) * 512],
                                    KVB[:, c * 128:(c + 1) * 128],
                                    b2v(c, lo + nn * 512, 512),
                                    start=True, stop=True)
                tok_ctxv[(c, half, "t")] = T.tok(ins)
            sec(T, t_ctx)

            def v_ctx(v, c=c, half=half, lo=lo):
                V.wait(tok_ctxv[(c, half, "t")])
                t1 = V.tok(v.tensor_copy(SSB[:], PD[:]))
                V.wait(t1)
                ins = v.tensor_mul(b1v(c, lo, HALF), PA[:], SSB[:])
                tok_ctxv[(c, half)] = V.tok(ins)
                prev_ctx[0] = tok_ctxv[(c, half)]
            sec(V, v_ctx)
    tok_ctx_done = prev_ctx[0]

    if "d_ctx" in dbg:
        def y_d4(y):
            Y.wait(tok_ctx_done)
            g_dbg.start(Y, dbg["d_ctx"][:], B1[:])
        sec(Y, y_d4)

    CHK("ctx")

    state = {
        "gelu_loaded": False,
        "lnlast": None,
        "statfree": tok_ctx_done,
        "accfree": None,
        "wslot": {},
        "hpw": {0: None, 1: None},
        "hread": {},
        "woslot": {0: None, 1: None},
    }

    def ln_block(src_region, dst_region, g_slot, be_slot, g_row_off, eps_col,
                 pre_tok, out_cast=None, pool_sub=None):
        if stopped[0]:
            return {NCH - 1: None, 0: None}, {}, {}
        tok_st = {}
        tok_zc = {}
        zslot_war = {0: None, 1: None}
        for c in range(NCH):
            def c_stats(cc, c=c):
                C.wait(pre_tok(c))
                if zslot_war[c % 2] is not None:
                    C.wait(zslot_war[c % 2])
                sl = (c % 2) * 1024
                cc.copy(SBF[:, sl:sl + 512], src_region(c))
                ins = cc.activation(SBF[:, sl + 512:sl + 1024], src_region(c), AF.Square)
                tok_zc[c] = C.tok(ins)
            sec(C, c_stats)

            def t_stats(tt, c=c):
                T.wait(tok_zc[c])
                T.wait(g_cstG.tok())
                if c == 0 and state["statfree"] is not None:
                    T.wait(state["statfree"])
                sl = (c % 2) * 1024
                tt.matmul(PC[0:1, 0:512], ONESC[:], SBF[:, sl:sl + 512],
                          start=(c == 0), stop=(c == NCH - 1))
                ins = tt.matmul(PC[0:1, 512:1024], ONESC[:], SBF[:, sl + 512:sl + 1024],
                                start=(c == 0), stop=(c == NCH - 1))
                tok_st[c] = T.tok(ins)
            sec(T, t_stats)
            zslot_war[c % 2] = tok_st[c]

        tok_var = {}
        def v_rows(v):
            V.wait(tok_st[NCH - 1])
            v.tensor_scalar(LNR[0:1, 0:TSUB], PC[0:1, 0:512], 1.0 / D, None, ALU.mult)
            t2 = V.tok(v.tensor_scalar(LNR[0:1, TSUB:2 * TSUB], PC[0:1, 512:1024], 1.0 / D, None, ALU.mult))
            V.wait(t2)
            t3 = V.tok(v.tensor_mul(LNR[0:1, 2 * TSUB:3 * TSUB], LNR[0:1, 0:TSUB], LNR[0:1, 0:TSUB]))
            V.wait(t3)
            tok_var[0] = V.tok(v.tensor_sub(LNR[0:1, TSUB:2 * TSUB], LNR[0:1, TSUB:2 * TSUB], LNR[0:1, 2 * TSUB:3 * TSUB]))
        sec(V, v_rows)
        state["statfree"] = tok_var[0]

        tok_rstd = {}
        r1 = LNR[0:1, TSUB:2 * TSUB]
        r2 = LNR[0:1, 2 * TSUB:3 * TSUB]
        r3 = LNR[0:1, 3 * TSUB:4 * TSUB]
        r4 = LNR[0:1, 4 * TSUB:5 * TSUB]
        eps_val = EPS_IN if eps_col == 41 else EPS_OUT

        def v_rstd(v):
            def stepi(ins):
                tk = V.tok(ins)
                V.wait(tk)
                return tk
            V.wait(tok_var[0])
            stepi(v.tensor_scalar(r1, r1, float(eps_val), None, ALU.add))
            stepi(v.tensor_scalar(r2, r1, -0.5, None, ALU.mult))
            stepi(v.tensor_scalar(r3.bitcast(I32), r1.bitcast(I32), 1, None,
                                  ALU.logical_shift_right))
            stepi(v.tensor_scalar(r3.bitcast(I32), r3.bitcast(I32), -1, 0x5F3759DF,
                                  ALU.mult, ALU.add))
            for _ in range(2):
                stepi(v.tensor_mul(r4, r3, r3))
                stepi(v.tensor_mul(r4, r4, r2))
                stepi(v.tensor_scalar(r4, r4, 1.5, None, ALU.add))
                stepi(v.tensor_mul(r3, r3, r4))
            tok_rstd[0] = V.tok(v.tensor_copy(r1, r3))
        sec(V, v_rstd)

        tok_mr = {}
        def v_mr(v):
            V.wait(tok_rstd[0])
            tok_mr[0] = V.tok(v.tensor_mul(LNR[0:1, 0:TSUB], LNR[0:1, 0:TSUB], LNR[0:1, TSUB:2 * TSUB]))
        sec(V, v_mr)

        tok_sb = {}
        def t_sb(tt):
            T.wait(tok_mr[0])
            if state["lnlast"] is not None:
                T.wait(state["lnlast"])
            tok_sb[0] = T.tok(tt.matmul(PD[:, 0:512], ONESR[0:1, 0:128], LNR[0:1, TSUB:2 * TSUB],
                                        start=True, stop=True))
        sec(T, t_sb)

        cast_toks = {}
        pool_toks = {}
        tok_mg = {}
        tok_app = {}
        mgb = [PD[:, 512:1024], PB[:, 0:512]]
        for c in range(NCH):
            def t_mg(tt, c=c):
                if c >= 2:
                    T.wait(tok_app[c - 2])
                if g_row_off is not None:
                    lhs = ROWS[0:1, g_row_off + c * 128: g_row_off + (c + 1) * 128]
                else:
                    lhs = ONESR[0:1, 0:128]
                tok_mg[c] = T.tok(tt.matmul(mgb[c % 2], lhs, LNR[0:1, 0:TSUB],
                                            start=True, stop=True))
            sec(T, t_mg)

            def v_app(v, c=c):
                V.wait(tok_sb[0])
                V.wait(tok_mg[c])
                if g_slot is not None:
                    t1 = V.tok(v.scalar_tensor_tensor(
                        src_region(c), src_region(c),
                        BIA[:, g_slot * 8 + c:g_slot * 8 + c + 1],
                        PD[:, 0:512], ALU.mult, ALU.mult))
                else:
                    t1 = V.tok(v.scalar_tensor_tensor(
                        src_region(c), src_region(c), 1.0,
                        PD[:, 0:512], ALU.mult, ALU.mult))
                V.wait(t1)
                if be_slot is not None:
                    ins = v.scalar_tensor_tensor(
                        dst_region(c), src_region(c),
                        BIA[:, be_slot * 8 + c:be_slot * 8 + c + 1],
                        mgb[c % 2], ALU.add, ALU.subtract)
                else:
                    ins = v.scalar_tensor_tensor(
                        dst_region(c), src_region(c), 0.0,
                        mgb[c % 2], ALU.add, ALU.subtract)
                tok_app[c] = V.tok(ins)
            sec(V, v_app)

            if out_cast is not None:
                def c_cast(cc, c=c):
                    C.wait(tok_app[c])
                    cast_toks[c] = C.tok(cc.copy(out_cast(c), dst_region(c)))
                sec(C, c_cast)
            if pool_sub is not None:
                def v_pool(v, c=c):
                    V.wait(tok_app[c])
                    ins = v.reduce_sum(POOL[:, c * 4 + pool_sub:c * 4 + pool_sub + 1],
                                       dst_region(c), axis=AX.X)
                    pool_toks[c] = V.tok(ins)
                sec(V, v_pool)
        state["lnlast"] = tok_app[NCH - 1]
        return tok_app, cast_toks, pool_toks

    def ff_block(in_wait_toks, in_cols, b1col0, seed_region, dst_region,
                 scale, scale_wait, w1_src_fn, w2_src_fn, b2_row_off, wslot_key,
                 slot_extra=None, dst_wait=None, fs_wait=None):
        ev_toks = {}
        if stopped[0]:
            return {(NFS - 1, 0): None, (NFS - 1, 1): None}
        for fs in range(NFS):
            slot = fs % 2
            w1v = (WFS if slot == 0 else B2)[:, 0:8 * FSW]
            w2v = (WFS if slot == 0 else B2)[:, 8 * FSW:16 * FSW]
            grp = g_wfsA if slot == 0 else g_wfsB

            def g_wload(g, fs=fs, slot=slot, grp=grp, w1v=w1v, w2v=w2v):
                if state["wslot"].get((wslot_key, slot)) is not None:
                    G.wait(state["wslot"][(wslot_key, slot)])
                if slot_extra:
                    for tk in slot_extra.get(slot, ()):
                        G.wait(tk)
                if fs_wait is not None:
                    G.wait(fs_wait(fs))
                w1_src = w1_src_fn()
                w2_src = w2_src_fn()
                for hh in range(2):
                    grp.start(G, w1v.rearrange("p (k c) -> p k c", k=8)[:, hh * 4:(hh + 1) * 4, :],
                              w1_src.rearrange("(k p) c -> p k c", p=128)[:, hh * 4:(hh + 1) * 4,
                                                                          fs * FSW:(fs + 1) * FSW])
                for hh in range(2):
                    grp.start(G, w2v.rearrange("p (f c) -> p f c", f=8)[:, hh * 4:(hh + 1) * 4, :],
                              w2_src.rearrange("(f p) c -> p f c", p=128)[:, fs * 8 + hh * 4: fs * 8 + (hh + 1) * 4, :])
            sec(G, g_wload)
            ld_tok = (grp.sem, grp.count)

            h_toks = {}
            ffstage = int(os.environ.get("KERN_FFSTAGE", "9"))
            for f in range(8 if ffstage >= 1 else 0):
                fg = fs * 8 + f

                def t_w1(tt, f=f, ld_tok=ld_tok, w1v=w1v):
                    T.wait(ld_tok)
                    if state["hpw"][f % 2] is not None:
                        T.wait(state["hpw"][f % 2])
                    ph = PA[:, (f % 2) * 512:(f % 2) * 512 + 512]
                    for k in range(NCH):
                        if in_wait_toks:
                            T.wait(in_wait_toks[k])
                        ins = tt.matmul(ph, w1v[:, k * FSW + f * 128: k * FSW + (f + 1) * 128],
                                        in_cols(k), start=(k == 0), stop=(k == NCH - 1))
                    h_toks[(f, "mm")] = T.tok(ins)
                sec(T, t_w1)

                if ffstage < 2:
                    h_toks[f] = h_toks[(f, "mm")]
                    continue

                def c_gelu(cc, f=f, fg=fg):
                    C.wait(h_toks[(f, "mm")])
                    if state["hread"].get(f) is not None:
                        C.wait(state["hread"][f])
                    ph = PA[:, (f % 2) * 512:(f % 2) * 512 + 512]
                    ins = cc.activation(SBF[:, f * 512:(f + 1) * 512], ph, AF.Gelu,
                                        bias=B1F[:, b1col0 + fg:b1col0 + fg + 1])
                    h_toks[f] = C.tok(ins)
                    state["hpw"][f % 2] = h_toks[f]
                sec(C, c_gelu)

            accv = [PC[:, 0:512], PC[:, 512:1024], PD[:, 0:512], PD[:, 512:1024]]
            tok_w2 = {}
            for oh in range(2 if ffstage >= 3 else 0):
                for f in range(8):
                    def t_w2(tt, f=f, oh=oh, fs=fs, w2v=w2v):
                        T.wait(h_toks[f])
                        if f == 0 and oh == 0:
                            if state["accfree"] is not None:
                                T.wait(state["accfree"])
                            if state["statfree"] is not None:
                                T.wait(state["statfree"])
                            if state["lnlast"] is not None:
                                T.wait(state["lnlast"])
                        if f == 0 and oh == 1:
                            T.wait(ev_toks[(fs, 0)])
                        last_f = (f == 7)
                        fold = (b2_row_off is not None and fs == 0
                                and not os.environ.get("KERN_NOFOLD"))
                        for ot in range(4):
                            og = oh * 4 + ot
                            ins = tt.matmul(accv[ot],
                                            w2v[:, f * 1024 + og * 128: f * 1024 + (og + 1) * 128],
                                            SBF[:, f * 512:(f + 1) * 512],
                                            start=(f == 0), stop=(last_f and not fold))
                        if last_f and fold:
                            for ot in range(4):
                                og = oh * 4 + ot
                                ins = tt.matmul(
                                    accv[ot],
                                    ROWS[0:1, b2_row_off + og * 128: b2_row_off + (og + 1) * 128],
                                    ONESR[0:1, 0:512], start=False, stop=True)
                        tok_w2[(oh, f)] = T.tok(ins)
                        state["hread"][f] = tok_w2[(oh, f)]
                    sec(T, t_w2)

                if ffstage < 4:
                    ev_toks[(fs, oh)] = tok_w2[(oh, 7)]
                    continue

                def v_acc(v, oh=oh, fs=fs):
                    V.wait(tok_w2[(oh, 7)])
                    if scale_wait is not None:
                        V.wait(scale_wait)
                    if dst_wait is not None:
                        V.wait(dst_wait)
                    last = None
                    for ot in range(4):
                        oc = oh * 4 + ot
                        if fs == 0 and (seed_region is not None):
                            last = v.scalar_tensor_tensor(dst_region(oc), accv[ot], scale,
                                                          seed_region(oc), ALU.mult, ALU.add)
                        elif os.environ.get("KERN_ACC_COPY"):
                            last = v.tensor_copy(dst_region(oc), accv[ot])
                        elif os.environ.get("KERN_ACC_SCRATCH"):
                            last = v.scalar_tensor_tensor(X2S[:, oc * TSUB:(oc + 1) * TSUB],
                                                          accv[ot], scale,
                                                          dst_region(oc), ALU.mult, ALU.add)
                        else:
                            last = v.scalar_tensor_tensor(dst_region(oc), accv[ot], scale,
                                                          dst_region(oc), ALU.mult, ALU.add)
                    ev_toks[(fs, oh)] = V.tok(last)
                sec(V, v_acc)
            state["accfree"] = ev_toks.get((fs, 1))
            state["wslot"][(wslot_key, slot)] = (tok_w2.get((1, 7)) or h_toks.get(7)
                                                 or h_toks.get((7, "mm")) or ld_tok)
        if not ev_toks:
            ev_toks = {(NFS - 1, 0): None, (NFS - 1, 1): None}
        for fs2 in range(NFS):
            for oh2 in range(2):
                ev_toks.setdefault((fs2, oh2), None)
        return ev_toks

    tok_x2cast = {}
    tok_pool_all = {}
    tok_zo_prev = {0: None}
    tok_xs = {}
    for sub in range(NSUB):
        slo = sub * TSUB

        def y_xs(y, sub=sub, slo=slo):
            Y.wait(g_xfm[0].tok())
            Y.wait(g_xfm[1].tok())
            Y.wait(tok_xbf[TCH - 1])
            if tok_zo_prev[0] is not None:
                Y.wait(tok_zo_prev[0])
            g_xs.start(Y, XS[:].rearrange("p (c n) -> p c n", c=NCH),
                       xfm_d[:, :, slo:slo + TSUB].rearrange("c p n -> p c n"))
            tok_xs[sub] = g_xs.tok()
        sec(Y, y_xs)
        CHK("xs%d" % sub)

        tok_wo = {}
        for e in range(NCH):
            slot = e % 2
            grp = g_wroA if slot == 0 else g_wroB

            def g_wo(g, e=e, slot=slot, grp=grp):
                if state["woslot"][slot] is not None:
                    G.wait(state["woslot"][slot])
                else:
                    G.wait(tok_ctx_done)
                grp.start(G, wro_slots[slot], Wo_h[e * 128:(e + 1) * 128, :])
            sec(G, g_wo)
            ld_tok = (grp.sem, grp.count)

            def t_wo(tt, e=e, slot=slot, ld_tok=ld_tok, slo=slo):
                T.wait(ld_tok)
                if e == 0:
                    if state["accfree"] is not None:
                        T.wait(state["accfree"])
                    if state["statfree"] is not None:
                        T.wait(state["statfree"])
                    if state["lnlast"] is not None:
                        T.wait(state["lnlast"])
                    T.wait(tok_ctx_done)
                for o in range(NCH):
                    pbank = [PA, PB, PC, PD][o // 2][:, (o % 2) * 512:(o % 2) * 512 + 512]
                    ins = tt.matmul(pbank, wro_slots[slot][:, o * 128:(o + 1) * 128],
                                    b1v(e, slo, TSUB), start=(e == 0), stop=False)
                tok_wo[e] = T.tok(ins)
                state["woslot"][slot] = tok_wo[e]
            sec(T, t_wo)

        def t_wof(tt, sub=sub):
            T.wait(g_rows2.tok())
            for o in range(NCH):
                pbank = [PA, PB, PC, PD][o // 2][:, (o % 2) * 512:(o % 2) * 512 + 512]
                ins = tt.matmul(pbank, ROWS[0:1, RO_S0 + o * 128: RO_S0 + (o + 1) * 128],
                                ONESR[0:1, 0:512], start=False, stop=True)
            tok_wo["fold"] = T.tok(ins)
        sec(T, t_wof)
        CHK("wo%d" % sub)

        tok_z1 = {}
        def v_z1(v, sub=sub, slo=slo):
            V.wait(tok_wo["fold"])
            V.wait(tok_xs[sub])
            for o in range(NCH):
                pbank = [PA, PB, PC, PD][o // 2][:, (o % 2) * 512:(o % 2) * 512 + 512]
                ins = v.scalar_tensor_tensor(rsv(o), pbank, 1.0,
                                             xsv(o), ALU.mult, ALU.add)
                tok_z1[o] = V.tok(ins)
        sec(V, v_z1)
        CHK("z1%d" % sub)
        state["accfree"] = tok_z1.get(NCH - 1)
        state["statfree"] = tok_z1.get(NCH - 1)

        if "d_z1" in dbg and sub == 0:
            def y_dz1(y):
                Y.wait(tok_z1[NCH - 1])
                g_dbg.start(Y, dbg["d_z1"][:], RS[:])
            sec(Y, y_dz1)

        app1, cast1, _ = ln_block(
            rsv, rsv, S_G1, S_BE1, RO_G1, 41,
            pre_tok=lambda c: tok_z1.get(c),
            out_cast=lambda c, slo=slo: b1v(c, slo, TSUB))
        CHK("ln1%d" % sub)

        if "d_attn" in dbg and sub == 0:
            def y_dat(y):
                Y.wait(app1[NCH - 1])
                g_dbg.start(Y, dbg["d_attn"][:], RS[:])
            sec(Y, y_dat)

        ff1 = ff_block(cast1, lambda k, slo=slo: b1v(k, slo, TSUB), 0,
                       rsv, rtv, 1.0, None, (lambda: W1_h), (lambda: W2_h), RO_S1, "ff",
                       slot_extra=({0: [tok_kmm.get(TCH - 1)],
                                    1: [tok_ctx_done]} if sub == 0 else None))
        CHK("ff%d" % sub)

        app2, _, _ = ln_block(
            rtv, rtv, S_G2, S_BE2, RO_G2, 41,
            pre_tok=lambda c: ff1.get((NFS - 1, 1)))
        CHK("ln2%d" % sub)

        tok_zo = {}
        def v_zo(v, sub=sub, slo=slo):
            for c in range(NCH):
                V.wait(app2[c])
                ins = v.tensor_add(rtv(c), rtv(c), xsv(c))
                tok_zo[c] = V.tok(ins)
        sec(V, v_zo)
        tok_zo_prev[0] = tok_zo.get(NCH - 1)

        app3, cast3, pools = ln_block(
            rtv, rtv, None, None, None, 42,
            pre_tok=lambda c: tok_zo[c],
            out_cast=lambda c, slo=slo: b1v(c, slo, TSUB),
            pool_sub=sub)
        for c in range(NCH):
            tok_x2cast[(sub, c)] = cast3.get(c)
            tok_pool_all[(sub, c)] = pools.get(c)
        CHK("p2s%d" % sub)

    if "d_x2b" in dbg:
        def y_dx2(y):
            for c in range(NCH):
                Y.wait(tok_x2cast[(NSUB - 1, c)])
            g_dbg.start(Y, dbg["d_x2b"][:], B1[:])
        sec(Y, y_dx2)

    CHK("p2")

    tok_lgT = {}
    def t_lg(tt):
        for sub in range(NSUB):
            for c in range(NCH):
                T.wait(tok_pool_all[(sub, c)])
        T.wait(state["statfree"])
        n = 0
        for c in range(NCH):
            for sub in range(NSUB):
                ins = tt.matmul(PB[0:1, 0:8], POOL[:, c * 4 + sub:c * 4 + sub + 1],
                                WRFM[:, c * NE:(c + 1) * NE],
                                start=(n == 0), stop=(n == 31))
                n += 1
        tok_lgT[0] = T.tok(ins)
    sec(T, t_lg)

    tok_rtr = {}
    def v_lg(v):
        V.wait(tok_lgT[0])
        tok_rtr[0] = V.tok(v.tensor_scalar(RTR[0:1, 0:8], PB[0:1, 0:8],
                                           1.0 / S, None, ALU.mult))
    sec(V, v_lg)

    def g_ar2(g):
        G.wait(tok_rtr[0])
        g_lg.start(G, cc2i.ap(), RTR[0:1, 0:8])
        g.wait_ge(g_lg.sem, g_lg.count)
        g.collective_compute("AllReduce", ALU.add,
                             replica_groups=[list(range(NCORES))],
                             ins=[cc2i.ap()], outs=[cc2o.ap()]).then_inc(cc_sem, 1)
        g.wait_ge(cc_sem, 2)
        g_lg.start(G, RTR[0:1, 0:8], cc2o.ap())
    sec(G, g_ar2)

    tok_rt2 = {}
    def v_rt(v):
        V.wait(g_lg.tok())
        V.wait(g_const.tok())
        def step(ins):
            tk = V.tok(ins)
            V.wait(tk)
            return tk
        step(v.reduce_max(RTR[0:1, 32:33], RTR[0:1, 0:8], axis=AX.X))
        step(v.tensor_tensor(RTR[0:1, 8:16], RTR[0:1, 0:8],
                             RTR[0:1, 32:33].broadcast_to([1, 8]), ALU.is_ge))
        step(v.tensor_scalar(RTR[0:1, 16:24], RTR[0:1, 8:16], -1e30, None, ALU.mult))
        step(v.tensor_add(RTR[0:1, 16:24], RTR[0:1, 16:24], RTR[0:1, 0:8]))
        step(v.reduce_max(RTR[0:1, 33:34], RTR[0:1, 16:24], axis=AX.X))
        step(v.tensor_tensor(RTR[0:1, 16:24], RTR[0:1, 16:24],
                             RTR[0:1, 33:34].broadcast_to([1, 8]), ALU.is_ge))
        step(v.tensor_mul(RTR[0:1, 24:32], RTR[0:1, 8:16], IOTA8))
        step(v.reduce_sum(RTR[0:1, 38:39], RTR[0:1, 24:32], axis=AX.X))
        step(v.tensor_mul(RTR[0:1, 24:32], RTR[0:1, 16:24], IOTA8))
        step(v.reduce_sum(RTR[0:1, 39:40], RTR[0:1, 24:32], axis=AX.X))
        ins = v.tensor_sub(RTR[0:1, 34:35], RTR[0:1, 33:34], RTR[0:1, 32:33])
        tok_rt2[0] = V.tok(ins)
    sec(V, v_rt)

    def v_rtexp(v):
        V.wait(tok_rt2[0])
        u = RTR[0:1, 40:41]
        p = RTR[0:1, 35:36]
        def stepi(ins):
            tk = V.tok(ins)
            V.wait(tk)
            return tk
        stepi(v.tensor_scalar(u, RTR[0:1, 34:35], -24.0, None, ALU.max))
        stepi(v.tensor_scalar(u, u, 1.0 / 64.0, None, ALU.mult))
        stepi(v.tensor_scalar(p, u, 0.25, 1.0, ALU.mult, ALU.add))
        stepi(v.tensor_mul(p, p, u))
        stepi(v.tensor_scalar(p, p, 1.0 / 3.0, 1.0, ALU.mult, ALU.add))
        stepi(v.tensor_mul(p, p, u))
        stepi(v.tensor_scalar(p, p, 0.5, 1.0, ALU.mult, ALU.add))
        stepi(v.tensor_mul(p, p, u))
        stepi(v.tensor_scalar(p, p, 1.0, 1.0, ALU.mult, ALU.add))
        for _ in range(5):
            stepi(v.tensor_mul(p, p, p))
        tok_rt2[1] = V.tok(v.tensor_mul(p, p, p))
    sec(V, v_rtexp)

    def v_rt2(v):
        V.wait(tok_rt2[1])
        def step(ins):
            tk = V.tok(ins)
            V.wait(tk)
            return tk
        step(v.tensor_scalar(RTR[0:1, 36:37], RTR[0:1, 35:36], 1.0, None, ALU.add))
        step(v.reciprocal(RTR[0:1, 36:37], RTR[0:1, 36:37]))
        ins = v.tensor_scalar(RTR[0:1, 37:38], RTR[0:1, 36:37], -1.0, 1.0,
                              ALU.mult, ALU.add)
        tok_rt2[2] = V.tok(ins)
    sec(V, v_rt2)

    if "d_lg" in dbg:
        def y_dlg(y):
            Y.wait(tok_rt2[2])
            g_dbg.start(Y, dbg["d_lg"][:], RTR[0:1, :])
        sec(Y, y_dlg)

    tok_web = {}
    def t_web(tt):
        T.wait(tok_rt2[2])
        tt.matmul(PB[:, 0:1], ONESR[0:1, 0:128], RTR[0:1, 36:37], start=True, stop=True)
        ins = tt.matmul(PB[:, 512:513], ONESR[0:1, 0:128], RTR[0:1, 37:38],
                        start=True, stop=True)
        tok_web[0] = T.tok(ins)
    sec(T, t_web)

    def v_web(v):
        V.wait(tok_web[0])
        v.tensor_copy(WEB[:, 0:1], PB[:, 0:1])
        tok_web[1] = V.tok(v.tensor_copy(WEB[:, 1:2], PB[:, 512:513]))
    sec(V, v_web)

    tok_rti = {}
    def c_rti(cc):
        C.wait(tok_rt2[2])
        tok_rti[0] = C.tok(cc.copy(RTI[0:1, 0:2], RTR[0:1, 38:40]))
    sec(C, c_rti)

    CHK("router")

    iv = [None, None]
    def g_snap(g):
        G.wait(tok_rti[0])
        g.reg_load(reg0, RTI[0:1, 0:1])
        g.reg_load(reg1, RTI[0:1, 1:2])
        iv[0] = g.snap(reg0, min_val=0, max_val=NE - 1)
        iv[1] = g.snap(reg1, min_val=0, max_val=NE - 1)
    sec(G, g_snap)

    def y_wsel(y):
        Y.wait(tok_rti[0])
        y.reg_load(yreg0, RTI[0:1, 0:1])
        y.reg_load(yreg1, RTI[0:1, 1:2])
        jv0 = y.snap(yreg0, min_val=0, max_val=NE - 1)
        jv1 = y.snap(yreg1, min_val=0, max_val=NE - 1)
        for es, jv in ((0, jv0), (1, jv1)):
            for fs in range(NFS):
                grp = g_wsel[es * NFS + fs]
                grp.start(Y, W1sel[es].ap()[:, fs * FSW:(fs + 1) * FSW],
                          W1m_h[jv][:, fs * FSW:(fs + 1) * FSW])
                grp.start(Y, W2sel[es].ap()[fs * FSW:(fs + 1) * FSW, :],
                          W2m_h[jv][fs * FSW:(fs + 1) * FSW, :])
    sec(Y, y_wsel)

    tok_moeb = {}
    for es in range(2):
        def g_mb(g, es=es):
            if es == 0:
                G.wait(state["woslot"][0])
                G.wait(state["woslot"][1])
                G.wait(state["wslot"][("ff", 0)])
                G.wait(state["wslot"][("ff", 1)])
            else:
                G.wait(tok_moeb[(0, "v")])
            roff = RO_S0 if es == 0 else RO_S1
            g_moeb.start(G, ROWS[0:1, roff:roff + D],
                         b2m_h[iv[es]].rearrange("d -> () d"))
            g_moeb.start(G, BRX[:, :], b1m_h[iv[es]].rearrange("(a b) -> a b", b=128))
        sec(G, g_mb)
        ld = (g_moeb.sem, (es + 1) * 32)

        def t_mb(tt, es=es, ld=ld):
            T.wait(ld)
            T.wait(tok_web[1])
            if state["statfree"] is not None:
                T.wait(state["statfree"])
            ins = tt.matmul(PB[:, 0:32] if es == 0 else PB[:, 512:544],
                            BRX[0:32, :], IDN[0:32, 0:32], start=True, stop=True)
            tok_moeb[(es, "t")] = T.tok(ins)
        sec(T, t_mb)

        def v_mb(v, es=es):
            V.wait(tok_moeb[(es, "t")])
            ins = v.tensor_copy(B1F[:, 32 + es * 32:64 + es * 32],
                                PB[:, 0:32] if es == 0 else PB[:, 512:544])
            tok_moeb[(es, "v")] = V.tok(ins)
        sec(V, v_mb)

    CHK("moeb")

    tok_otr = {}
    tok_oev = {}
    tok_upc = {}
    for sub in range(NSUB):
        slo = sub * TSUB

        def c_upc(cc, sub=sub, slo=slo):
            for c in range(NCH):
                C.wait(tok_x2cast[(sub, c)])
            if (sub - 1, "lnfree") in tok_upc:
                C.wait(tok_upc[(sub - 1, "lnfree")])
            ins = cc.copy(X2S[:].rearrange("p (c n) -> p c n", c=NCH),
                          B1[:].rearrange("p (c n) -> p c n", c=NCH)[:, :, slo:slo + TSUB])
            tok_upc[sub] = C.tok(ins)
        sec(C, c_upc)
        upc = tok_upc.get(sub)

        moe_last = None
        for es in range(2):
            in_toks = {c: tok_x2cast.get((sub, c)) for c in range(NCH)}
            ev = ff_block(in_toks, lambda k, slo=slo: b1v(k, slo, TSUB),
                          32 + es * 32,
                          None, x2sv,
                          WEB[:, es:es + 1], tok_web.get(1),
                          (lambda es=es: W1sel[es].ap()), (lambda es=es: W2sel[es].ap()),
                          (RO_S0 if es == 0 else RO_S1), "ff",
                          slot_extra={0: [tok_moeb.get((es, "v"))],
                                      1: [tok_moeb.get((es, "v"))]},
                          dst_wait=upc,
                          fs_wait=(lambda fs, es=es: (g_wsel[es * NFS + fs].sem, 32)))
            moe_last = ev

        appF, _, _ = ln_block(
            x2sv, rsv, None, None, None, 42,
            pre_tok=lambda c, ev=moe_last: ev[(NFS - 1, 1)])
        tok_upc[(sub, "lnfree")] = appF.get(NCH - 1)

        for tcq in range(TSUB // 128):
            tglob = sub * 4 + tcq
            pw = PA if tcq % 2 == 0 else PB

            def t_otr(tt, sub=sub, tcq=tcq, pw=pw, appF=appF):
                for c in range(NCH):
                    T.wait(appF[c])
                if (sub, tcq - 2) in tok_oev:
                    T.wait(tok_oev[(sub, tcq - 2)])
                if tcq < 2 and (sub - 1, tcq + 2) in tok_oev:
                    T.wait(tok_oev[(sub - 1, tcq + 2)])
                for c in range(NCH):
                    ins = tt.matmul(pw[:, c * 128:(c + 1) * 128],
                                    RS[:, c * TSUB + tcq * 128: c * TSUB + (tcq + 1) * 128],
                                    IDN[:], start=True, stop=True)
                tok_otr[(sub, tcq)] = T.tok(ins)
            sec(T, t_otr)

            def c_oev(cc, sub=sub, tcq=tcq, pw=pw):
                C.wait(tok_otr[(sub, tcq)])
                if sub >= 1:
                    C.wait((g_out[tcq].sem, sub * 16))
                ins = cc.copy(RT[:, tcq * 1024:(tcq + 1) * 1024], pw[:])
                tok_oev[(sub, tcq)] = C.tok(ins)
            sec(C, c_oev)

            def y_ost(y, sub=sub, tcq=tcq, tglob=tglob):
                Y.wait(tok_oev[(sub, tcq)])
                g_out[tcq].start(Y, out_h[tglob * 128:(tglob + 1) * 128, :],
                                 RT[:, tcq * 1024:(tcq + 1) * 1024])
            sec(Y, y_ost)
        state["hpw"][0] = tok_oev[(sub, 3)]
        state["hpw"][1] = tok_oev[(sub, 3)]
        CHK("p3s%d" % sub)

    def y_fin(y):
        for i in range(4):
            Y.wait(g_out[i].tok())
        if g_dbg.count:
            Y.wait(g_dbg.tok())
    sec(Y, y_fin)

    st.close()
    return nc


_NC_CACHE = {}


def kernel(**inputs):
    if "main" not in _NC_CACHE:
        _NC_CACHE["main"] = build()
    nc = _NC_CACHE["main"]
    x = np.ascontiguousarray(np.asarray(inputs["L0_attn_x"], dtype=np.float32)[0])
    base = {
        "Wq": inputs["L0_attn_Wq"], "bq": inputs["L0_attn_bq"],
        "Wk": inputs["L0_attn_Wk"], "bk": inputs["L0_attn_bk"],
        "Wv": inputs["L0_attn_Wv"], "bv": inputs["L0_attn_bv"],
        "Wo": inputs["L0_attn_Wo"], "bo": inputs["L0_attn_bo"],
        "g1": inputs["L0_attn_g1"], "be1": inputs["L0_attn_be1"],
        "W1": inputs["L0_attn_W1"], "b1": inputs["L0_attn_b1"],
        "W2": inputs["L0_attn_W2"], "b2": inputs["L0_attn_b2"],
        "g2": inputs["L0_attn_g2"], "be2": inputs["L0_attn_be2"],
        "Wr": inputs["L0_ffn_Wr"], "W1m": inputs["L0_ffn_W1"],
        "b1m": inputs["L0_ffn_b1"], "W2m": inputs["L0_ffn_W2"],
        "b2m": inputs["L0_ffn_b2"],
    }
    base = {k: np.ascontiguousarray(np.asarray(v, dtype=np.float32)) for k, v in base.items()}
    in_maps = []
    for i in range(NCORES):
        m = dict(base)
        m["x"] = np.ascontiguousarray(x[i * TC:(i + 1) * TC, :])
        in_maps.append(m)
    res = run_bass_kernel_spmd(nc, in_maps, core_ids=list(range(NCORES)))
    out = np.concatenate([np.asarray(r["out"]) for r in res.results], axis=0)
    return out.reshape(1, S, D)


# version 14
# speedup vs baseline: 1.0040x; 1.0012x over previous
import contextlib
import math
import os

import numpy as np

import concourse.bass as bass
import concourse.mybir as mybir
from concourse.bass_utils import run_bass_kernel_spmd

F32 = mybir.dt.float32
BF16 = mybir.dt.bfloat16
I32 = mybir.dt.int32
AF = mybir.ActivationFunctionType
ALU = mybir.AluOpType
AX = mybir.AxisListType

D = 1024
NH = 16
FF = 4096
NE = 8
S = 16384
NCORES = 8
TC = S // NCORES
NCH = D // 128
TCH = TC // 128
HALF = TC // 2
NSUB = 4
TSUB = TC // NSUB
NFS = 4
FSW = FF // NFS
EPS_IN = 1e-12
EPS_OUT = 1e-5
LN8 = math.log(8.0)

RO_S0, RO_S1, RO_G1, RO_G2 = 0, D, 2 * D, 3 * D
S_BQ, S_G1, S_BE1, S_G2, S_BE2 = range(5)


class Eng:
    def __init__(self, nc, stack, name):
        self.e = getattr(nc, name)
        self.name = name
        self.sem = stack.enter_context(nc.semaphore("sf_" + name))
        self.n = 0
        self.seen = {}

    def tok(self, ins):
        self.n += 1
        ins.then_inc(self.sem, 1)
        return (self.sem, self.n)

    def wait(self, *toks):
        for t in toks:
            if t is None:
                continue
            sem, val = t
            k = id(sem)
            if val > self.seen.get(k, 0):
                self.e.wait_ge(sem, val)
                self.seen[k] = val


class DmaGrp:
    def __init__(self, nc, stack, name):
        self.sem = stack.enter_context(nc.semaphore(name))
        self.count = 0

    def start(self, eng, out, in_, **kw):
        ins = eng.e.dma_start(out, in_, **kw)
        self.count += 16
        ins.then_inc(self.sem, 16)
        return ins

    def tok(self):
        return (self.sem, self.count)


class _StopBuild(Exception):
    pass


def build(debug=(), stop_after=None):
    nc = bass.Bass("TRN2", target_bir_lowering=False)

    x_h = nc.dram_tensor("x", [TC, D], F32, kind="ExternalInput").ap()
    Wq_h = nc.dram_tensor("Wq", [D, D], F32, kind="ExternalInput").ap()
    bq_h = nc.dram_tensor("bq", [D], F32, kind="ExternalInput").ap()
    Wk_h = nc.dram_tensor("Wk", [D, D], F32, kind="ExternalInput").ap()
    bk_h = nc.dram_tensor("bk", [D], F32, kind="ExternalInput").ap()
    Wv_h = nc.dram_tensor("Wv", [D, D], F32, kind="ExternalInput").ap()
    bv_h = nc.dram_tensor("bv", [D], F32, kind="ExternalInput").ap()
    Wo_h = nc.dram_tensor("Wo", [D, D], F32, kind="ExternalInput").ap()
    bo_h = nc.dram_tensor("bo", [D], F32, kind="ExternalInput").ap()
    g1_h = nc.dram_tensor("g1", [D], F32, kind="ExternalInput").ap()
    be1_h = nc.dram_tensor("be1", [D], F32, kind="ExternalInput").ap()
    W1_h = nc.dram_tensor("W1", [D, FF], F32, kind="ExternalInput").ap()
    b1_h = nc.dram_tensor("b1", [FF], F32, kind="ExternalInput").ap()
    W2_h = nc.dram_tensor("W2", [FF, D], F32, kind="ExternalInput").ap()
    b2_h = nc.dram_tensor("b2", [D], F32, kind="ExternalInput").ap()
    g2_h = nc.dram_tensor("g2", [D], F32, kind="ExternalInput").ap()
    be2_h = nc.dram_tensor("be2", [D], F32, kind="ExternalInput").ap()
    Wr_h = nc.dram_tensor("Wr", [D, NE], F32, kind="ExternalInput").ap()
    W1m_h = nc.dram_tensor("W1m", [NE, D, FF], F32, kind="ExternalInput").ap()
    b1m_h = nc.dram_tensor("b1m", [NE, FF], F32, kind="ExternalInput").ap()
    W2m_h = nc.dram_tensor("W2m", [NE, FF, D], F32, kind="ExternalInput").ap()
    b2m_h = nc.dram_tensor("b2m", [NE, D], F32, kind="ExternalInput").ap()
    out_h = nc.dram_tensor("out", [TC, D], F32, kind="ExternalOutput").ap()
    xfm_d = nc.dram_tensor("xfm_d", [NCH, 128, TC], F32)
    W1sel = [nc.dram_tensor("W1sel%d" % e, [D, FF], F32) for e in range(2)]
    W2sel = [nc.dram_tensor("W2sel%d" % e, [FF, D], F32) for e in range(2)]
    cc1i = nc.dram_tensor("cc1i", [128, NH * 64], F32)
    cc1o = nc.dram_tensor("cc1o", [128, NH * 64], F32)
    cc2i = nc.dram_tensor("cc2i", [1, NE], F32)
    cc2o = nc.dram_tensor("cc2o", [1, NE], F32)

    dbg_specs = {
        "d_xfm": ([NCH, 128, TC], F32), "d_eq": ([128, NCH * TC], BF16),
        "d_kv": ([128, NH * 64], F32), "d_ctx": ([128, NCH * TC], BF16),
        "d_attn": ([128, NCH * TSUB], F32), "d_x2b": ([128, NCH * TC], BF16),
        "d_lg": ([1, 64], F32), "d_rq": ([16, TC], F32),
        "d_z1": ([128, NCH * TSUB], F32), "d_kn": ([128, 2048], BF16),
    }
    dbg = {}
    for name in debug:
        shp, dt = dbg_specs[name]
        dbg[name] = nc.dram_tensor(name, shp, dt, kind="ExternalOutput").ap()

    st = contextlib.ExitStack()

    RS = st.enter_context(nc.sbuf_tensor("RS", [128, NCH * TSUB], F32))
    RT = st.enter_context(nc.sbuf_tensor("RT", [128, NCH * TSUB], F32))
    XS = st.enter_context(nc.sbuf_tensor("XS", [128, NCH * TSUB], F32))
    X2S = st.enter_context(nc.sbuf_tensor("X2S", [128, NCH * TSUB], F32))
    B1 = st.enter_context(nc.sbuf_tensor("B1", [128, NCH * TC], BF16))
    B2 = st.enter_context(nc.sbuf_tensor("B2", [128, NCH * TC], BF16))
    WFS = st.enter_context(nc.sbuf_tensor("WFS", [128, NCH * TC], BF16))
    SBF = st.enter_context(nc.sbuf_tensor("SBF", [128, 4096], BF16))
    SSB = st.enter_context(nc.sbuf_tensor("SSB", [128, 1024], BF16))
    KNV = st.enter_context(nc.sbuf_tensor("KNV", [128, 2048], BF16))
    LNR = st.enter_context(nc.sbuf_tensor("LNR", [1, 5 * TSUB], F32))
    IDN = st.enter_context(nc.sbuf_tensor("IDN", [128, 128], F32))
    KVB = st.enter_context(nc.sbuf_tensor("KVB", [128, NH * 64], BF16))
    BIA = st.enter_context(nc.sbuf_tensor("BIA", [128, 84], F32))
    B1F = st.enter_context(nc.sbuf_tensor("B1F", [128, 96], F32))
    ROWS = st.enter_context(nc.sbuf_tensor("ROWS", [1, 4 * D], F32))
    RTR = st.enter_context(nc.sbuf_tensor("RTR", [1, 64], F32))
    RTI = st.enter_context(nc.sbuf_tensor("RTI", [1, 2], I32))
    POOL = st.enter_context(nc.sbuf_tensor("POOL", [128, 32], F32))
    WEB = st.enter_context(nc.sbuf_tensor("WEB", [128, 2], F32))
    BRX = st.enter_context(nc.sbuf_tensor("BRX", [32, 128], F32))
    IND16 = st.enter_context(nc.sbuf_tensor("IND16", [128, 128], BF16))
    ONESC = st.enter_context(nc.sbuf_tensor("ONESC", [128, 1], BF16))
    ONESCF = st.enter_context(nc.sbuf_tensor("ONESCF", [128, 1], F32))
    ONESR = st.enter_context(nc.sbuf_tensor("ONESR", [1, 512], F32))
    WRFM = st.enter_context(nc.sbuf_tensor("WRFM", [128, NCH * NE], F32))

    PA = st.enter_context(nc.psum_tensor("PA", [128, 1024], F32))
    PB = st.enter_context(nc.psum_tensor("PB", [128, 1024], F32))
    PC = st.enter_context(nc.psum_tensor("PC", [128, 1024], F32))
    PD = st.enter_context(nc.psum_tensor("PD", [128, 1024], F32))

    ind16_np = np.zeros((128, 128), dtype=np.float32)
    ind16T_np = np.zeros((16, 8 * 128), dtype=np.float32)
    for c in range(8):
        for p in range(128):
            ind16_np[p, c * 16 + 2 * c + (p // 64)] = 1.0
            ind16T_np[2 * c + (p // 64), c * 128 + p] = 1.0
    identC = nc.inline_tensor(np.eye(128, dtype=np.float32), name="identC").ap()
    ind16C = nc.inline_tensor(ind16_np, name="ind16C").ap()
    ind16TC = nc.inline_tensor(ind16T_np, name="ind16TC").ap()
    ones128cC = nc.inline_tensor(np.ones((128, 1), np.float32), name="ones128cC").ap()
    ones1rC = nc.inline_tensor(np.ones((1, 512), np.float32), name="ones1rC").ap()
    iota8C = nc.inline_tensor(np.arange(8, dtype=np.float32).reshape(1, 8), name="iota8C").ap()

    blk = st.enter_context(nc.Block())
    T = Eng(nc, st, "tensor")
    V = Eng(nc, st, "vector")
    C = Eng(nc, st, "scalar")
    G = Eng(nc, st, "gpsimd")
    Y = Eng(nc, st, "sync")

    g_const = DmaGrp(nc, st, "g_const")
    g_cstG = DmaGrp(nc, st, "g_cstG")
    g_x = DmaGrp(nc, st, "g_x")
    g_xfm = [DmaGrp(nc, st, "g_xfmA"), DmaGrp(nc, st, "g_xfmB")]
    g_xs = DmaGrp(nc, st, "g_xs")
    g_wq = DmaGrp(nc, st, "g_wq")
    g_wkv = DmaGrp(nc, st, "g_wkv")
    g_bias = DmaGrp(nc, st, "g_bias")
    g_rows2 = DmaGrp(nc, st, "g_rows2")
    g_kv1 = DmaGrp(nc, st, "g_kv1")
    g_wfsA = DmaGrp(nc, st, "g_wfsA")
    g_wfsB = DmaGrp(nc, st, "g_wfsB")
    g_wroA = DmaGrp(nc, st, "g_wroA")
    g_wroB = DmaGrp(nc, st, "g_wroB")
    g_lg = DmaGrp(nc, st, "g_lg")
    g_moeb = DmaGrp(nc, st, "g_moeb")
    g_wsel = [DmaGrp(nc, st, "g_wsel%d" % i) for i in range(8)]
    g_out = [DmaGrp(nc, st, "g_out%d" % i) for i in range(4)]
    g_dbg = DmaGrp(nc, st, "g_dbg")
    cc_sem = st.enter_context(nc.semaphore("cc_sem"))
    reg0 = st.enter_context(nc.gpsimd.register("eidx0"))
    reg1 = st.enter_context(nc.gpsimd.register("eidx1"))
    yreg0 = st.enter_context(nc.sync.register("yidx0"))
    yreg1 = st.enter_context(nc.sync.register("yidx1"))

    stopped = [False]

    def sec(eng, fn):
        if stopped[0]:
            return
        getattr(blk, eng.name)(fn)

    def CHK(name):
        if stop_after == name:
            stopped[0] = True

    from concourse.hw_specs import get_activation_tables
    _tabs = list(get_activation_tables(nc.m.arch).items())
    SET_LNEXP = next(i for i, (n, fs_) in enumerate(_tabs)
                     if AF.Exp in fs_ and AF.Ln in fs_)
    SET_GELU = next(i for i, (n, fs_) in enumerate(_tabs) if AF.Gelu in fs_)

    def load_act_set(cc, idx):
        cc.add_instruction(mybir.InstLoadActFuncSet(
            name=nc.get_next_instruction_name(), act_func_set_id=idx,
            ins=[], outs=[]))

    def b1v(c, lo, n):
        return B1[:, c * TC + lo: c * TC + lo + n]

    def b2v(c, lo, n):
        return B2[:, c * TC + lo: c * TC + lo + n]

    def rsv(c):
        return RS[:, c * TSUB:(c + 1) * TSUB]

    def rtv(c):
        return RT[:, c * TSUB:(c + 1) * TSUB]

    def xsv(c):
        return XS[:, c * TSUB:(c + 1) * TSUB]

    def x2sv(c):
        return X2S[:, c * TSUB:(c + 1) * TSUB]

    rq_rows = RT[0:16, 2048:4096]
    IND16T = SBF[0:16, 0:2048].bitcast(F32)
    wro_slots = [KVB[:, 0:1024], SSB[:, 0:1024]]
    IOTA8 = RTR[0:1, 40:48]


    def c_pre(cc):
        load_act_set(cc, SET_GELU)
    sec(C, c_pre)

    tok_cbias = {}
    def v_cb(v):
        v.memset(BIA[:, 40:41], -LN8)
        v.memset(BIA[:, 41:42], EPS_IN)
        tok_cbias[0] = V.tok(v.memset(BIA[:, 42:43], EPS_OUT))
    sec(V, v_cb)

    def p0_y(y):
        g_const.start(Y, IDN[:], identC[:])
        g_const.start(Y, ONESCF[:], ones128cC[:])
        g_const.start(Y, ONESR[:], ones1rC[:])
        g_const.start(Y, IOTA8, iota8C[:])
        for c in range(NCH):
            g_const.start(Y, WRFM[:, c * NE:(c + 1) * NE], Wr_h[c * 128:(c + 1) * 128, :])
        for off, src in [(RO_S0, bk_h), (RO_S1, bv_h), (RO_G1, g1_h), (RO_G2, g2_h)]:
            g_const.start(Y, ROWS[0:1, off:off + D], src.rearrange("d -> () d"))
    sec(Y, p0_y)
    tok_constY = g_const.tok()

    def p0_g(g):
        g_cstG.start(G, IND16[:], ind16C[:])
        g_cstG.start(G, ONESC[:], ones128cC[:])
        for hh in range(2):
            g_wq.start(G, WFS[:, hh * 4096:(hh + 1) * 4096].rearrange("p (k c) -> p k c", k=4),
                       Wq_h.rearrange("(k p) c -> p k c", p=128)[:, hh * 4:(hh + 1) * 4, :])
    sec(G, p0_g)

    tok_tr = {}
    tok_ev = {}
    tok_xbf = {}
    for t in range(TCH):
        xv = RS[:, (t % 2) * 1024:(t % 2) * 1024 + 1024]
        pw = PA if t % 2 == 0 else PB
        xslot = XS[:, (t % 2) * 2048:(t % 2) * 2048 + 1024]

        def y_load(y, t=t, xv=xv):
            if t >= 2:
                Y.wait(tok_tr[t - 2])
            g_x.start(Y, xv, x_h[t * 128:(t + 1) * 128, :])
        sec(Y, y_load)

        def t_tr(tt, t=t, xv=xv, pw=pw):
            T.wait(tok_constY)
            T.wait((g_x.sem, (t + 1) * 16))
            if t >= 2:
                T.wait(tok_ev[t - 2])
            for c in range(NCH):
                ins = tt.matmul(pw[:, c * 128:(c + 1) * 128],
                                xv[:, c * 128:(c + 1) * 128], IDN[:],
                                start=True, stop=True)
            tok_tr[t] = T.tok(ins)
        sec(T, t_tr)

        def c_ev(cc, t=t, pw=pw, xslot=xslot):
            C.wait(tok_tr[t])
            if t >= 2:
                C.wait((g_xfm[t % 2].sem, (t // 2) * 16))
            t1 = C.tok(cc.copy(xslot, pw[:]))
            C.wait(t1)
            ins = cc.copy(
                B1[:].rearrange("p (c n) -> p c n", c=NCH)[:, :, t * 128:(t + 1) * 128],
                xslot.rearrange("p (c n) -> p c n", c=NCH))
            tok_ev[t] = t1
            tok_xbf[t] = C.tok(ins)
        sec(C, c_ev)

        def y_xfm(y, t=t, xslot=xslot):
            Y.wait(tok_xbf[t])
            g_xfm[t % 2].start(Y, xfm_d[:, :, t * 128:(t + 1) * 128].rearrange("c p n -> p c n"),
                               xslot.rearrange("p (c n) -> p c n", c=NCH))
        sec(Y, y_xfm)

    if "d_xfm" in dbg:
        def y_d(y):
            Y.wait(g_xfm[0].tok())
            Y.wait(g_xfm[1].tok())
            g_dbg.start(Y, dbg["d_xfm"][:], xfm_d.ap())
        sec(Y, y_d)

    CHK("ingest")

    tok_biaT = {}
    tok_biaV = {}
    specs = [(bq_h, S_BQ), (g1_h, S_G1), (be1_h, S_BE1), (g2_h, S_G2), (be2_h, S_BE2)]
    for i, (src, slotc) in enumerate(specs):
        def g_ld(g, src=src, i=i):
            if i >= 1:
                G.wait(tok_biaT[i - 1])
            g_bias.start(G, BRX[0:8, :], src.rearrange("(a b) -> a b", b=128))
        sec(G, g_ld)

        def t_tp(tt, i=i):
            T.wait(tok_constY)
            T.wait((g_bias.sem, (i + 1) * 16))
            if i >= 2:
                T.wait(tok_biaV[i - 2])
            ins = tt.matmul(PC[:, (i % 2) * 512:(i % 2) * 512 + 8], BRX[0:8, :],
                            IDN[0:8, 0:8], start=True, stop=True)
            tok_biaT[i] = T.tok(ins)
        sec(T, t_tp)

        def v_ev(v, i=i, slotc=slotc):
            V.wait(tok_biaT[i])
            t0 = V.tok(v.tensor_copy(BIA[:, slotc * 8:(slotc + 1) * 8],
                                     PC[:, (i % 2) * 512:(i % 2) * 512 + 8]))
            if i == 0:
                V.wait(t0)
                t0 = V.tok(v.tensor_scalar(BIA[:, 44:52], BIA[:, 0:8], -1.0,
                                           None, ALU.mult))
            tok_biaV[i] = t0
        sec(V, v_ev)

    def g_ldb1(g):
        G.wait(tok_biaT[4])
        g_bias.start(G, BRX[:, :], b1_h.rearrange("(a b) -> a b", b=128))
    sec(G, g_ldb1)

    def t_tpb1(tt):
        T.wait(tok_constY)
        T.wait((g_bias.sem, 6 * 16))
        T.wait(tok_biaV[4])
        ins = tt.matmul(PC[:, 0:32], BRX[0:32, :], IDN[0:32, 0:32], start=True, stop=True)
        tok_biaT[5] = T.tok(ins)
    sec(T, t_tpb1)

    def v_evb1(v):
        V.wait(tok_biaT[5])
        tok_biaV[5] = V.tok(v.tensor_copy(B1F[:, 0:32], PC[:, 0:32]))
    sec(V, v_evb1)

    tok_rep = {}
    for j in range(2):
        def t_rep(tt, j=j):
            T.wait(g_const.tok())
            T.wait(tok_biaV[5] if j == 0 else tok_rep[0])
            for h in range(2):
                ins = tt.matmul(PD[:, h * 512:(h + 1) * 512], ONESR[0:1, 0:128],
                                ROWS[0:1, j * D + h * 512: j * D + (h + 1) * 512],
                                start=True, stop=True)
            tok_rep[(j, "t")] = T.tok(ins)
        sec(T, t_rep)

        def v_rep(v, j=j):
            V.wait(tok_rep[(j, "t")])
            tok_rep[j] = V.tok(v.tensor_copy(RT[:, j * 1024:(j + 1) * 1024], PD[:]))
        sec(V, v_rep)
    tok_bias_done = tok_rep.get(1)

    def y_rows2(y):
        Y.wait(tok_rep[(1, "t")])
        g_rows2.start(Y, ROWS[0:1, RO_S0:RO_S0 + D], bo_h.rearrange("d -> () d"))
        g_rows2.start(Y, ROWS[0:1, RO_S1:RO_S1 + D], b2_h.rearrange("d -> () d"))
    sec(Y, y_rows2)

    CHK("bias")

    tok_qmm = {}
    tok_elur = {}
    tok_eluy = {}
    tok_eq = {}
    tok_qsq = {}
    tok_qss = {}
    tok_rqv = {}
    tok_rq = {}
    def emit_qmm(qi):
        half, c = qi // NCH, qi % NCH
        lo = half * HALF
        pq = (PA, PB, PD)[qi % 3]

        def t_qmm(tt, qi=qi, c=c, lo=lo, pq=pq):
            T.wait(g_wq.tok())
            T.wait(g_x.tok())
            T.wait(tok_xbf[TCH - 1])
            if qi == 2:
                T.wait(tok_bias_done)
            if qi >= 3:
                T.wait(tok_elur[qi - 3])
            for k in range(NCH):
                for nn in range(2):
                    ins = tt.matmul(pq[:, nn * 512:(nn + 1) * 512],
                                    WFS[:, k * 1024 + c * 128: k * 1024 + (c + 1) * 128],
                                    b1v(k, lo + nn * 512, 512),
                                    start=(k == 0), stop=(k == NCH - 1))
            tok_qmm[qi] = T.tok(ins)
        sec(T, t_qmm)

    def emit_qrest(qi):
        half, c = qi // NCH, qi % NCH
        lo = half * HALF
        pq = (PA, PB, PD)[qi % 3]

        def c_elu(cc, qi=qi, c=c, pq=pq):
            C.wait(tok_qmm[qi])
            C.wait(tok_bias_done)
            if qi >= 1:
                C.wait(tok_eq[qi - 1])
            t_, r_, d_, y_ = RS[:, 0:1024], RS[:, 1024:2048], RS[:, 2048:3072], RS[:, 3072:4096]
            t1 = C.tok(cc.activation(t_, pq[:], AF.Relu, scale=-1.0,
                                     bias=BIA[:, 44 + c:45 + c]))
            C.wait(t1)
            t2 = C.tok(cc.activation(t_, t_, AF.Tanh, scale=-0.5))
            tok_elur[qi] = C.tok(cc.activation(r_, pq[:], AF.Relu,
                                               bias=BIA[:, S_BQ * 8 + c:S_BQ * 8 + c + 1]))
            C.wait(t2)
            t3 = C.tok(cc.activation(d_, t_, AF.Copy, scale=-1.0, bias=1.0))
            C.wait(t3)
            tok_eluy[qi] = C.tok(cc.activation(y_, d_, AF.Copy, scale=-0.4854,
                                               bias=1.4571))
        sec(C, c_elu)

        def v_eq(v, qi=qi, c=c, lo=lo):
            V.wait(tok_eluy[qi])
            def stepi(ins):
                tk = V.tok(ins)
                V.wait(tk)
                return tk
            t_, r_, d_, y_ = RS[:, 0:1024], RS[:, 1024:2048], RS[:, 2048:3072], RS[:, 3072:4096]
            stepi(v.tensor_mul(d_, d_, y_))
            stepi(v.tensor_scalar(d_, d_, -1.0, 2.0, ALU.mult, ALU.add))
            stepi(v.tensor_mul(y_, y_, d_))
            stepi(v.scalar_tensor_tensor(d_, t_, 2.0, y_, ALU.mult, ALU.mult))
            ins = v.scalar_tensor_tensor(b2v(c, lo, HALF), d_, 1.0, r_,
                                         ALU.mult, ALU.add)
            tok_eq[qi] = V.tok(ins)
        sec(V, v_eq)

        def c_sq(cc, qi=qi, c=c, lo=lo):
            C.wait(tok_eq[qi])
            if qi >= 2:
                C.wait(tok_qss[qi - 2])
            sv = SBF[:, (qi % 2) * 1024:(qi % 2) * 1024 + 1024]
            tok_qsq[qi] = C.tok(cc.activation(sv, b2v(c, lo, HALF), AF.Square))
        sec(C, c_sq)

        def t_ss(tt, qi=qi, half=half, c=c):
            T.wait(tok_qsq[qi])
            T.wait(g_cstG.tok())
            if c == 0 and half == 1:
                T.wait(tok_rqv[0])
            sv = SBF[:, (qi % 2) * 1024:(qi % 2) * 1024 + 1024]
            for nn in range(2):
                ins = tt.matmul(PC[0:16, nn * 512:(nn + 1) * 512],
                                IND16[:, c * 16:(c + 1) * 16],
                                sv[:, nn * 512:(nn + 1) * 512],
                                start=(c == 0), stop=(c == NCH - 1))
            tok_qss[qi] = T.tok(ins)
        sec(T, t_ss)

        if c == NCH - 1:
            def v_rqev(v, half=half, lo=lo, qi=qi):
                V.wait(tok_qss[qi])
                tok_rqv[half] = V.tok(v.tensor_copy(rq_rows[:, lo:lo + HALF],
                                                    PC[0:16, 0:1024]))
            sec(V, v_rqev)

            def v_rq(v, half=half, lo=lo):
                V.wait(tok_rqv[half])
                def stepi(ins):
                    tk = V.tok(ins)
                    V.wait(tk)
                    return tk
                r = rq_rows[:, lo:lo + HALF]
                scrA = KNV[0:16, 0:2048].bitcast(F32)
                scrB = SBF[0:16, 2048:4096].bitcast(F32)
                stepi(v.tensor_scalar(scrB.bitcast(I32), r.bitcast(I32), 1, None,
                                      ALU.logical_shift_right))
                stepi(v.tensor_scalar(scrB.bitcast(I32), scrB.bitcast(I32), -1,
                                      0x5F3759DF, ALU.mult, ALU.add))
                stepi(v.tensor_scalar(scrA, r, -0.5, None, ALU.mult))
                for _ in range(3):
                    stepi(v.tensor_mul(r, scrB, scrB))
                    stepi(v.tensor_mul(r, r, scrA))
                    stepi(v.tensor_scalar(r, r, 1.5, None, ALU.add))
                    stepi(v.tensor_mul(scrB, scrB, r))
                tok_rq[half] = stepi(v.tensor_scalar(r, scrB, 0.125, None, ALU.mult))
            sec(V, v_rq)


    for qi in range(2 * NCH):
        emit_qmm(qi)
        if qi >= 1:
            emit_qrest(qi - 1)
    emit_qrest(2 * NCH - 1)
    if "d_eq" in dbg:
        def y_d2(y):
            Y.wait(tok_eq[2 * NCH - 1])
            g_dbg.start(Y, dbg["d_eq"][:], B2[:])
        sec(Y, y_d2)
    if "d_rq" in dbg:
        def y_d2b(y):
            Y.wait(tok_rq[1])
            g_dbg.start(Y, dbg["d_rq"][:], rq_rows[:, :])
        sec(Y, y_d2b)

    CHK("p1q")

    def g_wkvload(g):
        G.wait(tok_qmm[2 * NCH - 1])
        for hh in range(2):
            g_wkv.start(G, WFS[:, 8192 + hh * 4096: 8192 + (hh + 1) * 4096].rearrange("p (k c) -> p k c", k=4),
                        Wk_h.rearrange("(k p) c -> p k c", p=128)[:, hh * 4:(hh + 1) * 4, :])
        for hh in range(2):
            g_wkv.start(G, WFS[:, hh * 4096:(hh + 1) * 4096].rearrange("p (k c) -> p k c", k=4),
                        Wv_h.rearrange("(k p) c -> p k c", p=128)[:, hh * 4:(hh + 1) * 4, :])
    sec(G, g_wkvload)

    tok_kmm = {}
    tok_kb = {}
    tok_kex = {}
    tok_keluy = {}
    tok_kel = {}
    tok_ksq = {}
    tok_red = {}
    tok_krs = {}
    tok_kn = {}
    tok_vev = {}
    tok_kvmm = {}
    tok_kvadd = {}
    KRD = POOL[:, 0:32]

    def emit_tkv(t):
        def t_kv(tt, t=t):
            T.wait(g_wkv.tok())
            if t == 0:
                T.wait(tok_rqv[1])
            if t >= 1:
                T.wait(tok_kb[t - 1])
                T.wait(tok_vev[t - 1])
            for k in range(NCH):
                xst = b1v(k, t * 128, 128)
                for nn in range(2):
                    tt.matmul(PA[:, nn * 512:(nn + 1) * 512], xst,
                              WFS[:, 8192 + k * 1024 + nn * 512: 8192 + k * 1024 + (nn + 1) * 512],
                              start=(k == 0), stop=(k == NCH - 1))
                for nn in range(2):
                    ins = tt.matmul(PB[:, nn * 512:(nn + 1) * 512], xst,
                                    WFS[:, k * 1024 + nn * 512: k * 1024 + (nn + 1) * 512],
                                    start=(k == 0), stop=(k == NCH - 1))
            tok_kmm[t] = T.tok(ins)
        sec(T, t_kv)

    KVSB = X2S[:, 0:1024]

    def emit_kvacc(t):
        def t_kvacc(tt, t=t):
            T.wait(tok_kn[t])
            T.wait(tok_vev[t])
            if t >= 2:
                T.wait(tok_kvadd[t - 2])
            for p in range(8):
                ins = tt.matmul(PC[:, p * 128:(p + 1) * 128],
                                KNV[:, p * 128:(p + 1) * 128],
                                KNV[:, 1024 + p * 128:1024 + (p + 1) * 128],
                                start=True, stop=True)
            tok_kvmm[t] = T.tok(ins)
        sec(T, t_kvacc)

        def v_kvadd(v, t=t):
            V.wait(tok_kvmm[t])
            if t == 0:
                ins = v.tensor_copy(KVSB, PC[:])
            else:
                ins = v.tensor_add(KVSB, KVSB, PC[:])
            tok_kvadd[t] = V.tok(ins)
        sec(V, v_kvadd)

    for t in range(TCH):
        emit_tkv(t)
        if t >= 1:
            emit_kvacc(t - 1)

        def v_kb(v, t=t):
            V.wait(tok_kmm[t])
            V.wait(tok_bias_done)
            if t >= 1:
                V.wait(tok_kex[t - 1])
            ins = v.scalar_tensor_tensor(RS[:, 0:1024], PA[:], 1.0,
                                         RT[:, 0:1024], ALU.mult, ALU.add)
            tok_kb[t] = V.tok(ins)
        sec(V, v_kb)

        def v_vev(v, t=t):
            if t >= 1:
                V.wait(tok_kvmm[t - 1])
            ins = v.scalar_tensor_tensor(KNV[:, 1024:2048], PB[:], 1.0,
                                         RT[:, 1024:2048], ALU.mult, ALU.add)
            tok_vev[t] = V.tok(ins)
        sec(V, v_vev)

        def c_kelu(cc, t=t):
            C.wait(tok_kb[t])
            t_, r_ = RS[:, 1024:2048], RS[:, 2048:3072]
            d_, y_ = RS[:, 3072:4096], RS[:, 0:1024]
            t1 = C.tok(cc.activation(t_, RS[:, 0:1024], AF.Relu, scale=-1.0))
            C.wait(t1)
            t2 = C.tok(cc.activation(t_, t_, AF.Tanh, scale=-0.5))
            tok_kex[t] = C.tok(cc.activation(r_, RS[:, 0:1024], AF.Relu))
            C.wait(t2)
            C.wait(tok_kex[t])
            t3 = C.tok(cc.activation(d_, t_, AF.Copy, scale=-1.0, bias=1.0))
            C.wait(t3)
            tok_keluy[t] = C.tok(cc.activation(y_, d_, AF.Copy, scale=-0.4854,
                                               bias=1.4571))
        sec(C, c_kelu)

        def v_ek(v, t=t):
            V.wait(tok_keluy[t])
            if t >= 1:
                V.wait(tok_kvmm[t - 1])
            def stepi(ins):
                tk = V.tok(ins)
                V.wait(tk)
                return tk
            t_, r_ = RS[:, 1024:2048], RS[:, 2048:3072]
            d_, y_ = RS[:, 3072:4096], RS[:, 0:1024]
            stepi(v.tensor_mul(d_, d_, y_))
            stepi(v.tensor_scalar(d_, d_, -1.0, 2.0, ALU.mult, ALU.add))
            stepi(v.tensor_mul(y_, y_, d_))
            stepi(v.scalar_tensor_tensor(d_, t_, 2.0, y_, ALU.mult, ALU.mult))
            ins = v.scalar_tensor_tensor(KNV[:, 0:1024], d_, 1.0, r_,
                                         ALU.mult, ALU.add)
            tok_kel[t] = V.tok(ins)
        sec(V, v_ek)

        def c_ksq(cc, t=t):
            C.wait(tok_kel[t])
            if t >= 1:
                C.wait(tok_red[t - 1])
            tok_ksq[t] = C.tok(cc.activation(SBF[:, 2048:3072], KNV[:, 0:1024], AF.Square))
        sec(C, c_ksq)

        def v_red(v, t=t):
            V.wait(tok_ksq[t])
            ins = v.reduce_sum(KRD[:, 0:16],
                               SBF[:, 2048:3072].rearrange("p (h d) -> p h d", h=16),
                               axis=AX.X)
            tok_red[t] = V.tok(ins)
        sec(V, v_red)

        def v_krs(v, t=t):
            V.wait(tok_red[t])
            def stepi(ins):
                tk = V.tok(ins)
                V.wait(tk)
                return tk
            r = KRD[:, 16:32]
            sA = BIA[:, 52:68]
            sB = BIA[:, 68:84]
            stepi(v.tensor_scalar(r.bitcast(I32), KRD[:, 0:16].bitcast(I32), 1,
                                  None, ALU.logical_shift_right))
            stepi(v.tensor_scalar(r.bitcast(I32), r.bitcast(I32), -1, 0x5F3759DF,
                                  ALU.mult, ALU.add))
            stepi(v.tensor_scalar(sA, KRD[:, 0:16], -0.5, None, ALU.mult))
            last = None
            for _ in range(2):
                stepi(v.tensor_mul(sB, r, r))
                stepi(v.tensor_mul(sB, sB, sA))
                stepi(v.tensor_scalar(sB, sB, 1.5, None, ALU.add))
                last = stepi(v.tensor_mul(r, r, sB))
            tok_krs[t] = last
        sec(V, v_krs)

        def v_kn(v, t=t):
            V.wait(tok_krs[t])
            ins = v.tensor_mul(KNV[:, 0:1024].rearrange("p (h d) -> p h d", h=16),
                               KNV[:, 0:1024].rearrange("p (h d) -> p h d", h=16),
                               KRD[:, 16:32].rearrange("p h -> p h ()").broadcast_to([128, 16, 64]))
            tok_kn[t] = V.tok(ins)
        sec(V, v_kn)
    emit_kvacc(TCH - 1)

    if "d_kn" in dbg:
        def y_dkn(y):
            Y.wait(tok_kn[TCH - 1])
            Y.wait(tok_vev[TCH - 1])
            g_dbg.start(Y, dbg["d_kn"][:], KNV[:])
        sec(Y, y_dkn)

    CHK("p1kv")

    tok_kvcp = {}
    def v_kvz(v):
        V.wait(tok_kvadd[TCH - 1])
        V.wait(tok_kex[TCH - 1])
        t1 = V.tok(v.memset(RT[:, 0:1024], 0.0))
        V.wait(t1)
        last = None
        for p in range(8):
            for hh in range(2):
                sl = slice(hh * 64, (hh + 1) * 64)
                last = v.tensor_copy(RT[sl, p * 128 + hh * 64: p * 128 + (hh + 1) * 64],
                                     KVSB[sl, p * 128 + hh * 64: p * 128 + (hh + 1) * 64])
        tok_kvcp[0] = V.tok(last)
    sec(V, v_kvz)

    def g_ar1(g):
        G.wait(tok_kvcp[0])
        g_kv1.start(G, cc1i.ap(), RT[:, 0:1024])
        g.wait_ge(g_kv1.sem, g_kv1.count)
        g.collective_compute("AllReduce", ALU.add,
                             replica_groups=[list(range(NCORES))],
                             ins=[cc1i.ap()], outs=[cc1o.ap()]).then_inc(cc_sem, 1)
        g.wait_ge(cc_sem, 1)
        g_kv1.start(G, RT[:, 0:1024], cc1o.ap())
    sec(G, g_ar1)

    tok_kvbf = {}
    def v_kvbf(v):
        V.wait(g_kv1.tok())
        tok_kvbf[0] = V.tok(v.tensor_copy(KVB[:], RT[:, 0:1024]))
    sec(V, v_kvbf)

    if "d_kv" in dbg:
        def y_d3(y):
            Y.wait(tok_kvbf[0])
            g_dbg.start(Y, dbg["d_kv"][:], RT[:, 0:1024])
        sec(Y, y_d3)

    CHK("ar1")

    def g_ind16t(g):
        G.wait(tok_red[TCH - 1])
        G.wait(tok_qss[2 * NCH - 1])
        g_cstG.start(G, IND16T[:], ind16TC[:])
    sec(G, g_ind16t)
    tok_ind16t = g_cstG.tok()

    tok_ctxv = {}
    prev_ctx = {0: None}
    for c in range(NCH):
        for half in range(2):
            lo = half * HALF

            def t_ctx(tt, c=c, half=half, lo=lo):
                T.wait(tok_rq[1])
                T.wait(tok_kvbf[0])
                T.wait(tok_ind16t)
                if prev_ctx[0] is not None:
                    T.wait(prev_ctx[0])
                for nn in range(2):
                    tt.matmul(PD[:, nn * 512:(nn + 1) * 512],
                              IND16T[:, c * 128:(c + 1) * 128],
                              rq_rows[:, lo + nn * 512: lo + (nn + 1) * 512],
                              start=True, stop=True)
                for nn in range(2):
                    ins = tt.matmul(PA[:, nn * 512:(nn + 1) * 512],
                                    KVB[:, c * 128:(c + 1) * 128],
                                    b2v(c, lo + nn * 512, 512),
                                    start=True, stop=True)
                tok_ctxv[(c, half, "t")] = T.tok(ins)
            sec(T, t_ctx)

            def v_ctx(v, c=c, half=half, lo=lo):
                V.wait(tok_ctxv[(c, half, "t")])
                t1 = V.tok(v.tensor_copy(SSB[:], PD[:]))
                V.wait(t1)
                ins = v.tensor_mul(b1v(c, lo, HALF), PA[:], SSB[:])
                tok_ctxv[(c, half)] = V.tok(ins)
                prev_ctx[0] = tok_ctxv[(c, half)]
            sec(V, v_ctx)
    tok_ctx_done = prev_ctx[0]

    if "d_ctx" in dbg:
        def y_d4(y):
            Y.wait(tok_ctx_done)
            g_dbg.start(Y, dbg["d_ctx"][:], B1[:])
        sec(Y, y_d4)

    CHK("ctx")

    state = {
        "gelu_loaded": False,
        "lnlast": None,
        "statfree": tok_ctx_done,
        "accfree": None,
        "wslot": {},
        "hpw": {0: None, 1: None},
        "hread": {},
        "woslot": {0: None, 1: None},
    }

    def ln_block(src_region, dst_region, g_slot, be_slot, g_row_off, eps_col,
                 pre_tok, out_cast=None, pool_sub=None):
        if stopped[0]:
            return {NCH - 1: None, 0: None}, {}, {}
        tok_st = {}
        tok_zc = {}
        zslot_war = {0: None, 1: None}
        for c in range(NCH):
            def c_stats(cc, c=c):
                C.wait(pre_tok(c))
                if zslot_war[c % 2] is not None:
                    C.wait(zslot_war[c % 2])
                sl = (c % 2) * 1024
                cc.copy(SBF[:, sl:sl + 512], src_region(c))
                ins = cc.activation(SBF[:, sl + 512:sl + 1024], src_region(c), AF.Square)
                tok_zc[c] = C.tok(ins)
            sec(C, c_stats)

            def t_stats(tt, c=c):
                T.wait(tok_zc[c])
                T.wait(g_cstG.tok())
                if c == 0 and state["statfree"] is not None:
                    T.wait(state["statfree"])
                sl = (c % 2) * 1024
                tt.matmul(PC[0:1, 0:512], ONESC[:], SBF[:, sl:sl + 512],
                          start=(c == 0), stop=(c == NCH - 1))
                ins = tt.matmul(PC[0:1, 512:1024], ONESC[:], SBF[:, sl + 512:sl + 1024],
                                start=(c == 0), stop=(c == NCH - 1))
                tok_st[c] = T.tok(ins)
            sec(T, t_stats)
            zslot_war[c % 2] = tok_st[c]

        tok_var = {}
        def v_rows(v):
            V.wait(tok_st[NCH - 1])
            v.tensor_scalar(LNR[0:1, 0:TSUB], PC[0:1, 0:512], 1.0 / D, None, ALU.mult)
            t2 = V.tok(v.tensor_scalar(LNR[0:1, TSUB:2 * TSUB], PC[0:1, 512:1024], 1.0 / D, None, ALU.mult))
            V.wait(t2)
            t3 = V.tok(v.tensor_mul(LNR[0:1, 2 * TSUB:3 * TSUB], LNR[0:1, 0:TSUB], LNR[0:1, 0:TSUB]))
            V.wait(t3)
            tok_var[0] = V.tok(v.tensor_sub(LNR[0:1, TSUB:2 * TSUB], LNR[0:1, TSUB:2 * TSUB], LNR[0:1, 2 * TSUB:3 * TSUB]))
        sec(V, v_rows)
        state["statfree"] = tok_var[0]

        tok_rstd = {}
        r1 = LNR[0:1, TSUB:2 * TSUB]
        r2 = LNR[0:1, 2 * TSUB:3 * TSUB]
        r3 = LNR[0:1, 3 * TSUB:4 * TSUB]
        r4 = LNR[0:1, 4 * TSUB:5 * TSUB]
        eps_val = EPS_IN if eps_col == 41 else EPS_OUT

        def v_rstd(v):
            def stepi(ins):
                tk = V.tok(ins)
                V.wait(tk)
                return tk
            V.wait(tok_var[0])
            stepi(v.tensor_scalar(r1, r1, float(eps_val), None, ALU.add))
            stepi(v.tensor_scalar(r2, r1, -0.5, None, ALU.mult))
            stepi(v.tensor_scalar(r3.bitcast(I32), r1.bitcast(I32), 1, None,
                                  ALU.logical_shift_right))
            stepi(v.tensor_scalar(r3.bitcast(I32), r3.bitcast(I32), -1, 0x5F3759DF,
                                  ALU.mult, ALU.add))
            for _ in range(2):
                stepi(v.tensor_mul(r4, r3, r3))
                stepi(v.tensor_mul(r4, r4, r2))
                stepi(v.tensor_scalar(r4, r4, 1.5, None, ALU.add))
                stepi(v.tensor_mul(r3, r3, r4))
            tok_rstd[0] = V.tok(v.tensor_copy(r1, r3))
        sec(V, v_rstd)

        tok_mr = {}
        def v_mr(v):
            V.wait(tok_rstd[0])
            tok_mr[0] = V.tok(v.tensor_mul(LNR[0:1, 0:TSUB], LNR[0:1, 0:TSUB], LNR[0:1, TSUB:2 * TSUB]))
        sec(V, v_mr)

        tok_sb = {}
        def t_sb(tt):
            T.wait(tok_mr[0])
            if state["lnlast"] is not None:
                T.wait(state["lnlast"])
            tok_sb[0] = T.tok(tt.matmul(PD[:, 0:512], ONESR[0:1, 0:128], LNR[0:1, TSUB:2 * TSUB],
                                        start=True, stop=True))
        sec(T, t_sb)

        cast_toks = {}
        pool_toks = {}
        tok_mg = {}
        tok_app = {}
        mgb = [PD[:, 512:1024], PB[:, 0:512]]
        for c in range(NCH):
            def t_mg(tt, c=c):
                if c >= 2:
                    T.wait(tok_app[c - 2])
                if g_row_off is not None:
                    lhs = ROWS[0:1, g_row_off + c * 128: g_row_off + (c + 1) * 128]
                else:
                    lhs = ONESR[0:1, 0:128]
                tok_mg[c] = T.tok(tt.matmul(mgb[c % 2], lhs, LNR[0:1, 0:TSUB],
                                            start=True, stop=True))
            sec(T, t_mg)

            def v_app(v, c=c):
                V.wait(tok_sb[0])
                V.wait(tok_mg[c])
                if g_slot is not None:
                    t1 = V.tok(v.scalar_tensor_tensor(
                        src_region(c), src_region(c),
                        BIA[:, g_slot * 8 + c:g_slot * 8 + c + 1],
                        PD[:, 0:512], ALU.mult, ALU.mult))
                else:
                    t1 = V.tok(v.scalar_tensor_tensor(
                        src_region(c), src_region(c), 1.0,
                        PD[:, 0:512], ALU.mult, ALU.mult))
                V.wait(t1)
                if be_slot is not None:
                    ins = v.scalar_tensor_tensor(
                        dst_region(c), src_region(c),
                        BIA[:, be_slot * 8 + c:be_slot * 8 + c + 1],
                        mgb[c % 2], ALU.add, ALU.subtract)
                else:
                    ins = v.scalar_tensor_tensor(
                        dst_region(c), src_region(c), 0.0,
                        mgb[c % 2], ALU.add, ALU.subtract)
                tok_app[c] = V.tok(ins)
            sec(V, v_app)

            if out_cast is not None:
                def c_cast(cc, c=c):
                    C.wait(tok_app[c])
                    cast_toks[c] = C.tok(cc.copy(out_cast(c), dst_region(c)))
                sec(C, c_cast)
            if pool_sub is not None:
                def v_pool(v, c=c):
                    V.wait(tok_app[c])
                    ins = v.reduce_sum(POOL[:, c * 4 + pool_sub:c * 4 + pool_sub + 1],
                                       dst_region(c), axis=AX.X)
                    pool_toks[c] = V.tok(ins)
                sec(V, v_pool)
        state["lnlast"] = tok_app[NCH - 1]
        return tok_app, cast_toks, pool_toks

    def ff_block(in_wait_toks, in_cols, b1col0, seed_region, dst_region,
                 scale, scale_wait, w1_src_fn, w2_src_fn, b2_row_off, wslot_key,
                 slot_extra=None, dst_wait=None, fs_wait=None):
        ev_toks = {}
        if stopped[0]:
            return {(NFS - 1, 0): None, (NFS - 1, 1): None}
        for fs in range(NFS):
            slot = fs % 2
            w1v = (WFS if slot == 0 else B2)[:, 0:8 * FSW]
            w2v = (WFS if slot == 0 else B2)[:, 8 * FSW:16 * FSW]
            grp = g_wfsA if slot == 0 else g_wfsB

            def g_wload(g, fs=fs, slot=slot, grp=grp, w1v=w1v, w2v=w2v):
                if state["wslot"].get((wslot_key, slot)) is not None:
                    G.wait(state["wslot"][(wslot_key, slot)])
                if slot_extra:
                    for tk in slot_extra.get(slot, ()):
                        G.wait(tk)
                if fs_wait is not None:
                    G.wait(fs_wait(fs))
                w1_src = w1_src_fn()
                w2_src = w2_src_fn()
                for hh in range(2):
                    grp.start(G, w1v.rearrange("p (k c) -> p k c", k=8)[:, hh * 4:(hh + 1) * 4, :],
                              w1_src.rearrange("(k p) c -> p k c", p=128)[:, hh * 4:(hh + 1) * 4,
                                                                          fs * FSW:(fs + 1) * FSW])
                for hh in range(2):
                    grp.start(G, w2v.rearrange("p (f c) -> p f c", f=8)[:, hh * 4:(hh + 1) * 4, :],
                              w2_src.rearrange("(f p) c -> p f c", p=128)[:, fs * 8 + hh * 4: fs * 8 + (hh + 1) * 4, :])
            sec(G, g_wload)
            ld_tok = (grp.sem, grp.count)

            h_toks = {}
            ffstage = int(os.environ.get("KERN_FFSTAGE", "9"))
            for f in range(8 if ffstage >= 1 else 0):
                fg = fs * 8 + f

                def t_w1(tt, f=f, ld_tok=ld_tok, w1v=w1v):
                    T.wait(ld_tok)
                    if state["hpw"][f % 2] is not None:
                        T.wait(state["hpw"][f % 2])
                    ph = PA[:, (f % 2) * 512:(f % 2) * 512 + 512]
                    for k in range(NCH):
                        if in_wait_toks:
                            T.wait(in_wait_toks[k])
                        ins = tt.matmul(ph, w1v[:, k * FSW + f * 128: k * FSW + (f + 1) * 128],
                                        in_cols(k), start=(k == 0), stop=(k == NCH - 1))
                    h_toks[(f, "mm")] = T.tok(ins)
                sec(T, t_w1)

                if ffstage < 2:
                    h_toks[f] = h_toks[(f, "mm")]
                    continue

                def c_gelu(cc, f=f, fg=fg):
                    C.wait(h_toks[(f, "mm")])
                    if state["hread"].get(f) is not None:
                        C.wait(state["hread"][f])
                    ph = PA[:, (f % 2) * 512:(f % 2) * 512 + 512]
                    ins = cc.activation(SBF[:, f * 512:(f + 1) * 512], ph, AF.Gelu,
                                        bias=B1F[:, b1col0 + fg:b1col0 + fg + 1])
                    h_toks[f] = C.tok(ins)
                    state["hpw"][f % 2] = h_toks[f]
                sec(C, c_gelu)

            accv = [PC[:, 0:512], PC[:, 512:1024], PD[:, 0:512], PD[:, 512:1024]]
            tok_w2 = {}
            for oh in range(2 if ffstage >= 3 else 0):
                for f in range(8):
                    def t_w2(tt, f=f, oh=oh, fs=fs, w2v=w2v):
                        T.wait(h_toks[f])
                        if f == 0 and oh == 0:
                            if state["accfree"] is not None:
                                T.wait(state["accfree"])
                            if state["statfree"] is not None:
                                T.wait(state["statfree"])
                            if state["lnlast"] is not None:
                                T.wait(state["lnlast"])
                        if f == 0 and oh == 1:
                            T.wait(ev_toks[(fs, 0)])
                        last_f = (f == 7)
                        fold = (b2_row_off is not None and fs == 0
                                and not os.environ.get("KERN_NOFOLD"))
                        for ot in range(4):
                            og = oh * 4 + ot
                            ins = tt.matmul(accv[ot],
                                            w2v[:, f * 1024 + og * 128: f * 1024 + (og + 1) * 128],
                                            SBF[:, f * 512:(f + 1) * 512],
                                            start=(f == 0), stop=(last_f and not fold))
                        if last_f and fold:
                            for ot in range(4):
                                og = oh * 4 + ot
                                ins = tt.matmul(
                                    accv[ot],
                                    ROWS[0:1, b2_row_off + og * 128: b2_row_off + (og + 1) * 128],
                                    ONESR[0:1, 0:512], start=False, stop=True)
                        tok_w2[(oh, f)] = T.tok(ins)
                        state["hread"][f] = tok_w2[(oh, f)]
                    sec(T, t_w2)

                if ffstage < 4:
                    ev_toks[(fs, oh)] = tok_w2[(oh, 7)]
                    continue

                def v_acc(v, oh=oh, fs=fs):
                    V.wait(tok_w2[(oh, 7)])
                    if scale_wait is not None:
                        V.wait(scale_wait)
                    if dst_wait is not None:
                        V.wait(dst_wait)
                    last = None
                    for ot in range(4):
                        oc = oh * 4 + ot
                        if fs == 0 and (seed_region is not None):
                            last = v.scalar_tensor_tensor(dst_region(oc), accv[ot], scale,
                                                          seed_region(oc), ALU.mult, ALU.add)
                        elif os.environ.get("KERN_ACC_COPY"):
                            last = v.tensor_copy(dst_region(oc), accv[ot])
                        elif os.environ.get("KERN_ACC_SCRATCH"):
                            last = v.scalar_tensor_tensor(X2S[:, oc * TSUB:(oc + 1) * TSUB],
                                                          accv[ot], scale,
                                                          dst_region(oc), ALU.mult, ALU.add)
                        else:
                            last = v.scalar_tensor_tensor(dst_region(oc), accv[ot], scale,
                                                          dst_region(oc), ALU.mult, ALU.add)
                    ev_toks[(fs, oh)] = V.tok(last)
                sec(V, v_acc)
            state["accfree"] = ev_toks.get((fs, 1))
            state["wslot"][(wslot_key, slot)] = (tok_w2.get((1, 7)) or h_toks.get(7)
                                                 or h_toks.get((7, "mm")) or ld_tok)
        if not ev_toks:
            ev_toks = {(NFS - 1, 0): None, (NFS - 1, 1): None}
        for fs2 in range(NFS):
            for oh2 in range(2):
                ev_toks.setdefault((fs2, oh2), None)
        return ev_toks

    tok_x2cast = {}
    tok_pool_all = {}
    tok_zo_prev = {0: None}
    tok_xs = {}
    for sub in range(NSUB):
        slo = sub * TSUB

        def y_xs(y, sub=sub, slo=slo):
            Y.wait(g_xfm[0].tok())
            Y.wait(g_xfm[1].tok())
            Y.wait(tok_xbf[TCH - 1])
            if tok_zo_prev[0] is not None:
                Y.wait(tok_zo_prev[0])
            g_xs.start(Y, XS[:].rearrange("p (c n) -> p c n", c=NCH),
                       xfm_d[:, :, slo:slo + TSUB].rearrange("c p n -> p c n"))
            tok_xs[sub] = g_xs.tok()
        sec(Y, y_xs)
        CHK("xs%d" % sub)

        tok_wo = {}
        for e in range(NCH):
            slot = e % 2
            grp = g_wroA if slot == 0 else g_wroB

            def g_wo(g, e=e, slot=slot, grp=grp):
                if state["woslot"][slot] is not None:
                    G.wait(state["woslot"][slot])
                else:
                    G.wait(tok_ctx_done)
                grp.start(G, wro_slots[slot], Wo_h[e * 128:(e + 1) * 128, :])
            sec(G, g_wo)
            ld_tok = (grp.sem, grp.count)

            def t_wo(tt, e=e, slot=slot, ld_tok=ld_tok, slo=slo):
                T.wait(ld_tok)
                if e == 0:
                    if state["accfree"] is not None:
                        T.wait(state["accfree"])
                    if state["statfree"] is not None:
                        T.wait(state["statfree"])
                    if state["lnlast"] is not None:
                        T.wait(state["lnlast"])
                    T.wait(tok_ctx_done)
                for o in range(NCH):
                    pbank = [PA, PB, PC, PD][o // 2][:, (o % 2) * 512:(o % 2) * 512 + 512]
                    ins = tt.matmul(pbank, wro_slots[slot][:, o * 128:(o + 1) * 128],
                                    b1v(e, slo, TSUB), start=(e == 0), stop=False)
                tok_wo[e] = T.tok(ins)
                state["woslot"][slot] = tok_wo[e]
            sec(T, t_wo)

        def t_wof(tt, sub=sub):
            T.wait(g_rows2.tok())
            for o in range(NCH):
                pbank = [PA, PB, PC, PD][o // 2][:, (o % 2) * 512:(o % 2) * 512 + 512]
                ins = tt.matmul(pbank, ROWS[0:1, RO_S0 + o * 128: RO_S0 + (o + 1) * 128],
                                ONESR[0:1, 0:512], start=False, stop=True)
            tok_wo["fold"] = T.tok(ins)
        sec(T, t_wof)
        CHK("wo%d" % sub)

        tok_z1 = {}
        def v_z1(v, sub=sub, slo=slo):
            V.wait(tok_wo["fold"])
            V.wait(tok_xs[sub])
            for o in range(NCH):
                pbank = [PA, PB, PC, PD][o // 2][:, (o % 2) * 512:(o % 2) * 512 + 512]
                ins = v.scalar_tensor_tensor(rsv(o), pbank, 1.0,
                                             xsv(o), ALU.mult, ALU.add)
                tok_z1[o] = V.tok(ins)
        sec(V, v_z1)
        CHK("z1%d" % sub)
        state["accfree"] = tok_z1.get(NCH - 1)
        state["statfree"] = tok_z1.get(NCH - 1)

        if "d_z1" in dbg and sub == 0:
            def y_dz1(y):
                Y.wait(tok_z1[NCH - 1])
                g_dbg.start(Y, dbg["d_z1"][:], RS[:])
            sec(Y, y_dz1)

        app1, cast1, _ = ln_block(
            rsv, rsv, S_G1, S_BE1, RO_G1, 41,
            pre_tok=lambda c: tok_z1.get(c),
            out_cast=lambda c, slo=slo: b1v(c, slo, TSUB))
        CHK("ln1%d" % sub)

        if "d_attn" in dbg and sub == 0:
            def y_dat(y):
                Y.wait(app1[NCH - 1])
                g_dbg.start(Y, dbg["d_attn"][:], RS[:])
            sec(Y, y_dat)

        ff1 = ff_block(cast1, lambda k, slo=slo: b1v(k, slo, TSUB), 0,
                       rsv, rtv, 1.0, None, (lambda: W1_h), (lambda: W2_h), RO_S1, "ff",
                       slot_extra=({0: [tok_kmm.get(TCH - 1)],
                                    1: [tok_ctx_done]} if sub == 0 else None))
        CHK("ff%d" % sub)

        app2, _, _ = ln_block(
            rtv, rtv, S_G2, S_BE2, RO_G2, 41,
            pre_tok=lambda c: ff1.get((NFS - 1, 1)))
        CHK("ln2%d" % sub)

        tok_zo = {}
        def v_zo(v, sub=sub, slo=slo):
            for c in range(NCH):
                V.wait(app2[c])
                ins = v.tensor_add(rtv(c), rtv(c), xsv(c))
                tok_zo[c] = V.tok(ins)
        sec(V, v_zo)
        tok_zo_prev[0] = tok_zo.get(NCH - 1)

        app3, cast3, pools = ln_block(
            rtv, rtv, None, None, None, 42,
            pre_tok=lambda c: tok_zo[c],
            out_cast=lambda c, slo=slo: b1v(c, slo, TSUB),
            pool_sub=sub)
        for c in range(NCH):
            tok_x2cast[(sub, c)] = cast3.get(c)
            tok_pool_all[(sub, c)] = pools.get(c)
        CHK("p2s%d" % sub)

    if "d_x2b" in dbg:
        def y_dx2(y):
            for c in range(NCH):
                Y.wait(tok_x2cast[(NSUB - 1, c)])
            g_dbg.start(Y, dbg["d_x2b"][:], B1[:])
        sec(Y, y_dx2)

    CHK("p2")

    tok_lgT = {}
    def t_lg(tt):
        for sub in range(NSUB):
            for c in range(NCH):
                T.wait(tok_pool_all[(sub, c)])
        T.wait(state["statfree"])
        n = 0
        for c in range(NCH):
            for sub in range(NSUB):
                ins = tt.matmul(PB[0:1, 0:8], POOL[:, c * 4 + sub:c * 4 + sub + 1],
                                WRFM[:, c * NE:(c + 1) * NE],
                                start=(n == 0), stop=(n == 31))
                n += 1
        tok_lgT[0] = T.tok(ins)
    sec(T, t_lg)

    tok_rtr = {}
    def v_lg(v):
        V.wait(tok_lgT[0])
        tok_rtr[0] = V.tok(v.tensor_scalar(RTR[0:1, 0:8], PB[0:1, 0:8],
                                           1.0 / S, None, ALU.mult))
    sec(V, v_lg)

    def g_ar2(g):
        G.wait(tok_rtr[0])
        g_lg.start(G, cc2i.ap(), RTR[0:1, 0:8])
        g.wait_ge(g_lg.sem, g_lg.count)
        g.collective_compute("AllReduce", ALU.add,
                             replica_groups=[list(range(NCORES))],
                             ins=[cc2i.ap()], outs=[cc2o.ap()]).then_inc(cc_sem, 1)
        g.wait_ge(cc_sem, 2)
        g_lg.start(G, RTR[0:1, 0:8], cc2o.ap())
    sec(G, g_ar2)

    tok_rt2 = {}
    def v_rt(v):
        V.wait(g_lg.tok())
        V.wait(g_const.tok())
        def step(ins):
            tk = V.tok(ins)
            V.wait(tk)
            return tk
        step(v.reduce_max(RTR[0:1, 32:33], RTR[0:1, 0:8], axis=AX.X))
        step(v.tensor_tensor(RTR[0:1, 8:16], RTR[0:1, 0:8],
                             RTR[0:1, 32:33].broadcast_to([1, 8]), ALU.is_ge))
        step(v.tensor_scalar(RTR[0:1, 16:24], RTR[0:1, 8:16], -1e30, None, ALU.mult))
        step(v.tensor_add(RTR[0:1, 16:24], RTR[0:1, 16:24], RTR[0:1, 0:8]))
        step(v.reduce_max(RTR[0:1, 33:34], RTR[0:1, 16:24], axis=AX.X))
        step(v.tensor_tensor(RTR[0:1, 16:24], RTR[0:1, 16:24],
                             RTR[0:1, 33:34].broadcast_to([1, 8]), ALU.is_ge))
        step(v.tensor_mul(RTR[0:1, 24:32], RTR[0:1, 8:16], IOTA8))
        step(v.reduce_sum(RTR[0:1, 38:39], RTR[0:1, 24:32], axis=AX.X))
        step(v.tensor_mul(RTR[0:1, 24:32], RTR[0:1, 16:24], IOTA8))
        step(v.reduce_sum(RTR[0:1, 39:40], RTR[0:1, 24:32], axis=AX.X))
        ins = v.tensor_sub(RTR[0:1, 34:35], RTR[0:1, 33:34], RTR[0:1, 32:33])
        tok_rt2[0] = V.tok(ins)
    sec(V, v_rt)

    def v_rtexp(v):
        V.wait(tok_rt2[0])
        u = RTR[0:1, 40:41]
        p = RTR[0:1, 35:36]
        def stepi(ins):
            tk = V.tok(ins)
            V.wait(tk)
            return tk
        stepi(v.tensor_scalar(u, RTR[0:1, 34:35], -24.0, None, ALU.max))
        stepi(v.tensor_scalar(u, u, 1.0 / 64.0, None, ALU.mult))
        stepi(v.tensor_scalar(p, u, 0.25, 1.0, ALU.mult, ALU.add))
        stepi(v.tensor_mul(p, p, u))
        stepi(v.tensor_scalar(p, p, 1.0 / 3.0, 1.0, ALU.mult, ALU.add))
        stepi(v.tensor_mul(p, p, u))
        stepi(v.tensor_scalar(p, p, 0.5, 1.0, ALU.mult, ALU.add))
        stepi(v.tensor_mul(p, p, u))
        stepi(v.tensor_scalar(p, p, 1.0, 1.0, ALU.mult, ALU.add))
        for _ in range(5):
            stepi(v.tensor_mul(p, p, p))
        tok_rt2[1] = V.tok(v.tensor_mul(p, p, p))
    sec(V, v_rtexp)

    def v_rt2(v):
        V.wait(tok_rt2[1])
        def step(ins):
            tk = V.tok(ins)
            V.wait(tk)
            return tk
        step(v.tensor_scalar(RTR[0:1, 36:37], RTR[0:1, 35:36], 1.0, None, ALU.add))
        step(v.reciprocal(RTR[0:1, 36:37], RTR[0:1, 36:37]))
        ins = v.tensor_scalar(RTR[0:1, 37:38], RTR[0:1, 36:37], -1.0, 1.0,
                              ALU.mult, ALU.add)
        tok_rt2[2] = V.tok(ins)
    sec(V, v_rt2)

    if "d_lg" in dbg:
        def y_dlg(y):
            Y.wait(tok_rt2[2])
            g_dbg.start(Y, dbg["d_lg"][:], RTR[0:1, :])
        sec(Y, y_dlg)

    tok_web = {}
    def t_web(tt):
        T.wait(tok_rt2[2])
        tt.matmul(PB[:, 0:1], ONESR[0:1, 0:128], RTR[0:1, 36:37], start=True, stop=True)
        ins = tt.matmul(PB[:, 512:513], ONESR[0:1, 0:128], RTR[0:1, 37:38],
                        start=True, stop=True)
        tok_web[0] = T.tok(ins)
    sec(T, t_web)

    def v_web(v):
        V.wait(tok_web[0])
        v.tensor_copy(WEB[:, 0:1], PB[:, 0:1])
        tok_web[1] = V.tok(v.tensor_copy(WEB[:, 1:2], PB[:, 512:513]))
    sec(V, v_web)

    tok_rti = {}
    def c_rti(cc):
        C.wait(tok_rt2[2])
        tok_rti[0] = C.tok(cc.copy(RTI[0:1, 0:2], RTR[0:1, 38:40]))
    sec(C, c_rti)

    CHK("router")

    iv = [None, None]
    def g_snap(g):
        G.wait(tok_rti[0])
        g.reg_load(reg0, RTI[0:1, 0:1])
        g.reg_load(reg1, RTI[0:1, 1:2])
        iv[0] = g.snap(reg0, min_val=0, max_val=NE - 1)
        iv[1] = g.snap(reg1, min_val=0, max_val=NE - 1)
    sec(G, g_snap)

    def y_wsel(y):
        Y.wait(tok_rti[0])
        y.reg_load(yreg0, RTI[0:1, 0:1])
        y.reg_load(yreg1, RTI[0:1, 1:2])
        jv0 = y.snap(yreg0, min_val=0, max_val=NE - 1)
        jv1 = y.snap(yreg1, min_val=0, max_val=NE - 1)
        for es, jv in ((0, jv0), (1, jv1)):
            for fs in range(NFS):
                grp = g_wsel[es * NFS + fs]
                grp.start(Y, W1sel[es].ap()[:, fs * FSW:(fs + 1) * FSW],
                          W1m_h[jv][:, fs * FSW:(fs + 1) * FSW])
                grp.start(Y, W2sel[es].ap()[fs * FSW:(fs + 1) * FSW, :],
                          W2m_h[jv][fs * FSW:(fs + 1) * FSW, :])
    sec(Y, y_wsel)

    tok_moeb = {}
    for es in range(2):
        def g_mb(g, es=es):
            if es == 0:
                G.wait(state["woslot"][0])
                G.wait(state["woslot"][1])
                G.wait(state["wslot"][("ff", 0)])
                G.wait(state["wslot"][("ff", 1)])
            else:
                G.wait(tok_moeb[(0, "v")])
            roff = RO_S0 if es == 0 else RO_S1
            g_moeb.start(G, ROWS[0:1, roff:roff + D],
                         b2m_h[iv[es]].rearrange("d -> () d"))
            g_moeb.start(G, BRX[:, :], b1m_h[iv[es]].rearrange("(a b) -> a b", b=128))
        sec(G, g_mb)
        ld = (g_moeb.sem, (es + 1) * 32)

        def t_mb(tt, es=es, ld=ld):
            T.wait(ld)
            T.wait(tok_web[1])
            if state["statfree"] is not None:
                T.wait(state["statfree"])
            ins = tt.matmul(PB[:, 0:32] if es == 0 else PB[:, 512:544],
                            BRX[0:32, :], IDN[0:32, 0:32], start=True, stop=True)
            tok_moeb[(es, "t")] = T.tok(ins)
        sec(T, t_mb)

        def v_mb(v, es=es):
            V.wait(tok_moeb[(es, "t")])
            ins = v.tensor_copy(B1F[:, 32 + es * 32:64 + es * 32],
                                PB[:, 0:32] if es == 0 else PB[:, 512:544])
            tok_moeb[(es, "v")] = V.tok(ins)
        sec(V, v_mb)

    CHK("moeb")

    tok_otr = {}
    tok_oev = {}
    tok_upc = {}
    for sub in range(NSUB):
        slo = sub * TSUB

        def c_upc(cc, sub=sub, slo=slo):
            for c in range(NCH):
                C.wait(tok_x2cast[(sub, c)])
            if (sub - 1, "lnfree") in tok_upc:
                C.wait(tok_upc[(sub - 1, "lnfree")])
            ins = cc.copy(X2S[:].rearrange("p (c n) -> p c n", c=NCH),
                          B1[:].rearrange("p (c n) -> p c n", c=NCH)[:, :, slo:slo + TSUB])
            tok_upc[sub] = C.tok(ins)
        sec(C, c_upc)
        upc = tok_upc.get(sub)

        moe_last = None
        for es in range(2):
            in_toks = {c: tok_x2cast.get((sub, c)) for c in range(NCH)}
            ev = ff_block(in_toks, lambda k, slo=slo: b1v(k, slo, TSUB),
                          32 + es * 32,
                          None, x2sv,
                          WEB[:, es:es + 1], tok_web.get(1),
                          (lambda es=es: W1sel[es].ap()), (lambda es=es: W2sel[es].ap()),
                          (RO_S0 if es == 0 else RO_S1), "ff",
                          slot_extra={0: [tok_moeb.get((es, "v"))],
                                      1: [tok_moeb.get((es, "v"))]},
                          dst_wait=upc,
                          fs_wait=(lambda fs, es=es: (g_wsel[es * NFS + fs].sem, 32)))
            moe_last = ev

        appF, _, _ = ln_block(
            x2sv, rsv, None, None, None, 42,
            pre_tok=lambda c, ev=moe_last: ev[(NFS - 1, 1)])
        tok_upc[(sub, "lnfree")] = appF.get(NCH - 1)

        for tcq in range(TSUB // 128):
            tglob = sub * 4 + tcq
            pw = PA if tcq % 2 == 0 else PB

            def t_otr(tt, sub=sub, tcq=tcq, pw=pw, appF=appF):
                for c in range(NCH):
                    T.wait(appF[c])
                if (sub, tcq - 2) in tok_oev:
                    T.wait(tok_oev[(sub, tcq - 2)])
                if tcq < 2 and (sub - 1, tcq + 2) in tok_oev:
                    T.wait(tok_oev[(sub - 1, tcq + 2)])
                for c in range(NCH):
                    ins = tt.matmul(pw[:, c * 128:(c + 1) * 128],
                                    RS[:, c * TSUB + tcq * 128: c * TSUB + (tcq + 1) * 128],
                                    IDN[:], start=True, stop=True)
                tok_otr[(sub, tcq)] = T.tok(ins)
            sec(T, t_otr)

            def c_oev(cc, sub=sub, tcq=tcq, pw=pw):
                C.wait(tok_otr[(sub, tcq)])
                if sub >= 1:
                    C.wait((g_out[tcq].sem, sub * 16))
                ins = cc.copy(RT[:, tcq * 1024:(tcq + 1) * 1024], pw[:])
                tok_oev[(sub, tcq)] = C.tok(ins)
            sec(C, c_oev)

            def y_ost(y, sub=sub, tcq=tcq, tglob=tglob):
                Y.wait(tok_oev[(sub, tcq)])
                g_out[tcq].start(Y, out_h[tglob * 128:(tglob + 1) * 128, :],
                                 RT[:, tcq * 1024:(tcq + 1) * 1024])
            sec(Y, y_ost)
        if (sub, 3) in tok_oev:
            state["hpw"][0] = tok_oev[(sub, 3)]
            state["hpw"][1] = tok_oev[(sub, 3)]
        CHK("p3s%d" % sub)

    def y_fin(y):
        for i in range(4):
            Y.wait(g_out[i].tok())
        if g_dbg.count:
            Y.wait(g_dbg.tok())
    sec(Y, y_fin)

    st.close()
    return nc


_NC_CACHE = {}


def kernel(**inputs):
    if "main" not in _NC_CACHE:
        _NC_CACHE["main"] = build()
    nc = _NC_CACHE["main"]
    x = np.ascontiguousarray(np.asarray(inputs["L0_attn_x"], dtype=np.float32)[0])
    base = {
        "Wq": inputs["L0_attn_Wq"], "bq": inputs["L0_attn_bq"],
        "Wk": inputs["L0_attn_Wk"], "bk": inputs["L0_attn_bk"],
        "Wv": inputs["L0_attn_Wv"], "bv": inputs["L0_attn_bv"],
        "Wo": inputs["L0_attn_Wo"], "bo": inputs["L0_attn_bo"],
        "g1": inputs["L0_attn_g1"], "be1": inputs["L0_attn_be1"],
        "W1": inputs["L0_attn_W1"], "b1": inputs["L0_attn_b1"],
        "W2": inputs["L0_attn_W2"], "b2": inputs["L0_attn_b2"],
        "g2": inputs["L0_attn_g2"], "be2": inputs["L0_attn_be2"],
        "Wr": inputs["L0_ffn_Wr"], "W1m": inputs["L0_ffn_W1"],
        "b1m": inputs["L0_ffn_b1"], "W2m": inputs["L0_ffn_W2"],
        "b2m": inputs["L0_ffn_b2"],
    }
    base = {k: np.ascontiguousarray(np.asarray(v, dtype=np.float32)) for k, v in base.items()}
    in_maps = []
    for i in range(NCORES):
        m = dict(base)
        m["x"] = np.ascontiguousarray(x[i * TC:(i + 1) * TC, :])
        in_maps.append(m)
    res = run_bass_kernel_spmd(nc, in_maps, core_ids=list(range(NCORES)))
    out = np.concatenate([np.asarray(r["out"]) for r in res.results], axis=0)
    return out.reshape(1, S, D)


# version 15
# speedup vs baseline: 1.0161x; 1.0120x over previous
import contextlib
import math
import os

import numpy as np

import concourse.bass as bass
import concourse.mybir as mybir
from concourse.bass_utils import run_bass_kernel_spmd

F32 = mybir.dt.float32
BF16 = mybir.dt.bfloat16
I32 = mybir.dt.int32
AF = mybir.ActivationFunctionType
ALU = mybir.AluOpType
AX = mybir.AxisListType

D = 1024
NH = 16
FF = 4096
NE = 8
S = 16384
NCORES = 8
TC = S // NCORES
NCH = D // 128
TCH = TC // 128
HALF = TC // 2
NSUB = 4
TSUB = TC // NSUB
NFS = 4
FSW = FF // NFS
EPS_IN = 1e-12
EPS_OUT = 1e-5
LN8 = math.log(8.0)

RO_S0, RO_S1, RO_G1, RO_G2 = 0, D, 2 * D, 3 * D
S_BQ, S_G1, S_BE1, S_G2, S_BE2 = range(5)


class Eng:
    def __init__(self, nc, stack, name):
        self.e = getattr(nc, name)
        self.name = name
        self.sem = stack.enter_context(nc.semaphore("sf_" + name))
        self.n = 0
        self.seen = {}

    def tok(self, ins):
        self.n += 1
        ins.then_inc(self.sem, 1)
        return (self.sem, self.n)

    def wait(self, *toks):
        for t in toks:
            if t is None:
                continue
            sem, val = t
            k = id(sem)
            if val > self.seen.get(k, 0):
                self.e.wait_ge(sem, val)
                self.seen[k] = val


class DmaGrp:
    def __init__(self, nc, stack, name):
        self.sem = stack.enter_context(nc.semaphore(name))
        self.count = 0

    def start(self, eng, out, in_, **kw):
        ins = eng.e.dma_start(out, in_, **kw)
        self.count += 16
        ins.then_inc(self.sem, 16)
        return ins

    def tok(self):
        return (self.sem, self.count)


class _StopBuild(Exception):
    pass


def build(debug=(), stop_after=None):
    nc = bass.Bass("TRN2", target_bir_lowering=False)

    x_h = nc.dram_tensor("x", [TC, D], F32, kind="ExternalInput").ap()
    Wq_h = nc.dram_tensor("Wq", [D, D], F32, kind="ExternalInput").ap()
    bq_h = nc.dram_tensor("bq", [D], F32, kind="ExternalInput").ap()
    Wk_h = nc.dram_tensor("Wk", [D, D], F32, kind="ExternalInput").ap()
    bk_h = nc.dram_tensor("bk", [D], F32, kind="ExternalInput").ap()
    Wv_h = nc.dram_tensor("Wv", [D, D], F32, kind="ExternalInput").ap()
    bv_h = nc.dram_tensor("bv", [D], F32, kind="ExternalInput").ap()
    Wo_h = nc.dram_tensor("Wo", [D, D], F32, kind="ExternalInput").ap()
    bo_h = nc.dram_tensor("bo", [D], F32, kind="ExternalInput").ap()
    g1_h = nc.dram_tensor("g1", [D], F32, kind="ExternalInput").ap()
    be1_h = nc.dram_tensor("be1", [D], F32, kind="ExternalInput").ap()
    W1_h = nc.dram_tensor("W1", [D, FF], F32, kind="ExternalInput").ap()
    b1_h = nc.dram_tensor("b1", [FF], F32, kind="ExternalInput").ap()
    W2_h = nc.dram_tensor("W2", [FF, D], F32, kind="ExternalInput").ap()
    b2_h = nc.dram_tensor("b2", [D], F32, kind="ExternalInput").ap()
    g2_h = nc.dram_tensor("g2", [D], F32, kind="ExternalInput").ap()
    be2_h = nc.dram_tensor("be2", [D], F32, kind="ExternalInput").ap()
    Wr_h = nc.dram_tensor("Wr", [D, NE], F32, kind="ExternalInput").ap()
    W1m_h = nc.dram_tensor("W1m", [NE, D, FF], F32, kind="ExternalInput").ap()
    b1m_h = nc.dram_tensor("b1m", [NE, FF], F32, kind="ExternalInput").ap()
    W2m_h = nc.dram_tensor("W2m", [NE, FF, D], F32, kind="ExternalInput").ap()
    b2m_h = nc.dram_tensor("b2m", [NE, D], F32, kind="ExternalInput").ap()
    out_h = nc.dram_tensor("out", [TC, D], F32, kind="ExternalOutput").ap()
    xfm_d = nc.dram_tensor("xfm_d", [NCH, 128, TC], F32)
    W1sel = [nc.dram_tensor("W1sel%d" % e, [D, FF], BF16) for e in range(2)]
    W2sel = [nc.dram_tensor("W2sel%d" % e, [FF, D], BF16) for e in range(2)]
    cc1i = nc.dram_tensor("cc1i", [128, NH * 64], F32)
    cc1o = nc.dram_tensor("cc1o", [128, NH * 64], F32)
    cc2i = nc.dram_tensor("cc2i", [1, NE], F32)
    cc2o = nc.dram_tensor("cc2o", [1, NE], F32)

    dbg_specs = {
        "d_xfm": ([NCH, 128, TC], F32), "d_eq": ([128, NCH * TC], BF16),
        "d_kv": ([128, NH * 64], F32), "d_ctx": ([128, NCH * TC], BF16),
        "d_attn": ([128, NCH * TSUB], F32), "d_x2b": ([128, NCH * TC], BF16),
        "d_lg": ([1, 64], F32), "d_rq": ([16, TC], F32),
        "d_z1": ([128, NCH * TSUB], F32), "d_kn": ([128, 2048], BF16),
    }
    dbg = {}
    for name in debug:
        shp, dt = dbg_specs[name]
        dbg[name] = nc.dram_tensor(name, shp, dt, kind="ExternalOutput").ap()

    st = contextlib.ExitStack()

    RS = st.enter_context(nc.sbuf_tensor("RS", [128, NCH * TSUB], F32))
    RT = st.enter_context(nc.sbuf_tensor("RT", [128, NCH * TSUB], F32))
    XS = st.enter_context(nc.sbuf_tensor("XS", [128, NCH * TSUB], F32))
    X2S = st.enter_context(nc.sbuf_tensor("X2S", [128, NCH * TSUB], F32))
    B1 = st.enter_context(nc.sbuf_tensor("B1", [128, NCH * TC], BF16))
    B2 = st.enter_context(nc.sbuf_tensor("B2", [128, NCH * TC], BF16))
    WFS = st.enter_context(nc.sbuf_tensor("WFS", [128, NCH * TC], BF16))
    SBF = st.enter_context(nc.sbuf_tensor("SBF", [128, 4096], BF16))
    SSB = st.enter_context(nc.sbuf_tensor("SSB", [128, 1024], BF16))
    KNV = st.enter_context(nc.sbuf_tensor("KNV", [128, 2048], BF16))
    LNR = st.enter_context(nc.sbuf_tensor("LNR", [1, 5 * TSUB], F32))
    IDN = st.enter_context(nc.sbuf_tensor("IDN", [128, 128], F32))
    KVB = st.enter_context(nc.sbuf_tensor("KVB", [128, NH * 64], BF16))
    BIA = st.enter_context(nc.sbuf_tensor("BIA", [128, 84], F32))
    B1F = st.enter_context(nc.sbuf_tensor("B1F", [128, 96], F32))
    ROWS = st.enter_context(nc.sbuf_tensor("ROWS", [1, 4 * D], F32))
    RTR = st.enter_context(nc.sbuf_tensor("RTR", [1, 64], F32))
    RTI = st.enter_context(nc.sbuf_tensor("RTI", [1, 2], I32))
    POOL = st.enter_context(nc.sbuf_tensor("POOL", [128, 32], F32))
    WEB = st.enter_context(nc.sbuf_tensor("WEB", [128, 2], F32))
    BRX = st.enter_context(nc.sbuf_tensor("BRX", [32, 128], F32))
    IND16 = st.enter_context(nc.sbuf_tensor("IND16", [128, 128], BF16))
    ONESC = st.enter_context(nc.sbuf_tensor("ONESC", [128, 1], BF16))
    ONESCF = st.enter_context(nc.sbuf_tensor("ONESCF", [128, 1], F32))
    ONESR = st.enter_context(nc.sbuf_tensor("ONESR", [1, 512], F32))
    WRFM = st.enter_context(nc.sbuf_tensor("WRFM", [128, NCH * NE], F32))

    PA = st.enter_context(nc.psum_tensor("PA", [128, 1024], F32))
    PB = st.enter_context(nc.psum_tensor("PB", [128, 1024], F32))
    PC = st.enter_context(nc.psum_tensor("PC", [128, 1024], F32))
    PD = st.enter_context(nc.psum_tensor("PD", [128, 1024], F32))

    ind16_np = np.zeros((128, 128), dtype=np.float32)
    ind16T_np = np.zeros((16, 8 * 128), dtype=np.float32)
    for c in range(8):
        for p in range(128):
            ind16_np[p, c * 16 + 2 * c + (p // 64)] = 1.0
            ind16T_np[2 * c + (p // 64), c * 128 + p] = 1.0
    identC = nc.inline_tensor(np.eye(128, dtype=np.float32), name="identC").ap()
    ind16C = nc.inline_tensor(ind16_np, name="ind16C").ap()
    ind16TC = nc.inline_tensor(ind16T_np, name="ind16TC").ap()
    ones128cC = nc.inline_tensor(np.ones((128, 1), np.float32), name="ones128cC").ap()
    ones1rC = nc.inline_tensor(np.ones((1, 512), np.float32), name="ones1rC").ap()
    iota8C = nc.inline_tensor(np.arange(8, dtype=np.float32).reshape(1, 8), name="iota8C").ap()

    blk = st.enter_context(nc.Block())
    T = Eng(nc, st, "tensor")
    V = Eng(nc, st, "vector")
    C = Eng(nc, st, "scalar")
    G = Eng(nc, st, "gpsimd")
    Y = Eng(nc, st, "sync")

    g_const = DmaGrp(nc, st, "g_const")
    g_cstG = DmaGrp(nc, st, "g_cstG")
    g_x = DmaGrp(nc, st, "g_x")
    g_xfm = [DmaGrp(nc, st, "g_xfmA"), DmaGrp(nc, st, "g_xfmB")]
    g_xs = DmaGrp(nc, st, "g_xs")
    g_wq = DmaGrp(nc, st, "g_wq")
    g_wkv = DmaGrp(nc, st, "g_wkv")
    g_bias = DmaGrp(nc, st, "g_bias")
    g_rows2 = DmaGrp(nc, st, "g_rows2")
    g_kv1 = DmaGrp(nc, st, "g_kv1")
    g_wfsA = DmaGrp(nc, st, "g_wfsA")
    g_wfsB = DmaGrp(nc, st, "g_wfsB")
    g_wroA = DmaGrp(nc, st, "g_wroA")
    g_wroB = DmaGrp(nc, st, "g_wroB")
    g_lg = DmaGrp(nc, st, "g_lg")
    g_moeb = DmaGrp(nc, st, "g_moeb")
    g_wsel = [DmaGrp(nc, st, "g_wsel%d" % i) for i in range(8)]
    g_out = [DmaGrp(nc, st, "g_out%d" % i) for i in range(4)]
    g_dbg = DmaGrp(nc, st, "g_dbg")
    cc_sem = st.enter_context(nc.semaphore("cc_sem"))
    reg0 = st.enter_context(nc.gpsimd.register("eidx0"))
    reg1 = st.enter_context(nc.gpsimd.register("eidx1"))
    yreg0 = st.enter_context(nc.sync.register("yidx0"))
    yreg1 = st.enter_context(nc.sync.register("yidx1"))

    stopped = [False]

    def sec(eng, fn):
        if stopped[0]:
            return
        getattr(blk, eng.name)(fn)

    def CHK(name):
        if stop_after == name:
            stopped[0] = True

    from concourse.hw_specs import get_activation_tables
    _tabs = list(get_activation_tables(nc.m.arch).items())
    SET_LNEXP = next(i for i, (n, fs_) in enumerate(_tabs)
                     if AF.Exp in fs_ and AF.Ln in fs_)
    SET_GELU = next(i for i, (n, fs_) in enumerate(_tabs) if AF.Gelu in fs_)

    def load_act_set(cc, idx):
        cc.add_instruction(mybir.InstLoadActFuncSet(
            name=nc.get_next_instruction_name(), act_func_set_id=idx,
            ins=[], outs=[]))

    def b1v(c, lo, n):
        return B1[:, c * TC + lo: c * TC + lo + n]

    def b2v(c, lo, n):
        return B2[:, c * TC + lo: c * TC + lo + n]

    def rsv(c):
        return RS[:, c * TSUB:(c + 1) * TSUB]

    def rtv(c):
        return RT[:, c * TSUB:(c + 1) * TSUB]

    def xsv(c):
        return XS[:, c * TSUB:(c + 1) * TSUB]

    def x2sv(c):
        return X2S[:, c * TSUB:(c + 1) * TSUB]

    rq_rows = RT[0:16, 2048:4096]
    IND16T = SBF[0:16, 0:2048].bitcast(F32)
    wro_slots = [KVB[:, 0:1024], SSB[:, 0:1024]]
    IOTA8 = RTR[0:1, 40:48]


    def c_pre(cc):
        load_act_set(cc, SET_GELU)
    sec(C, c_pre)

    tok_cbias = {}
    def v_cb(v):
        v.memset(BIA[:, 40:41], -LN8)
        v.memset(BIA[:, 41:42], EPS_IN)
        tok_cbias[0] = V.tok(v.memset(BIA[:, 42:43], EPS_OUT))
    sec(V, v_cb)

    def p0_y(y):
        g_const.start(Y, IDN[:], identC[:])
        g_const.start(Y, ONESCF[:], ones128cC[:])
        g_const.start(Y, ONESR[:], ones1rC[:])
        g_const.start(Y, IOTA8, iota8C[:])
        for c in range(NCH):
            g_const.start(Y, WRFM[:, c * NE:(c + 1) * NE], Wr_h[c * 128:(c + 1) * 128, :])
        for off, src in [(RO_S0, bk_h), (RO_S1, bv_h), (RO_G1, g1_h), (RO_G2, g2_h)]:
            g_const.start(Y, ROWS[0:1, off:off + D], src.rearrange("d -> () d"))
    sec(Y, p0_y)
    tok_constY = g_const.tok()

    def p0_g(g):
        g_cstG.start(G, IND16[:], ind16C[:])
        g_cstG.start(G, ONESC[:], ones128cC[:])
        for hh in range(2):
            g_wq.start(G, WFS[:, hh * 4096:(hh + 1) * 4096].rearrange("p (k c) -> p k c", k=4),
                       Wq_h.rearrange("(k p) c -> p k c", p=128)[:, hh * 4:(hh + 1) * 4, :])
    sec(G, p0_g)

    tok_tr = {}
    tok_ev = {}
    tok_xbf = {}
    for t in range(TCH):
        xv = RS[:, (t % 2) * 1024:(t % 2) * 1024 + 1024]
        pw = PA if t % 2 == 0 else PB
        xslot = XS[:, (t % 2) * 2048:(t % 2) * 2048 + 1024]

        def y_load(y, t=t, xv=xv):
            if t >= 2:
                Y.wait(tok_tr[t - 2])
            g_x.start(Y, xv, x_h[t * 128:(t + 1) * 128, :])
        sec(Y, y_load)

        def t_tr(tt, t=t, xv=xv, pw=pw):
            T.wait(tok_constY)
            T.wait((g_x.sem, (t + 1) * 16))
            if t >= 2:
                T.wait(tok_ev[t - 2])
            for c in range(NCH):
                ins = tt.matmul(pw[:, c * 128:(c + 1) * 128],
                                xv[:, c * 128:(c + 1) * 128], IDN[:],
                                start=True, stop=True)
            tok_tr[t] = T.tok(ins)
        sec(T, t_tr)

        def c_ev(cc, t=t, pw=pw, xslot=xslot):
            C.wait(tok_tr[t])
            if t >= 2:
                C.wait((g_xfm[t % 2].sem, (t // 2) * 16))
            t1 = C.tok(cc.copy(xslot, pw[:]))
            C.wait(t1)
            ins = cc.copy(
                B1[:].rearrange("p (c n) -> p c n", c=NCH)[:, :, t * 128:(t + 1) * 128],
                xslot.rearrange("p (c n) -> p c n", c=NCH))
            tok_ev[t] = t1
            tok_xbf[t] = C.tok(ins)
        sec(C, c_ev)

        def y_xfm(y, t=t, xslot=xslot):
            Y.wait(tok_xbf[t])
            g_xfm[t % 2].start(Y, xfm_d[:, :, t * 128:(t + 1) * 128].rearrange("c p n -> p c n"),
                               xslot.rearrange("p (c n) -> p c n", c=NCH))
        sec(Y, y_xfm)

    if "d_xfm" in dbg:
        def y_d(y):
            Y.wait(g_xfm[0].tok())
            Y.wait(g_xfm[1].tok())
            g_dbg.start(Y, dbg["d_xfm"][:], xfm_d.ap())
        sec(Y, y_d)

    CHK("ingest")

    tok_biaT = {}
    tok_biaV = {}
    specs = [(bq_h, S_BQ), (g1_h, S_G1), (be1_h, S_BE1), (g2_h, S_G2), (be2_h, S_BE2)]
    for i, (src, slotc) in enumerate(specs):
        def g_ld(g, src=src, i=i):
            if i >= 1:
                G.wait(tok_biaT[i - 1])
            g_bias.start(G, BRX[0:8, :], src.rearrange("(a b) -> a b", b=128))
        sec(G, g_ld)

        def t_tp(tt, i=i):
            T.wait(tok_constY)
            T.wait((g_bias.sem, (i + 1) * 16))
            if i >= 2:
                T.wait(tok_biaV[i - 2])
            ins = tt.matmul(PC[:, (i % 2) * 512:(i % 2) * 512 + 8], BRX[0:8, :],
                            IDN[0:8, 0:8], start=True, stop=True)
            tok_biaT[i] = T.tok(ins)
        sec(T, t_tp)

        def v_ev(v, i=i, slotc=slotc):
            V.wait(tok_biaT[i])
            t0 = V.tok(v.tensor_copy(BIA[:, slotc * 8:(slotc + 1) * 8],
                                     PC[:, (i % 2) * 512:(i % 2) * 512 + 8]))
            if i == 0:
                V.wait(t0)
                t0 = V.tok(v.tensor_scalar(BIA[:, 44:52], BIA[:, 0:8], -1.0,
                                           None, ALU.mult))
            tok_biaV[i] = t0
        sec(V, v_ev)

    def g_ldb1(g):
        G.wait(tok_biaT[4])
        g_bias.start(G, BRX[:, :], b1_h.rearrange("(a b) -> a b", b=128))
    sec(G, g_ldb1)

    def t_tpb1(tt):
        T.wait(tok_constY)
        T.wait((g_bias.sem, 6 * 16))
        T.wait(tok_biaV[4])
        ins = tt.matmul(PC[:, 0:32], BRX[0:32, :], IDN[0:32, 0:32], start=True, stop=True)
        tok_biaT[5] = T.tok(ins)
    sec(T, t_tpb1)

    def v_evb1(v):
        V.wait(tok_biaT[5])
        tok_biaV[5] = V.tok(v.tensor_copy(B1F[:, 0:32], PC[:, 0:32]))
    sec(V, v_evb1)

    tok_rep = {}
    for j in range(2):
        def t_rep(tt, j=j):
            T.wait(g_const.tok())
            T.wait(tok_biaV[5] if j == 0 else tok_rep[0])
            for h in range(2):
                ins = tt.matmul(PD[:, h * 512:(h + 1) * 512], ONESR[0:1, 0:128],
                                ROWS[0:1, j * D + h * 512: j * D + (h + 1) * 512],
                                start=True, stop=True)
            tok_rep[(j, "t")] = T.tok(ins)
        sec(T, t_rep)

        def v_rep(v, j=j):
            V.wait(tok_rep[(j, "t")])
            tok_rep[j] = V.tok(v.tensor_copy(RT[:, j * 1024:(j + 1) * 1024], PD[:]))
        sec(V, v_rep)
    tok_bias_done = tok_rep.get(1)

    def y_rows2(y):
        Y.wait(tok_rep[(1, "t")])
        g_rows2.start(Y, ROWS[0:1, RO_S0:RO_S0 + D], bo_h.rearrange("d -> () d"))
        g_rows2.start(Y, ROWS[0:1, RO_S1:RO_S1 + D], b2_h.rearrange("d -> () d"))
    sec(Y, y_rows2)

    CHK("bias")

    tok_qmm = {}
    tok_elur = {}
    tok_eluy = {}
    tok_eq = {}
    tok_qsq = {}
    tok_qss = {}
    tok_rqv = {}
    tok_rq = {}
    def emit_qmm(qi):
        half, c = qi // NCH, qi % NCH
        lo = half * HALF
        pq = (PA, PB, PD)[qi % 3]

        def t_qmm(tt, qi=qi, c=c, lo=lo, pq=pq):
            T.wait(g_wq.tok())
            T.wait(g_x.tok())
            T.wait(tok_xbf[TCH - 1])
            if qi == 2:
                T.wait(tok_bias_done)
            if qi >= 3:
                T.wait(tok_elur[qi - 3])
            for k in range(NCH):
                for nn in range(2):
                    ins = tt.matmul(pq[:, nn * 512:(nn + 1) * 512],
                                    WFS[:, k * 1024 + c * 128: k * 1024 + (c + 1) * 128],
                                    b1v(k, lo + nn * 512, 512),
                                    start=(k == 0), stop=(k == NCH - 1))
            tok_qmm[qi] = T.tok(ins)
        sec(T, t_qmm)

    def emit_qrest(qi):
        half, c = qi // NCH, qi % NCH
        lo = half * HALF
        pq = (PA, PB, PD)[qi % 3]

        def c_elu(cc, qi=qi, c=c, pq=pq):
            C.wait(tok_qmm[qi])
            C.wait(tok_bias_done)
            if qi >= 1:
                C.wait(tok_eq[qi - 1])
            t_, r_, d_, y_ = RS[:, 0:1024], RS[:, 1024:2048], RS[:, 2048:3072], RS[:, 3072:4096]
            t1 = C.tok(cc.activation(t_, pq[:], AF.Relu, scale=-1.0,
                                     bias=BIA[:, 44 + c:45 + c]))
            C.wait(t1)
            t2 = C.tok(cc.activation(t_, t_, AF.Tanh, scale=-0.5))
            tok_elur[qi] = C.tok(cc.activation(r_, pq[:], AF.Relu,
                                               bias=BIA[:, S_BQ * 8 + c:S_BQ * 8 + c + 1]))
            C.wait(t2)
            t3 = C.tok(cc.activation(d_, t_, AF.Copy, scale=-1.0, bias=1.0))
            C.wait(t3)
            tok_eluy[qi] = C.tok(cc.activation(y_, d_, AF.Copy, scale=-0.4854,
                                               bias=1.4571))
        sec(C, c_elu)

        def v_eq(v, qi=qi, c=c, lo=lo):
            V.wait(tok_eluy[qi])
            def stepi(ins):
                tk = V.tok(ins)
                V.wait(tk)
                return tk
            t_, r_, d_, y_ = RS[:, 0:1024], RS[:, 1024:2048], RS[:, 2048:3072], RS[:, 3072:4096]
            stepi(v.tensor_mul(d_, d_, y_))
            stepi(v.tensor_scalar(d_, d_, -1.0, 2.0, ALU.mult, ALU.add))
            stepi(v.tensor_mul(y_, y_, d_))
            stepi(v.scalar_tensor_tensor(d_, t_, 2.0, y_, ALU.mult, ALU.mult))
            ins = v.scalar_tensor_tensor(b2v(c, lo, HALF), d_, 1.0, r_,
                                         ALU.mult, ALU.add)
            tok_eq[qi] = V.tok(ins)
        sec(V, v_eq)

        def c_sq(cc, qi=qi, c=c, lo=lo):
            C.wait(tok_eq[qi])
            if qi >= 2:
                C.wait(tok_qss[qi - 2])
            sv = SBF[:, (qi % 2) * 1024:(qi % 2) * 1024 + 1024]
            tok_qsq[qi] = C.tok(cc.activation(sv, b2v(c, lo, HALF), AF.Square))
        sec(C, c_sq)

        def t_ss(tt, qi=qi, half=half, c=c):
            T.wait(tok_qsq[qi])
            T.wait(g_cstG.tok())
            if c == 0 and half == 1:
                T.wait(tok_rqv[0])
            sv = SBF[:, (qi % 2) * 1024:(qi % 2) * 1024 + 1024]
            for nn in range(2):
                ins = tt.matmul(PC[0:16, nn * 512:(nn + 1) * 512],
                                IND16[:, c * 16:(c + 1) * 16],
                                sv[:, nn * 512:(nn + 1) * 512],
                                start=(c == 0), stop=(c == NCH - 1))
            tok_qss[qi] = T.tok(ins)
        sec(T, t_ss)

        if c == NCH - 1:
            def v_rqev(v, half=half, lo=lo, qi=qi):
                V.wait(tok_qss[qi])
                tok_rqv[half] = V.tok(v.tensor_copy(rq_rows[:, lo:lo + HALF],
                                                    PC[0:16, 0:1024]))
            sec(V, v_rqev)

            def v_rq(v, half=half, lo=lo):
                V.wait(tok_rqv[half])
                def stepi(ins):
                    tk = V.tok(ins)
                    V.wait(tk)
                    return tk
                r = rq_rows[:, lo:lo + HALF]
                scrA = KNV[0:16, 0:2048].bitcast(F32)
                scrB = SBF[0:16, 2048:4096].bitcast(F32)
                stepi(v.tensor_scalar(scrB.bitcast(I32), r.bitcast(I32), 1, None,
                                      ALU.logical_shift_right))
                stepi(v.tensor_scalar(scrB.bitcast(I32), scrB.bitcast(I32), -1,
                                      0x5F3759DF, ALU.mult, ALU.add))
                stepi(v.tensor_scalar(scrA, r, -0.5, None, ALU.mult))
                for _ in range(3):
                    stepi(v.tensor_mul(r, scrB, scrB))
                    stepi(v.tensor_mul(r, r, scrA))
                    stepi(v.tensor_scalar(r, r, 1.5, None, ALU.add))
                    stepi(v.tensor_mul(scrB, scrB, r))
                tok_rq[half] = stepi(v.tensor_scalar(r, scrB, 0.125, None, ALU.mult))
            sec(V, v_rq)


    for qi in range(2 * NCH):
        emit_qmm(qi)
        if qi >= 1:
            emit_qrest(qi - 1)
    emit_qrest(2 * NCH - 1)
    if "d_eq" in dbg:
        def y_d2(y):
            Y.wait(tok_eq[2 * NCH - 1])
            g_dbg.start(Y, dbg["d_eq"][:], B2[:])
        sec(Y, y_d2)
    if "d_rq" in dbg:
        def y_d2b(y):
            Y.wait(tok_rq[1])
            g_dbg.start(Y, dbg["d_rq"][:], rq_rows[:, :])
        sec(Y, y_d2b)

    CHK("p1q")

    def g_wkvload(g):
        G.wait(tok_qmm[2 * NCH - 1])
        for hh in range(2):
            g_wkv.start(G, WFS[:, 8192 + hh * 4096: 8192 + (hh + 1) * 4096].rearrange("p (k c) -> p k c", k=4),
                        Wk_h.rearrange("(k p) c -> p k c", p=128)[:, hh * 4:(hh + 1) * 4, :])
        for hh in range(2):
            g_wkv.start(G, WFS[:, hh * 4096:(hh + 1) * 4096].rearrange("p (k c) -> p k c", k=4),
                        Wv_h.rearrange("(k p) c -> p k c", p=128)[:, hh * 4:(hh + 1) * 4, :])
    sec(G, g_wkvload)

    tok_kmm = {}
    tok_kb = {}
    tok_kex = {}
    tok_keluy = {}
    tok_kel = {}
    tok_ksq = {}
    tok_red = {}
    tok_krs = {}
    tok_kn = {}
    tok_vev = {}
    tok_kvmm = {}
    tok_kvadd = {}
    KRD = POOL[:, 0:32]

    def emit_tkv(t):
        def t_kv(tt, t=t):
            T.wait(g_wkv.tok())
            if t == 0:
                T.wait(tok_rqv[1])
            if t >= 1:
                T.wait(tok_kb[t - 1])
                T.wait(tok_vev[t - 1])
            for k in range(NCH):
                xst = b1v(k, t * 128, 128)
                for nn in range(2):
                    tt.matmul(PA[:, nn * 512:(nn + 1) * 512], xst,
                              WFS[:, 8192 + k * 1024 + nn * 512: 8192 + k * 1024 + (nn + 1) * 512],
                              start=(k == 0), stop=(k == NCH - 1))
                for nn in range(2):
                    ins = tt.matmul(PB[:, nn * 512:(nn + 1) * 512], xst,
                                    WFS[:, k * 1024 + nn * 512: k * 1024 + (nn + 1) * 512],
                                    start=(k == 0), stop=(k == NCH - 1))
            tok_kmm[t] = T.tok(ins)
        sec(T, t_kv)

    KVSB = X2S[:, 0:1024]

    def emit_kvacc(t):
        def t_kvacc(tt, t=t):
            T.wait(tok_kn[t])
            T.wait(tok_vev[t])
            if t >= 2:
                T.wait(tok_kvadd[t - 2])
            for p in range(8):
                ins = tt.matmul(PC[:, p * 128:(p + 1) * 128],
                                KNV[:, p * 128:(p + 1) * 128],
                                KNV[:, 1024 + p * 128:1024 + (p + 1) * 128],
                                start=True, stop=True)
            tok_kvmm[t] = T.tok(ins)
        sec(T, t_kvacc)

        def v_kvadd(v, t=t):
            V.wait(tok_kvmm[t])
            if t == 0:
                ins = v.tensor_copy(KVSB, PC[:])
            else:
                ins = v.tensor_add(KVSB, KVSB, PC[:])
            tok_kvadd[t] = V.tok(ins)
        sec(V, v_kvadd)

    for t in range(TCH):
        emit_tkv(t)
        if t >= 1:
            emit_kvacc(t - 1)

        def v_kb(v, t=t):
            V.wait(tok_kmm[t])
            V.wait(tok_bias_done)
            if t >= 1:
                V.wait(tok_kex[t - 1])
            ins = v.scalar_tensor_tensor(RS[:, 0:1024], PA[:], 1.0,
                                         RT[:, 0:1024], ALU.mult, ALU.add)
            tok_kb[t] = V.tok(ins)
        sec(V, v_kb)

        def v_vev(v, t=t):
            if t >= 1:
                V.wait(tok_kvmm[t - 1])
            ins = v.scalar_tensor_tensor(KNV[:, 1024:2048], PB[:], 1.0,
                                         RT[:, 1024:2048], ALU.mult, ALU.add)
            tok_vev[t] = V.tok(ins)
        sec(V, v_vev)

        def c_kelu(cc, t=t):
            C.wait(tok_kb[t])
            t_, r_ = RS[:, 1024:2048], RS[:, 2048:3072]
            d_, y_ = RS[:, 3072:4096], RS[:, 0:1024]
            t1 = C.tok(cc.activation(t_, RS[:, 0:1024], AF.Relu, scale=-1.0))
            C.wait(t1)
            t2 = C.tok(cc.activation(t_, t_, AF.Tanh, scale=-0.5))
            tok_kex[t] = C.tok(cc.activation(r_, RS[:, 0:1024], AF.Relu))
            C.wait(t2)
            C.wait(tok_kex[t])
            t3 = C.tok(cc.activation(d_, t_, AF.Copy, scale=-1.0, bias=1.0))
            C.wait(t3)
            tok_keluy[t] = C.tok(cc.activation(y_, d_, AF.Copy, scale=-0.4854,
                                               bias=1.4571))
        sec(C, c_kelu)

        def v_ek(v, t=t):
            V.wait(tok_keluy[t])
            if t >= 1:
                V.wait(tok_kvmm[t - 1])
            def stepi(ins):
                tk = V.tok(ins)
                V.wait(tk)
                return tk
            t_, r_ = RS[:, 1024:2048], RS[:, 2048:3072]
            d_, y_ = RS[:, 3072:4096], RS[:, 0:1024]
            stepi(v.tensor_mul(d_, d_, y_))
            stepi(v.tensor_scalar(d_, d_, -1.0, 2.0, ALU.mult, ALU.add))
            stepi(v.tensor_mul(y_, y_, d_))
            stepi(v.scalar_tensor_tensor(d_, t_, 2.0, y_, ALU.mult, ALU.mult))
            ins = v.scalar_tensor_tensor(KNV[:, 0:1024], d_, 1.0, r_,
                                         ALU.mult, ALU.add)
            tok_kel[t] = V.tok(ins)
        sec(V, v_ek)

        def c_ksq(cc, t=t):
            C.wait(tok_kel[t])
            if t >= 1:
                C.wait(tok_red[t - 1])
            tok_ksq[t] = C.tok(cc.activation(SBF[:, 2048:3072], KNV[:, 0:1024], AF.Square))
        sec(C, c_ksq)

        def v_red(v, t=t):
            V.wait(tok_ksq[t])
            ins = v.reduce_sum(KRD[:, 0:16],
                               SBF[:, 2048:3072].rearrange("p (h d) -> p h d", h=16),
                               axis=AX.X)
            tok_red[t] = V.tok(ins)
        sec(V, v_red)

        def v_krs(v, t=t):
            V.wait(tok_red[t])
            def stepi(ins):
                tk = V.tok(ins)
                V.wait(tk)
                return tk
            r = KRD[:, 16:32]
            sA = BIA[:, 52:68]
            sB = BIA[:, 68:84]
            stepi(v.tensor_scalar(r.bitcast(I32), KRD[:, 0:16].bitcast(I32), 1,
                                  None, ALU.logical_shift_right))
            stepi(v.tensor_scalar(r.bitcast(I32), r.bitcast(I32), -1, 0x5F3759DF,
                                  ALU.mult, ALU.add))
            stepi(v.tensor_scalar(sA, KRD[:, 0:16], -0.5, None, ALU.mult))
            last = None
            for _ in range(2):
                stepi(v.tensor_mul(sB, r, r))
                stepi(v.tensor_mul(sB, sB, sA))
                stepi(v.tensor_scalar(sB, sB, 1.5, None, ALU.add))
                last = stepi(v.tensor_mul(r, r, sB))
            tok_krs[t] = last
        sec(V, v_krs)

        def v_kn(v, t=t):
            V.wait(tok_krs[t])
            ins = v.tensor_mul(KNV[:, 0:1024].rearrange("p (h d) -> p h d", h=16),
                               KNV[:, 0:1024].rearrange("p (h d) -> p h d", h=16),
                               KRD[:, 16:32].rearrange("p h -> p h ()").broadcast_to([128, 16, 64]))
            tok_kn[t] = V.tok(ins)
        sec(V, v_kn)
    emit_kvacc(TCH - 1)

    if "d_kn" in dbg:
        def y_dkn(y):
            Y.wait(tok_kn[TCH - 1])
            Y.wait(tok_vev[TCH - 1])
            g_dbg.start(Y, dbg["d_kn"][:], KNV[:])
        sec(Y, y_dkn)

    CHK("p1kv")

    tok_kvcp = {}
    def v_kvz(v):
        V.wait(tok_kvadd[TCH - 1])
        V.wait(tok_kex[TCH - 1])
        t1 = V.tok(v.memset(RT[:, 0:1024], 0.0))
        V.wait(t1)
        last = None
        for p in range(8):
            for hh in range(2):
                sl = slice(hh * 64, (hh + 1) * 64)
                last = v.tensor_copy(RT[sl, p * 128 + hh * 64: p * 128 + (hh + 1) * 64],
                                     KVSB[sl, p * 128 + hh * 64: p * 128 + (hh + 1) * 64])
        tok_kvcp[0] = V.tok(last)
    sec(V, v_kvz)

    def g_ar1(g):
        G.wait(tok_kvcp[0])
        g_kv1.start(G, cc1i.ap(), RT[:, 0:1024])
        g.wait_ge(g_kv1.sem, g_kv1.count)
        g.collective_compute("AllReduce", ALU.add,
                             replica_groups=[list(range(NCORES))],
                             ins=[cc1i.ap()], outs=[cc1o.ap()]).then_inc(cc_sem, 1)
        g.wait_ge(cc_sem, 1)
        g_kv1.start(G, RT[:, 0:1024], cc1o.ap())
    sec(G, g_ar1)

    tok_kvbf = {}
    def v_kvbf(v):
        V.wait(g_kv1.tok())
        tok_kvbf[0] = V.tok(v.tensor_copy(KVB[:], RT[:, 0:1024]))
    sec(V, v_kvbf)

    if "d_kv" in dbg:
        def y_d3(y):
            Y.wait(tok_kvbf[0])
            g_dbg.start(Y, dbg["d_kv"][:], RT[:, 0:1024])
        sec(Y, y_d3)

    CHK("ar1")

    def g_ind16t(g):
        G.wait(tok_red[TCH - 1])
        G.wait(tok_qss[2 * NCH - 1])
        g_cstG.start(G, IND16T[:], ind16TC[:])
    sec(G, g_ind16t)
    tok_ind16t = g_cstG.tok()

    tok_ctxv = {}
    prev_ctx = {0: None}
    for c in range(NCH):
        for half in range(2):
            lo = half * HALF

            def t_ctx(tt, c=c, half=half, lo=lo):
                T.wait(tok_rq[1])
                T.wait(tok_kvbf[0])
                T.wait(tok_ind16t)
                if prev_ctx[0] is not None:
                    T.wait(prev_ctx[0])
                for nn in range(2):
                    tt.matmul(PD[:, nn * 512:(nn + 1) * 512],
                              IND16T[:, c * 128:(c + 1) * 128],
                              rq_rows[:, lo + nn * 512: lo + (nn + 1) * 512],
                              start=True, stop=True)
                for nn in range(2):
                    ins = tt.matmul(PA[:, nn * 512:(nn + 1) * 512],
                                    KVB[:, c * 128:(c + 1) * 128],
                                    b2v(c, lo + nn * 512, 512),
                                    start=True, stop=True)
                tok_ctxv[(c, half, "t")] = T.tok(ins)
            sec(T, t_ctx)

            def v_ctx(v, c=c, half=half, lo=lo):
                V.wait(tok_ctxv[(c, half, "t")])
                t1 = V.tok(v.tensor_copy(SSB[:], PD[:]))
                V.wait(t1)
                ins = v.tensor_mul(b1v(c, lo, HALF), PA[:], SSB[:])
                tok_ctxv[(c, half)] = V.tok(ins)
                prev_ctx[0] = tok_ctxv[(c, half)]
            sec(V, v_ctx)
    tok_ctx_done = prev_ctx[0]

    if "d_ctx" in dbg:
        def y_d4(y):
            Y.wait(tok_ctx_done)
            g_dbg.start(Y, dbg["d_ctx"][:], B1[:])
        sec(Y, y_d4)

    CHK("ctx")

    state = {
        "gelu_loaded": False,
        "lnlast": None,
        "statfree": tok_ctx_done,
        "accfree": None,
        "wslot": {},
        "hpw": {0: None, 1: None},
        "hread": {},
        "woslot": {0: None, 1: None},
    }

    def ln_block(src_region, dst_region, g_slot, be_slot, g_row_off, eps_col,
                 pre_tok, out_cast=None, pool_sub=None):
        if stopped[0]:
            return {NCH - 1: None, 0: None}, {}, {}
        tok_st = {}
        tok_zc = {}
        zslot_war = {0: None, 1: None}
        for c in range(NCH):
            def c_stats(cc, c=c):
                C.wait(pre_tok(c))
                if zslot_war[c % 2] is not None:
                    C.wait(zslot_war[c % 2])
                sl = (c % 2) * 1024
                cc.copy(SBF[:, sl:sl + 512], src_region(c))
                ins = cc.activation(SBF[:, sl + 512:sl + 1024], src_region(c), AF.Square)
                tok_zc[c] = C.tok(ins)
            sec(C, c_stats)

            def t_stats(tt, c=c):
                T.wait(tok_zc[c])
                T.wait(g_cstG.tok())
                if c == 0 and state["statfree"] is not None:
                    T.wait(state["statfree"])
                sl = (c % 2) * 1024
                tt.matmul(PC[0:1, 0:512], ONESC[:], SBF[:, sl:sl + 512],
                          start=(c == 0), stop=(c == NCH - 1))
                ins = tt.matmul(PC[0:1, 512:1024], ONESC[:], SBF[:, sl + 512:sl + 1024],
                                start=(c == 0), stop=(c == NCH - 1))
                tok_st[c] = T.tok(ins)
            sec(T, t_stats)
            zslot_war[c % 2] = tok_st[c]

        tok_var = {}
        def v_rows(v):
            V.wait(tok_st[NCH - 1])
            v.tensor_scalar(LNR[0:1, 0:TSUB], PC[0:1, 0:512], 1.0 / D, None, ALU.mult)
            t2 = V.tok(v.tensor_scalar(LNR[0:1, TSUB:2 * TSUB], PC[0:1, 512:1024], 1.0 / D, None, ALU.mult))
            V.wait(t2)
            t3 = V.tok(v.tensor_mul(LNR[0:1, 2 * TSUB:3 * TSUB], LNR[0:1, 0:TSUB], LNR[0:1, 0:TSUB]))
            V.wait(t3)
            tok_var[0] = V.tok(v.tensor_sub(LNR[0:1, TSUB:2 * TSUB], LNR[0:1, TSUB:2 * TSUB], LNR[0:1, 2 * TSUB:3 * TSUB]))
        sec(V, v_rows)
        state["statfree"] = tok_var[0]

        tok_rstd = {}
        r1 = LNR[0:1, TSUB:2 * TSUB]
        r2 = LNR[0:1, 2 * TSUB:3 * TSUB]
        r3 = LNR[0:1, 3 * TSUB:4 * TSUB]
        r4 = LNR[0:1, 4 * TSUB:5 * TSUB]
        eps_val = EPS_IN if eps_col == 41 else EPS_OUT

        def v_rstd(v):
            def stepi(ins):
                tk = V.tok(ins)
                V.wait(tk)
                return tk
            V.wait(tok_var[0])
            stepi(v.tensor_scalar(r1, r1, float(eps_val), None, ALU.add))
            stepi(v.tensor_scalar(r2, r1, -0.5, None, ALU.mult))
            stepi(v.tensor_scalar(r3.bitcast(I32), r1.bitcast(I32), 1, None,
                                  ALU.logical_shift_right))
            stepi(v.tensor_scalar(r3.bitcast(I32), r3.bitcast(I32), -1, 0x5F3759DF,
                                  ALU.mult, ALU.add))
            for _ in range(2):
                stepi(v.tensor_mul(r4, r3, r3))
                stepi(v.tensor_mul(r4, r4, r2))
                stepi(v.tensor_scalar(r4, r4, 1.5, None, ALU.add))
                stepi(v.tensor_mul(r3, r3, r4))
            tok_rstd[0] = V.tok(v.tensor_copy(r1, r3))
        sec(V, v_rstd)

        tok_mr = {}
        def v_mr(v):
            V.wait(tok_rstd[0])
            tok_mr[0] = V.tok(v.tensor_mul(LNR[0:1, 0:TSUB], LNR[0:1, 0:TSUB], LNR[0:1, TSUB:2 * TSUB]))
        sec(V, v_mr)

        tok_sb = {}
        def t_sb(tt):
            T.wait(tok_mr[0])
            if state["lnlast"] is not None:
                T.wait(state["lnlast"])
            tok_sb[0] = T.tok(tt.matmul(PD[:, 0:512], ONESR[0:1, 0:128], LNR[0:1, TSUB:2 * TSUB],
                                        start=True, stop=True))
        sec(T, t_sb)

        cast_toks = {}
        pool_toks = {}
        tok_mg = {}
        tok_app = {}
        mgb = [PD[:, 512:1024], PB[:, 0:512]]
        for c in range(NCH):
            def t_mg(tt, c=c):
                if c >= 2:
                    T.wait(tok_app[c - 2])
                if g_row_off is not None:
                    lhs = ROWS[0:1, g_row_off + c * 128: g_row_off + (c + 1) * 128]
                else:
                    lhs = ONESR[0:1, 0:128]
                tok_mg[c] = T.tok(tt.matmul(mgb[c % 2], lhs, LNR[0:1, 0:TSUB],
                                            start=True, stop=True))
            sec(T, t_mg)

            def v_app(v, c=c):
                V.wait(tok_sb[0])
                V.wait(tok_mg[c])
                if g_slot is not None:
                    t1 = V.tok(v.scalar_tensor_tensor(
                        src_region(c), src_region(c),
                        BIA[:, g_slot * 8 + c:g_slot * 8 + c + 1],
                        PD[:, 0:512], ALU.mult, ALU.mult))
                else:
                    t1 = V.tok(v.scalar_tensor_tensor(
                        src_region(c), src_region(c), 1.0,
                        PD[:, 0:512], ALU.mult, ALU.mult))
                V.wait(t1)
                if be_slot is not None:
                    ins = v.scalar_tensor_tensor(
                        dst_region(c), src_region(c),
                        BIA[:, be_slot * 8 + c:be_slot * 8 + c + 1],
                        mgb[c % 2], ALU.add, ALU.subtract)
                else:
                    ins = v.scalar_tensor_tensor(
                        dst_region(c), src_region(c), 0.0,
                        mgb[c % 2], ALU.add, ALU.subtract)
                tok_app[c] = V.tok(ins)
            sec(V, v_app)

            if out_cast is not None:
                def c_cast(cc, c=c):
                    C.wait(tok_app[c])
                    cast_toks[c] = C.tok(cc.copy(out_cast(c), dst_region(c)))
                sec(C, c_cast)
            if pool_sub is not None:
                def v_pool(v, c=c):
                    V.wait(tok_app[c])
                    ins = v.reduce_sum(POOL[:, c * 4 + pool_sub:c * 4 + pool_sub + 1],
                                       dst_region(c), axis=AX.X)
                    pool_toks[c] = V.tok(ins)
                sec(V, v_pool)
        state["lnlast"] = tok_app[NCH - 1]
        return tok_app, cast_toks, pool_toks

    def ff_block(in_wait_toks, in_cols, b1col0, seed_region, dst_region,
                 scale, scale_wait, w1_src_fn, w2_src_fn, b2_row_off, wslot_key,
                 slot_extra=None, dst_wait=None, fs_wait=None):
        ev_toks = {}
        if stopped[0]:
            return {(NFS - 1, 0): None, (NFS - 1, 1): None}
        for fs in range(NFS):
            slot = fs % 2
            w1v = (WFS if slot == 0 else B2)[:, 0:8 * FSW]
            w2v = (WFS if slot == 0 else B2)[:, 8 * FSW:16 * FSW]
            grp = g_wfsA if slot == 0 else g_wfsB

            def g_wload(g, fs=fs, slot=slot, grp=grp, w1v=w1v, w2v=w2v):
                if state["wslot"].get((wslot_key, slot)) is not None:
                    G.wait(state["wslot"][(wslot_key, slot)])
                if slot_extra:
                    for tk in slot_extra.get(slot, ()):
                        G.wait(tk)
                if fs_wait is not None:
                    G.wait(fs_wait(fs))
                w1_src = w1_src_fn()
                w2_src = w2_src_fn()
                for hh in range(2):
                    grp.start(G, w1v.rearrange("p (k c) -> p k c", k=8)[:, hh * 4:(hh + 1) * 4, :],
                              w1_src.rearrange("(k p) c -> p k c", p=128)[:, hh * 4:(hh + 1) * 4,
                                                                          fs * FSW:(fs + 1) * FSW])
                for hh in range(2):
                    grp.start(G, w2v.rearrange("p (f c) -> p f c", f=8)[:, hh * 4:(hh + 1) * 4, :],
                              w2_src.rearrange("(f p) c -> p f c", p=128)[:, fs * 8 + hh * 4: fs * 8 + (hh + 1) * 4, :])
            sec(G, g_wload)
            ld_tok = (grp.sem, grp.count)

            h_toks = {}
            ffstage = int(os.environ.get("KERN_FFSTAGE", "9"))
            for f in range(8 if ffstage >= 1 else 0):
                fg = fs * 8 + f

                def t_w1(tt, f=f, ld_tok=ld_tok, w1v=w1v):
                    T.wait(ld_tok)
                    if state["hpw"][f % 2] is not None:
                        T.wait(state["hpw"][f % 2])
                    ph = PA[:, (f % 2) * 512:(f % 2) * 512 + 512]
                    for k in range(NCH):
                        if in_wait_toks:
                            T.wait(in_wait_toks[k])
                        ins = tt.matmul(ph, w1v[:, k * FSW + f * 128: k * FSW + (f + 1) * 128],
                                        in_cols(k), start=(k == 0), stop=(k == NCH - 1))
                    h_toks[(f, "mm")] = T.tok(ins)
                sec(T, t_w1)

                if ffstage < 2:
                    h_toks[f] = h_toks[(f, "mm")]
                    continue

                def c_gelu(cc, f=f, fg=fg):
                    C.wait(h_toks[(f, "mm")])
                    if state["hread"].get(f) is not None:
                        C.wait(state["hread"][f])
                    ph = PA[:, (f % 2) * 512:(f % 2) * 512 + 512]
                    ins = cc.activation(SBF[:, f * 512:(f + 1) * 512], ph, AF.Gelu,
                                        bias=B1F[:, b1col0 + fg:b1col0 + fg + 1])
                    h_toks[f] = C.tok(ins)
                    state["hpw"][f % 2] = h_toks[f]
                sec(C, c_gelu)

            accv = [PC[:, 0:512], PC[:, 512:1024], PD[:, 0:512], PD[:, 512:1024]]
            tok_w2 = {}
            for oh in range(2 if ffstage >= 3 else 0):
                for f in range(8):
                    def t_w2(tt, f=f, oh=oh, fs=fs, w2v=w2v):
                        T.wait(h_toks[f])
                        if f == 0 and oh == 0:
                            if state["accfree"] is not None:
                                T.wait(state["accfree"])
                            if state["statfree"] is not None:
                                T.wait(state["statfree"])
                            if state["lnlast"] is not None:
                                T.wait(state["lnlast"])
                        if f == 0 and oh == 1:
                            T.wait(ev_toks[(fs, 0)])
                        last_f = (f == 7)
                        fold = (b2_row_off is not None and fs == 0
                                and not os.environ.get("KERN_NOFOLD"))
                        for ot in range(4):
                            og = oh * 4 + ot
                            ins = tt.matmul(accv[ot],
                                            w2v[:, f * 1024 + og * 128: f * 1024 + (og + 1) * 128],
                                            SBF[:, f * 512:(f + 1) * 512],
                                            start=(f == 0), stop=(last_f and not fold))
                        if last_f and fold:
                            for ot in range(4):
                                og = oh * 4 + ot
                                ins = tt.matmul(
                                    accv[ot],
                                    ROWS[0:1, b2_row_off + og * 128: b2_row_off + (og + 1) * 128],
                                    ONESR[0:1, 0:512], start=False, stop=True)
                        tok_w2[(oh, f)] = T.tok(ins)
                        state["hread"][f] = tok_w2[(oh, f)]
                    sec(T, t_w2)

                if ffstage < 4:
                    ev_toks[(fs, oh)] = tok_w2[(oh, 7)]
                    continue

                def v_acc(v, oh=oh, fs=fs):
                    V.wait(tok_w2[(oh, 7)])
                    if scale_wait is not None:
                        V.wait(scale_wait)
                    if dst_wait is not None:
                        V.wait(dst_wait)
                    last = None
                    for ot in range(4):
                        oc = oh * 4 + ot
                        if fs == 0 and (seed_region is not None):
                            last = v.scalar_tensor_tensor(dst_region(oc), accv[ot], scale,
                                                          seed_region(oc), ALU.mult, ALU.add)
                        elif os.environ.get("KERN_ACC_COPY"):
                            last = v.tensor_copy(dst_region(oc), accv[ot])
                        elif os.environ.get("KERN_ACC_SCRATCH"):
                            last = v.scalar_tensor_tensor(X2S[:, oc * TSUB:(oc + 1) * TSUB],
                                                          accv[ot], scale,
                                                          dst_region(oc), ALU.mult, ALU.add)
                        else:
                            last = v.scalar_tensor_tensor(dst_region(oc), accv[ot], scale,
                                                          dst_region(oc), ALU.mult, ALU.add)
                    ev_toks[(fs, oh)] = V.tok(last)
                sec(V, v_acc)
            state["accfree"] = ev_toks.get((fs, 1))
            state["wslot"][(wslot_key, slot)] = (tok_w2.get((1, 7)) or h_toks.get(7)
                                                 or h_toks.get((7, "mm")) or ld_tok)
        if not ev_toks:
            ev_toks = {(NFS - 1, 0): None, (NFS - 1, 1): None}
        for fs2 in range(NFS):
            for oh2 in range(2):
                ev_toks.setdefault((fs2, oh2), None)
        return ev_toks

    tok_x2cast = {}
    tok_pool_all = {}
    tok_zo_prev = {0: None}
    tok_xs = {}
    for sub in range(NSUB):
        slo = sub * TSUB

        def y_xs(y, sub=sub, slo=slo):
            Y.wait(g_xfm[0].tok())
            Y.wait(g_xfm[1].tok())
            Y.wait(tok_xbf[TCH - 1])
            if tok_zo_prev[0] is not None:
                Y.wait(tok_zo_prev[0])
            g_xs.start(Y, XS[:].rearrange("p (c n) -> p c n", c=NCH),
                       xfm_d[:, :, slo:slo + TSUB].rearrange("c p n -> p c n"))
            tok_xs[sub] = g_xs.tok()
        sec(Y, y_xs)
        CHK("xs%d" % sub)

        tok_wo = {}
        for e in range(NCH):
            slot = e % 2
            grp = g_wroA if slot == 0 else g_wroB

            def g_wo(g, e=e, slot=slot, grp=grp):
                if state["woslot"][slot] is not None:
                    G.wait(state["woslot"][slot])
                else:
                    G.wait(tok_ctx_done)
                grp.start(G, wro_slots[slot], Wo_h[e * 128:(e + 1) * 128, :])
            sec(G, g_wo)
            ld_tok = (grp.sem, grp.count)

            def t_wo(tt, e=e, slot=slot, ld_tok=ld_tok, slo=slo):
                T.wait(ld_tok)
                if e == 0:
                    if state["accfree"] is not None:
                        T.wait(state["accfree"])
                    if state["statfree"] is not None:
                        T.wait(state["statfree"])
                    if state["lnlast"] is not None:
                        T.wait(state["lnlast"])
                    T.wait(tok_ctx_done)
                for o in range(NCH):
                    pbank = [PA, PB, PC, PD][o // 2][:, (o % 2) * 512:(o % 2) * 512 + 512]
                    ins = tt.matmul(pbank, wro_slots[slot][:, o * 128:(o + 1) * 128],
                                    b1v(e, slo, TSUB), start=(e == 0), stop=False)
                tok_wo[e] = T.tok(ins)
                state["woslot"][slot] = tok_wo[e]
            sec(T, t_wo)

        def t_wof(tt, sub=sub):
            T.wait(g_rows2.tok())
            for o in range(NCH):
                pbank = [PA, PB, PC, PD][o // 2][:, (o % 2) * 512:(o % 2) * 512 + 512]
                ins = tt.matmul(pbank, ROWS[0:1, RO_S0 + o * 128: RO_S0 + (o + 1) * 128],
                                ONESR[0:1, 0:512], start=False, stop=True)
            tok_wo["fold"] = T.tok(ins)
        sec(T, t_wof)
        CHK("wo%d" % sub)

        tok_z1 = {}
        def v_z1(v, sub=sub, slo=slo):
            V.wait(tok_wo["fold"])
            V.wait(tok_xs[sub])
            for o in range(NCH):
                pbank = [PA, PB, PC, PD][o // 2][:, (o % 2) * 512:(o % 2) * 512 + 512]
                ins = v.scalar_tensor_tensor(rsv(o), pbank, 1.0,
                                             xsv(o), ALU.mult, ALU.add)
                tok_z1[o] = V.tok(ins)
        sec(V, v_z1)
        CHK("z1%d" % sub)
        state["accfree"] = tok_z1.get(NCH - 1)
        state["statfree"] = tok_z1.get(NCH - 1)

        if "d_z1" in dbg and sub == 0:
            def y_dz1(y):
                Y.wait(tok_z1[NCH - 1])
                g_dbg.start(Y, dbg["d_z1"][:], RS[:])
            sec(Y, y_dz1)

        app1, cast1, _ = ln_block(
            rsv, rsv, S_G1, S_BE1, RO_G1, 41,
            pre_tok=lambda c: tok_z1.get(c),
            out_cast=lambda c, slo=slo: b1v(c, slo, TSUB))
        CHK("ln1%d" % sub)

        if "d_attn" in dbg and sub == 0:
            def y_dat(y):
                Y.wait(app1[NCH - 1])
                g_dbg.start(Y, dbg["d_attn"][:], RS[:])
            sec(Y, y_dat)

        ff1 = ff_block(cast1, lambda k, slo=slo: b1v(k, slo, TSUB), 0,
                       rsv, rtv, 1.0, None, (lambda: W1_h), (lambda: W2_h), RO_S1, "ff",
                       slot_extra=({0: [tok_kmm.get(TCH - 1)],
                                    1: [tok_ctx_done]} if sub == 0 else None))
        CHK("ff%d" % sub)

        app2, _, _ = ln_block(
            rtv, rtv, S_G2, S_BE2, RO_G2, 41,
            pre_tok=lambda c: ff1.get((NFS - 1, 1)))
        CHK("ln2%d" % sub)

        tok_zo = {}
        def v_zo(v, sub=sub, slo=slo):
            for c in range(NCH):
                V.wait(app2[c])
                ins = v.tensor_add(rtv(c), rtv(c), xsv(c))
                tok_zo[c] = V.tok(ins)
        sec(V, v_zo)
        tok_zo_prev[0] = tok_zo.get(NCH - 1)

        app3, cast3, pools = ln_block(
            rtv, rtv, None, None, None, 42,
            pre_tok=lambda c: tok_zo[c],
            out_cast=lambda c, slo=slo: b1v(c, slo, TSUB),
            pool_sub=sub)
        for c in range(NCH):
            tok_x2cast[(sub, c)] = cast3.get(c)
            tok_pool_all[(sub, c)] = pools.get(c)
        CHK("p2s%d" % sub)

    if "d_x2b" in dbg:
        def y_dx2(y):
            for c in range(NCH):
                Y.wait(tok_x2cast[(NSUB - 1, c)])
            g_dbg.start(Y, dbg["d_x2b"][:], B1[:])
        sec(Y, y_dx2)

    CHK("p2")

    tok_lgT = {}
    def t_lg(tt):
        for sub in range(NSUB):
            for c in range(NCH):
                T.wait(tok_pool_all[(sub, c)])
        T.wait(state["statfree"])
        n = 0
        for c in range(NCH):
            for sub in range(NSUB):
                ins = tt.matmul(PB[0:1, 0:8], POOL[:, c * 4 + sub:c * 4 + sub + 1],
                                WRFM[:, c * NE:(c + 1) * NE],
                                start=(n == 0), stop=(n == 31))
                n += 1
        tok_lgT[0] = T.tok(ins)
    sec(T, t_lg)

    tok_rtr = {}
    def v_lg(v):
        V.wait(tok_lgT[0])
        tok_rtr[0] = V.tok(v.tensor_scalar(RTR[0:1, 0:8], PB[0:1, 0:8],
                                           1.0 / S, None, ALU.mult))
    sec(V, v_lg)

    def g_ar2(g):
        G.wait(tok_rtr[0])
        g_lg.start(G, cc2i.ap(), RTR[0:1, 0:8])
        g.wait_ge(g_lg.sem, g_lg.count)
        g.collective_compute("AllReduce", ALU.add,
                             replica_groups=[list(range(NCORES))],
                             ins=[cc2i.ap()], outs=[cc2o.ap()]).then_inc(cc_sem, 1)
        g.wait_ge(cc_sem, 2)
        g_lg.start(G, RTR[0:1, 0:8], cc2o.ap())
    sec(G, g_ar2)

    tok_rt2 = {}
    def v_rt(v):
        V.wait(g_lg.tok())
        V.wait(g_const.tok())
        def step(ins):
            tk = V.tok(ins)
            V.wait(tk)
            return tk
        step(v.reduce_max(RTR[0:1, 32:33], RTR[0:1, 0:8], axis=AX.X))
        step(v.tensor_tensor(RTR[0:1, 8:16], RTR[0:1, 0:8],
                             RTR[0:1, 32:33].broadcast_to([1, 8]), ALU.is_ge))
        step(v.tensor_scalar(RTR[0:1, 16:24], RTR[0:1, 8:16], -1e30, None, ALU.mult))
        step(v.tensor_add(RTR[0:1, 16:24], RTR[0:1, 16:24], RTR[0:1, 0:8]))
        step(v.reduce_max(RTR[0:1, 33:34], RTR[0:1, 16:24], axis=AX.X))
        step(v.tensor_tensor(RTR[0:1, 16:24], RTR[0:1, 16:24],
                             RTR[0:1, 33:34].broadcast_to([1, 8]), ALU.is_ge))
        step(v.tensor_mul(RTR[0:1, 24:32], RTR[0:1, 8:16], IOTA8))
        step(v.reduce_sum(RTR[0:1, 38:39], RTR[0:1, 24:32], axis=AX.X))
        step(v.tensor_mul(RTR[0:1, 24:32], RTR[0:1, 16:24], IOTA8))
        step(v.reduce_sum(RTR[0:1, 39:40], RTR[0:1, 24:32], axis=AX.X))
        ins = v.tensor_sub(RTR[0:1, 34:35], RTR[0:1, 33:34], RTR[0:1, 32:33])
        tok_rt2[0] = V.tok(ins)
    sec(V, v_rt)

    def v_rtexp(v):
        V.wait(tok_rt2[0])
        u = RTR[0:1, 40:41]
        p = RTR[0:1, 35:36]
        def stepi(ins):
            tk = V.tok(ins)
            V.wait(tk)
            return tk
        stepi(v.tensor_scalar(u, RTR[0:1, 34:35], -24.0, None, ALU.max))
        stepi(v.tensor_scalar(u, u, 1.0 / 64.0, None, ALU.mult))
        stepi(v.tensor_scalar(p, u, 0.25, 1.0, ALU.mult, ALU.add))
        stepi(v.tensor_mul(p, p, u))
        stepi(v.tensor_scalar(p, p, 1.0 / 3.0, 1.0, ALU.mult, ALU.add))
        stepi(v.tensor_mul(p, p, u))
        stepi(v.tensor_scalar(p, p, 0.5, 1.0, ALU.mult, ALU.add))
        stepi(v.tensor_mul(p, p, u))
        stepi(v.tensor_scalar(p, p, 1.0, 1.0, ALU.mult, ALU.add))
        for _ in range(5):
            stepi(v.tensor_mul(p, p, p))
        tok_rt2[1] = V.tok(v.tensor_mul(p, p, p))
    sec(V, v_rtexp)

    def v_rt2(v):
        V.wait(tok_rt2[1])
        def step(ins):
            tk = V.tok(ins)
            V.wait(tk)
            return tk
        step(v.tensor_scalar(RTR[0:1, 36:37], RTR[0:1, 35:36], 1.0, None, ALU.add))
        step(v.reciprocal(RTR[0:1, 36:37], RTR[0:1, 36:37]))
        ins = v.tensor_scalar(RTR[0:1, 37:38], RTR[0:1, 36:37], -1.0, 1.0,
                              ALU.mult, ALU.add)
        tok_rt2[2] = V.tok(ins)
    sec(V, v_rt2)

    if "d_lg" in dbg:
        def y_dlg(y):
            Y.wait(tok_rt2[2])
            g_dbg.start(Y, dbg["d_lg"][:], RTR[0:1, :])
        sec(Y, y_dlg)

    tok_web = {}
    def t_web(tt):
        T.wait(tok_rt2[2])
        tt.matmul(PB[:, 0:1], ONESR[0:1, 0:128], RTR[0:1, 36:37], start=True, stop=True)
        ins = tt.matmul(PB[:, 512:513], ONESR[0:1, 0:128], RTR[0:1, 37:38],
                        start=True, stop=True)
        tok_web[0] = T.tok(ins)
    sec(T, t_web)

    def v_web(v):
        V.wait(tok_web[0])
        v.tensor_copy(WEB[:, 0:1], PB[:, 0:1])
        tok_web[1] = V.tok(v.tensor_copy(WEB[:, 1:2], PB[:, 512:513]))
    sec(V, v_web)

    tok_rti = {}
    def c_rti(cc):
        C.wait(tok_rt2[2])
        tok_rti[0] = C.tok(cc.copy(RTI[0:1, 0:2], RTR[0:1, 38:40]))
    sec(C, c_rti)

    CHK("router")

    iv = [None, None]
    def g_snap(g):
        G.wait(tok_rti[0])
        g.reg_load(reg0, RTI[0:1, 0:1])
        g.reg_load(reg1, RTI[0:1, 1:2])
        iv[0] = g.snap(reg0, min_val=0, max_val=NE - 1)
        iv[1] = g.snap(reg1, min_val=0, max_val=NE - 1)
    sec(G, g_snap)



    tok_moeb = {}
    for es in range(2):
        def g_mb(g, es=es):
            if es == 0:
                G.wait(state["woslot"][0])
                G.wait(state["woslot"][1])
                G.wait(state["wslot"][("ff", 0)])
                G.wait(state["wslot"][("ff", 1)])
            else:
                G.wait(tok_moeb[(0, "v")])
            roff = RO_S0 if es == 0 else RO_S1
            g_moeb.start(G, ROWS[0:1, roff:roff + D],
                         b2m_h[iv[es]].rearrange("d -> () d"))
            g_moeb.start(G, BRX[:, :], b1m_h[iv[es]].rearrange("(a b) -> a b", b=128))
        sec(G, g_mb)
        ld = (g_moeb.sem, (es + 1) * 32)

        def t_mb(tt, es=es, ld=ld):
            T.wait(ld)
            T.wait(tok_web[1])
            if state["statfree"] is not None:
                T.wait(state["statfree"])
            ins = tt.matmul(PB[:, 0:32] if es == 0 else PB[:, 512:544],
                            BRX[0:32, :], IDN[0:32, 0:32], start=True, stop=True)
            tok_moeb[(es, "t")] = T.tok(ins)
        sec(T, t_mb)

        def v_mb(v, es=es):
            V.wait(tok_moeb[(es, "t")])
            ins = v.tensor_copy(B1F[:, 32 + es * 32:64 + es * 32],
                                PB[:, 0:32] if es == 0 else PB[:, 512:544])
            tok_moeb[(es, "v")] = V.tok(ins)
        sec(V, v_mb)

    CHK("moeb")

    def g_wselc(g):
        for es in range(2):
            for fs in range(NFS):
                grp = g_wsel[es * NFS + fs]
                grp.start(G, W1sel[es].ap()[:, fs * FSW:(fs + 1) * FSW],
                          W1m_h[iv[es]][:, fs * FSW:(fs + 1) * FSW])
                grp.start(G, W2sel[es].ap()[fs * FSW:(fs + 1) * FSW, :],
                          W2m_h[iv[es]][fs * FSW:(fs + 1) * FSW, :])
    sec(G, g_wselc)

    tok_otr = {}
    tok_oev = {}
    tok_upc = {}
    for sub in range(NSUB):
        slo = sub * TSUB

        def c_upc(cc, sub=sub, slo=slo):
            for c in range(NCH):
                C.wait(tok_x2cast[(sub, c)])
            if (sub - 1, "lnfree") in tok_upc:
                C.wait(tok_upc[(sub - 1, "lnfree")])
            ins = cc.copy(X2S[:].rearrange("p (c n) -> p c n", c=NCH),
                          B1[:].rearrange("p (c n) -> p c n", c=NCH)[:, :, slo:slo + TSUB])
            tok_upc[sub] = C.tok(ins)
        sec(C, c_upc)
        upc = tok_upc.get(sub)

        moe_last = None
        for es in range(2):
            in_toks = {c: tok_x2cast.get((sub, c)) for c in range(NCH)}
            ev = ff_block(in_toks, lambda k, slo=slo: b1v(k, slo, TSUB),
                          32 + es * 32,
                          None, x2sv,
                          WEB[:, es:es + 1], tok_web.get(1),
                          (lambda es=es: W1sel[es].ap()), (lambda es=es: W2sel[es].ap()),
                          (RO_S0 if es == 0 else RO_S1), "ff",
                          slot_extra={0: [tok_moeb.get((es, "v"))],
                                      1: [tok_moeb.get((es, "v"))]},
                          dst_wait=upc,
                          fs_wait=(lambda fs, es=es: (g_wsel[es * NFS + fs].sem, 32)))
            moe_last = ev

        appF, _, _ = ln_block(
            x2sv, rsv, None, None, None, 42,
            pre_tok=lambda c, ev=moe_last: ev[(NFS - 1, 1)])
        tok_upc[(sub, "lnfree")] = appF.get(NCH - 1)

        for tcq in range(TSUB // 128):
            tglob = sub * 4 + tcq
            pw = PA if tcq % 2 == 0 else PB

            def t_otr(tt, sub=sub, tcq=tcq, pw=pw, appF=appF):
                for c in range(NCH):
                    T.wait(appF[c])
                if (sub, tcq - 2) in tok_oev:
                    T.wait(tok_oev[(sub, tcq - 2)])
                if tcq < 2 and (sub - 1, tcq + 2) in tok_oev:
                    T.wait(tok_oev[(sub - 1, tcq + 2)])
                for c in range(NCH):
                    ins = tt.matmul(pw[:, c * 128:(c + 1) * 128],
                                    RS[:, c * TSUB + tcq * 128: c * TSUB + (tcq + 1) * 128],
                                    IDN[:], start=True, stop=True)
                tok_otr[(sub, tcq)] = T.tok(ins)
            sec(T, t_otr)

            def c_oev(cc, sub=sub, tcq=tcq, pw=pw):
                C.wait(tok_otr[(sub, tcq)])
                if sub >= 1:
                    C.wait((g_out[tcq].sem, sub * 16))
                ins = cc.copy(RT[:, tcq * 1024:(tcq + 1) * 1024], pw[:])
                tok_oev[(sub, tcq)] = C.tok(ins)
            sec(C, c_oev)

            def y_ost(y, sub=sub, tcq=tcq, tglob=tglob):
                Y.wait(tok_oev[(sub, tcq)])
                g_out[tcq].start(Y, out_h[tglob * 128:(tglob + 1) * 128, :],
                                 RT[:, tcq * 1024:(tcq + 1) * 1024])
            sec(Y, y_ost)
        if (sub, 3) in tok_oev:
            state["hpw"][0] = tok_oev[(sub, 3)]
            state["hpw"][1] = tok_oev[(sub, 3)]
        CHK("p3s%d" % sub)

    def y_fin(y):
        for i in range(4):
            Y.wait(g_out[i].tok())
        if g_dbg.count:
            Y.wait(g_dbg.tok())
    sec(Y, y_fin)

    st.close()
    return nc


_NC_CACHE = {}


def kernel(**inputs):
    if "main" not in _NC_CACHE:
        _NC_CACHE["main"] = build()
    nc = _NC_CACHE["main"]
    x = np.ascontiguousarray(np.asarray(inputs["L0_attn_x"], dtype=np.float32)[0])
    base = {
        "Wq": inputs["L0_attn_Wq"], "bq": inputs["L0_attn_bq"],
        "Wk": inputs["L0_attn_Wk"], "bk": inputs["L0_attn_bk"],
        "Wv": inputs["L0_attn_Wv"], "bv": inputs["L0_attn_bv"],
        "Wo": inputs["L0_attn_Wo"], "bo": inputs["L0_attn_bo"],
        "g1": inputs["L0_attn_g1"], "be1": inputs["L0_attn_be1"],
        "W1": inputs["L0_attn_W1"], "b1": inputs["L0_attn_b1"],
        "W2": inputs["L0_attn_W2"], "b2": inputs["L0_attn_b2"],
        "g2": inputs["L0_attn_g2"], "be2": inputs["L0_attn_be2"],
        "Wr": inputs["L0_ffn_Wr"], "W1m": inputs["L0_ffn_W1"],
        "b1m": inputs["L0_ffn_b1"], "W2m": inputs["L0_ffn_W2"],
        "b2m": inputs["L0_ffn_b2"],
    }
    base = {k: np.ascontiguousarray(np.asarray(v, dtype=np.float32)) for k, v in base.items()}
    in_maps = []
    for i in range(NCORES):
        m = dict(base)
        m["x"] = np.ascontiguousarray(x[i * TC:(i + 1) * TC, :])
        in_maps.append(m)
    res = run_bass_kernel_spmd(nc, in_maps, core_ids=list(range(NCORES)))
    out = np.concatenate([np.asarray(r["out"]) for r in res.results], axis=0)
    return out.reshape(1, S, D)


# version 16
# speedup vs baseline: 1.0537x; 1.0370x over previous
import contextlib
import math
import os

import numpy as np

import concourse.bass as bass
import concourse.mybir as mybir
from concourse.bass_utils import run_bass_kernel_spmd

F32 = mybir.dt.float32
BF16 = mybir.dt.bfloat16
I32 = mybir.dt.int32
AF = mybir.ActivationFunctionType
ALU = mybir.AluOpType
AX = mybir.AxisListType

D = 1024
NH = 16
FF = 4096
NE = 8
S = 16384
NCORES = 8
TC = S // NCORES
NCH = D // 128
TCH = TC // 128
HALF = TC // 2
NSUB = 4
TSUB = TC // NSUB
NFS = 4
FSW = FF // NFS
EPS_IN = 1e-12
EPS_OUT = 1e-5
LN8 = math.log(8.0)

RO_S0, RO_S1, RO_G1, RO_G2 = 0, D, 2 * D, 3 * D
S_BQ, S_G1, S_BE1, S_G2, S_BE2 = range(5)


class Eng:
    def __init__(self, nc, stack, name):
        self.e = getattr(nc, name)
        self.name = name
        self.sem = stack.enter_context(nc.semaphore("sf_" + name))
        self.n = 0
        self.seen = {}

    def tok(self, ins):
        self.n += 1
        ins.then_inc(self.sem, 1)
        return (self.sem, self.n)

    def wait(self, *toks):
        for t in toks:
            if t is None:
                continue
            sem, val = t
            k = id(sem)
            if val > self.seen.get(k, 0):
                self.e.wait_ge(sem, val)
                self.seen[k] = val


class DmaGrp:
    def __init__(self, nc, stack, name):
        self.sem = stack.enter_context(nc.semaphore(name))
        self.count = 0

    def start(self, eng, out, in_, **kw):
        ins = eng.e.dma_start(out, in_, **kw)
        self.count += 16
        ins.then_inc(self.sem, 16)
        return ins

    def tok(self):
        return (self.sem, self.count)


class _StopBuild(Exception):
    pass


def build(debug=(), stop_after=None):
    nc = bass.Bass("TRN2", target_bir_lowering=False)

    x_h = nc.dram_tensor("x", [TC, D], F32, kind="ExternalInput").ap()
    Wq_h = nc.dram_tensor("Wq", [D, D], F32, kind="ExternalInput").ap()
    bq_h = nc.dram_tensor("bq", [D], F32, kind="ExternalInput").ap()
    Wk_h = nc.dram_tensor("Wk", [D, D], F32, kind="ExternalInput").ap()
    bk_h = nc.dram_tensor("bk", [D], F32, kind="ExternalInput").ap()
    Wv_h = nc.dram_tensor("Wv", [D, D], F32, kind="ExternalInput").ap()
    bv_h = nc.dram_tensor("bv", [D], F32, kind="ExternalInput").ap()
    Wo_h = nc.dram_tensor("Wo", [D, D], F32, kind="ExternalInput").ap()
    bo_h = nc.dram_tensor("bo", [D], F32, kind="ExternalInput").ap()
    g1_h = nc.dram_tensor("g1", [D], F32, kind="ExternalInput").ap()
    be1_h = nc.dram_tensor("be1", [D], F32, kind="ExternalInput").ap()
    W1_h = nc.dram_tensor("W1", [D, FF], F32, kind="ExternalInput").ap()
    b1_h = nc.dram_tensor("b1", [FF], F32, kind="ExternalInput").ap()
    W2_h = nc.dram_tensor("W2", [FF, D], F32, kind="ExternalInput").ap()
    b2_h = nc.dram_tensor("b2", [D], F32, kind="ExternalInput").ap()
    g2_h = nc.dram_tensor("g2", [D], F32, kind="ExternalInput").ap()
    be2_h = nc.dram_tensor("be2", [D], F32, kind="ExternalInput").ap()
    Wr_h = nc.dram_tensor("Wr", [D, NE], F32, kind="ExternalInput").ap()
    W1m_h = nc.dram_tensor("W1m", [NE, D, FF], F32, kind="ExternalInput").ap()
    b1m_h = nc.dram_tensor("b1m", [NE, FF], F32, kind="ExternalInput").ap()
    W2m_h = nc.dram_tensor("W2m", [NE, FF, D], F32, kind="ExternalInput").ap()
    b2m_h = nc.dram_tensor("b2m", [NE, D], F32, kind="ExternalInput").ap()
    out_h = nc.dram_tensor("out", [TC, D], F32, kind="ExternalOutput").ap()
    xfm_d = nc.dram_tensor("xfm_d", [NCH, 128, TC], F32)
    W1sel = [nc.dram_tensor("W1sel%d" % e, [D, FF], BF16) for e in range(2)]
    W2sel = [nc.dram_tensor("W2sel%d" % e, [FF, D], BF16) for e in range(2)]
    cc1i = nc.dram_tensor("cc1i", [128, NH * 64], F32)
    cc1o = nc.dram_tensor("cc1o", [128, NH * 64], F32)
    cc2i = nc.dram_tensor("cc2i", [1, NE], F32)
    cc2o = nc.dram_tensor("cc2o", [1, NE], F32)

    dbg_specs = {
        "d_xfm": ([NCH, 128, TC], F32), "d_eq": ([128, NCH * TC], BF16),
        "d_kv": ([128, NH * 64], F32), "d_ctx": ([128, NCH * TC], BF16),
        "d_attn": ([128, NCH * TSUB], F32), "d_x2b": ([128, NCH * TC], BF16),
        "d_lg": ([1, 64], F32), "d_rq": ([16, TC], F32),
        "d_z1": ([128, NCH * TSUB], F32), "d_kn": ([128, 2048], BF16),
    }
    dbg = {}
    for name in debug:
        shp, dt = dbg_specs[name]
        dbg[name] = nc.dram_tensor(name, shp, dt, kind="ExternalOutput").ap()

    st = contextlib.ExitStack()

    RS = st.enter_context(nc.sbuf_tensor("RS", [128, NCH * TSUB], F32))
    RT = st.enter_context(nc.sbuf_tensor("RT", [128, NCH * TSUB], F32))
    XS = st.enter_context(nc.sbuf_tensor("XS", [128, NCH * TSUB], F32))
    X2S = st.enter_context(nc.sbuf_tensor("X2S", [128, NCH * TSUB], F32))
    B1 = st.enter_context(nc.sbuf_tensor("B1", [128, NCH * TC], BF16))
    B2 = st.enter_context(nc.sbuf_tensor("B2", [128, NCH * TC], BF16))
    WFS = st.enter_context(nc.sbuf_tensor("WFS", [128, NCH * TC], BF16))
    SBF = st.enter_context(nc.sbuf_tensor("SBF", [128, 4096], BF16))
    SSB = st.enter_context(nc.sbuf_tensor("SSB", [128, 1024], BF16))
    KNV = st.enter_context(nc.sbuf_tensor("KNV", [128, 2048], BF16))
    LNR = st.enter_context(nc.sbuf_tensor("LNR", [1, 6 * TSUB], F32))
    IDN = st.enter_context(nc.sbuf_tensor("IDN", [128, 128], F32))
    KVB = st.enter_context(nc.sbuf_tensor("KVB", [128, NH * 64], BF16))
    BIA = st.enter_context(nc.sbuf_tensor("BIA", [128, 84], F32))
    B1F = st.enter_context(nc.sbuf_tensor("B1F", [128, 96], F32))
    ROWS = st.enter_context(nc.sbuf_tensor("ROWS", [1, 4 * D], BF16))
    ONESRB = st.enter_context(nc.sbuf_tensor("ONESRB", [1, 512], BF16))
    RTR = st.enter_context(nc.sbuf_tensor("RTR", [1, 64], F32))
    RTI = st.enter_context(nc.sbuf_tensor("RTI", [1, 2], I32))
    POOL = st.enter_context(nc.sbuf_tensor("POOL", [128, 32], F32))
    WEB = st.enter_context(nc.sbuf_tensor("WEB", [128, 2], F32))
    BRX = st.enter_context(nc.sbuf_tensor("BRX", [32, 128], F32))
    IND16 = st.enter_context(nc.sbuf_tensor("IND16", [128, 128], BF16))
    ONESC = st.enter_context(nc.sbuf_tensor("ONESC", [128, 1], BF16))
    ONESCF = st.enter_context(nc.sbuf_tensor("ONESCF", [128, 1], F32))
    ONESR = st.enter_context(nc.sbuf_tensor("ONESR", [1, 512], F32))
    WRFM = st.enter_context(nc.sbuf_tensor("WRFM", [128, NCH * NE], F32))

    PA = st.enter_context(nc.psum_tensor("PA", [128, 1024], F32))
    PB = st.enter_context(nc.psum_tensor("PB", [128, 1024], F32))
    PC = st.enter_context(nc.psum_tensor("PC", [128, 1024], F32))
    PD = st.enter_context(nc.psum_tensor("PD", [128, 1024], F32))

    ind16_np = np.zeros((128, 128), dtype=np.float32)
    ind16T_np = np.zeros((16, 8 * 128), dtype=np.float32)
    for c in range(8):
        for p in range(128):
            ind16_np[p, c * 16 + 2 * c + (p // 64)] = 1.0
            ind16T_np[2 * c + (p // 64), c * 128 + p] = 1.0
    identC = nc.inline_tensor(np.eye(128, dtype=np.float32), name="identC").ap()
    ind16C = nc.inline_tensor(ind16_np, name="ind16C").ap()
    ind16TC = nc.inline_tensor(ind16T_np, name="ind16TC").ap()
    ones128cC = nc.inline_tensor(np.ones((128, 1), np.float32), name="ones128cC").ap()
    ones1rC = nc.inline_tensor(np.ones((1, 512), np.float32), name="ones1rC").ap()
    iota8C = nc.inline_tensor(np.arange(8, dtype=np.float32).reshape(1, 8), name="iota8C").ap()

    blk = st.enter_context(nc.Block())
    T = Eng(nc, st, "tensor")
    V = Eng(nc, st, "vector")
    C = Eng(nc, st, "scalar")
    G = Eng(nc, st, "gpsimd")
    Y = Eng(nc, st, "sync")

    g_const = DmaGrp(nc, st, "g_const")
    g_cstG = DmaGrp(nc, st, "g_cstG")
    g_x = DmaGrp(nc, st, "g_x")
    g_xfm = [DmaGrp(nc, st, "g_xfmA"), DmaGrp(nc, st, "g_xfmB")]
    g_xs = DmaGrp(nc, st, "g_xs")
    g_wq = DmaGrp(nc, st, "g_wq")
    g_wkv = DmaGrp(nc, st, "g_wkv")
    g_bias = DmaGrp(nc, st, "g_bias")
    g_rows2 = DmaGrp(nc, st, "g_rows2")
    g_kv1 = DmaGrp(nc, st, "g_kv1")
    g_wfsA = DmaGrp(nc, st, "g_wfsA")
    g_wfsB = DmaGrp(nc, st, "g_wfsB")
    g_wroA = DmaGrp(nc, st, "g_wroA")
    g_wroB = DmaGrp(nc, st, "g_wroB")
    g_lg = DmaGrp(nc, st, "g_lg")
    g_moeb = DmaGrp(nc, st, "g_moeb")
    g_wsel = [DmaGrp(nc, st, "g_wsel%d" % i) for i in range(8)]
    g_out = [DmaGrp(nc, st, "g_out%d" % i) for i in range(4)]
    g_dbg = DmaGrp(nc, st, "g_dbg")
    cc_sem = st.enter_context(nc.semaphore("cc_sem"))
    reg0 = st.enter_context(nc.gpsimd.register("eidx0"))
    reg1 = st.enter_context(nc.gpsimd.register("eidx1"))
    yreg0 = st.enter_context(nc.sync.register("yidx0"))
    yreg1 = st.enter_context(nc.sync.register("yidx1"))

    stopped = [False]

    def sec(eng, fn):
        if stopped[0]:
            return
        getattr(blk, eng.name)(fn)

    def CHK(name):
        if stop_after == name:
            stopped[0] = True

    from concourse.hw_specs import get_activation_tables
    _tabs = list(get_activation_tables(nc.m.arch).items())
    SET_LNEXP = next(i for i, (n, fs_) in enumerate(_tabs)
                     if AF.Exp in fs_ and AF.Ln in fs_)
    SET_GELU = next(i for i, (n, fs_) in enumerate(_tabs) if AF.Gelu in fs_)

    def load_act_set(cc, idx):
        cc.add_instruction(mybir.InstLoadActFuncSet(
            name=nc.get_next_instruction_name(), act_func_set_id=idx,
            ins=[], outs=[]))

    def b1v(c, lo, n):
        return B1[:, c * TC + lo: c * TC + lo + n]

    def b2v(c, lo, n):
        return B2[:, c * TC + lo: c * TC + lo + n]

    def rsv(c):
        return RS[:, c * TSUB:(c + 1) * TSUB]

    def rtv(c):
        return RT[:, c * TSUB:(c + 1) * TSUB]

    def xsv(c):
        return XS[:, c * TSUB:(c + 1) * TSUB]

    def x2sv(c):
        return X2S[:, c * TSUB:(c + 1) * TSUB]

    rq_rows = RT[0:16, 3072:4096]
    IND16T = SBF[0:16, 0:1024]
    RQBF = RT[0:16, 2048:3072].bitcast(BF16)
    wro_slots = [KVB[:, 0:1024], SSB[:, 0:1024]]
    IOTA8 = RTR[0:1, 40:48]


    def c_pre(cc):
        load_act_set(cc, SET_GELU)
    sec(C, c_pre)

    tok_cbias = {}
    def v_cb(v):
        v.memset(BIA[:, 40:41], -LN8)
        v.memset(BIA[:, 41:42], EPS_IN)
        tok_cbias[0] = V.tok(v.memset(BIA[:, 42:43], EPS_OUT))
    sec(V, v_cb)

    def p0_y(y):
        g_const.start(Y, IDN[:], identC[:])
        g_const.start(Y, ONESCF[:], ones128cC[:])
        g_const.start(Y, ONESR[:], ones1rC[:])
        g_const.start(Y, IOTA8, iota8C[:])
        for c in range(NCH):
            g_const.start(Y, WRFM[:, c * NE:(c + 1) * NE], Wr_h[c * 128:(c + 1) * 128, :])
    sec(Y, p0_y)
    tok_constY = g_const.tok()

    def p0_g(g):
        g_cstG.start(G, IND16[:], ind16C[:])
        g_cstG.start(G, ONESC[:], ones128cC[:])
        g_cstG.start(G, ONESRB[:], ones1rC[:])
        for off, rsrc in [(RO_S0, bk_h), (RO_S1, bv_h), (RO_G1, g1_h), (RO_G2, g2_h)]:
            g_cstG.start(G, ROWS[0:1, off:off + D], rsrc.rearrange("d -> () d"))
        for hh in range(2):
            g_wq.start(G, WFS[:, hh * 4096:(hh + 1) * 4096].rearrange("p (k c) -> p k c", k=4),
                       Wq_h.rearrange("(k p) c -> p k c", p=128)[:, hh * 4:(hh + 1) * 4, :])
    sec(G, p0_g)

    tok_tr = {}
    tok_ev = {}
    tok_xbf = {}
    for t in range(TCH):
        xv = RS[:, (t % 2) * 1024:(t % 2) * 1024 + 1024]
        pw = PA if t % 2 == 0 else PB
        xslot = XS[:, (t % 2) * 2048:(t % 2) * 2048 + 1024]

        def y_load(y, t=t, xv=xv):
            if t >= 2:
                Y.wait(tok_tr[t - 2])
            g_x.start(Y, xv, x_h[t * 128:(t + 1) * 128, :])
        sec(Y, y_load)

        def t_tr(tt, t=t, xv=xv, pw=pw):
            T.wait(tok_constY)
            T.wait((g_x.sem, (t + 1) * 16))
            if t >= 2:
                T.wait(tok_ev[t - 2])
            for c in range(NCH):
                ins = tt.matmul(pw[:, c * 128:(c + 1) * 128],
                                xv[:, c * 128:(c + 1) * 128], IDN[:],
                                start=True, stop=True)
            tok_tr[t] = T.tok(ins)
        sec(T, t_tr)

        def c_ev(cc, t=t, pw=pw, xslot=xslot):
            C.wait(tok_tr[t])
            if t >= 2:
                C.wait((g_xfm[t % 2].sem, (t // 2) * 16))
            t1 = C.tok(cc.copy(xslot, pw[:]))
            C.wait(t1)
            ins = cc.copy(
                B1[:].rearrange("p (c n) -> p c n", c=NCH)[:, :, t * 128:(t + 1) * 128],
                xslot.rearrange("p (c n) -> p c n", c=NCH))
            tok_ev[t] = t1
            tok_xbf[t] = C.tok(ins)
        sec(C, c_ev)

        def y_xfm(y, t=t, xslot=xslot):
            Y.wait(tok_xbf[t])
            g_xfm[t % 2].start(Y, xfm_d[:, :, t * 128:(t + 1) * 128].rearrange("c p n -> p c n"),
                               xslot.rearrange("p (c n) -> p c n", c=NCH))
        sec(Y, y_xfm)

    if "d_xfm" in dbg:
        def y_d(y):
            Y.wait(g_xfm[0].tok())
            Y.wait(g_xfm[1].tok())
            g_dbg.start(Y, dbg["d_xfm"][:], xfm_d.ap())
        sec(Y, y_d)

    CHK("ingest")

    tok_biaT = {}
    tok_biaV = {}
    specs = [(bq_h, S_BQ), (g1_h, S_G1), (be1_h, S_BE1), (g2_h, S_G2), (be2_h, S_BE2)]
    for i, (src, slotc) in enumerate(specs):
        def g_ld(g, src=src, i=i):
            if i >= 1:
                G.wait(tok_biaT[i - 1])
            g_bias.start(G, BRX[0:8, :], src.rearrange("(a b) -> a b", b=128))
        sec(G, g_ld)

        def t_tp(tt, i=i):
            T.wait(tok_constY)
            T.wait((g_bias.sem, (i + 1) * 16))
            if i >= 2:
                T.wait(tok_biaV[i - 2])
            ins = tt.matmul(PC[:, (i % 2) * 512:(i % 2) * 512 + 8], BRX[0:8, :],
                            IDN[0:8, 0:8], start=True, stop=True)
            tok_biaT[i] = T.tok(ins)
        sec(T, t_tp)

        def v_ev(v, i=i, slotc=slotc):
            V.wait(tok_biaT[i])
            t0 = V.tok(v.tensor_copy(BIA[:, slotc * 8:(slotc + 1) * 8],
                                     PC[:, (i % 2) * 512:(i % 2) * 512 + 8]))
            if i == 0:
                V.wait(t0)
                t0 = V.tok(v.tensor_scalar(BIA[:, 44:52], BIA[:, 0:8], -1.0,
                                           None, ALU.mult))
            tok_biaV[i] = t0
        sec(V, v_ev)

    def g_ldb1(g):
        G.wait(tok_biaT[4])
        g_bias.start(G, BRX[:, :], b1_h.rearrange("(a b) -> a b", b=128))
    sec(G, g_ldb1)

    def t_tpb1(tt):
        T.wait(tok_constY)
        T.wait((g_bias.sem, 6 * 16))
        T.wait(tok_biaV[4])
        ins = tt.matmul(PC[:, 0:32], BRX[0:32, :], IDN[0:32, 0:32], start=True, stop=True)
        tok_biaT[5] = T.tok(ins)
    sec(T, t_tpb1)

    def v_evb1(v):
        V.wait(tok_biaT[5])
        tok_biaV[5] = V.tok(v.tensor_copy(B1F[:, 0:32], PC[:, 0:32]))
    sec(V, v_evb1)

    tok_rep = {}
    for j in range(2):
        def t_rep(tt, j=j):
            T.wait(g_const.tok())
            T.wait(g_cstG.tok())
            T.wait(tok_biaV[5] if j == 0 else tok_rep[0])
            for h in range(2):
                ins = tt.matmul(PD[:, h * 512:(h + 1) * 512], ONESRB[0:1, 0:128],
                                ROWS[0:1, j * D + h * 512: j * D + (h + 1) * 512],
                                start=True, stop=True)
            tok_rep[(j, "t")] = T.tok(ins)
        sec(T, t_rep)

        def v_rep(v, j=j):
            V.wait(tok_rep[(j, "t")])
            tok_rep[j] = V.tok(v.tensor_copy(RT[:, j * 1024:(j + 1) * 1024], PD[:]))
        sec(V, v_rep)
    tok_bias_done = tok_rep.get(1)

    def g_rows2f(g):
        G.wait(tok_rep[(1, "t")])
        g_rows2.start(G, ROWS[0:1, RO_S0:RO_S0 + D], bo_h.rearrange("d -> () d"))
        g_rows2.start(G, ROWS[0:1, RO_S1:RO_S1 + D], b2_h.rearrange("d -> () d"))
    sec(G, g_rows2f)

    CHK("bias")

    tok_qmm = {}
    tok_elur = {}
    tok_eluy = {}
    tok_eq = {}
    tok_qsq = {}
    tok_qss = {}
    tok_rqv = {}
    tok_rq = {}
    def emit_qmm(qi):
        half, c = qi // NCH, qi % NCH
        lo = half * HALF
        pq = (PA, PB, PD)[qi % 3]

        def t_qmm(tt, qi=qi, c=c, lo=lo, pq=pq):
            T.wait(g_wq.tok())
            T.wait(g_x.tok())
            T.wait(tok_xbf[TCH - 1])
            if qi == 2:
                T.wait(tok_bias_done)
            if qi >= 3:
                T.wait(tok_elur[qi - 3])
            for k in range(NCH):
                for nn in range(2):
                    ins = tt.matmul(pq[:, nn * 512:(nn + 1) * 512],
                                    WFS[:, k * 1024 + c * 128: k * 1024 + (c + 1) * 128],
                                    b1v(k, lo + nn * 512, 512),
                                    start=(k == 0), stop=(k == NCH - 1))
            tok_qmm[qi] = T.tok(ins)
        sec(T, t_qmm)

    def emit_qrest(qi):
        half, c = qi // NCH, qi % NCH
        lo = half * HALF
        pq = (PA, PB, PD)[qi % 3]

        def c_elu(cc, qi=qi, c=c, pq=pq):
            C.wait(tok_qmm[qi])
            C.wait(tok_bias_done)
            if qi >= 1:
                C.wait(tok_eq[qi - 1])
            t_, r_, d_, y_ = RS[:, 0:1024], RS[:, 1024:2048], RS[:, 2048:3072], RS[:, 3072:4096]
            t1 = C.tok(cc.activation(t_, pq[:], AF.Relu, scale=-1.0,
                                     bias=BIA[:, 44 + c:45 + c]))
            C.wait(t1)
            t2 = C.tok(cc.activation(t_, t_, AF.Tanh, scale=-0.5))
            tok_elur[qi] = C.tok(cc.activation(r_, pq[:], AF.Relu,
                                               bias=BIA[:, S_BQ * 8 + c:S_BQ * 8 + c + 1]))
            C.wait(t2)
            t3 = C.tok(cc.activation(d_, t_, AF.Copy, scale=-1.0, bias=1.0))
            C.wait(t3)
            tok_eluy[qi] = C.tok(cc.activation(y_, d_, AF.Copy, scale=-0.4854,
                                               bias=1.4571))
        sec(C, c_elu)

        def v_eq(v, qi=qi, c=c, lo=lo):
            V.wait(tok_eluy[qi])
            def stepi(ins):
                tk = V.tok(ins)
                V.wait(tk)
                return tk
            t_, r_, d_, y_ = RS[:, 0:1024], RS[:, 1024:2048], RS[:, 2048:3072], RS[:, 3072:4096]
            stepi(v.tensor_mul(d_, d_, y_))
            stepi(v.tensor_scalar(d_, d_, -1.0, 2.0, ALU.mult, ALU.add))
            stepi(v.tensor_mul(y_, y_, d_))
            stepi(v.scalar_tensor_tensor(d_, t_, 2.0, y_, ALU.mult, ALU.mult))
            ins = v.scalar_tensor_tensor(b2v(c, lo, HALF), d_, 1.0, r_,
                                         ALU.mult, ALU.add)
            tok_eq[qi] = V.tok(ins)
        sec(V, v_eq)

        def c_sq(cc, qi=qi, c=c, lo=lo):
            C.wait(tok_eq[qi])
            if qi >= 2:
                C.wait(tok_qss[qi - 2])
            sv = SBF[:, (qi % 2) * 1024:(qi % 2) * 1024 + 1024]
            tok_qsq[qi] = C.tok(cc.activation(sv, b2v(c, lo, HALF), AF.Square))
        sec(C, c_sq)

        def t_ss(tt, qi=qi, half=half, c=c):
            T.wait(tok_qsq[qi])
            T.wait(g_cstG.tok())
            if c == 0 and half == 1:
                T.wait(tok_rqv[0])
            sv = SBF[:, (qi % 2) * 1024:(qi % 2) * 1024 + 1024]
            for nn in range(2):
                ins = tt.matmul(PC[0:16, nn * 512:(nn + 1) * 512],
                                IND16[:, c * 16:(c + 1) * 16],
                                sv[:, nn * 512:(nn + 1) * 512],
                                start=(c == 0), stop=(c == NCH - 1))
            tok_qss[qi] = T.tok(ins)
        sec(T, t_ss)

        if c == NCH - 1:
            def v_rqev(v, half=half, qi=qi):
                V.wait(tok_qss[qi])
                tok_rqv[half] = V.tok(v.tensor_copy(rq_rows[:, 0:1024],
                                                    PC[0:16, 0:1024]))
            sec(V, v_rqev)

            def v_rq(v, half=half):
                V.wait(tok_rqv[half])
                def stepi(ins):
                    tk = V.tok(ins)
                    V.wait(tk)
                    return tk
                r = rq_rows[:, 0:1024]
                scrA = KNV[0:16, 0:2048].bitcast(F32)
                scrB = SBF[0:16, 2048:4096].bitcast(F32)
                stepi(v.tensor_scalar(scrB.bitcast(I32), r.bitcast(I32), 1, None,
                                      ALU.logical_shift_right))
                stepi(v.tensor_scalar(scrB.bitcast(I32), scrB.bitcast(I32), -1,
                                      0x5F3759DF, ALU.mult, ALU.add))
                stepi(v.tensor_scalar(scrA, r, -0.5, None, ALU.mult))
                for _ in range(3):
                    stepi(v.tensor_mul(r, scrB, scrB))
                    stepi(v.tensor_mul(r, r, scrA))
                    stepi(v.tensor_scalar(r, r, 1.5, None, ALU.add))
                    stepi(v.tensor_mul(scrB, scrB, r))
                tok_rq[half] = stepi(v.tensor_scalar(
                    RQBF[:, half * 1024:(half + 1) * 1024], scrB, 0.125, None, ALU.mult))
            sec(V, v_rq)


    for qi in range(2 * NCH):
        emit_qmm(qi)
        if qi >= 1:
            emit_qrest(qi - 1)
    emit_qrest(2 * NCH - 1)
    if "d_eq" in dbg:
        def y_d2(y):
            Y.wait(tok_eq[2 * NCH - 1])
            g_dbg.start(Y, dbg["d_eq"][:], B2[:])
        sec(Y, y_d2)
    if "d_rq" in dbg:
        def y_d2b(y):
            Y.wait(tok_rq[1])
            g_dbg.start(Y, dbg["d_rq"][:], rq_rows[:, :])
        sec(Y, y_d2b)

    CHK("p1q")

    def g_wkvload(g):
        G.wait(tok_qmm[2 * NCH - 1])
        for hh in range(2):
            g_wkv.start(G, WFS[:, 8192 + hh * 4096: 8192 + (hh + 1) * 4096].rearrange("p (k c) -> p k c", k=4),
                        Wk_h.rearrange("(k p) c -> p k c", p=128)[:, hh * 4:(hh + 1) * 4, :])
        for hh in range(2):
            g_wkv.start(G, WFS[:, hh * 4096:(hh + 1) * 4096].rearrange("p (k c) -> p k c", k=4),
                        Wv_h.rearrange("(k p) c -> p k c", p=128)[:, hh * 4:(hh + 1) * 4, :])
    sec(G, g_wkvload)

    tok_kmm = {}
    tok_kb = {}
    tok_kex = {}
    tok_keluy = {}
    tok_kel = {}
    tok_ksq = {}
    tok_red = {}
    tok_krs = {}
    tok_kn = {}
    tok_vev = {}
    tok_kvmm = {}
    tok_kvadd = {}
    KRD = POOL[:, 0:32]

    def emit_tkv(t):
        def t_kv(tt, t=t):
            T.wait(g_wkv.tok())
            if t == 0:
                T.wait(tok_rqv[1])
            if t >= 1:
                T.wait(tok_kb[t - 1])
                T.wait(tok_vev[t - 1])
            for k in range(NCH):
                xst = b1v(k, t * 128, 128)
                for nn in range(2):
                    tt.matmul(PA[:, nn * 512:(nn + 1) * 512], xst,
                              WFS[:, 8192 + k * 1024 + nn * 512: 8192 + k * 1024 + (nn + 1) * 512],
                              start=(k == 0), stop=(k == NCH - 1))
                for nn in range(2):
                    ins = tt.matmul(PB[:, nn * 512:(nn + 1) * 512], xst,
                                    WFS[:, k * 1024 + nn * 512: k * 1024 + (nn + 1) * 512],
                                    start=(k == 0), stop=(k == NCH - 1))
            tok_kmm[t] = T.tok(ins)
        sec(T, t_kv)

    KVSB = X2S[:, 0:1024]

    def emit_kvacc(t):
        def t_kvacc(tt, t=t):
            T.wait(tok_kn[t])
            T.wait(tok_vev[t])
            if t >= 2:
                T.wait(tok_kvadd[t - 2])
            for p in range(8):
                ins = tt.matmul(PC[:, p * 128:(p + 1) * 128],
                                KNV[:, p * 128:(p + 1) * 128],
                                KNV[:, 1024 + p * 128:1024 + (p + 1) * 128],
                                start=True, stop=True)
            tok_kvmm[t] = T.tok(ins)
        sec(T, t_kvacc)

        def v_kvadd(v, t=t):
            V.wait(tok_kvmm[t])
            if t == 0:
                ins = v.tensor_copy(KVSB, PC[:])
            else:
                ins = v.tensor_add(KVSB, KVSB, PC[:])
            tok_kvadd[t] = V.tok(ins)
        sec(V, v_kvadd)

    for t in range(TCH):
        emit_tkv(t)
        if t >= 1:
            emit_kvacc(t - 1)

        def v_kb(v, t=t):
            V.wait(tok_kmm[t])
            V.wait(tok_bias_done)
            if t >= 1:
                V.wait(tok_kex[t - 1])
            ins = v.scalar_tensor_tensor(RS[:, 0:1024], PA[:], 1.0,
                                         RT[:, 0:1024], ALU.mult, ALU.add)
            tok_kb[t] = V.tok(ins)
        sec(V, v_kb)

        def v_vev(v, t=t):
            if t >= 1:
                V.wait(tok_kvmm[t - 1])
            ins = v.scalar_tensor_tensor(KNV[:, 1024:2048], PB[:], 1.0,
                                         RT[:, 1024:2048], ALU.mult, ALU.add)
            tok_vev[t] = V.tok(ins)
        sec(V, v_vev)

        def c_kelu(cc, t=t):
            C.wait(tok_kb[t])
            t_, r_ = RS[:, 1024:2048], RS[:, 2048:3072]
            d_, y_ = RS[:, 3072:4096], RS[:, 0:1024]
            t1 = C.tok(cc.activation(t_, RS[:, 0:1024], AF.Relu, scale=-1.0))
            C.wait(t1)
            t2 = C.tok(cc.activation(t_, t_, AF.Tanh, scale=-0.5))
            tok_kex[t] = C.tok(cc.activation(r_, RS[:, 0:1024], AF.Relu))
            C.wait(t2)
            C.wait(tok_kex[t])
            t3 = C.tok(cc.activation(d_, t_, AF.Copy, scale=-1.0, bias=1.0))
            C.wait(t3)
            tok_keluy[t] = C.tok(cc.activation(y_, d_, AF.Copy, scale=-0.4854,
                                               bias=1.4571))
        sec(C, c_kelu)

        def v_ek(v, t=t):
            V.wait(tok_keluy[t])
            if t >= 1:
                V.wait(tok_kvmm[t - 1])
            def stepi(ins):
                tk = V.tok(ins)
                V.wait(tk)
                return tk
            t_, r_ = RS[:, 1024:2048], RS[:, 2048:3072]
            d_, y_ = RS[:, 3072:4096], RS[:, 0:1024]
            stepi(v.tensor_mul(d_, d_, y_))
            stepi(v.tensor_scalar(d_, d_, -1.0, 2.0, ALU.mult, ALU.add))
            stepi(v.tensor_mul(y_, y_, d_))
            stepi(v.scalar_tensor_tensor(d_, t_, 2.0, y_, ALU.mult, ALU.mult))
            ins = v.scalar_tensor_tensor(KNV[:, 0:1024], d_, 1.0, r_,
                                         ALU.mult, ALU.add)
            tok_kel[t] = V.tok(ins)
        sec(V, v_ek)

        def c_ksq(cc, t=t):
            C.wait(tok_kel[t])
            if t >= 1:
                C.wait(tok_red[t - 1])
            tok_ksq[t] = C.tok(cc.activation(SBF[:, 2048:3072], KNV[:, 0:1024], AF.Square))
        sec(C, c_ksq)

        def v_red(v, t=t):
            V.wait(tok_ksq[t])
            ins = v.reduce_sum(KRD[:, 0:16],
                               SBF[:, 2048:3072].rearrange("p (h d) -> p h d", h=16),
                               axis=AX.X)
            tok_red[t] = V.tok(ins)
        sec(V, v_red)

        def v_krs(v, t=t):
            V.wait(tok_red[t])
            def stepi(ins):
                tk = V.tok(ins)
                V.wait(tk)
                return tk
            r = KRD[:, 16:32]
            sA = BIA[:, 52:68]
            sB = BIA[:, 68:84]
            stepi(v.tensor_scalar(r.bitcast(I32), KRD[:, 0:16].bitcast(I32), 1,
                                  None, ALU.logical_shift_right))
            stepi(v.tensor_scalar(r.bitcast(I32), r.bitcast(I32), -1, 0x5F3759DF,
                                  ALU.mult, ALU.add))
            stepi(v.tensor_scalar(sA, KRD[:, 0:16], -0.5, None, ALU.mult))
            last = None
            for _ in range(2):
                stepi(v.tensor_mul(sB, r, r))
                stepi(v.tensor_mul(sB, sB, sA))
                stepi(v.tensor_scalar(sB, sB, 1.5, None, ALU.add))
                last = stepi(v.tensor_mul(r, r, sB))
            tok_krs[t] = last
        sec(V, v_krs)

        def v_kn(v, t=t):
            V.wait(tok_krs[t])
            ins = v.tensor_mul(KNV[:, 0:1024].rearrange("p (h d) -> p h d", h=16),
                               KNV[:, 0:1024].rearrange("p (h d) -> p h d", h=16),
                               KRD[:, 16:32].rearrange("p h -> p h ()").broadcast_to([128, 16, 64]))
            tok_kn[t] = V.tok(ins)
        sec(V, v_kn)
    emit_kvacc(TCH - 1)

    if "d_kn" in dbg:
        def y_dkn(y):
            Y.wait(tok_kn[TCH - 1])
            Y.wait(tok_vev[TCH - 1])
            g_dbg.start(Y, dbg["d_kn"][:], KNV[:])
        sec(Y, y_dkn)

    CHK("p1kv")

    tok_kvcp = {}
    def v_kvz(v):
        V.wait(tok_kvadd[TCH - 1])
        V.wait(tok_kex[TCH - 1])
        t1 = V.tok(v.memset(RT[:, 0:1024], 0.0))
        V.wait(t1)
        last = None
        for p in range(8):
            for hh in range(2):
                sl = slice(hh * 64, (hh + 1) * 64)
                last = v.tensor_copy(RT[sl, p * 128 + hh * 64: p * 128 + (hh + 1) * 64],
                                     KVSB[sl, p * 128 + hh * 64: p * 128 + (hh + 1) * 64])
        tok_kvcp[0] = V.tok(last)
    sec(V, v_kvz)

    def g_ar1(g):
        G.wait(tok_kvcp[0])
        g_kv1.start(G, cc1i.ap(), RT[:, 0:1024])
        g.wait_ge(g_kv1.sem, g_kv1.count)
        g.collective_compute("AllReduce", ALU.add,
                             replica_groups=[list(range(NCORES))],
                             ins=[cc1i.ap()], outs=[cc1o.ap()]).then_inc(cc_sem, 1)
        g.wait_ge(cc_sem, 1)
        g_kv1.start(G, RT[:, 0:1024], cc1o.ap())
    sec(G, g_ar1)

    tok_kvbf = {}
    def v_kvbf(v):
        V.wait(g_kv1.tok())
        tok_kvbf[0] = V.tok(v.tensor_copy(KVB[:], RT[:, 0:1024]))
    sec(V, v_kvbf)

    if "d_kv" in dbg:
        def y_d3(y):
            Y.wait(tok_kvbf[0])
            g_dbg.start(Y, dbg["d_kv"][:], RT[:, 0:1024])
        sec(Y, y_d3)

    CHK("ar1")

    def g_ind16t(g):
        G.wait(tok_red[TCH - 1])
        G.wait(tok_qss[2 * NCH - 1])
        g_cstG.start(G, IND16T[:], ind16TC[:])
    sec(G, g_ind16t)
    tok_ind16t = g_cstG.tok()

    tok_ctxv = {}
    prev_ctx = {0: None}
    for c in range(NCH):
        for half in range(2):
            lo = half * HALF

            def t_ctx(tt, c=c, half=half, lo=lo):
                T.wait(tok_rq[1])
                T.wait(tok_kvbf[0])
                T.wait(tok_ind16t)
                if prev_ctx[0] is not None:
                    T.wait(prev_ctx[0])
                for nn in range(2):
                    tt.matmul(PD[:, nn * 512:(nn + 1) * 512],
                              IND16T[:, c * 128:(c + 1) * 128],
                              RQBF[:, lo + nn * 512: lo + (nn + 1) * 512],
                              start=True, stop=True)
                for nn in range(2):
                    ins = tt.matmul(PA[:, nn * 512:(nn + 1) * 512],
                                    KVB[:, c * 128:(c + 1) * 128],
                                    b2v(c, lo + nn * 512, 512),
                                    start=True, stop=True)
                tok_ctxv[(c, half, "t")] = T.tok(ins)
            sec(T, t_ctx)

            def v_ctx(v, c=c, half=half, lo=lo):
                V.wait(tok_ctxv[(c, half, "t")])
                t1 = V.tok(v.tensor_copy(SSB[:], PD[:]))
                V.wait(t1)
                ins = v.tensor_mul(b1v(c, lo, HALF), PA[:], SSB[:])
                tok_ctxv[(c, half)] = V.tok(ins)
                prev_ctx[0] = tok_ctxv[(c, half)]
            sec(V, v_ctx)
    tok_ctx_done = prev_ctx[0]

    if "d_ctx" in dbg:
        def y_d4(y):
            Y.wait(tok_ctx_done)
            g_dbg.start(Y, dbg["d_ctx"][:], B1[:])
        sec(Y, y_d4)

    CHK("ctx")

    state = {
        "gelu_loaded": False,
        "lnlast": None,
        "statfree": tok_ctx_done,
        "accfree": None,
        "wslot": {},
        "hpw": {0: None, 1: None},
        "hread": {},
        "woslot": {0: None, 1: None},
    }

    def ln_block(src_region, dst_region, g_slot, be_slot, g_row_off, eps_col,
                 pre_tok, out_cast=None, pool_sub=None):
        if stopped[0]:
            return {NCH - 1: None, 0: None}, {}, {}
        tok_st = {}
        tok_zc = {}
        zslot_war = {0: None, 1: None}
        for c in range(NCH):
            def c_stats(cc, c=c):
                C.wait(pre_tok(c))
                if zslot_war[c % 2] is not None:
                    C.wait(zslot_war[c % 2])
                sl = (c % 2) * 1024
                cc.copy(SBF[:, sl:sl + 512], src_region(c))
                ins = cc.activation(SBF[:, sl + 512:sl + 1024], src_region(c), AF.Square)
                tok_zc[c] = C.tok(ins)
            sec(C, c_stats)

            def t_stats(tt, c=c):
                T.wait(tok_zc[c])
                T.wait(g_cstG.tok())
                if c == 0 and state["statfree"] is not None:
                    T.wait(state["statfree"])
                sl = (c % 2) * 1024
                tt.matmul(PC[0:1, 0:512], ONESC[:], SBF[:, sl:sl + 512],
                          start=(c == 0), stop=(c == NCH - 1))
                ins = tt.matmul(PC[0:1, 512:1024], ONESC[:], SBF[:, sl + 512:sl + 1024],
                                start=(c == 0), stop=(c == NCH - 1))
                tok_st[c] = T.tok(ins)
            sec(T, t_stats)
            zslot_war[c % 2] = tok_st[c]

        tok_var = {}
        def v_rows(v):
            V.wait(tok_st[NCH - 1])
            v.tensor_scalar(LNR[0:1, 0:TSUB], PC[0:1, 0:512], 1.0 / D, None, ALU.mult)
            t2 = V.tok(v.tensor_scalar(LNR[0:1, TSUB:2 * TSUB], PC[0:1, 512:1024], 1.0 / D, None, ALU.mult))
            V.wait(t2)
            t3 = V.tok(v.tensor_mul(LNR[0:1, 2 * TSUB:3 * TSUB], LNR[0:1, 0:TSUB], LNR[0:1, 0:TSUB]))
            V.wait(t3)
            tok_var[0] = V.tok(v.tensor_sub(LNR[0:1, TSUB:2 * TSUB], LNR[0:1, TSUB:2 * TSUB], LNR[0:1, 2 * TSUB:3 * TSUB]))
        sec(V, v_rows)
        state["statfree"] = tok_var[0]

        tok_rstd = {}
        r1 = LNR[0:1, TSUB:2 * TSUB]
        r2 = LNR[0:1, 2 * TSUB:3 * TSUB]
        r3 = LNR[0:1, 3 * TSUB:4 * TSUB]
        r4 = LNR[0:1, 4 * TSUB:5 * TSUB]
        eps_val = EPS_IN if eps_col == 41 else EPS_OUT

        def v_rstd(v):
            def stepi(ins):
                tk = V.tok(ins)
                V.wait(tk)
                return tk
            V.wait(tok_var[0])
            stepi(v.tensor_scalar(r1, r1, float(eps_val), None, ALU.add))
            stepi(v.tensor_scalar(r2, r1, -0.5, None, ALU.mult))
            stepi(v.tensor_scalar(r3.bitcast(I32), r1.bitcast(I32), 1, None,
                                  ALU.logical_shift_right))
            stepi(v.tensor_scalar(r3.bitcast(I32), r3.bitcast(I32), -1, 0x5F3759DF,
                                  ALU.mult, ALU.add))
            for _ in range(2):
                stepi(v.tensor_mul(r4, r3, r3))
                stepi(v.tensor_mul(r4, r4, r2))
                stepi(v.tensor_scalar(r4, r4, 1.5, None, ALU.add))
                stepi(v.tensor_mul(r3, r3, r4))
            tok_rstd[0] = V.tok(v.tensor_copy(r1, r3))
        sec(V, v_rstd)

        murstd_bf = LNR[0:1, 5 * TSUB:6 * TSUB].bitcast(BF16)[0:1, 0:TSUB]
        tok_mr = {}
        def v_mr(v):
            V.wait(tok_rstd[0])
            t1 = V.tok(v.tensor_mul(LNR[0:1, 0:TSUB], LNR[0:1, 0:TSUB], LNR[0:1, TSUB:2 * TSUB]))
            V.wait(t1)
            tok_mr[0] = V.tok(v.tensor_copy(murstd_bf, LNR[0:1, 0:TSUB]))
        sec(V, v_mr)

        tok_sb = {}
        def t_sb(tt):
            T.wait(tok_mr[0])
            if state["lnlast"] is not None:
                T.wait(state["lnlast"])
            tok_sb[0] = T.tok(tt.matmul(PD[:, 0:512], ONESR[0:1, 0:128], LNR[0:1, TSUB:2 * TSUB],
                                        start=True, stop=True))
        sec(T, t_sb)

        cast_toks = {}
        pool_toks = {}
        tok_mg = {}
        tok_app = {}
        mgb = [PD[:, 512:1024], PB[:, 0:512]]
        for c in range(NCH):
            def t_mg(tt, c=c):
                if c >= 2:
                    T.wait(tok_app[c - 2])
                if g_row_off is not None:
                    lhs = ROWS[0:1, g_row_off + c * 128: g_row_off + (c + 1) * 128]
                else:
                    lhs = ONESRB[0:1, 0:128]
                tok_mg[c] = T.tok(tt.matmul(mgb[c % 2], lhs, murstd_bf,
                                            start=True, stop=True))
            sec(T, t_mg)

            def v_app(v, c=c):
                V.wait(tok_sb[0])
                V.wait(tok_mg[c])
                if g_slot is not None:
                    t1 = V.tok(v.scalar_tensor_tensor(
                        src_region(c), src_region(c),
                        BIA[:, g_slot * 8 + c:g_slot * 8 + c + 1],
                        PD[:, 0:512], ALU.mult, ALU.mult))
                else:
                    t1 = V.tok(v.scalar_tensor_tensor(
                        src_region(c), src_region(c), 1.0,
                        PD[:, 0:512], ALU.mult, ALU.mult))
                V.wait(t1)
                if be_slot is not None:
                    ins = v.scalar_tensor_tensor(
                        dst_region(c), src_region(c),
                        BIA[:, be_slot * 8 + c:be_slot * 8 + c + 1],
                        mgb[c % 2], ALU.add, ALU.subtract)
                else:
                    ins = v.scalar_tensor_tensor(
                        dst_region(c), src_region(c), 0.0,
                        mgb[c % 2], ALU.add, ALU.subtract)
                tok_app[c] = V.tok(ins)
            sec(V, v_app)

            if out_cast is not None:
                def c_cast(cc, c=c):
                    C.wait(tok_app[c])
                    cast_toks[c] = C.tok(cc.copy(out_cast(c), dst_region(c)))
                sec(C, c_cast)
            if pool_sub is not None:
                def v_pool(v, c=c):
                    V.wait(tok_app[c])
                    ins = v.reduce_sum(POOL[:, c * 4 + pool_sub:c * 4 + pool_sub + 1],
                                       dst_region(c), axis=AX.X)
                    pool_toks[c] = V.tok(ins)
                sec(V, v_pool)
        state["lnlast"] = tok_app[NCH - 1]
        return tok_app, cast_toks, pool_toks

    def ff_block(in_wait_toks, in_cols, b1col0, seed_region, dst_region,
                 scale, scale_wait, w1_src_fn, w2_src_fn, b2_row_off, wslot_key,
                 slot_extra=None, dst_wait=None, fs_wait=None):
        ev_toks = {}
        if stopped[0]:
            return {(NFS - 1, 0): None, (NFS - 1, 1): None}
        for fs in range(NFS):
            slot = fs % 2
            w1v = (WFS if slot == 0 else B2)[:, 0:8 * FSW]
            w2v = (WFS if slot == 0 else B2)[:, 8 * FSW:16 * FSW]
            grp = g_wfsA if slot == 0 else g_wfsB

            def g_wload(g, fs=fs, slot=slot, grp=grp, w1v=w1v, w2v=w2v):
                if state["wslot"].get((wslot_key, slot)) is not None:
                    G.wait(state["wslot"][(wslot_key, slot)])
                if slot_extra:
                    for tk in slot_extra.get(slot, ()):
                        G.wait(tk)
                if fs_wait is not None:
                    G.wait(fs_wait(fs))
                w1_src = w1_src_fn()
                w2_src = w2_src_fn()
                for hh in range(2):
                    grp.start(G, w1v.rearrange("p (k c) -> p k c", k=8)[:, hh * 4:(hh + 1) * 4, :],
                              w1_src.rearrange("(k p) c -> p k c", p=128)[:, hh * 4:(hh + 1) * 4,
                                                                          fs * FSW:(fs + 1) * FSW])
                for hh in range(2):
                    grp.start(G, w2v.rearrange("p (f c) -> p f c", f=8)[:, hh * 4:(hh + 1) * 4, :],
                              w2_src.rearrange("(f p) c -> p f c", p=128)[:, fs * 8 + hh * 4: fs * 8 + (hh + 1) * 4, :])
            sec(G, g_wload)
            ld_tok = (grp.sem, grp.count)

            h_toks = {}
            ffstage = int(os.environ.get("KERN_FFSTAGE", "9"))
            for f in range(8 if ffstage >= 1 else 0):
                fg = fs * 8 + f

                def t_w1(tt, f=f, ld_tok=ld_tok, w1v=w1v):
                    T.wait(ld_tok)
                    if state["hpw"][f % 2] is not None:
                        T.wait(state["hpw"][f % 2])
                    ph = PA[:, (f % 2) * 512:(f % 2) * 512 + 512]
                    for k in range(NCH):
                        if in_wait_toks:
                            T.wait(in_wait_toks[k])
                        ins = tt.matmul(ph, w1v[:, k * FSW + f * 128: k * FSW + (f + 1) * 128],
                                        in_cols(k), start=(k == 0), stop=(k == NCH - 1))
                    h_toks[(f, "mm")] = T.tok(ins)
                sec(T, t_w1)

                if ffstage < 2:
                    h_toks[f] = h_toks[(f, "mm")]
                    continue

                def c_gelu(cc, f=f, fg=fg):
                    C.wait(h_toks[(f, "mm")])
                    if state["hread"].get(f) is not None:
                        C.wait(state["hread"][f])
                    ph = PA[:, (f % 2) * 512:(f % 2) * 512 + 512]
                    ins = cc.activation(SBF[:, f * 512:(f + 1) * 512], ph, AF.Gelu,
                                        bias=B1F[:, b1col0 + fg:b1col0 + fg + 1])
                    h_toks[f] = C.tok(ins)
                    state["hpw"][f % 2] = h_toks[f]
                sec(C, c_gelu)

            accv = [PC[:, 0:512], PC[:, 512:1024], PD[:, 0:512], PD[:, 512:1024]]
            tok_w2 = {}
            for oh in range(2 if ffstage >= 3 else 0):
                for f in range(8):
                    def t_w2(tt, f=f, oh=oh, fs=fs, w2v=w2v):
                        T.wait(h_toks[f])
                        if f == 0 and oh == 0:
                            if state["accfree"] is not None:
                                T.wait(state["accfree"])
                            if state["statfree"] is not None:
                                T.wait(state["statfree"])
                            if state["lnlast"] is not None:
                                T.wait(state["lnlast"])
                        if f == 0 and oh == 1:
                            T.wait(ev_toks[(fs, 0)])
                        last_f = (f == 7)
                        fold = (b2_row_off is not None and fs == 0
                                and not os.environ.get("KERN_NOFOLD"))
                        for ot in range(4):
                            og = oh * 4 + ot
                            ins = tt.matmul(accv[ot],
                                            w2v[:, f * 1024 + og * 128: f * 1024 + (og + 1) * 128],
                                            SBF[:, f * 512:(f + 1) * 512],
                                            start=(f == 0), stop=(last_f and not fold))
                        if last_f and fold:
                            for ot in range(4):
                                og = oh * 4 + ot
                                ins = tt.matmul(
                                    accv[ot],
                                    ROWS[0:1, b2_row_off + og * 128: b2_row_off + (og + 1) * 128],
                                    ONESRB[0:1, 0:512], start=False, stop=True)
                        tok_w2[(oh, f)] = T.tok(ins)
                        state["hread"][f] = tok_w2[(oh, f)]
                    sec(T, t_w2)

                if ffstage < 4:
                    ev_toks[(fs, oh)] = tok_w2[(oh, 7)]
                    continue

                def v_acc(v, oh=oh, fs=fs):
                    V.wait(tok_w2[(oh, 7)])
                    if scale_wait is not None:
                        V.wait(scale_wait)
                    if dst_wait is not None:
                        V.wait(dst_wait)
                    last = None
                    for ot in range(4):
                        oc = oh * 4 + ot
                        if fs == 0 and (seed_region is not None):
                            last = v.scalar_tensor_tensor(dst_region(oc), accv[ot], scale,
                                                          seed_region(oc), ALU.mult, ALU.add)
                        elif os.environ.get("KERN_ACC_COPY"):
                            last = v.tensor_copy(dst_region(oc), accv[ot])
                        elif os.environ.get("KERN_ACC_SCRATCH"):
                            last = v.scalar_tensor_tensor(X2S[:, oc * TSUB:(oc + 1) * TSUB],
                                                          accv[ot], scale,
                                                          dst_region(oc), ALU.mult, ALU.add)
                        else:
                            last = v.scalar_tensor_tensor(dst_region(oc), accv[ot], scale,
                                                          dst_region(oc), ALU.mult, ALU.add)
                    ev_toks[(fs, oh)] = V.tok(last)
                sec(V, v_acc)
            state["accfree"] = ev_toks.get((fs, 1))
            state["wslot"][(wslot_key, slot)] = (tok_w2.get((1, 7)) or h_toks.get(7)
                                                 or h_toks.get((7, "mm")) or ld_tok)
        if not ev_toks:
            ev_toks = {(NFS - 1, 0): None, (NFS - 1, 1): None}
        for fs2 in range(NFS):
            for oh2 in range(2):
                ev_toks.setdefault((fs2, oh2), None)
        return ev_toks

    tok_x2cast = {}
    tok_pool_all = {}
    tok_zo_prev = {0: None}
    tok_xs = {}
    for sub in range(NSUB):
        slo = sub * TSUB

        def y_xs(y, sub=sub, slo=slo):
            Y.wait(g_xfm[0].tok())
            Y.wait(g_xfm[1].tok())
            Y.wait(tok_xbf[TCH - 1])
            if tok_zo_prev[0] is not None:
                Y.wait(tok_zo_prev[0])
            g_xs.start(Y, XS[:].rearrange("p (c n) -> p c n", c=NCH),
                       xfm_d[:, :, slo:slo + TSUB].rearrange("c p n -> p c n"))
            tok_xs[sub] = g_xs.tok()
        sec(Y, y_xs)
        CHK("xs%d" % sub)

        tok_wo = {}
        for e in range(NCH):
            slot = e % 2
            grp = g_wroA if slot == 0 else g_wroB

            def g_wo(g, e=e, slot=slot, grp=grp):
                if state["woslot"][slot] is not None:
                    G.wait(state["woslot"][slot])
                else:
                    G.wait(tok_ctx_done)
                grp.start(G, wro_slots[slot], Wo_h[e * 128:(e + 1) * 128, :])
            sec(G, g_wo)
            ld_tok = (grp.sem, grp.count)

            def t_wo(tt, e=e, slot=slot, ld_tok=ld_tok, slo=slo):
                T.wait(ld_tok)
                if e == 0:
                    if state["accfree"] is not None:
                        T.wait(state["accfree"])
                    if state["statfree"] is not None:
                        T.wait(state["statfree"])
                    if state["lnlast"] is not None:
                        T.wait(state["lnlast"])
                    T.wait(tok_ctx_done)
                for o in range(NCH):
                    pbank = [PA, PB, PC, PD][o // 2][:, (o % 2) * 512:(o % 2) * 512 + 512]
                    ins = tt.matmul(pbank, wro_slots[slot][:, o * 128:(o + 1) * 128],
                                    b1v(e, slo, TSUB), start=(e == 0), stop=False)
                tok_wo[e] = T.tok(ins)
                state["woslot"][slot] = tok_wo[e]
            sec(T, t_wo)

        def t_wof(tt, sub=sub):
            T.wait(g_rows2.tok())
            for o in range(NCH):
                pbank = [PA, PB, PC, PD][o // 2][:, (o % 2) * 512:(o % 2) * 512 + 512]
                ins = tt.matmul(pbank, ROWS[0:1, RO_S0 + o * 128: RO_S0 + (o + 1) * 128],
                                ONESRB[0:1, 0:512], start=False, stop=True)
            tok_wo["fold"] = T.tok(ins)
        sec(T, t_wof)
        CHK("wo%d" % sub)

        tok_z1 = {}
        def v_z1(v, sub=sub, slo=slo):
            V.wait(tok_wo["fold"])
            V.wait(tok_xs[sub])
            for o in range(NCH):
                pbank = [PA, PB, PC, PD][o // 2][:, (o % 2) * 512:(o % 2) * 512 + 512]
                ins = v.scalar_tensor_tensor(rsv(o), pbank, 1.0,
                                             xsv(o), ALU.mult, ALU.add)
                tok_z1[o] = V.tok(ins)
        sec(V, v_z1)
        CHK("z1%d" % sub)
        state["accfree"] = tok_z1.get(NCH - 1)
        state["statfree"] = tok_z1.get(NCH - 1)

        if "d_z1" in dbg and sub == 0:
            def y_dz1(y):
                Y.wait(tok_z1[NCH - 1])
                g_dbg.start(Y, dbg["d_z1"][:], RS[:])
            sec(Y, y_dz1)

        app1, cast1, _ = ln_block(
            rsv, rsv, S_G1, S_BE1, RO_G1, 41,
            pre_tok=lambda c: tok_z1.get(c),
            out_cast=lambda c, slo=slo: b1v(c, slo, TSUB))
        CHK("ln1%d" % sub)

        if "d_attn" in dbg and sub == 0:
            def y_dat(y):
                Y.wait(app1[NCH - 1])
                g_dbg.start(Y, dbg["d_attn"][:], RS[:])
            sec(Y, y_dat)

        ff1 = ff_block(cast1, lambda k, slo=slo: b1v(k, slo, TSUB), 0,
                       rsv, rtv, 1.0, None, (lambda: W1_h), (lambda: W2_h), RO_S1, "ff",
                       slot_extra=({0: [tok_kmm.get(TCH - 1)],
                                    1: [tok_ctx_done]} if sub == 0 else None))
        CHK("ff%d" % sub)

        app2, _, _ = ln_block(
            rtv, rtv, S_G2, S_BE2, RO_G2, 41,
            pre_tok=lambda c: ff1.get((NFS - 1, 1)))
        CHK("ln2%d" % sub)

        tok_zo = {}
        def v_zo(v, sub=sub, slo=slo):
            for c in range(NCH):
                V.wait(app2[c])
                ins = v.tensor_add(rtv(c), rtv(c), xsv(c))
                tok_zo[c] = V.tok(ins)
        sec(V, v_zo)
        tok_zo_prev[0] = tok_zo.get(NCH - 1)

        app3, cast3, pools = ln_block(
            rtv, rtv, None, None, None, 42,
            pre_tok=lambda c: tok_zo[c],
            out_cast=lambda c, slo=slo: b1v(c, slo, TSUB),
            pool_sub=sub)
        for c in range(NCH):
            tok_x2cast[(sub, c)] = cast3.get(c)
            tok_pool_all[(sub, c)] = pools.get(c)
        CHK("p2s%d" % sub)

    if "d_x2b" in dbg:
        def y_dx2(y):
            for c in range(NCH):
                Y.wait(tok_x2cast[(NSUB - 1, c)])
            g_dbg.start(Y, dbg["d_x2b"][:], B1[:])
        sec(Y, y_dx2)

    CHK("p2")

    tok_lgT = {}
    def t_lg(tt):
        for sub in range(NSUB):
            for c in range(NCH):
                T.wait(tok_pool_all[(sub, c)])
        T.wait(state["statfree"])
        n = 0
        for c in range(NCH):
            for sub in range(NSUB):
                ins = tt.matmul(PB[0:1, 0:8], POOL[:, c * 4 + sub:c * 4 + sub + 1],
                                WRFM[:, c * NE:(c + 1) * NE],
                                start=(n == 0), stop=(n == 31))
                n += 1
        tok_lgT[0] = T.tok(ins)
    sec(T, t_lg)

    tok_rtr = {}
    def v_lg(v):
        V.wait(tok_lgT[0])
        tok_rtr[0] = V.tok(v.tensor_scalar(RTR[0:1, 0:8], PB[0:1, 0:8],
                                           1.0 / S, None, ALU.mult))
    sec(V, v_lg)

    def g_ar2(g):
        G.wait(tok_rtr[0])
        g_lg.start(G, cc2i.ap(), RTR[0:1, 0:8])
        g.wait_ge(g_lg.sem, g_lg.count)
        g.collective_compute("AllReduce", ALU.add,
                             replica_groups=[list(range(NCORES))],
                             ins=[cc2i.ap()], outs=[cc2o.ap()]).then_inc(cc_sem, 1)
        g.wait_ge(cc_sem, 2)
        g_lg.start(G, RTR[0:1, 0:8], cc2o.ap())
    sec(G, g_ar2)

    tok_rt2 = {}
    def v_rt(v):
        V.wait(g_lg.tok())
        V.wait(g_const.tok())
        def step(ins):
            tk = V.tok(ins)
            V.wait(tk)
            return tk
        step(v.reduce_max(RTR[0:1, 32:33], RTR[0:1, 0:8], axis=AX.X))
        step(v.tensor_tensor(RTR[0:1, 8:16], RTR[0:1, 0:8],
                             RTR[0:1, 32:33].broadcast_to([1, 8]), ALU.is_ge))
        step(v.tensor_scalar(RTR[0:1, 16:24], RTR[0:1, 8:16], -1e30, None, ALU.mult))
        step(v.tensor_add(RTR[0:1, 16:24], RTR[0:1, 16:24], RTR[0:1, 0:8]))
        step(v.reduce_max(RTR[0:1, 33:34], RTR[0:1, 16:24], axis=AX.X))
        step(v.tensor_tensor(RTR[0:1, 16:24], RTR[0:1, 16:24],
                             RTR[0:1, 33:34].broadcast_to([1, 8]), ALU.is_ge))
        step(v.tensor_mul(RTR[0:1, 24:32], RTR[0:1, 8:16], IOTA8))
        step(v.reduce_sum(RTR[0:1, 38:39], RTR[0:1, 24:32], axis=AX.X))
        step(v.tensor_mul(RTR[0:1, 24:32], RTR[0:1, 16:24], IOTA8))
        step(v.reduce_sum(RTR[0:1, 39:40], RTR[0:1, 24:32], axis=AX.X))
        ins = v.tensor_sub(RTR[0:1, 34:35], RTR[0:1, 33:34], RTR[0:1, 32:33])
        tok_rt2[0] = V.tok(ins)
    sec(V, v_rt)

    def v_rtexp(v):
        V.wait(tok_rt2[0])
        u = RTR[0:1, 40:41]
        p = RTR[0:1, 35:36]
        def stepi(ins):
            tk = V.tok(ins)
            V.wait(tk)
            return tk
        stepi(v.tensor_scalar(u, RTR[0:1, 34:35], -24.0, None, ALU.max))
        stepi(v.tensor_scalar(u, u, 1.0 / 64.0, None, ALU.mult))
        stepi(v.tensor_scalar(p, u, 0.25, 1.0, ALU.mult, ALU.add))
        stepi(v.tensor_mul(p, p, u))
        stepi(v.tensor_scalar(p, p, 1.0 / 3.0, 1.0, ALU.mult, ALU.add))
        stepi(v.tensor_mul(p, p, u))
        stepi(v.tensor_scalar(p, p, 0.5, 1.0, ALU.mult, ALU.add))
        stepi(v.tensor_mul(p, p, u))
        stepi(v.tensor_scalar(p, p, 1.0, 1.0, ALU.mult, ALU.add))
        for _ in range(5):
            stepi(v.tensor_mul(p, p, p))
        tok_rt2[1] = V.tok(v.tensor_mul(p, p, p))
    sec(V, v_rtexp)

    def v_rt2(v):
        V.wait(tok_rt2[1])
        def step(ins):
            tk = V.tok(ins)
            V.wait(tk)
            return tk
        step(v.tensor_scalar(RTR[0:1, 36:37], RTR[0:1, 35:36], 1.0, None, ALU.add))
        step(v.reciprocal(RTR[0:1, 36:37], RTR[0:1, 36:37]))
        ins = v.tensor_scalar(RTR[0:1, 37:38], RTR[0:1, 36:37], -1.0, 1.0,
                              ALU.mult, ALU.add)
        tok_rt2[2] = V.tok(ins)
    sec(V, v_rt2)

    if "d_lg" in dbg:
        def y_dlg(y):
            Y.wait(tok_rt2[2])
            g_dbg.start(Y, dbg["d_lg"][:], RTR[0:1, :])
        sec(Y, y_dlg)

    tok_web = {}
    def t_web(tt):
        T.wait(tok_rt2[2])
        tt.matmul(PB[:, 0:1], ONESR[0:1, 0:128], RTR[0:1, 36:37], start=True, stop=True)
        ins = tt.matmul(PB[:, 512:513], ONESR[0:1, 0:128], RTR[0:1, 37:38],
                        start=True, stop=True)
        tok_web[0] = T.tok(ins)
    sec(T, t_web)

    def v_web(v):
        V.wait(tok_web[0])
        v.tensor_copy(WEB[:, 0:1], PB[:, 0:1])
        tok_web[1] = V.tok(v.tensor_copy(WEB[:, 1:2], PB[:, 512:513]))
    sec(V, v_web)

    tok_rti = {}
    def c_rti(cc):
        C.wait(tok_rt2[2])
        tok_rti[0] = C.tok(cc.copy(RTI[0:1, 0:2], RTR[0:1, 38:40]))
    sec(C, c_rti)

    CHK("router")

    iv = [None, None]
    def g_snap(g):
        G.wait(tok_rti[0])
        g.reg_load(reg0, RTI[0:1, 0:1])
        g.reg_load(reg1, RTI[0:1, 1:2])
        iv[0] = g.snap(reg0, min_val=0, max_val=NE - 1)
        iv[1] = g.snap(reg1, min_val=0, max_val=NE - 1)
    sec(G, g_snap)



    tok_moeb = {}
    for es in range(2):
        def g_mb(g, es=es):
            if es == 0:
                G.wait(state["woslot"][0])
                G.wait(state["woslot"][1])
                G.wait(state["wslot"][("ff", 0)])
                G.wait(state["wslot"][("ff", 1)])
            else:
                G.wait(tok_moeb[(0, "v")])
            roff = RO_S0 if es == 0 else RO_S1
            g_moeb.start(G, ROWS[0:1, roff:roff + D],
                         b2m_h[iv[es]].rearrange("d -> () d"))
            g_moeb.start(G, BRX[:, :], b1m_h[iv[es]].rearrange("(a b) -> a b", b=128))
        sec(G, g_mb)
        ld = (g_moeb.sem, (es + 1) * 32)

        def t_mb(tt, es=es, ld=ld):
            T.wait(ld)
            T.wait(tok_web[1])
            if state["statfree"] is not None:
                T.wait(state["statfree"])
            ins = tt.matmul(PB[:, 0:32] if es == 0 else PB[:, 512:544],
                            BRX[0:32, :], IDN[0:32, 0:32], start=True, stop=True)
            tok_moeb[(es, "t")] = T.tok(ins)
        sec(T, t_mb)

        def v_mb(v, es=es):
            V.wait(tok_moeb[(es, "t")])
            ins = v.tensor_copy(B1F[:, 32 + es * 32:64 + es * 32],
                                PB[:, 0:32] if es == 0 else PB[:, 512:544])
            tok_moeb[(es, "v")] = V.tok(ins)
        sec(V, v_mb)

    CHK("moeb")

    def g_wselc(g):
        for es in range(2):
            for fs in range(NFS):
                grp = g_wsel[es * NFS + fs]
                grp.start(G, W1sel[es].ap()[:, fs * FSW:(fs + 1) * FSW],
                          W1m_h[iv[es]][:, fs * FSW:(fs + 1) * FSW])
                grp.start(G, W2sel[es].ap()[fs * FSW:(fs + 1) * FSW, :],
                          W2m_h[iv[es]][fs * FSW:(fs + 1) * FSW, :])
    sec(G, g_wselc)

    tok_otr = {}
    tok_oev = {}
    tok_upc = {}
    for sub in range(NSUB):
        slo = sub * TSUB

        def c_upc(cc, sub=sub, slo=slo):
            for c in range(NCH):
                C.wait(tok_x2cast[(sub, c)])
            if (sub - 1, "lnfree") in tok_upc:
                C.wait(tok_upc[(sub - 1, "lnfree")])
            ins = cc.copy(X2S[:].rearrange("p (c n) -> p c n", c=NCH),
                          B1[:].rearrange("p (c n) -> p c n", c=NCH)[:, :, slo:slo + TSUB])
            tok_upc[sub] = C.tok(ins)
        sec(C, c_upc)
        upc = tok_upc.get(sub)

        moe_last = None
        for es in range(2):
            in_toks = {c: tok_x2cast.get((sub, c)) for c in range(NCH)}
            ev = ff_block(in_toks, lambda k, slo=slo: b1v(k, slo, TSUB),
                          32 + es * 32,
                          None, x2sv,
                          WEB[:, es:es + 1], tok_web.get(1),
                          (lambda es=es: W1sel[es].ap()), (lambda es=es: W2sel[es].ap()),
                          (RO_S0 if es == 0 else RO_S1), "ff",
                          slot_extra={0: [tok_moeb.get((es, "v"))],
                                      1: [tok_moeb.get((es, "v"))]},
                          dst_wait=upc,
                          fs_wait=(lambda fs, es=es: (g_wsel[es * NFS + fs].sem, 32)))
            moe_last = ev

        appF, _, _ = ln_block(
            x2sv, rsv, None, None, None, 42,
            pre_tok=lambda c, ev=moe_last: ev[(NFS - 1, 1)])
        tok_upc[(sub, "lnfree")] = appF.get(NCH - 1)

        for tcq in range(TSUB // 128):
            tglob = sub * 4 + tcq
            pw = PA if tcq % 2 == 0 else PB

            def t_otr(tt, sub=sub, tcq=tcq, pw=pw, appF=appF):
                for c in range(NCH):
                    T.wait(appF[c])
                if (sub, tcq - 2) in tok_oev:
                    T.wait(tok_oev[(sub, tcq - 2)])
                if tcq < 2 and (sub - 1, tcq + 2) in tok_oev:
                    T.wait(tok_oev[(sub - 1, tcq + 2)])
                for c in range(NCH):
                    ins = tt.matmul(pw[:, c * 128:(c + 1) * 128],
                                    RS[:, c * TSUB + tcq * 128: c * TSUB + (tcq + 1) * 128],
                                    IDN[:], start=True, stop=True)
                tok_otr[(sub, tcq)] = T.tok(ins)
            sec(T, t_otr)

            def c_oev(cc, sub=sub, tcq=tcq, pw=pw):
                C.wait(tok_otr[(sub, tcq)])
                if sub >= 1:
                    C.wait((g_out[tcq].sem, sub * 16))
                ins = cc.copy(RT[:, tcq * 1024:(tcq + 1) * 1024], pw[:])
                tok_oev[(sub, tcq)] = C.tok(ins)
            sec(C, c_oev)

            def y_ost(y, sub=sub, tcq=tcq, tglob=tglob):
                Y.wait(tok_oev[(sub, tcq)])
                g_out[tcq].start(Y, out_h[tglob * 128:(tglob + 1) * 128, :],
                                 RT[:, tcq * 1024:(tcq + 1) * 1024])
            sec(Y, y_ost)
        if (sub, 3) in tok_oev:
            state["hpw"][0] = tok_oev[(sub, 3)]
            state["hpw"][1] = tok_oev[(sub, 3)]
        CHK("p3s%d" % sub)

    def y_fin(y):
        for i in range(4):
            Y.wait(g_out[i].tok())
        if g_dbg.count:
            Y.wait(g_dbg.tok())
    sec(Y, y_fin)

    st.close()
    return nc


_NC_CACHE = {}


def kernel(**inputs):
    if "main" not in _NC_CACHE:
        _NC_CACHE["main"] = build()
    nc = _NC_CACHE["main"]
    x = np.ascontiguousarray(np.asarray(inputs["L0_attn_x"], dtype=np.float32)[0])
    base = {
        "Wq": inputs["L0_attn_Wq"], "bq": inputs["L0_attn_bq"],
        "Wk": inputs["L0_attn_Wk"], "bk": inputs["L0_attn_bk"],
        "Wv": inputs["L0_attn_Wv"], "bv": inputs["L0_attn_bv"],
        "Wo": inputs["L0_attn_Wo"], "bo": inputs["L0_attn_bo"],
        "g1": inputs["L0_attn_g1"], "be1": inputs["L0_attn_be1"],
        "W1": inputs["L0_attn_W1"], "b1": inputs["L0_attn_b1"],
        "W2": inputs["L0_attn_W2"], "b2": inputs["L0_attn_b2"],
        "g2": inputs["L0_attn_g2"], "be2": inputs["L0_attn_be2"],
        "Wr": inputs["L0_ffn_Wr"], "W1m": inputs["L0_ffn_W1"],
        "b1m": inputs["L0_ffn_b1"], "W2m": inputs["L0_ffn_W2"],
        "b2m": inputs["L0_ffn_b2"],
    }
    base = {k: np.ascontiguousarray(np.asarray(v, dtype=np.float32)) for k, v in base.items()}
    in_maps = []
    for i in range(NCORES):
        m = dict(base)
        m["x"] = np.ascontiguousarray(x[i * TC:(i + 1) * TC, :])
        in_maps.append(m)
    res = run_bass_kernel_spmd(nc, in_maps, core_ids=list(range(NCORES)))
    out = np.concatenate([np.asarray(r["out"]) for r in res.results], axis=0)
    return out.reshape(1, S, D)


# version 17
# speedup vs baseline: 1.0831x; 1.0278x over previous
import contextlib
import math
import os

import numpy as np

import concourse.bass as bass
import concourse.mybir as mybir
from concourse.bass_utils import run_bass_kernel_spmd

F32 = mybir.dt.float32
BF16 = mybir.dt.bfloat16
I32 = mybir.dt.int32
AF = mybir.ActivationFunctionType
ALU = mybir.AluOpType
AX = mybir.AxisListType

D = 1024
NH = 16
FF = 4096
NE = 8
S = 16384
NCORES = 8
TC = S // NCORES
NCH = D // 128
TCH = TC // 128
HALF = TC // 2
NSUB = 4
TSUB = TC // NSUB
NFS = 4
FSW = FF // NFS
EPS_IN = 1e-12
EPS_OUT = 1e-5
LN8 = math.log(8.0)

RO_S0, RO_S1, RO_G1, RO_G2 = 0, D, 2 * D, 3 * D
S_BQ, S_G1, S_BE1, S_G2, S_BE2 = range(5)


class Eng:
    def __init__(self, nc, stack, name):
        self.e = getattr(nc, name)
        self.name = name
        self.sem = stack.enter_context(nc.semaphore("sf_" + name))
        self.n = 0
        self.seen = {}

    def tok(self, ins):
        self.n += 1
        ins.then_inc(self.sem, 1)
        return (self.sem, self.n)

    def wait(self, *toks):
        for t in toks:
            if t is None:
                continue
            sem, val = t
            k = id(sem)
            if val > self.seen.get(k, 0):
                self.e.wait_ge(sem, val)
                self.seen[k] = val


class DmaGrp:
    def __init__(self, nc, stack, name):
        self.sem = stack.enter_context(nc.semaphore(name))
        self.count = 0

    def start(self, eng, out, in_, **kw):
        ins = eng.e.dma_start(out, in_, **kw)
        self.count += 16
        ins.then_inc(self.sem, 16)
        return ins

    def tok(self):
        return (self.sem, self.count)


class _StopBuild(Exception):
    pass


def build(debug=(), stop_after=None):
    nc = bass.Bass("TRN2", target_bir_lowering=False)

    x_h = nc.dram_tensor("x", [TC, D], F32, kind="ExternalInput").ap()
    Wq_h = nc.dram_tensor("Wq", [D, D], F32, kind="ExternalInput").ap()
    bq_h = nc.dram_tensor("bq", [D], F32, kind="ExternalInput").ap()
    Wk_h = nc.dram_tensor("Wk", [D, D], F32, kind="ExternalInput").ap()
    bk_h = nc.dram_tensor("bk", [D], F32, kind="ExternalInput").ap()
    Wv_h = nc.dram_tensor("Wv", [D, D], F32, kind="ExternalInput").ap()
    bv_h = nc.dram_tensor("bv", [D], F32, kind="ExternalInput").ap()
    Wo_h = nc.dram_tensor("Wo", [D, D], F32, kind="ExternalInput").ap()
    bo_h = nc.dram_tensor("bo", [D], F32, kind="ExternalInput").ap()
    g1_h = nc.dram_tensor("g1", [D], F32, kind="ExternalInput").ap()
    be1_h = nc.dram_tensor("be1", [D], F32, kind="ExternalInput").ap()
    W1_h = nc.dram_tensor("W1", [D, FF], F32, kind="ExternalInput").ap()
    b1_h = nc.dram_tensor("b1", [FF], F32, kind="ExternalInput").ap()
    W2_h = nc.dram_tensor("W2", [FF, D], F32, kind="ExternalInput").ap()
    b2_h = nc.dram_tensor("b2", [D], F32, kind="ExternalInput").ap()
    g2_h = nc.dram_tensor("g2", [D], F32, kind="ExternalInput").ap()
    be2_h = nc.dram_tensor("be2", [D], F32, kind="ExternalInput").ap()
    Wr_h = nc.dram_tensor("Wr", [D, NE], F32, kind="ExternalInput").ap()
    W1m_h = nc.dram_tensor("W1m", [NE, D, FF], F32, kind="ExternalInput").ap()
    b1m_h = nc.dram_tensor("b1m", [NE, FF], F32, kind="ExternalInput").ap()
    W2m_h = nc.dram_tensor("W2m", [NE, FF, D], F32, kind="ExternalInput").ap()
    b2m_h = nc.dram_tensor("b2m", [NE, D], F32, kind="ExternalInput").ap()
    out_h = nc.dram_tensor("out", [TC, D], F32, kind="ExternalOutput").ap()
    xfm_d = nc.dram_tensor("xfm_d", [NCH, 128, TC], F32)
    W1sel = [nc.dram_tensor("W1sel%d" % e, [D, FF], BF16) for e in range(2)]
    W2sel = [nc.dram_tensor("W2sel%d" % e, [FF, D], BF16) for e in range(2)]
    cc1i = nc.dram_tensor("cc1i", [128, NH * 64], F32)
    cc1o = nc.dram_tensor("cc1o", [128, NH * 64], F32)
    cc2i = nc.dram_tensor("cc2i", [1, NE], F32)
    cc2o = nc.dram_tensor("cc2o", [1, NE], F32)

    dbg_specs = {
        "d_xfm": ([NCH, 128, TC], F32), "d_eq": ([128, NCH * TC], BF16),
        "d_kv": ([128, NH * 64], F32), "d_ctx": ([128, NCH * TC], BF16),
        "d_attn": ([128, NCH * TSUB], F32), "d_x2b": ([128, NCH * TC], BF16),
        "d_lg": ([1, 64], F32), "d_rq": ([16, TC], F32),
        "d_z1": ([128, NCH * TSUB], F32), "d_kn": ([128, 2048], BF16),
    }
    dbg = {}
    for name in debug:
        shp, dt = dbg_specs[name]
        dbg[name] = nc.dram_tensor(name, shp, dt, kind="ExternalOutput").ap()

    st = contextlib.ExitStack()

    RS = st.enter_context(nc.sbuf_tensor("RS", [128, NCH * TSUB], F32))
    RT = st.enter_context(nc.sbuf_tensor("RT", [128, NCH * TSUB], F32))
    XS = st.enter_context(nc.sbuf_tensor("XS", [128, NCH * TSUB], F32))
    X2S = st.enter_context(nc.sbuf_tensor("X2S", [128, NCH * TSUB], F32))
    B1 = st.enter_context(nc.sbuf_tensor("B1", [128, NCH * TC], BF16))
    B2 = st.enter_context(nc.sbuf_tensor("B2", [128, NCH * TC], BF16))
    WFS = st.enter_context(nc.sbuf_tensor("WFS", [128, NCH * TC], BF16))
    SBF = st.enter_context(nc.sbuf_tensor("SBF", [128, 4096], BF16))
    SSB = st.enter_context(nc.sbuf_tensor("SSB", [128, 1024], BF16))
    KNV = st.enter_context(nc.sbuf_tensor("KNV", [128, 2048], BF16))
    LNR = st.enter_context(nc.sbuf_tensor("LNR", [1, 6 * TSUB], F32))
    IDN = st.enter_context(nc.sbuf_tensor("IDN", [128, 128], F32))
    KVB = st.enter_context(nc.sbuf_tensor("KVB", [128, NH * 64], BF16))
    BIA = st.enter_context(nc.sbuf_tensor("BIA", [128, 84], F32))
    B1F = st.enter_context(nc.sbuf_tensor("B1F", [128, 96], F32))
    ROWS = st.enter_context(nc.sbuf_tensor("ROWS", [1, 4 * D], BF16))
    ONESRB = st.enter_context(nc.sbuf_tensor("ONESRB", [1, 512], BF16))
    RTR = st.enter_context(nc.sbuf_tensor("RTR", [1, 64], F32))
    RTI = st.enter_context(nc.sbuf_tensor("RTI", [1, 2], I32))
    POOL = st.enter_context(nc.sbuf_tensor("POOL", [128, 32], F32))
    WEB = st.enter_context(nc.sbuf_tensor("WEB", [128, 2], F32))
    BRX = st.enter_context(nc.sbuf_tensor("BRX", [32, 128], F32))
    IND16 = st.enter_context(nc.sbuf_tensor("IND16", [128, 128], BF16))
    ONESC = st.enter_context(nc.sbuf_tensor("ONESC", [128, 1], BF16))
    ONESCF = st.enter_context(nc.sbuf_tensor("ONESCF", [128, 1], F32))
    ONESR = st.enter_context(nc.sbuf_tensor("ONESR", [1, 512], F32))
    WRFM = st.enter_context(nc.sbuf_tensor("WRFM", [128, NCH * NE], F32))

    PA = st.enter_context(nc.psum_tensor("PA", [128, 1024], F32))
    PB = st.enter_context(nc.psum_tensor("PB", [128, 1024], F32))
    PC = st.enter_context(nc.psum_tensor("PC", [128, 1024], F32))
    PD = st.enter_context(nc.psum_tensor("PD", [128, 1024], F32))

    ind16_np = np.zeros((128, 128), dtype=np.float32)
    ind16T_np = np.zeros((16, 8 * 128), dtype=np.float32)
    for c in range(8):
        for p in range(128):
            ind16_np[p, c * 16 + 2 * c + (p // 64)] = 1.0
            ind16T_np[2 * c + (p // 64), c * 128 + p] = 1.0
    identC = nc.inline_tensor(np.eye(128, dtype=np.float32), name="identC").ap()
    ind16C = nc.inline_tensor(ind16_np, name="ind16C").ap()
    ind16TC = nc.inline_tensor(ind16T_np, name="ind16TC").ap()
    ones128cC = nc.inline_tensor(np.ones((128, 1), np.float32), name="ones128cC").ap()
    ones1rC = nc.inline_tensor(np.ones((1, 512), np.float32), name="ones1rC").ap()
    iota8C = nc.inline_tensor(np.arange(8, dtype=np.float32).reshape(1, 8), name="iota8C").ap()

    blk = st.enter_context(nc.Block())
    T = Eng(nc, st, "tensor")
    V = Eng(nc, st, "vector")
    C = Eng(nc, st, "scalar")
    G = Eng(nc, st, "gpsimd")
    Y = Eng(nc, st, "sync")

    g_const = DmaGrp(nc, st, "g_const")
    g_cstG = DmaGrp(nc, st, "g_cstG")
    g_x = DmaGrp(nc, st, "g_x")
    g_xfm = [DmaGrp(nc, st, "g_xfmA"), DmaGrp(nc, st, "g_xfmB")]
    g_xs = DmaGrp(nc, st, "g_xs")
    g_wq = DmaGrp(nc, st, "g_wq")
    g_wkv = DmaGrp(nc, st, "g_wkv")
    g_bias = DmaGrp(nc, st, "g_bias")
    g_rows2 = DmaGrp(nc, st, "g_rows2")
    g_kv1 = DmaGrp(nc, st, "g_kv1")
    g_wfsA = DmaGrp(nc, st, "g_wfsA")
    g_wfsB = DmaGrp(nc, st, "g_wfsB")
    g_wfsYA = DmaGrp(nc, st, "g_wfsYA")
    g_wfsYB = DmaGrp(nc, st, "g_wfsYB")
    g_wroA = DmaGrp(nc, st, "g_wroA")
    g_wroB = DmaGrp(nc, st, "g_wroB")
    g_lg = DmaGrp(nc, st, "g_lg")
    g_moeb = DmaGrp(nc, st, "g_moeb")
    g_wsel = [DmaGrp(nc, st, "g_wsel%d" % i) for i in range(8)]
    g_out = [DmaGrp(nc, st, "g_out%d" % i) for i in range(4)]
    g_dbg = DmaGrp(nc, st, "g_dbg")
    cc_sem = st.enter_context(nc.semaphore("cc_sem"))
    reg0 = st.enter_context(nc.gpsimd.register("eidx0"))
    reg1 = st.enter_context(nc.gpsimd.register("eidx1"))
    yreg0 = st.enter_context(nc.sync.register("yidx0"))
    yreg1 = st.enter_context(nc.sync.register("yidx1"))

    stopped = [False]

    def sec(eng, fn):
        if stopped[0]:
            return
        getattr(blk, eng.name)(fn)

    def CHK(name):
        if stop_after == name:
            stopped[0] = True

    from concourse.hw_specs import get_activation_tables
    _tabs = list(get_activation_tables(nc.m.arch).items())
    SET_LNEXP = next(i for i, (n, fs_) in enumerate(_tabs)
                     if AF.Exp in fs_ and AF.Ln in fs_)
    SET_GELU = next(i for i, (n, fs_) in enumerate(_tabs) if AF.Gelu in fs_)

    def load_act_set(cc, idx):
        cc.add_instruction(mybir.InstLoadActFuncSet(
            name=nc.get_next_instruction_name(), act_func_set_id=idx,
            ins=[], outs=[]))

    def b1v(c, lo, n):
        return B1[:, c * TC + lo: c * TC + lo + n]

    def b2v(c, lo, n):
        return B2[:, c * TC + lo: c * TC + lo + n]

    def rsv(c):
        return RS[:, c * TSUB:(c + 1) * TSUB]

    def rtv(c):
        return RT[:, c * TSUB:(c + 1) * TSUB]

    def xsv(c):
        return XS[:, c * TSUB:(c + 1) * TSUB]

    def x2sv(c):
        return X2S[:, c * TSUB:(c + 1) * TSUB]

    rq_rows = RT[0:16, 3072:4096]
    IND16T = SBF[0:16, 0:1024]
    RQBF = RT[0:16, 2048:3072].bitcast(BF16)
    wro_slots = [KVB[:, 0:1024], SSB[:, 0:1024]]
    IOTA8 = RTR[0:1, 40:48]


    def c_pre(cc):
        load_act_set(cc, SET_GELU)
    sec(C, c_pre)

    tok_cbias = {}
    def v_cb(v):
        v.memset(BIA[:, 40:41], -LN8)
        v.memset(BIA[:, 41:42], EPS_IN)
        tok_cbias[0] = V.tok(v.memset(BIA[:, 42:43], EPS_OUT))
    sec(V, v_cb)

    def p0_y(y):
        g_const.start(Y, IDN[:], identC[:])
        g_const.start(Y, ONESCF[:], ones128cC[:])
        g_const.start(Y, ONESR[:], ones1rC[:])
        g_const.start(Y, IOTA8, iota8C[:])
        for c in range(NCH):
            g_const.start(Y, WRFM[:, c * NE:(c + 1) * NE], Wr_h[c * 128:(c + 1) * 128, :])
    sec(Y, p0_y)
    tok_constY = g_const.tok()

    def p0_g(g):
        g_cstG.start(G, IND16[:], ind16C[:])
        g_cstG.start(G, ONESC[:], ones128cC[:])
        g_cstG.start(G, ONESRB[:], ones1rC[:])
        for off, rsrc in [(RO_S0, bk_h), (RO_S1, bv_h), (RO_G1, g1_h), (RO_G2, g2_h)]:
            g_cstG.start(G, ROWS[0:1, off:off + D], rsrc.rearrange("d -> () d"))
        for hh in range(2):
            g_wq.start(G, WFS[:, hh * 4096:(hh + 1) * 4096].rearrange("p (k c) -> p k c", k=4),
                       Wq_h.rearrange("(k p) c -> p k c", p=128)[:, hh * 4:(hh + 1) * 4, :])
    sec(G, p0_g)

    tok_tr = {}
    tok_ev = {}
    tok_xbf = {}
    for t in range(TCH):
        xv = RS[:, (t % 2) * 1024:(t % 2) * 1024 + 1024]
        pw = PA if t % 2 == 0 else PB
        xslot = XS[:, (t % 2) * 2048:(t % 2) * 2048 + 1024]

        def y_load(y, t=t, xv=xv):
            if t >= 2:
                Y.wait(tok_tr[t - 2])
            g_x.start(Y, xv, x_h[t * 128:(t + 1) * 128, :])
        sec(Y, y_load)

        def t_tr(tt, t=t, xv=xv, pw=pw):
            T.wait(tok_constY)
            T.wait((g_x.sem, (t + 1) * 16))
            if t >= 2:
                T.wait(tok_ev[t - 2])
            for c in range(NCH):
                ins = tt.matmul(pw[:, c * 128:(c + 1) * 128],
                                xv[:, c * 128:(c + 1) * 128], IDN[:],
                                start=True, stop=True)
            tok_tr[t] = T.tok(ins)
        sec(T, t_tr)

        def c_ev(cc, t=t, pw=pw, xslot=xslot):
            C.wait(tok_tr[t])
            if t >= 2:
                C.wait((g_xfm[t % 2].sem, (t // 2) * 16))
            t1 = C.tok(cc.copy(xslot, pw[:]))
            C.wait(t1)
            ins = cc.copy(
                B1[:].rearrange("p (c n) -> p c n", c=NCH)[:, :, t * 128:(t + 1) * 128],
                xslot.rearrange("p (c n) -> p c n", c=NCH))
            tok_ev[t] = t1
            tok_xbf[t] = C.tok(ins)
        sec(C, c_ev)

        def y_xfm(y, t=t, xslot=xslot):
            Y.wait(tok_xbf[t])
            g_xfm[t % 2].start(Y, xfm_d[:, :, t * 128:(t + 1) * 128].rearrange("c p n -> p c n"),
                               xslot.rearrange("p (c n) -> p c n", c=NCH))
        sec(Y, y_xfm)

    if "d_xfm" in dbg:
        def y_d(y):
            Y.wait(g_xfm[0].tok())
            Y.wait(g_xfm[1].tok())
            g_dbg.start(Y, dbg["d_xfm"][:], xfm_d.ap())
        sec(Y, y_d)

    CHK("ingest")

    tok_biaT = {}
    tok_biaV = {}
    specs = [(bq_h, S_BQ), (g1_h, S_G1), (be1_h, S_BE1), (g2_h, S_G2), (be2_h, S_BE2)]
    for i, (src, slotc) in enumerate(specs):
        def g_ld(g, src=src, i=i):
            if i >= 1:
                G.wait(tok_biaT[i - 1])
            g_bias.start(G, BRX[0:8, :], src.rearrange("(a b) -> a b", b=128))
        sec(G, g_ld)

        def t_tp(tt, i=i):
            T.wait(tok_constY)
            T.wait((g_bias.sem, (i + 1) * 16))
            if i >= 2:
                T.wait(tok_biaV[i - 2])
            ins = tt.matmul(PC[:, (i % 2) * 512:(i % 2) * 512 + 8], BRX[0:8, :],
                            IDN[0:8, 0:8], start=True, stop=True)
            tok_biaT[i] = T.tok(ins)
        sec(T, t_tp)

        def v_ev(v, i=i, slotc=slotc):
            V.wait(tok_biaT[i])
            t0 = V.tok(v.tensor_copy(BIA[:, slotc * 8:(slotc + 1) * 8],
                                     PC[:, (i % 2) * 512:(i % 2) * 512 + 8]))
            if i == 0:
                V.wait(t0)
                t0 = V.tok(v.tensor_scalar(BIA[:, 44:52], BIA[:, 0:8], -1.0,
                                           None, ALU.mult))
            tok_biaV[i] = t0
        sec(V, v_ev)

    def g_ldb1(g):
        G.wait(tok_biaT[4])
        g_bias.start(G, BRX[:, :], b1_h.rearrange("(a b) -> a b", b=128))
    sec(G, g_ldb1)

    def t_tpb1(tt):
        T.wait(tok_constY)
        T.wait((g_bias.sem, 6 * 16))
        T.wait(tok_biaV[4])
        ins = tt.matmul(PC[:, 0:32], BRX[0:32, :], IDN[0:32, 0:32], start=True, stop=True)
        tok_biaT[5] = T.tok(ins)
    sec(T, t_tpb1)

    def v_evb1(v):
        V.wait(tok_biaT[5])
        tok_biaV[5] = V.tok(v.tensor_copy(B1F[:, 0:32], PC[:, 0:32]))
    sec(V, v_evb1)

    tok_rep = {}
    for j in range(2):
        def t_rep(tt, j=j):
            T.wait(g_const.tok())
            T.wait(g_cstG.tok())
            T.wait(tok_biaV[5] if j == 0 else tok_rep[0])
            for h in range(2):
                ins = tt.matmul(PD[:, h * 512:(h + 1) * 512], ONESRB[0:1, 0:128],
                                ROWS[0:1, j * D + h * 512: j * D + (h + 1) * 512],
                                start=True, stop=True)
            tok_rep[(j, "t")] = T.tok(ins)
        sec(T, t_rep)

        def v_rep(v, j=j):
            V.wait(tok_rep[(j, "t")])
            tok_rep[j] = V.tok(v.tensor_copy(RT[:, j * 1024:(j + 1) * 1024], PD[:]))
        sec(V, v_rep)
    tok_bias_done = tok_rep.get(1)

    def g_rows2f(g):
        G.wait(tok_rep[(1, "t")])
        g_rows2.start(G, ROWS[0:1, RO_S0:RO_S0 + D], bo_h.rearrange("d -> () d"))
        g_rows2.start(G, ROWS[0:1, RO_S1:RO_S1 + D], b2_h.rearrange("d -> () d"))
    sec(G, g_rows2f)

    CHK("bias")

    tok_qmm = {}
    tok_elur = {}
    tok_eluy = {}
    tok_eq = {}
    tok_qsq = {}
    tok_qss = {}
    tok_rqv = {}
    tok_rq = {}
    def emit_qmm(qi):
        half, c = qi // NCH, qi % NCH
        lo = half * HALF
        pq = (PA, PB, PD)[qi % 3]

        def t_qmm(tt, qi=qi, c=c, lo=lo, pq=pq):
            T.wait(g_wq.tok())
            T.wait(g_x.tok())
            T.wait(tok_xbf[TCH - 1])
            if qi == 2:
                T.wait(tok_bias_done)
            if qi >= 3:
                T.wait(tok_elur[qi - 3])
            for k in range(NCH):
                for nn in range(2):
                    ins = tt.matmul(pq[:, nn * 512:(nn + 1) * 512],
                                    WFS[:, k * 1024 + c * 128: k * 1024 + (c + 1) * 128],
                                    b1v(k, lo + nn * 512, 512),
                                    start=(k == 0), stop=(k == NCH - 1))
            tok_qmm[qi] = T.tok(ins)
        sec(T, t_qmm)

    def emit_qrest(qi):
        half, c = qi // NCH, qi % NCH
        lo = half * HALF
        pq = (PA, PB, PD)[qi % 3]

        def c_elu(cc, qi=qi, c=c, pq=pq):
            C.wait(tok_qmm[qi])
            C.wait(tok_bias_done)
            if qi >= 1:
                C.wait(tok_eq[qi - 1])
            t_, r_, d_, y_ = RS[:, 0:1024], RS[:, 1024:2048], RS[:, 2048:3072], RS[:, 3072:4096]
            t1 = C.tok(cc.activation(t_, pq[:], AF.Relu, scale=-1.0,
                                     bias=BIA[:, 44 + c:45 + c]))
            C.wait(t1)
            t2 = C.tok(cc.activation(t_, t_, AF.Tanh, scale=-0.5))
            tok_elur[qi] = C.tok(cc.activation(r_, pq[:], AF.Relu,
                                               bias=BIA[:, S_BQ * 8 + c:S_BQ * 8 + c + 1]))
            C.wait(t2)
            t3 = C.tok(cc.activation(d_, t_, AF.Copy, scale=-1.0, bias=1.0))
            C.wait(t3)
            tok_eluy[qi] = C.tok(cc.activation(y_, d_, AF.Copy, scale=-0.4854,
                                               bias=1.4571))
        sec(C, c_elu)

        def v_eq(v, qi=qi, c=c, lo=lo):
            V.wait(tok_eluy[qi])
            def stepi(ins):
                tk = V.tok(ins)
                V.wait(tk)
                return tk
            t_, r_, d_, y_ = RS[:, 0:1024], RS[:, 1024:2048], RS[:, 2048:3072], RS[:, 3072:4096]
            stepi(v.tensor_mul(d_, d_, y_))
            stepi(v.tensor_scalar(d_, d_, -1.0, 2.0, ALU.mult, ALU.add))
            stepi(v.tensor_mul(y_, y_, d_))
            stepi(v.scalar_tensor_tensor(d_, t_, 2.0, y_, ALU.mult, ALU.mult))
            ins = v.scalar_tensor_tensor(b2v(c, lo, HALF), d_, 1.0, r_,
                                         ALU.mult, ALU.add)
            tok_eq[qi] = V.tok(ins)
        sec(V, v_eq)

        def c_sq(cc, qi=qi, c=c, lo=lo):
            C.wait(tok_eq[qi])
            if qi >= 2:
                C.wait(tok_qss[qi - 2])
            sv = SBF[:, (qi % 2) * 1024:(qi % 2) * 1024 + 1024]
            tok_qsq[qi] = C.tok(cc.activation(sv, b2v(c, lo, HALF), AF.Square))
        sec(C, c_sq)

        def t_ss(tt, qi=qi, half=half, c=c):
            T.wait(tok_qsq[qi])
            T.wait(g_cstG.tok())
            if c == 0 and half == 1:
                T.wait(tok_rqv[0])
            sv = SBF[:, (qi % 2) * 1024:(qi % 2) * 1024 + 1024]
            for nn in range(2):
                ins = tt.matmul(PC[0:16, nn * 512:(nn + 1) * 512],
                                IND16[:, c * 16:(c + 1) * 16],
                                sv[:, nn * 512:(nn + 1) * 512],
                                start=(c == 0), stop=(c == NCH - 1))
            tok_qss[qi] = T.tok(ins)
        sec(T, t_ss)

        if c == NCH - 1:
            def v_rqev(v, half=half, qi=qi):
                V.wait(tok_qss[qi])
                tok_rqv[half] = V.tok(v.tensor_copy(rq_rows[:, 0:1024],
                                                    PC[0:16, 0:1024]))
            sec(V, v_rqev)

            def v_rq(v, half=half):
                V.wait(tok_rqv[half])
                def stepi(ins):
                    tk = V.tok(ins)
                    V.wait(tk)
                    return tk
                r = rq_rows[:, 0:1024]
                scrA = KNV[0:16, 0:2048].bitcast(F32)
                scrB = SBF[0:16, 2048:4096].bitcast(F32)
                stepi(v.tensor_scalar(scrB.bitcast(I32), r.bitcast(I32), 1, None,
                                      ALU.logical_shift_right))
                stepi(v.tensor_scalar(scrB.bitcast(I32), scrB.bitcast(I32), -1,
                                      0x5F3759DF, ALU.mult, ALU.add))
                stepi(v.tensor_scalar(scrA, r, -0.5, None, ALU.mult))
                for _ in range(3):
                    stepi(v.tensor_mul(r, scrB, scrB))
                    stepi(v.tensor_mul(r, r, scrA))
                    stepi(v.tensor_scalar(r, r, 1.5, None, ALU.add))
                    stepi(v.tensor_mul(scrB, scrB, r))
                tok_rq[half] = stepi(v.tensor_scalar(
                    RQBF[:, half * 1024:(half + 1) * 1024], scrB, 0.125, None, ALU.mult))
            sec(V, v_rq)


    for qi in range(2 * NCH):
        emit_qmm(qi)
        if qi >= 1:
            emit_qrest(qi - 1)
    emit_qrest(2 * NCH - 1)
    if "d_eq" in dbg:
        def y_d2(y):
            Y.wait(tok_eq[2 * NCH - 1])
            g_dbg.start(Y, dbg["d_eq"][:], B2[:])
        sec(Y, y_d2)
    if "d_rq" in dbg:
        def y_d2b(y):
            Y.wait(tok_rq[1])
            g_dbg.start(Y, dbg["d_rq"][:], rq_rows[:, :])
        sec(Y, y_d2b)

    CHK("p1q")

    def g_wkvload(g):
        G.wait(tok_qmm[2 * NCH - 1])
        for hh in range(2):
            g_wkv.start(G, WFS[:, 8192 + hh * 4096: 8192 + (hh + 1) * 4096].rearrange("p (k c) -> p k c", k=4),
                        Wk_h.rearrange("(k p) c -> p k c", p=128)[:, hh * 4:(hh + 1) * 4, :])
        for hh in range(2):
            g_wkv.start(G, WFS[:, hh * 4096:(hh + 1) * 4096].rearrange("p (k c) -> p k c", k=4),
                        Wv_h.rearrange("(k p) c -> p k c", p=128)[:, hh * 4:(hh + 1) * 4, :])
    sec(G, g_wkvload)

    tok_kmm = {}
    tok_kb = {}
    tok_kex = {}
    tok_keluy = {}
    tok_kel = {}
    tok_ksq = {}
    tok_red = {}
    tok_krs = {}
    tok_kn = {}
    tok_vev = {}
    tok_kvmm = {}
    tok_kvadd = {}
    KRD = POOL[:, 0:32]

    def emit_tkv(t):
        def t_kv(tt, t=t):
            T.wait(g_wkv.tok())
            if t == 0:
                T.wait(tok_rqv[1])
            if t >= 1:
                T.wait(tok_kb[t - 1])
                T.wait(tok_vev[t - 1])
            for k in range(NCH):
                xst = b1v(k, t * 128, 128)
                for nn in range(2):
                    tt.matmul(PA[:, nn * 512:(nn + 1) * 512], xst,
                              WFS[:, 8192 + k * 1024 + nn * 512: 8192 + k * 1024 + (nn + 1) * 512],
                              start=(k == 0), stop=(k == NCH - 1))
                for nn in range(2):
                    ins = tt.matmul(PB[:, nn * 512:(nn + 1) * 512], xst,
                                    WFS[:, k * 1024 + nn * 512: k * 1024 + (nn + 1) * 512],
                                    start=(k == 0), stop=(k == NCH - 1))
            tok_kmm[t] = T.tok(ins)
        sec(T, t_kv)

    KVSB = X2S[:, 0:1024]

    def emit_kvacc(t):
        def t_kvacc(tt, t=t):
            T.wait(tok_kn[t])
            T.wait(tok_vev[t])
            if t >= 2:
                T.wait(tok_kvadd[t - 2])
            for p in range(8):
                ins = tt.matmul(PC[:, p * 128:(p + 1) * 128],
                                KNV[:, p * 128:(p + 1) * 128],
                                KNV[:, 1024 + p * 128:1024 + (p + 1) * 128],
                                start=True, stop=True)
            tok_kvmm[t] = T.tok(ins)
        sec(T, t_kvacc)

        def v_kvadd(v, t=t):
            V.wait(tok_kvmm[t])
            if t == 0:
                ins = v.tensor_copy(KVSB, PC[:])
            else:
                ins = v.tensor_add(KVSB, KVSB, PC[:])
            tok_kvadd[t] = V.tok(ins)
        sec(V, v_kvadd)

    for t in range(TCH):
        emit_tkv(t)
        if t >= 1:
            emit_kvacc(t - 1)

        def v_kb(v, t=t):
            V.wait(tok_kmm[t])
            V.wait(tok_bias_done)
            if t >= 1:
                V.wait(tok_kex[t - 1])
            ins = v.scalar_tensor_tensor(RS[:, 0:1024], PA[:], 1.0,
                                         RT[:, 0:1024], ALU.mult, ALU.add)
            tok_kb[t] = V.tok(ins)
        sec(V, v_kb)

        def v_vev(v, t=t):
            if t >= 1:
                V.wait(tok_kvmm[t - 1])
            ins = v.scalar_tensor_tensor(KNV[:, 1024:2048], PB[:], 1.0,
                                         RT[:, 1024:2048], ALU.mult, ALU.add)
            tok_vev[t] = V.tok(ins)
        sec(V, v_vev)

        def c_kelu(cc, t=t):
            C.wait(tok_kb[t])
            t_, r_ = RS[:, 1024:2048], RS[:, 2048:3072]
            d_, y_ = RS[:, 3072:4096], RS[:, 0:1024]
            t1 = C.tok(cc.activation(t_, RS[:, 0:1024], AF.Relu, scale=-1.0))
            C.wait(t1)
            t2 = C.tok(cc.activation(t_, t_, AF.Tanh, scale=-0.5))
            tok_kex[t] = C.tok(cc.activation(r_, RS[:, 0:1024], AF.Relu))
            C.wait(t2)
            C.wait(tok_kex[t])
            t3 = C.tok(cc.activation(d_, t_, AF.Copy, scale=-1.0, bias=1.0))
            C.wait(t3)
            tok_keluy[t] = C.tok(cc.activation(y_, d_, AF.Copy, scale=-0.4854,
                                               bias=1.4571))
        sec(C, c_kelu)

        def v_ek(v, t=t):
            V.wait(tok_keluy[t])
            if t >= 1:
                V.wait(tok_kvmm[t - 1])
            def stepi(ins):
                tk = V.tok(ins)
                V.wait(tk)
                return tk
            t_, r_ = RS[:, 1024:2048], RS[:, 2048:3072]
            d_, y_ = RS[:, 3072:4096], RS[:, 0:1024]
            stepi(v.tensor_mul(d_, d_, y_))
            stepi(v.tensor_scalar(d_, d_, -1.0, 2.0, ALU.mult, ALU.add))
            stepi(v.tensor_mul(y_, y_, d_))
            stepi(v.scalar_tensor_tensor(d_, t_, 2.0, y_, ALU.mult, ALU.mult))
            ins = v.scalar_tensor_tensor(KNV[:, 0:1024], d_, 1.0, r_,
                                         ALU.mult, ALU.add)
            tok_kel[t] = V.tok(ins)
        sec(V, v_ek)

        def c_ksq(cc, t=t):
            C.wait(tok_kel[t])
            if t >= 1:
                C.wait(tok_red[t - 1])
            tok_ksq[t] = C.tok(cc.activation(SBF[:, 2048:3072], KNV[:, 0:1024], AF.Square))
        sec(C, c_ksq)

        def v_red(v, t=t):
            V.wait(tok_ksq[t])
            ins = v.reduce_sum(KRD[:, 0:16],
                               SBF[:, 2048:3072].rearrange("p (h d) -> p h d", h=16),
                               axis=AX.X)
            tok_red[t] = V.tok(ins)
        sec(V, v_red)

        def v_krs(v, t=t):
            V.wait(tok_red[t])
            def stepi(ins):
                tk = V.tok(ins)
                V.wait(tk)
                return tk
            r = KRD[:, 16:32]
            sA = BIA[:, 52:68]
            sB = BIA[:, 68:84]
            stepi(v.tensor_scalar(r.bitcast(I32), KRD[:, 0:16].bitcast(I32), 1,
                                  None, ALU.logical_shift_right))
            stepi(v.tensor_scalar(r.bitcast(I32), r.bitcast(I32), -1, 0x5F3759DF,
                                  ALU.mult, ALU.add))
            stepi(v.tensor_scalar(sA, KRD[:, 0:16], -0.5, None, ALU.mult))
            last = None
            for _ in range(2):
                stepi(v.tensor_mul(sB, r, r))
                stepi(v.tensor_mul(sB, sB, sA))
                stepi(v.tensor_scalar(sB, sB, 1.5, None, ALU.add))
                last = stepi(v.tensor_mul(r, r, sB))
            tok_krs[t] = last
        sec(V, v_krs)

        def v_kn(v, t=t):
            V.wait(tok_krs[t])
            ins = v.tensor_mul(KNV[:, 0:1024].rearrange("p (h d) -> p h d", h=16),
                               KNV[:, 0:1024].rearrange("p (h d) -> p h d", h=16),
                               KRD[:, 16:32].rearrange("p h -> p h ()").broadcast_to([128, 16, 64]))
            tok_kn[t] = V.tok(ins)
        sec(V, v_kn)
    emit_kvacc(TCH - 1)

    if "d_kn" in dbg:
        def y_dkn(y):
            Y.wait(tok_kn[TCH - 1])
            Y.wait(tok_vev[TCH - 1])
            g_dbg.start(Y, dbg["d_kn"][:], KNV[:])
        sec(Y, y_dkn)

    CHK("p1kv")

    tok_kvcp = {}
    def v_kvz(v):
        V.wait(tok_kvadd[TCH - 1])
        V.wait(tok_kex[TCH - 1])
        t1 = V.tok(v.memset(RT[:, 0:1024], 0.0))
        V.wait(t1)
        last = None
        for p in range(8):
            for hh in range(2):
                sl = slice(hh * 64, (hh + 1) * 64)
                last = v.tensor_copy(RT[sl, p * 128 + hh * 64: p * 128 + (hh + 1) * 64],
                                     KVSB[sl, p * 128 + hh * 64: p * 128 + (hh + 1) * 64])
        tok_kvcp[0] = V.tok(last)
    sec(V, v_kvz)

    def g_ar1(g):
        G.wait(tok_kvcp[0])
        g_kv1.start(G, cc1i.ap(), RT[:, 0:1024])
        g.wait_ge(g_kv1.sem, g_kv1.count)
        g.collective_compute("AllReduce", ALU.add,
                             replica_groups=[list(range(NCORES))],
                             ins=[cc1i.ap()], outs=[cc1o.ap()]).then_inc(cc_sem, 1)
        g.wait_ge(cc_sem, 1)
        g_kv1.start(G, RT[:, 0:1024], cc1o.ap())
    sec(G, g_ar1)

    tok_kvbf = {}
    def v_kvbf(v):
        V.wait(g_kv1.tok())
        tok_kvbf[0] = V.tok(v.tensor_copy(KVB[:], RT[:, 0:1024]))
    sec(V, v_kvbf)

    if "d_kv" in dbg:
        def y_d3(y):
            Y.wait(tok_kvbf[0])
            g_dbg.start(Y, dbg["d_kv"][:], RT[:, 0:1024])
        sec(Y, y_d3)

    CHK("ar1")

    def g_ind16t(g):
        G.wait(tok_red[TCH - 1])
        G.wait(tok_qss[2 * NCH - 1])
        g_cstG.start(G, IND16T[:], ind16TC[:])
    sec(G, g_ind16t)
    tok_ind16t = g_cstG.tok()

    tok_ctxv = {}
    prev_ctx = {0: None}
    for c in range(NCH):
        for half in range(2):
            lo = half * HALF

            def t_ctx(tt, c=c, half=half, lo=lo):
                T.wait(tok_rq[1])
                T.wait(tok_kvbf[0])
                T.wait(tok_ind16t)
                if prev_ctx[0] is not None:
                    T.wait(prev_ctx[0])
                for nn in range(2):
                    tt.matmul(PD[:, nn * 512:(nn + 1) * 512],
                              IND16T[:, c * 128:(c + 1) * 128],
                              RQBF[:, lo + nn * 512: lo + (nn + 1) * 512],
                              start=True, stop=True)
                for nn in range(2):
                    ins = tt.matmul(PA[:, nn * 512:(nn + 1) * 512],
                                    KVB[:, c * 128:(c + 1) * 128],
                                    b2v(c, lo + nn * 512, 512),
                                    start=True, stop=True)
                tok_ctxv[(c, half, "t")] = T.tok(ins)
            sec(T, t_ctx)

            def v_ctx(v, c=c, half=half, lo=lo):
                V.wait(tok_ctxv[(c, half, "t")])
                t1 = V.tok(v.tensor_copy(SSB[:], PD[:]))
                V.wait(t1)
                ins = v.tensor_mul(b1v(c, lo, HALF), PA[:], SSB[:])
                tok_ctxv[(c, half)] = V.tok(ins)
                prev_ctx[0] = tok_ctxv[(c, half)]
            sec(V, v_ctx)
    tok_ctx_done = prev_ctx[0]

    if "d_ctx" in dbg:
        def y_d4(y):
            Y.wait(tok_ctx_done)
            g_dbg.start(Y, dbg["d_ctx"][:], B1[:])
        sec(Y, y_d4)

    CHK("ctx")

    state = {
        "gelu_loaded": False,
        "lnlast": None,
        "statfree": tok_ctx_done,
        "accfree": None,
        "wslot": {},
        "hpw": {0: None, 1: None},
        "hread": {},
        "woslot": {0: None, 1: None},
    }

    def ln_block(src_region, dst_region, g_slot, be_slot, g_row_off, eps_col,
                 pre_tok, out_cast=None, pool_sub=None):
        if stopped[0]:
            return {NCH - 1: None, 0: None}, {}, {}
        tok_st = {}
        tok_zc = {}
        zslot_war = {0: None, 1: None}
        for c in range(NCH):
            def c_stats(cc, c=c):
                C.wait(pre_tok(c))
                if zslot_war[c % 2] is not None:
                    C.wait(zslot_war[c % 2])
                sl = (c % 2) * 1024
                cc.copy(SBF[:, sl:sl + 512], src_region(c))
                ins = cc.activation(SBF[:, sl + 512:sl + 1024], src_region(c), AF.Square)
                tok_zc[c] = C.tok(ins)
            sec(C, c_stats)

            def t_stats(tt, c=c):
                T.wait(tok_zc[c])
                T.wait(g_cstG.tok())
                if c == 0 and state["statfree"] is not None:
                    T.wait(state["statfree"])
                sl = (c % 2) * 1024
                tt.matmul(PC[0:1, 0:512], ONESC[:], SBF[:, sl:sl + 512],
                          start=(c == 0), stop=(c == NCH - 1))
                ins = tt.matmul(PC[0:1, 512:1024], ONESC[:], SBF[:, sl + 512:sl + 1024],
                                start=(c == 0), stop=(c == NCH - 1))
                tok_st[c] = T.tok(ins)
            sec(T, t_stats)
            zslot_war[c % 2] = tok_st[c]

        tok_var = {}
        eps_val = EPS_IN if eps_col == 41 else EPS_OUT
        def v_rows(v):
            V.wait(tok_st[NCH - 1])
            v.tensor_scalar(LNR[0:1, 0:TSUB], PC[0:1, 0:512], 1.0 / D, None, ALU.mult)
            t2 = V.tok(v.tensor_scalar(LNR[0:1, TSUB:2 * TSUB], PC[0:1, 512:1024], 1.0 / D, None, ALU.mult))
            V.wait(t2)
            t3 = V.tok(v.tensor_mul(LNR[0:1, 2 * TSUB:3 * TSUB], LNR[0:1, 0:TSUB], LNR[0:1, 0:TSUB]))
            V.wait(t3)
            tok_var[0] = V.tok(v.scalar_tensor_tensor(
                LNR[0:1, TSUB:2 * TSUB], LNR[0:1, TSUB:2 * TSUB], float(eps_val),
                LNR[0:1, 2 * TSUB:3 * TSUB], ALU.add, ALU.subtract))
        sec(V, v_rows)
        state["statfree"] = tok_var[0]

        tok_rstd = {}
        r1 = LNR[0:1, TSUB:2 * TSUB]
        r3 = LNR[0:1, 3 * TSUB:4 * TSUB]
        r4 = LNR[0:1, 4 * TSUB:5 * TSUB]

        def v_rstd(v):
            def stepi(ins):
                tk = V.tok(ins)
                V.wait(tk)
                return tk
            V.wait(tok_var[0])
            stepi(v.tensor_scalar(r3.bitcast(I32), r1.bitcast(I32), 1, None,
                                  ALU.logical_shift_right))
            stepi(v.tensor_scalar(r3.bitcast(I32), r3.bitcast(I32), -1, 0x5F3759DF,
                                  ALU.mult, ALU.add))
            for _ in range(2):
                stepi(v.tensor_mul(r4, r3, r3))
                stepi(v.tensor_mul(r4, r4, r1))
                stepi(v.tensor_scalar(r4, r4, -0.5, 1.5, ALU.mult, ALU.add))
                tok_rstd[0] = stepi(v.tensor_mul(r3, r3, r4))
        sec(V, v_rstd)

        murstd_bf = LNR[0:1, 5 * TSUB:6 * TSUB].bitcast(BF16)[0:1, 0:TSUB]
        tok_mr = {}
        def v_mr(v):
            V.wait(tok_rstd[0])
            tok_mr[0] = V.tok(v.tensor_mul(murstd_bf, LNR[0:1, 0:TSUB], r3))
        sec(V, v_mr)

        tok_sb = {}
        def t_sb(tt):
            T.wait(tok_rstd[0])
            if state["lnlast"] is not None:
                T.wait(state["lnlast"])
            tok_sb[0] = T.tok(tt.matmul(PD[:, 0:512], ONESR[0:1, 0:128], r3,
                                        start=True, stop=True))
        sec(T, t_sb)

        cast_toks = {}
        pool_toks = {}
        tok_mg = {}
        tok_app = {}
        mgb = [PD[:, 512:1024], PB[:, 0:512]]
        for c in range(NCH):
            def t_mg(tt, c=c):
                if c == 0:
                    T.wait(tok_mr[0])
                if c >= 2:
                    T.wait(tok_app[c - 2])
                if g_row_off is not None:
                    lhs = ROWS[0:1, g_row_off + c * 128: g_row_off + (c + 1) * 128]
                else:
                    lhs = ONESRB[0:1, 0:128]
                tok_mg[c] = T.tok(tt.matmul(mgb[c % 2], lhs, murstd_bf,
                                            start=True, stop=True))
            sec(T, t_mg)

            def v_app(v, c=c):
                V.wait(tok_sb[0])
                V.wait(tok_mg[c])
                if g_slot is not None:
                    t1 = V.tok(v.scalar_tensor_tensor(
                        src_region(c), src_region(c),
                        BIA[:, g_slot * 8 + c:g_slot * 8 + c + 1],
                        PD[:, 0:512], ALU.mult, ALU.mult))
                else:
                    t1 = V.tok(v.scalar_tensor_tensor(
                        src_region(c), src_region(c), 1.0,
                        PD[:, 0:512], ALU.mult, ALU.mult))
                V.wait(t1)
                if be_slot is not None:
                    ins = v.scalar_tensor_tensor(
                        dst_region(c), src_region(c),
                        BIA[:, be_slot * 8 + c:be_slot * 8 + c + 1],
                        mgb[c % 2], ALU.add, ALU.subtract)
                else:
                    ins = v.scalar_tensor_tensor(
                        dst_region(c), src_region(c), 0.0,
                        mgb[c % 2], ALU.add, ALU.subtract)
                tok_app[c] = V.tok(ins)
            sec(V, v_app)

            if out_cast is not None:
                def c_cast(cc, c=c):
                    C.wait(tok_app[c])
                    cast_toks[c] = C.tok(cc.copy(out_cast(c), dst_region(c)))
                sec(C, c_cast)
            if pool_sub is not None:
                def v_pool(v, c=c):
                    V.wait(tok_app[c])
                    ins = v.reduce_sum(POOL[:, c * 4 + pool_sub:c * 4 + pool_sub + 1],
                                       dst_region(c), axis=AX.X)
                    pool_toks[c] = V.tok(ins)
                sec(V, v_pool)
        state["lnlast"] = tok_app[NCH - 1]
        return tok_app, cast_toks, pool_toks

    def ff_block(in_wait_toks, in_cols, b1col0, seed_region, dst_region,
                 scale, scale_wait, w1_src_fn, w2_src_fn, b2_row_off, wslot_key,
                 slot_extra=None, dst_wait=None, fs_wait=None, ld_eng=None):
        ev_toks = {}
        if stopped[0]:
            return {(NFS - 1, 0): None, (NFS - 1, 1): None}
        for fs in range(NFS):
            slot = fs % 2
            w1v = (WFS if slot == 0 else B2)[:, 0:8 * FSW]
            w2v = (WFS if slot == 0 else B2)[:, 8 * FSW:16 * FSW]
            if ld_eng is None:
                grp = g_wfsA if slot == 0 else g_wfsB
            else:
                grp = g_wfsYA if slot == 0 else g_wfsYB

            def g_wload(g, fs=fs, slot=slot, grp=grp, w1v=w1v, w2v=w2v):
                if state["wslot"].get((wslot_key, slot)) is not None:
                    G.wait(state["wslot"][(wslot_key, slot)])
                if slot_extra:
                    for tk in slot_extra.get(slot, ()):
                        G.wait(tk)
                if fs_wait is not None:
                    G.wait(fs_wait(fs))
                w1_src = w1_src_fn()
                w2_src = w2_src_fn()
                for hh in range(2):
                    grp.start(G, w1v.rearrange("p (k c) -> p k c", k=8)[:, hh * 4:(hh + 1) * 4, :],
                              w1_src.rearrange("(k p) c -> p k c", p=128)[:, hh * 4:(hh + 1) * 4,
                                                                          fs * FSW:(fs + 1) * FSW])
                for hh in range(2):
                    grp.start(G, w2v.rearrange("p (f c) -> p f c", f=8)[:, hh * 4:(hh + 1) * 4, :],
                              w2_src.rearrange("(f p) c -> p f c", p=128)[:, fs * 8 + hh * 4: fs * 8 + (hh + 1) * 4, :])
            sec(G, g_wload)
            ld_tok = (grp.sem, grp.count)

            h_toks = {}
            ffstage = int(os.environ.get("KERN_FFSTAGE", "9"))
            for f in range(8 if ffstage >= 1 else 0):
                fg = fs * 8 + f

                def t_w1(tt, f=f, ld_tok=ld_tok, w1v=w1v):
                    T.wait(ld_tok)
                    if state["hpw"][f % 2] is not None:
                        T.wait(state["hpw"][f % 2])
                    ph = PA[:, (f % 2) * 512:(f % 2) * 512 + 512]
                    for k in range(NCH):
                        if in_wait_toks:
                            T.wait(in_wait_toks[k])
                        ins = tt.matmul(ph, w1v[:, k * FSW + f * 128: k * FSW + (f + 1) * 128],
                                        in_cols(k), start=(k == 0), stop=(k == NCH - 1))
                    h_toks[(f, "mm")] = T.tok(ins)
                sec(T, t_w1)

                if ffstage < 2:
                    h_toks[f] = h_toks[(f, "mm")]
                    continue

                def c_gelu(cc, f=f, fg=fg):
                    C.wait(h_toks[(f, "mm")])
                    if state["hread"].get(f) is not None:
                        C.wait(state["hread"][f])
                    ph = PA[:, (f % 2) * 512:(f % 2) * 512 + 512]
                    ins = cc.activation(SBF[:, f * 512:(f + 1) * 512], ph, AF.Gelu,
                                        bias=B1F[:, b1col0 + fg:b1col0 + fg + 1])
                    h_toks[f] = C.tok(ins)
                    state["hpw"][f % 2] = h_toks[f]
                sec(C, c_gelu)

            accv = [PC[:, 0:512], PC[:, 512:1024], PD[:, 0:512], PD[:, 512:1024]]
            tok_w2 = {}
            for oh in range(2 if ffstage >= 3 else 0):
                for f in range(8):
                    def t_w2(tt, f=f, oh=oh, fs=fs, w2v=w2v):
                        T.wait(h_toks[f])
                        if f == 0 and oh == 0:
                            if state["accfree"] is not None:
                                T.wait(state["accfree"])
                            if state["statfree"] is not None:
                                T.wait(state["statfree"])
                            if state["lnlast"] is not None:
                                T.wait(state["lnlast"])
                        if f == 0 and oh == 1:
                            T.wait(ev_toks[(fs, 0)])
                        last_f = (f == 7)
                        fold = (b2_row_off is not None and fs == 0
                                and not os.environ.get("KERN_NOFOLD"))
                        for ot in range(4):
                            og = oh * 4 + ot
                            ins = tt.matmul(accv[ot],
                                            w2v[:, f * 1024 + og * 128: f * 1024 + (og + 1) * 128],
                                            SBF[:, f * 512:(f + 1) * 512],
                                            start=(f == 0), stop=(last_f and not fold))
                        if last_f and fold:
                            for ot in range(4):
                                og = oh * 4 + ot
                                ins = tt.matmul(
                                    accv[ot],
                                    ROWS[0:1, b2_row_off + og * 128: b2_row_off + (og + 1) * 128],
                                    ONESRB[0:1, 0:512], start=False, stop=True)
                        tok_w2[(oh, f)] = T.tok(ins)
                        state["hread"][f] = tok_w2[(oh, f)]
                    sec(T, t_w2)

                if ffstage < 4:
                    ev_toks[(fs, oh)] = tok_w2[(oh, 7)]
                    continue

                def v_acc(v, oh=oh, fs=fs):
                    V.wait(tok_w2[(oh, 7)])
                    if scale_wait is not None:
                        V.wait(scale_wait)
                    if dst_wait is not None:
                        V.wait(dst_wait)
                    last = None
                    for ot in range(4):
                        oc = oh * 4 + ot
                        if fs == 0 and (seed_region is not None):
                            last = v.scalar_tensor_tensor(dst_region(oc), accv[ot], scale,
                                                          seed_region(oc), ALU.mult, ALU.add)
                        elif os.environ.get("KERN_ACC_COPY"):
                            last = v.tensor_copy(dst_region(oc), accv[ot])
                        elif os.environ.get("KERN_ACC_SCRATCH"):
                            last = v.scalar_tensor_tensor(X2S[:, oc * TSUB:(oc + 1) * TSUB],
                                                          accv[ot], scale,
                                                          dst_region(oc), ALU.mult, ALU.add)
                        else:
                            last = v.scalar_tensor_tensor(dst_region(oc), accv[ot], scale,
                                                          dst_region(oc), ALU.mult, ALU.add)
                    ev_toks[(fs, oh)] = V.tok(last)
                sec(V, v_acc)
            state["accfree"] = ev_toks.get((fs, 1))
            state["wslot"][(wslot_key, slot)] = (tok_w2.get((1, 7)) or h_toks.get(7)
                                                 or h_toks.get((7, "mm")) or ld_tok)
        if not ev_toks:
            ev_toks = {(NFS - 1, 0): None, (NFS - 1, 1): None}
        for fs2 in range(NFS):
            for oh2 in range(2):
                ev_toks.setdefault((fs2, oh2), None)
        return ev_toks

    tok_x2cast = {}
    tok_pool_all = {}
    tok_zo_prev = {0: None}
    tok_xs = {}
    for sub in range(NSUB):
        slo = sub * TSUB

        def y_xs(y, sub=sub, slo=slo):
            Y.wait(g_xfm[0].tok())
            Y.wait(g_xfm[1].tok())
            Y.wait(tok_xbf[TCH - 1])
            if tok_zo_prev[0] is not None:
                Y.wait(tok_zo_prev[0])
            g_xs.start(Y, XS[:].rearrange("p (c n) -> p c n", c=NCH),
                       xfm_d[:, :, slo:slo + TSUB].rearrange("c p n -> p c n"))
            tok_xs[sub] = g_xs.tok()
        sec(Y, y_xs)
        CHK("xs%d" % sub)

        tok_wo = {}
        for e in range(NCH):
            slot = e % 2
            grp = g_wroA if slot == 0 else g_wroB

            def g_wo(g, e=e, slot=slot, grp=grp):
                if state["woslot"][slot] is not None:
                    G.wait(state["woslot"][slot])
                else:
                    G.wait(tok_ctx_done)
                grp.start(G, wro_slots[slot], Wo_h[e * 128:(e + 1) * 128, :])
            sec(G, g_wo)
            ld_tok = (grp.sem, grp.count)

            def t_wo(tt, e=e, slot=slot, ld_tok=ld_tok, slo=slo):
                T.wait(ld_tok)
                if e == 0:
                    if state["accfree"] is not None:
                        T.wait(state["accfree"])
                    if state["statfree"] is not None:
                        T.wait(state["statfree"])
                    if state["lnlast"] is not None:
                        T.wait(state["lnlast"])
                    T.wait(tok_ctx_done)
                for o in range(NCH):
                    pbank = [PA, PB, PC, PD][o // 2][:, (o % 2) * 512:(o % 2) * 512 + 512]
                    ins = tt.matmul(pbank, wro_slots[slot][:, o * 128:(o + 1) * 128],
                                    b1v(e, slo, TSUB), start=(e == 0), stop=False)
                tok_wo[e] = T.tok(ins)
                state["woslot"][slot] = tok_wo[e]
            sec(T, t_wo)

        def t_wof(tt, sub=sub):
            T.wait(g_rows2.tok())
            for o in range(NCH):
                pbank = [PA, PB, PC, PD][o // 2][:, (o % 2) * 512:(o % 2) * 512 + 512]
                ins = tt.matmul(pbank, ROWS[0:1, RO_S0 + o * 128: RO_S0 + (o + 1) * 128],
                                ONESRB[0:1, 0:512], start=False, stop=True)
            tok_wo["fold"] = T.tok(ins)
        sec(T, t_wof)
        CHK("wo%d" % sub)

        tok_z1 = {}
        def v_z1(v, sub=sub, slo=slo):
            V.wait(tok_wo["fold"])
            V.wait(tok_xs[sub])
            for o in range(NCH):
                pbank = [PA, PB, PC, PD][o // 2][:, (o % 2) * 512:(o % 2) * 512 + 512]
                ins = v.scalar_tensor_tensor(rsv(o), pbank, 1.0,
                                             xsv(o), ALU.mult, ALU.add)
                tok_z1[o] = V.tok(ins)
        sec(V, v_z1)
        CHK("z1%d" % sub)
        state["accfree"] = tok_z1.get(NCH - 1)
        state["statfree"] = tok_z1.get(NCH - 1)

        if "d_z1" in dbg and sub == 0:
            def y_dz1(y):
                Y.wait(tok_z1[NCH - 1])
                g_dbg.start(Y, dbg["d_z1"][:], RS[:])
            sec(Y, y_dz1)

        app1, cast1, _ = ln_block(
            rsv, rsv, S_G1, S_BE1, RO_G1, 41,
            pre_tok=lambda c: tok_z1.get(c),
            out_cast=lambda c, slo=slo: b1v(c, slo, TSUB))
        CHK("ln1%d" % sub)

        if "d_attn" in dbg and sub == 0:
            def y_dat(y):
                Y.wait(app1[NCH - 1])
                g_dbg.start(Y, dbg["d_attn"][:], RS[:])
            sec(Y, y_dat)

        ff1 = ff_block(cast1, lambda k, slo=slo: b1v(k, slo, TSUB), 0,
                       rsv, rtv, 1.0, None, (lambda: W1_h), (lambda: W2_h), RO_S1, "ff",
                       slot_extra=({0: [tok_kmm.get(TCH - 1)],
                                    1: [tok_ctx_done]} if sub == 0 else None))
        CHK("ff%d" % sub)

        app2, _, _ = ln_block(
            rtv, rtv, S_G2, S_BE2, RO_G2, 41,
            pre_tok=lambda c: ff1.get((NFS - 1, 1)))
        CHK("ln2%d" % sub)

        tok_zo = {}
        def v_zo(v, sub=sub, slo=slo):
            for c in range(NCH):
                V.wait(app2[c])
                ins = v.tensor_add(rtv(c), rtv(c), xsv(c))
                tok_zo[c] = V.tok(ins)
        sec(V, v_zo)
        tok_zo_prev[0] = tok_zo.get(NCH - 1)

        app3, cast3, pools = ln_block(
            rtv, rtv, None, None, None, 42,
            pre_tok=lambda c: tok_zo[c],
            out_cast=lambda c, slo=slo: b1v(c, slo, TSUB),
            pool_sub=sub)
        for c in range(NCH):
            tok_x2cast[(sub, c)] = cast3.get(c)
            tok_pool_all[(sub, c)] = pools.get(c)
        CHK("p2s%d" % sub)

    if "d_x2b" in dbg:
        def y_dx2(y):
            for c in range(NCH):
                Y.wait(tok_x2cast[(NSUB - 1, c)])
            g_dbg.start(Y, dbg["d_x2b"][:], B1[:])
        sec(Y, y_dx2)

    CHK("p2")

    tok_lgT = {}
    def t_lg(tt):
        for sub in range(NSUB):
            for c in range(NCH):
                T.wait(tok_pool_all[(sub, c)])
        T.wait(state["statfree"])
        n = 0
        for c in range(NCH):
            for sub in range(NSUB):
                ins = tt.matmul(PB[0:1, 0:8], POOL[:, c * 4 + sub:c * 4 + sub + 1],
                                WRFM[:, c * NE:(c + 1) * NE],
                                start=(n == 0), stop=(n == 31))
                n += 1
        tok_lgT[0] = T.tok(ins)
    sec(T, t_lg)

    tok_rtr = {}
    def v_lg(v):
        V.wait(tok_lgT[0])
        tok_rtr[0] = V.tok(v.tensor_scalar(RTR[0:1, 0:8], PB[0:1, 0:8],
                                           1.0 / S, None, ALU.mult))
    sec(V, v_lg)

    def g_ar2(g):
        G.wait(tok_rtr[0])
        g_lg.start(G, cc2i.ap(), RTR[0:1, 0:8])
        g.wait_ge(g_lg.sem, g_lg.count)
        g.collective_compute("AllReduce", ALU.add,
                             replica_groups=[list(range(NCORES))],
                             ins=[cc2i.ap()], outs=[cc2o.ap()]).then_inc(cc_sem, 1)
        g.wait_ge(cc_sem, 2)
        g_lg.start(G, RTR[0:1, 0:8], cc2o.ap())
    sec(G, g_ar2)

    tok_rt2 = {}
    def v_rt(v):
        V.wait(g_lg.tok())
        V.wait(g_const.tok())
        def step(ins):
            tk = V.tok(ins)
            V.wait(tk)
            return tk
        step(v.reduce_max(RTR[0:1, 32:33], RTR[0:1, 0:8], axis=AX.X))
        step(v.tensor_tensor(RTR[0:1, 8:16], RTR[0:1, 0:8],
                             RTR[0:1, 32:33].broadcast_to([1, 8]), ALU.is_ge))
        step(v.tensor_scalar(RTR[0:1, 16:24], RTR[0:1, 8:16], -1e30, None, ALU.mult))
        step(v.tensor_add(RTR[0:1, 16:24], RTR[0:1, 16:24], RTR[0:1, 0:8]))
        step(v.reduce_max(RTR[0:1, 33:34], RTR[0:1, 16:24], axis=AX.X))
        step(v.tensor_tensor(RTR[0:1, 16:24], RTR[0:1, 16:24],
                             RTR[0:1, 33:34].broadcast_to([1, 8]), ALU.is_ge))
        step(v.tensor_mul(RTR[0:1, 24:32], RTR[0:1, 8:16], IOTA8))
        step(v.reduce_sum(RTR[0:1, 38:39], RTR[0:1, 24:32], axis=AX.X))
        step(v.tensor_mul(RTR[0:1, 24:32], RTR[0:1, 16:24], IOTA8))
        step(v.reduce_sum(RTR[0:1, 39:40], RTR[0:1, 24:32], axis=AX.X))
        ins = v.tensor_sub(RTR[0:1, 34:35], RTR[0:1, 33:34], RTR[0:1, 32:33])
        tok_rt2[0] = V.tok(ins)
    sec(V, v_rt)

    def v_rtexp(v):
        V.wait(tok_rt2[0])
        u = RTR[0:1, 40:41]
        p = RTR[0:1, 35:36]
        def stepi(ins):
            tk = V.tok(ins)
            V.wait(tk)
            return tk
        stepi(v.tensor_scalar(u, RTR[0:1, 34:35], -24.0, None, ALU.max))
        stepi(v.tensor_scalar(u, u, 1.0 / 64.0, None, ALU.mult))
        stepi(v.tensor_scalar(p, u, 0.25, 1.0, ALU.mult, ALU.add))
        stepi(v.tensor_mul(p, p, u))
        stepi(v.tensor_scalar(p, p, 1.0 / 3.0, 1.0, ALU.mult, ALU.add))
        stepi(v.tensor_mul(p, p, u))
        stepi(v.tensor_scalar(p, p, 0.5, 1.0, ALU.mult, ALU.add))
        stepi(v.tensor_mul(p, p, u))
        stepi(v.tensor_scalar(p, p, 1.0, 1.0, ALU.mult, ALU.add))
        for _ in range(5):
            stepi(v.tensor_mul(p, p, p))
        tok_rt2[1] = V.tok(v.tensor_mul(p, p, p))
    sec(V, v_rtexp)

    def v_rt2(v):
        V.wait(tok_rt2[1])
        def step(ins):
            tk = V.tok(ins)
            V.wait(tk)
            return tk
        step(v.tensor_scalar(RTR[0:1, 36:37], RTR[0:1, 35:36], 1.0, None, ALU.add))
        step(v.reciprocal(RTR[0:1, 36:37], RTR[0:1, 36:37]))
        ins = v.tensor_scalar(RTR[0:1, 37:38], RTR[0:1, 36:37], -1.0, 1.0,
                              ALU.mult, ALU.add)
        tok_rt2[2] = V.tok(ins)
    sec(V, v_rt2)

    if "d_lg" in dbg:
        def y_dlg(y):
            Y.wait(tok_rt2[2])
            g_dbg.start(Y, dbg["d_lg"][:], RTR[0:1, :])
        sec(Y, y_dlg)

    tok_web = {}
    def t_web(tt):
        T.wait(tok_rt2[2])
        tt.matmul(PB[:, 0:1], ONESR[0:1, 0:128], RTR[0:1, 36:37], start=True, stop=True)
        ins = tt.matmul(PB[:, 512:513], ONESR[0:1, 0:128], RTR[0:1, 37:38],
                        start=True, stop=True)
        tok_web[0] = T.tok(ins)
    sec(T, t_web)

    def v_web(v):
        V.wait(tok_web[0])
        v.tensor_copy(WEB[:, 0:1], PB[:, 0:1])
        tok_web[1] = V.tok(v.tensor_copy(WEB[:, 1:2], PB[:, 512:513]))
    sec(V, v_web)

    tok_rti = {}
    def c_rti(cc):
        C.wait(tok_rt2[2])
        tok_rti[0] = C.tok(cc.copy(RTI[0:1, 0:2], RTR[0:1, 38:40]))
    sec(C, c_rti)

    CHK("router")

    iv = [None, None]
    def g_snap(g):
        G.wait(tok_rti[0])
        g.reg_load(reg0, RTI[0:1, 0:1])
        g.reg_load(reg1, RTI[0:1, 1:2])
        iv[0] = g.snap(reg0, min_val=0, max_val=NE - 1)
        iv[1] = g.snap(reg1, min_val=0, max_val=NE - 1)
    sec(G, g_snap)



    tok_moeb = {}
    for es in range(2):
        def g_mb(g, es=es):
            if es == 0:
                G.wait(state["woslot"][0])
                G.wait(state["woslot"][1])
                G.wait(state["wslot"][("ff", 0)])
                G.wait(state["wslot"][("ff", 1)])
            else:
                G.wait(tok_moeb[(0, "v")])
            roff = RO_S0 if es == 0 else RO_S1
            g_moeb.start(G, ROWS[0:1, roff:roff + D],
                         b2m_h[iv[es]].rearrange("d -> () d"))
            g_moeb.start(G, BRX[:, :], b1m_h[iv[es]].rearrange("(a b) -> a b", b=128))
        sec(G, g_mb)
        ld = (g_moeb.sem, (es + 1) * 32)

        def t_mb(tt, es=es, ld=ld):
            T.wait(ld)
            T.wait(tok_web[1])
            if state["statfree"] is not None:
                T.wait(state["statfree"])
            ins = tt.matmul(PB[:, 0:32] if es == 0 else PB[:, 512:544],
                            BRX[0:32, :], IDN[0:32, 0:32], start=True, stop=True)
            tok_moeb[(es, "t")] = T.tok(ins)
        sec(T, t_mb)

        def v_mb(v, es=es):
            V.wait(tok_moeb[(es, "t")])
            ins = v.tensor_copy(B1F[:, 32 + es * 32:64 + es * 32],
                                PB[:, 0:32] if es == 0 else PB[:, 512:544])
            tok_moeb[(es, "v")] = V.tok(ins)
        sec(V, v_mb)

    CHK("moeb")

    def g_wselc(g):
        for es in range(2):
            for fs in range(NFS):
                grp = g_wsel[es * NFS + fs]
                grp.start(G, W1sel[es].ap()[:, fs * FSW:(fs + 1) * FSW],
                          W1m_h[iv[es]][:, fs * FSW:(fs + 1) * FSW])
                grp.start(G, W2sel[es].ap()[fs * FSW:(fs + 1) * FSW, :],
                          W2m_h[iv[es]][fs * FSW:(fs + 1) * FSW, :])
    sec(G, g_wselc)

    tok_otr = {}
    tok_oev = {}
    tok_upc = {}
    for sub in range(NSUB):
        slo = sub * TSUB

        def c_upc(cc, sub=sub, slo=slo):
            for c in range(NCH):
                C.wait(tok_x2cast[(sub, c)])
            if (sub - 1, "lnfree") in tok_upc:
                C.wait(tok_upc[(sub - 1, "lnfree")])
            ins = cc.copy(X2S[:].rearrange("p (c n) -> p c n", c=NCH),
                          B1[:].rearrange("p (c n) -> p c n", c=NCH)[:, :, slo:slo + TSUB])
            tok_upc[sub] = C.tok(ins)
        sec(C, c_upc)
        upc = tok_upc.get(sub)

        moe_last = None
        for es in range(2):
            in_toks = {c: tok_x2cast.get((sub, c)) for c in range(NCH)}
            ev = ff_block(in_toks, lambda k, slo=slo: b1v(k, slo, TSUB),
                          32 + es * 32,
                          None, x2sv,
                          WEB[:, es:es + 1], tok_web.get(1),
                          (lambda es=es: W1sel[es].ap()), (lambda es=es: W2sel[es].ap()),
                          (RO_S0 if es == 0 else RO_S1), "ff",
                          slot_extra={0: [tok_moeb.get((es, "v"))],
                                      1: [tok_moeb.get((es, "v"))]},
                          dst_wait=upc,
                          fs_wait=(lambda fs, es=es: (g_wsel[es * NFS + fs].sem, 32)),
                          ld_eng=Y)
            moe_last = ev

        appF, _, _ = ln_block(
            x2sv, rsv, None, None, None, 42,
            pre_tok=lambda c, ev=moe_last: ev[(NFS - 1, 1)])
        tok_upc[(sub, "lnfree")] = appF.get(NCH - 1)

        for tcq in range(TSUB // 128):
            tglob = sub * 4 + tcq
            pw = PA if tcq % 2 == 0 else PB

            def t_otr(tt, sub=sub, tcq=tcq, pw=pw, appF=appF):
                for c in range(NCH):
                    T.wait(appF[c])
                if (sub, tcq - 2) in tok_oev:
                    T.wait(tok_oev[(sub, tcq - 2)])
                if tcq < 2 and (sub - 1, tcq + 2) in tok_oev:
                    T.wait(tok_oev[(sub - 1, tcq + 2)])
                for c in range(NCH):
                    ins = tt.matmul(pw[:, c * 128:(c + 1) * 128],
                                    RS[:, c * TSUB + tcq * 128: c * TSUB + (tcq + 1) * 128],
                                    IDN[:], start=True, stop=True)
                tok_otr[(sub, tcq)] = T.tok(ins)
            sec(T, t_otr)

            def c_oev(cc, sub=sub, tcq=tcq, pw=pw):
                C.wait(tok_otr[(sub, tcq)])
                if sub >= 1:
                    C.wait((g_out[tcq].sem, sub * 16))
                ins = cc.copy(RT[:, tcq * 1024:(tcq + 1) * 1024], pw[:])
                tok_oev[(sub, tcq)] = C.tok(ins)
            sec(C, c_oev)

            def y_ost(y, sub=sub, tcq=tcq, tglob=tglob):
                Y.wait(tok_oev[(sub, tcq)])
                g_out[tcq].start(Y, out_h[tglob * 128:(tglob + 1) * 128, :],
                                 RT[:, tcq * 1024:(tcq + 1) * 1024])
            sec(Y, y_ost)
        if (sub, 3) in tok_oev:
            state["hpw"][0] = tok_oev[(sub, 3)]
            state["hpw"][1] = tok_oev[(sub, 3)]
        CHK("p3s%d" % sub)

    def y_fin(y):
        for i in range(4):
            Y.wait(g_out[i].tok())
        if g_dbg.count:
            Y.wait(g_dbg.tok())
    sec(Y, y_fin)

    st.close()
    return nc


_NC_CACHE = {}


def kernel(**inputs):
    if "main" not in _NC_CACHE:
        _NC_CACHE["main"] = build()
    nc = _NC_CACHE["main"]
    x = np.ascontiguousarray(np.asarray(inputs["L0_attn_x"], dtype=np.float32)[0])
    base = {
        "Wq": inputs["L0_attn_Wq"], "bq": inputs["L0_attn_bq"],
        "Wk": inputs["L0_attn_Wk"], "bk": inputs["L0_attn_bk"],
        "Wv": inputs["L0_attn_Wv"], "bv": inputs["L0_attn_bv"],
        "Wo": inputs["L0_attn_Wo"], "bo": inputs["L0_attn_bo"],
        "g1": inputs["L0_attn_g1"], "be1": inputs["L0_attn_be1"],
        "W1": inputs["L0_attn_W1"], "b1": inputs["L0_attn_b1"],
        "W2": inputs["L0_attn_W2"], "b2": inputs["L0_attn_b2"],
        "g2": inputs["L0_attn_g2"], "be2": inputs["L0_attn_be2"],
        "Wr": inputs["L0_ffn_Wr"], "W1m": inputs["L0_ffn_W1"],
        "b1m": inputs["L0_ffn_b1"], "W2m": inputs["L0_ffn_W2"],
        "b2m": inputs["L0_ffn_b2"],
    }
    base = {k: np.ascontiguousarray(np.asarray(v, dtype=np.float32)) for k, v in base.items()}
    in_maps = []
    for i in range(NCORES):
        m = dict(base)
        m["x"] = np.ascontiguousarray(x[i * TC:(i + 1) * TC, :])
        in_maps.append(m)
    res = run_bass_kernel_spmd(nc, in_maps, core_ids=list(range(NCORES)))
    out = np.concatenate([np.asarray(r["out"]) for r in res.results], axis=0)
    return out.reshape(1, S, D)
